# version 21
# baseline (speedup 1.0000x reference)
import numpy as np
from contextlib import ExitStack

import concourse.bass as bass
import concourse.mybir as mybir
from concourse.bass_utils import run_bass_kernel_spmd

F32 = mybir.dt.float32
BF16 = mybir.dt.bfloat16
AF = mybir.ActivationFunctionType
ALU = mybir.AluOpType
AX = mybir.AxisListType

D = 1024
SEQ = 2048
NB = 2
NT = 16
EPS = 1e-5
NCORES = 8
ENGINES = ("sync", "gpsimd", "scalar", "vector", "tensor")


class Sched:

    def __init__(self):
        self.instrs = []
        self.acc = {}
        self._dma_count = {e: 0 for e in ENGINES}

    @staticmethod
    def r(name, lo=0, hi=1 << 60):
        return (name, lo, hi)

    def _overlap(self, key, want_writes):
        name, lo, hi = key
        out = []
        for (l, h, iid, w) in self.acc.get(name, ()):
            if w == want_writes and l < hi and lo < h:
                out.append(iid)
        return out

    def ins(self, engine, fn, reads=(), writes=(), dma=False, tag=""):
        assert engine in ENGINES
        deps = set()
        for k in reads:
            deps.update(self._overlap(k, True))
        for k in writes:
            deps.update(self._overlap(k, True))
            deps.update(self._overlap(k, False))
        iid = len(self.instrs)
        lane = self._dma_count[engine] % self.NDMA_LANES[engine] if dma else None

        def keep(d):
            dd = self.instrs[d]
            de = dd["engine"]
            if de != engine:
                return True
            if dd["dma"] and dma:
                return False
            if engine == "tensor":
                return False
            return True
        deps = {d for d in deps if keep(d)}
        best = {}
        for d in deps:
            dd = self.instrs[d]
            ch = (dd["engine"], dd["lane"]) if dd["dma"] else (dd["engine"],)
            if best.get(ch, -1) < d:
                best[ch] = d
        deps = set(best.values())
        self.instrs.append(dict(engine=engine, fn=fn, deps=deps, dma=dma,
                                lane=lane, tag=tag))
        if dma:
            self._dma_count[engine] += 1
        for k in reads:
            self.acc.setdefault(k[0], []).append((k[1], k[2], iid, False))
        for k in writes:
            self.acc.setdefault(k[0], []).append((k[1], k[2], iid, True))
        return iid

    NDMA_LANES = {"sync": 16, "gpsimd": 12, "scalar": 8, "vector": 4,
                  "tensor": 2}

    def emit(self, nc, block, sems, dsems, final_waits=("gpsimd",)):
        needed = set()
        for ins in self.instrs:
            needed.update(ins["deps"])
        cum = {}
        chan = {}
        val = {}
        for i, ins in enumerate(self.instrs):
            if ins["dma"]:
                ch = (ins["engine"], ins["lane"])
                cum[ch] = cum.get(ch, 0) + 16
                chan[i] = ch
                val[i] = cum[ch]
                needed.add(i)
            else:
                ch = (ins["engine"],)
                if i in needed:
                    cum[ch] = cum.get(ch, 0) + 1
                chan[i] = ch
                val[i] = cum.get(ch, 0)
        cum2 = {}
        for i, ins in enumerate(self.instrs):
            if not ins["dma"]:
                ch = chan[i]
                if i in needed:
                    cum2[ch] = cum2.get(ch, 0) + 1
                val[i] = cum2.get(ch, 0)
        order = {e: [i for i, ins in enumerate(self.instrs) if ins["engine"] == e]
                 for e in ENGINES}

        def hnd(ch):
            return dsems[ch[0]][ch[1]] if len(ch) == 2 else sems[ch[0]]

        totals = dict(cum)
        totals.update(cum2)

        def section(e):
            def body(eng):
                waited = {}
                for iid in order[e]:
                    ins = self.instrs[iid]
                    for d in sorted(ins["deps"], key=lambda d: val[d]):
                        dch = chan[d]
                        v = val[d]
                        assert v > 0, (d, iid)
                        if waited.get(dch, 0) < v:
                            eng.wait_ge(hnd(dch), v)
                            waited[dch] = v
                    bi = ins["fn"](eng)
                    if iid in needed:
                        ch = chan[iid]
                        bi.then_inc(hnd(ch), 16 if ins["dma"] else 1)
                if e in final_waits:
                    for ch, tot in totals.items():
                        if len(ch) == 2 and ch[0] == e and tot:
                            eng.wait_ge(hnd(ch), tot)
            return body

        block.sync(section("sync"))
        block.gpsimd(section("gpsimd"))
        block.scalar(section("scalar"))
        block.vector(section("vector"))
        block.tensor(section("tensor"))
        return totals


def build_nc(debug=False):
    nc = bass.Bass("TRN2", target_bir_lowering=False)
    P = Sched()
    R = Sched.r

    xs = nc.dram_tensor("xs", [NB, SEQ, D], F32, kind="ExternalInput").ap()
    wqkv_d = nc.dram_tensor("wqkv", [3 * D, D], F32, kind="ExternalInput").ap()
    bqkv_d = nc.dram_tensor("bqkv", [3 * D], F32, kind="ExternalInput").ap()
    gnw_d = nc.dram_tensor("gnw", [D], F32, kind="ExternalInput").ap()
    gnb_d = nc.dram_tensor("gnb", [D], F32, kind="ExternalInput").ap()
    outw_d = nc.dram_tensor("outw", [D, D], F32, kind="ExternalInput").ap()
    outb_d = nc.dram_tensor("outb", [D], F32, kind="ExternalInput").ap()
    wff_d = nc.dram_tensor("wff", [D, D], F32, kind="ExternalInput").ap()
    bff_d = nc.dram_tensor("bff", [D], F32, kind="ExternalInput").ap()
    ld_d = nc.dram_tensor("ld", [4, 100], F32, kind="ExternalInput").ap()
    lu_d = nc.dram_tensor("lu", [50, 4], F32, kind="ExternalInput").ap()
    da_d = nc.dram_tensor("da", [100, D], F32, kind="ExternalInput").ap()
    ua_d = nc.dram_tensor("ua", [D, 50], F32, kind="ExternalInput").ap()
    ys = nc.dram_tensor("ys", [NB, SEQ, D], F32, kind="ExternalOutput").ap()
    dbg_d = {}
    if debug:
        dbg_d["xn"] = nc.dram_tensor("dbg_xn", [NB, 128, 16 * D], BF16,
                                     kind="ExternalOutput").ap()
        dbg_d["X"] = nc.dram_tensor("dbg_X", [NB, 128, 8 * D], BF16,
                                    kind="ExternalOutput").ap()
        dbg_d["MT"] = nc.dram_tensor("dbg_MT", [NB, 128, D], BF16,
                                     kind="ExternalOutput").ap()
        dbg_d["CT"] = nc.dram_tensor("dbg_CT", [NB, 128, 8 * D], BF16,
                                     kind="ExternalOutput").ap()
        dbg_d["rows"] = nc.dram_tensor("dbg_rows", [NB, 1, 5 * D], F32,
                                       kind="ExternalOutput").ap()
        dbg_d["P"] = nc.dram_tensor("dbg_P", [NB, 128, 8 * D], BF16,
                                    kind="ExternalOutput").ap()
        dbg_d["weffT"] = nc.dram_tensor("dbg_weffT", [128, 8 * D], BF16,
                                        kind="ExternalOutput").ap()
        dbg_d["wqnat"] = nc.dram_tensor("dbg_wqnat", [128, 8 * D], BF16,
                                        kind="ExternalOutput").ap()
        dbg_d["lil"] = nc.dram_tensor("dbg_lil", [128, 4 * D], BF16,
                                      kind="ExternalOutput").ap()
        dbg_d["cols"] = nc.dram_tensor("dbg_cols", [NB, 128, 64], F32,
                                       kind="ExternalOutput").ap()

    with ExitStack() as st, nc.Block() as block:
        E = st.enter_context

        arena = E(nc.sbuf_tensor("arena", [128, 16 * D], BF16))
        arena2 = E(nc.sbuf_tensor("arena2", [128, 8 * D], BF16))
        wqkvT = E(nc.sbuf_tensor("wqkvT", [128, 24 * D], BF16))
        wq_nat = E(nc.sbuf_tensor("wq_nat", [128, 8 * D], BF16))
        outwT = E(nc.sbuf_tensor("outwT", [128, 8 * D], BF16))
        weffT = E(nc.sbuf_tensor("weffT", [128, 8 * D], BF16))
        stage = [E(nc.sbuf_tensor(f"stage{i}", [128, D], F32)) for i in range(3)]
        sqscr = E(nc.sbuf_tensor("sqscr", [128, D], F32))
        xnf = [E(nc.sbuf_tensor(f"xnf{i}", [128, D], BF16)) for i in range(2)]
        mblkT = E(nc.sbuf_tensor("mblkT", [128, D], BF16))
        mblk = E(nc.sbuf_tensor("mblk", [128, D], BF16))
        mblkTs = [E(nc.sbuf_tensor(f"mblkTs{i}", [128, 128], BF16)) for i in range(2)]
        lil_small = E(nc.sbuf_tensor("lil_small", [128, 128], BF16))
        ident_bf = E(nc.sbuf_tensor("ident_bf", [128, 128], BF16))
        ident_f = E(nc.sbuf_tensor("ident_f", [128, 128], F32))
        sel = E(nc.sbuf_tensor("sel", [128, 4], F32))
        selT = E(nc.sbuf_tensor("selT", [4, 128], F32))
        ones_col = E(nc.sbuf_tensor("ones_col", [128, 1], BF16))
        epscol = E(nc.sbuf_tensor("epscol", [128, 1], F32))
        rows_f = E(nc.sbuf_tensor("rows_f", [1, 5 * D], F32))
        h_sb = sqscr[:, 256:384]
        rows_b = E(nc.sbuf_tensor("rows_b", [1, 8 * D + 128], BF16))
        cols_f = E(nc.sbuf_tensor("cols_f", [128, 64], F32))
        cols_b = E(nc.sbuf_tensor("cols_b", [128, 32], BF16))
        lncols = E(nc.sbuf_tensor("lncols", [128, 32], F32))
        lnst = E(nc.sbuf_tensor("lnst", [128, 96], F32))
        st44 = E(nc.sbuf_tensor("st44", [128, 64], F32))

        RS = lambda k, o=0, n=D: (sqscr[0:1, o:o + n] if k == 0
                                  else rows_f[0:1, (k - 1) * D + o:(k - 1) * D + o + n])
        kRS = lambda k, o=0, n=D: (R("sqscr", o, o + n) if k == 0
                                   else R("rows_f", (k - 1) * D + o, (k - 1) * D + o + n))
        RB = lambda k, o=0, n=D: rows_b[0:1, k * D + o:k * D + o + n]
        kRB = lambda k, o=0, n=D: R("rows_b", k * D + o, k * D + o + n)
        ones_row = rows_b[0:1, 8 * D:8 * D + 128]
        k_ones_row = R("rows_b", 8 * D, 8 * D + 128)
        CF = lambda k, n=1: cols_f[:, k:k + n]
        kCF = lambda k, n=1: R("cols_f", k, k + n)
        CB = lambda k, n=1: cols_b[:, k:k + n]
        kCB = lambda k, n=1: R("cols_b", k, k + n)

        xn_tm = lambda i: arena[:, i * D:(i + 1) * D]
        k_xn = lambda i: R("arena", i * D, (i + 1) * D)
        t_cat = lambda f: arena[:, f * 2 * D:(f + 1) * 2 * D]
        k_tcat = lambda f: R("arena", f * 2 * D, (f + 1) * 2 * D)
        ct_sb = lambda f: arena[:, 8 * D + f * D:8 * D + (f + 1) * D]
        k_ct = lambda f: R("arena", 8 * D + f * D, 8 * D + (f + 1) * D)
        x_sb = lambda g: arena2[:, g * D:(g + 1) * D]
        k_xsb = lambda g: R("arena2", g * D, (g + 1) * D)
        p_sb = lambda e: arena2[:, e * D:(e + 1) * D]
        k_psb = k_xsb
        v_dab = xnf[0][0:100, :]
        k_dab = R("xnf0")
        v_uaT = xnf[1][0:50, :]
        k_uaT = R("xnf1")
        v_t1 = arena2[0:4, 0:D]
        k_t1 = R("arena2", 0, D)
        v_t2 = arena2[0:50, D:2 * D]
        k_t2 = R("arena2", D, 2 * D)
        v_ldT = lil_small[0:100, 0:4]
        k_ldT = R("lil_small", 0, 4)
        v_luT = lil_small[0:4, 64:114]
        k_luT = R("lil_small", 64, 114)

        def psum_key(handle):
            for alloc in nc.m.functions[0].allocations:
                mls = getattr(alloc, "memorylocations", None)
                if mls:
                    for ml in mls:
                        if ml.name == handle.name:
                            return R("PSUM", ml.addr, ml.addr + int(ml.dims[-1]))
            raise KeyError(handle.name)

        P.ins("gpsimd", lambda g: g.memset(ident_bf[:, :], 0.0),
              writes=[R("ident_bf")], tag="c0")
        P.ins("gpsimd", lambda g: g.affine_select(
            out=ident_bf[:, :], in_=ident_bf[:, :], compare_op=ALU.not_equal,
            fill=1.0, base=0, pattern=[[-1, 128]], channel_multiplier=1),
            reads=[R("ident_bf")], writes=[R("ident_bf")], tag="c1")
        P.ins("gpsimd", lambda g: g.memset(ident_f[:, :], 0.0),
              writes=[R("ident_f")])
        P.ins("gpsimd", lambda g: g.affine_select(
            out=ident_f[:, :], in_=ident_f[:, :], compare_op=ALU.not_equal,
            fill=1.0, base=0, pattern=[[-1, 128]], channel_multiplier=1),
            reads=[R("ident_f")], writes=[R("ident_f")])
        P.ins("gpsimd", lambda g: g.memset(sel[:, :], 1.0), writes=[R("sel")])
        P.ins("gpsimd", lambda g: g.affine_select(
            out=sel[:, :], in_=sel[:, :], compare_op=ALU.is_ge, fill=0.0,
            base=0, pattern=[[-32, 4]], channel_multiplier=1),
            reads=[R("sel")], writes=[R("sel")])
        P.ins("gpsimd", lambda g: g.affine_select(
            out=sel[:, :], in_=sel[:, :], compare_op=ALU.is_ge, fill=0.0,
            base=31, pattern=[[32, 4]], channel_multiplier=-1),
            reads=[R("sel")], writes=[R("sel")])
        P.ins("gpsimd", lambda g: g.memset(selT[:, :], 1.0), writes=[R("selT")])
        P.ins("gpsimd", lambda g: g.affine_select(
            out=selT[:, :], in_=selT[:, :], compare_op=ALU.is_ge, fill=0.0,
            base=0, pattern=[[1, 128]], channel_multiplier=-32),
            reads=[R("selT")], writes=[R("selT")])
        P.ins("gpsimd", lambda g: g.affine_select(
            out=selT[:, :], in_=selT[:, :], compare_op=ALU.is_ge, fill=0.0,
            base=31, pattern=[[-1, 128]], channel_multiplier=32),
            reads=[R("selT")], writes=[R("selT")])
        P.ins("gpsimd", lambda g: g.memset(ones_col[:, :], 1.0),
              writes=[R("ones_col")])
        P.ins("gpsimd", lambda g: g.memset(epscol[:, :], EPS), writes=[R("epscol")])
        P.ins("gpsimd", lambda g: g.memset(ones_row, 1.0), writes=[k_ones_row])
        P.ins("gpsimd", lambda g: g.memset(mblkT[:, :], 0.0), writes=[R("mblkT")])
        P.ins("gpsimd", lambda g: g.memset(mblk[:, :], 0.0), writes=[R("mblk")])


        def make_A1(b):
            def ln_stats(i):
                n = b * NT + i
                sl = stage[n % 3]
                ksl = R(f"stage{n % 3}")
                P.ins("gpsimd", (lambda sl=sl, b=b, i=i: lambda e: e.dma_start(
                    sl[:, :], xs[b, i * 128:(i + 1) * 128, :]))(),
                    writes=[ksl], dma=True, tag=f"dma_xA{b}_{i}")
                P.ins("tensor", (lambda: lambda e: e.ldweights(
                    ident_bf[:, :]))(),
                    reads=[ksl, R("ident_bf")], writes=[R("HEAT")],
                    tag=f"heat{b}_{i}")
                P.ins("vector", (lambda sl=sl, i=i: lambda e: e.reduce_sum(
                    lnst[:, i:i + 1], sl[:, :], axis=AX.X))(),
                    reads=[ksl], writes=[R("lnst", i, i + 1)], tag=f"ln_sum{i}")
                P.ins("scalar", (lambda sl=sl: lambda e: e.activation(
                    sqscr[:, :], sl[:, :], AF.Square))(),
                    reads=[ksl], writes=[R("sqscr")], tag=f"ln_sq{i}")
                P.ins("vector", (lambda i=i: lambda e: e.reduce_sum(
                    lnst[:, 16 + i:17 + i], sqscr[:, :], axis=AX.X))(),
                    reads=[R("sqscr")], writes=[R("lnst", 16 + i, 17 + i)],
                    tag=f"ln_ssq{i}")
                P.ins("vector", (lambda i=i: lambda e: e.tensor_scalar(
                    lnst[:, 32 + i:33 + i], lnst[:, i:i + 1], -1.0 / D, None,
                    ALU.mult))(),
                    reads=[R("lnst", i, i + 1)],
                    writes=[R("lnst", 32 + i, 33 + i)], tag=f"ln_negmean{i}")
                P.ins("vector", (lambda i=i: lambda e: e.tensor_tensor(
                    lnst[:, 48 + i:49 + i], lnst[:, 32 + i:33 + i],
                    lnst[:, 32 + i:33 + i], ALU.mult))(),
                    reads=[R("lnst", 32 + i, 33 + i)],
                    writes=[R("lnst", 48 + i, 49 + i)], tag=f"ln_msq{i}")
                P.ins("vector", (lambda i=i: lambda e: e.scalar_tensor_tensor(
                    lnst[:, 64 + i:65 + i], lnst[:, 16 + i:17 + i], 1.0 / D,
                    lnst[:, 48 + i:49 + i], ALU.mult, ALU.subtract))(),
                    reads=[R("lnst", 16 + i, 17 + i), R("lnst", 48 + i, 49 + i)],
                    writes=[R("lnst", 64 + i, 65 + i)], tag=f"ln_var{i}")
                P.ins("scalar", (lambda i=i: lambda e: e.activation(
                    lnst[:, 80 + i:81 + i], lnst[:, 64 + i:65 + i], AF.Ln,
                    bias=epscol[:, :]))(),
                    reads=[R("lnst", 64 + i, 65 + i), R("epscol")],
                    writes=[R("lnst", 80 + i, 81 + i)], tag=f"ln_lnv{i}")
                P.ins("scalar", (lambda i=i: lambda e: e.activation(
                    lncols[:, i:i + 1], lnst[:, 80 + i:81 + i], AF.Exp,
                    scale=-0.5))(),
                    reads=[R("lnst", 80 + i, 81 + i)],
                    writes=[R("lncols", i, i + 1)], tag=f"ln_rstd{i}")

            def ln_finish(i):
                n = b * NT + i
                sl = stage[n % 3]
                ksl = R(f"stage{n % 3}")
                P.ins("vector", (lambda i=i: lambda e: e.tensor_scalar(
                    lncols[:, 16 + i:17 + i], lnst[:, 32 + i:33 + i],
                    lncols[:, i:i + 1], None, ALU.mult))(),
                    reads=[R("lnst", 32 + i, 33 + i), R("lncols", i, i + 1)],
                    writes=[R("lncols", 16 + i, 17 + i)], tag=f"ln_negmr{i}")
                P.ins("vector", (lambda sl=sl, i=i, b=b: lambda e: e.tensor_scalar(
                    xn_tm(i), sl[:, :], lncols[:, i:i + 1],
                    lncols[:, 16 + i:17 + i], ALU.mult, ALU.add))(),
                    reads=[ksl, R("lncols", i, i + 1),
                           R("lncols", 16 + i, 17 + i)],
                    writes=[k_xn(i)], tag=f"xn{b}_{i}")

            def tile(k):
                if k >= 2:
                    ln_finish(k - 2)
                if k < NT:
                    ln_stats(k)
                if k == NT:
                    ln_finish(NT - 2)
                    ln_finish(NT - 1)
            return tile

        ps0b = ExitStack()
        ctp_ps = [ps0b.enter_context(nc.psum_tensor(f"ctp{i}", [128, 4], F32))
                  for i in range(2)]
        k_ctp = [psum_key(t) for t in ctp_ps]
        lil_ps = [ps0b.enter_context(nc.psum_tensor(f"lilp{i}", [128, 512], F32))
                  for i in range(2)]
        k_lilp = [psum_key(t) for t in lil_ps]
        ps0 = ExitStack()
        tp32_ps = [ps0.enter_context(nc.psum_tensor(f"tp32_{i}", [128, 512], BF16))
                   for i in range(4)]
        k_tp32 = [psum_key(t) for t in tp32_ps]

        wslot = [arena2[:, 0:8 * D].bitcast(F32),
                 arena[:, 6 * D:14 * D].bitcast(F32)]
        k_wslot = [R("arena2", 0, 8 * D), R("arena", 6 * D, 14 * D)]

        def w_grp_dram(g):
            if g < 6:
                src = wqkv_d[g * 512:(g + 1) * 512, :]
            elif g < 8:
                src = outw_d[(g - 6) * 512:(g - 5) * 512, :]
            else:
                src = wff_d[(g - 8) * 512:(g - 7) * 512, :]
            return src.rearrange("(a p) f -> p a f", p=128)

        def w_tp_dest(t, c):
            if t < 24:
                lo = c * 3 * D + t * 128
                return wqkvT[:, lo:lo + 128], R("wqkvT", lo, lo + 128)
            if t < 32:
                lo = c * D + (t - 24) * 128
                return outwT[:, lo:lo + 128], R("outwT", lo, lo + 128)
            lo = c * D + (t - 32) * 128
            return weffT[:, lo:lo + 128], R("weffT", lo, lo + 128)

        def emit_wgroup(g):
            sl = wslot[g % 2]
            ksl = k_wslot[g % 2]
            P.ins("sync" if g % 2 == 0 else "gpsimd",
                  (lambda sl=sl, g=g: lambda e: e.dma_start(
                      sl.rearrange("p (a f) -> p a f", a=4), w_grp_dram(g)))(),
                  writes=[ksl], dma=True, tag=f"dma_wg{g}")
            for q in range(4):
                t = g * 4 + q
                if t < 8:
                    cb, kcb = wq_nat[:, t * D:(t + 1) * D], R("wq_nat", t * D,
                                                              (t + 1) * D)
                else:
                    cb, kcb = xnf[t % 2][:, :], R(f"xnf{t % 2}")
                P.ins("vector", (lambda sl=sl, q=q, cb=cb: lambda e:
                                 e.tensor_copy(cb, sl[:, q * D:(q + 1) * D]))(),
                    reads=[ksl], writes=[kcb], tag=f"cast_w{t}")
                for c in range(8):
                    j = t * 8 + c
                    ps = tp32_ps[j % 4]
                    kps = k_tp32[j % 4]
                    P.ins("tensor", (lambda ps=ps, cb=cb, c=c: lambda e:
                                     e.transpose(ps[:, 0:128],
                                                 cb[:, c * 128:(c + 1) * 128],
                                                 ident_bf[:, :]))(),
                        reads=[kcb, R("ident_bf")], writes=[kps],
                        tag=f"tp_w{t}_{c}")
                    dv, kdv = w_tp_dest(t, c)
                    if c % 3 == 0:
                        P.ins("scalar", (lambda dv=dv, ps=ps: lambda e: e.copy(
                            dv, ps[:, 0:128]))(), reads=[kps], writes=[kdv],
                            tag=f"ev_w{t}_{c}")
                    else:
                        P.ins("vector", (lambda dv=dv, ps=ps: lambda e:
                                         e.tensor_copy(dv, ps[:, 0:128]))(),
                            reads=[kps], writes=[kdv], tag=f"ev_w{t}_{c}")

        a1_tile_b0 = make_A1(0)
        INTERLEAVE = True
        if INTERLEAVE:
            for k in range(NT + 5):
                if k < 10:
                    emit_wgroup(k)
                if k >= 4:
                    a1_tile_b0(k - 4)
        else:
            for k in range(10):
                emit_wgroup(k)
            for k in range(NT + 1):
                a1_tile_b0(k)

        ps0.close()
        st_da = arena2[0:100, 2 * D:4 * D].bitcast(F32)
        k_stda = R("arena2", 2 * D, 4 * D)
        st_ld = arena2[0:4, 4 * D:4 * D + 200].bitcast(F32)
        k_stld = R("arena2", 4 * D, 4 * D + 200)
        st_lu = arena2[0:50, 4 * D + 256:4 * D + 264].bitcast(F32)
        k_stlu = R("arena2", 4 * D + 256, 4 * D + 264)
        P.ins("sync", lambda e: e.dma_start(st_da, da_d[:, :]),
              writes=[k_stda], dma=True, tag="dma_da")
        P.ins("sync", lambda e: e.dma_start(st_ld, ld_d[:, :]),
              writes=[k_stld], dma=True, tag="dma_ld")
        P.ins("sync", lambda e: e.dma_start(st_lu, lu_d[:, :]),
              writes=[k_stlu], dma=True, tag="dma_lu")
        P.ins("gpsimd", lambda e: e.tensor_copy(v_dab, st_da),
              reads=[k_stda], writes=[k_dab], tag="cast_da")
        if debug:
            P.ins("gpsimd", lambda e: e.dma_start(
                dbg_d["lil"][:, 0:D], xnf[0][:, :]),
                reads=[R("xnf0")], dma=True, tag="dbglil0_early")
        P.ins("tensor", lambda e: e.transpose(
            ctp_ps[0][0:100, 0:4], st_ld[:, 0:100], ident_f[0:4, 0:4]),
            reads=[k_stld, R("ident_f")], writes=[k_ctp[0]],
            tag="tp_ldT")
        P.ins("scalar", lambda e: e.copy(v_ldT, ctp_ps[0][0:100, 0:4]),
              reads=[k_ctp[0]], writes=[k_ldT], tag="ev_ldT")
        P.ins("tensor", lambda e: e.transpose(
            lil_ps[0][0:4, 0:50], st_lu[:, 0:4], ident_f[0:50, 0:50]),
            reads=[k_stlu, R("ident_f")], writes=[k_lilp[0]],
            tag="tp_luT")
        P.ins("scalar", lambda e: e.copy(v_luT, lil_ps[0][0:4, 0:50]),
              reads=[k_lilp[0]], writes=[k_luT], tag="ev_luT")
        for t in range(8):
            lo = 5 * D + (t % 2) * 128
            sl = arena2[:, lo:lo + 100].bitcast(F32)
            ksl = R("arena2", lo, lo + 100)
            P.ins("sync", (lambda sl=sl, t=t: lambda e: e.dma_start(
                sl, ua_d[t * 128:(t + 1) * 128, :]))(), writes=[ksl], dma=True,
                tag=f"dma_ua{t}")
            pp = lil_ps[1] if t % 2 else lil_ps[0]
            kpp = k_lilp[1] if t % 2 else k_lilp[0]
            P.ins("tensor", (lambda pp=pp, sl=sl: lambda e: e.transpose(
                pp[0:50, 0:128], sl, ident_f[:, :]))(),
                reads=[ksl, R("ident_f")], writes=[kpp], tag=f"tp_ua{t}")
            dv = v_uaT[:, t * 128:(t + 1) * 128]
            P.ins("scalar", (lambda dv=dv, pp=pp: lambda e: e.copy(
                dv, pp[0:50, 0:128]))(), reads=[kpp],
                writes=[R("xnf1", t * 128, (t + 1) * 128)],
                tag=f"ev_ua{t}")
        for j in range(2):
            pp = lil_ps[j]
            P.ins("tensor", (lambda pp=pp, j=j: lambda e: e.matmul(
                pp[0:4, 0:512], v_ldT, v_dab[:, j * 512:(j + 1) * 512],
                start=True, stop=True))(),
                reads=[k_ldT, k_dab],
                writes=[k_lilp[j]], tag=f"mm_t1_{j}")
            P.ins("scalar", (lambda pp=pp, j=j: lambda e: e.copy(
                v_t1[:, j * 512:(j + 1) * 512], pp[0:4, 0:512]))(),
                reads=[k_lilp[j]],
                writes=[R("arena2", j * 512, (j + 1) * 512)],
                tag=f"ev_t1_{j}")
        for j in range(2):
            pp = lil_ps[j]
            P.ins("tensor", (lambda pp=pp, j=j: lambda e: e.matmul(
                pp[0:50, 0:512], v_luT, v_t1[:, j * 512:(j + 1) * 512],
                start=True, stop=True))(),
                reads=[k_luT, k_t1],
                writes=[k_lilp[j]], tag=f"mm_t2_{j}")
            P.ins("scalar", (lambda pp=pp, j=j: lambda e: e.copy(
                v_t2[:, j * 512:(j + 1) * 512], pp[0:50, 0:512]))(),
                reads=[k_lilp[j]],
                writes=[R("arena2", D + j * 512, D + (j + 1) * 512)],
                tag=f"ev_t2_{j}")
        for f in range(8):
            for j in range(2):
                pp = lil_ps[j]
                P.ins("tensor", (lambda pp=pp, f=f, j=j: lambda e: e.matmul(
                    pp[:, 0:512], v_t2[:, f * 128:(f + 1) * 128],
                    v_uaT[:, j * 512:(j + 1) * 512], start=True, stop=True))(),
                    reads=[k_t2, k_uaT],
                    writes=[k_lilp[j]], tag=f"mm_lt{f}_{j}")
                wv = weffT[:, f * D + j * 512:f * D + (j + 1) * 512]
                kw = R("weffT", f * D + j * 512, f * D + (j + 1) * 512)
                P.ins("vector", (lambda wv=wv, pp=pp: lambda e: e.tensor_tensor(
                    wv, pp[:, 0:512], wv, ALU.add))(),
                    reads=[k_lilp[j], kw], writes=[kw], tag=f"add_lt{f}_{j}")

        if debug:
            P.ins("gpsimd", lambda e: e.dma_start(
                dbg_d["lil"][:, D:2 * D], xnf[1][:, :]),
                reads=[R("xnf1")], dma=True, tag="dbglil1")
            P.ins("gpsimd", lambda e: e.dma_start(
                dbg_d["lil"][:, 2 * D:4 * D], arena2[:, 0:2 * D]),
                reads=[R("arena2", 0, 2 * D)], dma=True, tag="dbglil2")
        def row_dma(dram_ap, tag):
            ap2 = dram_ap.rearrange("(a d) -> a d", a=1)
            P.ins("sync", (lambda ap2=ap2: lambda e: e.dma_start(RS(0), ap2))(),
                  writes=[kRS(0)], dma=True, tag=tag)

        row_dma(bqkv_d[0:D], "dma_bq")
        P.ins("scalar", lambda e: e.activation(RB(0), RS(0), AF.Copy),
              reads=[kRS(0)], writes=[kRB(0)], tag="cast_bq")
        P.ins("scalar", lambda e: e.copy(RS(2), RS(0)),
              reads=[kRS(0)], writes=[kRS(2)], tag="keep_bq")
        for c in range(8):
            pp = ctp_ps[0]
            kpp = k_ctp[0]
            P.ins("tensor", (lambda pp=pp, c=c: lambda e: e.transpose(
                pp[:, 0:1], RS(2)[0:1, c * 128:(c + 1) * 128],
                ident_f[0:1, 0:1]))(),
                reads=[kRS(2), R("ident_f")], writes=[kpp], tag=f"tp_bqc{c}")
            P.ins("scalar", (lambda pp=pp, c=c: lambda e: e.copy(
                CF(16 + c), pp[:, 0:1]))(), reads=[kpp],
                writes=[kCF(16 + c)], tag=f"ev_bqc{c}")
        row_dma(bqkv_d[D:2 * D], "dma_bk")
        P.ins("scalar", lambda e: e.activation(RB(1), RS(0), AF.Copy),
              reads=[kRS(0)], writes=[kRB(1)], tag="cast_bk")
        row_dma(bqkv_d[2 * D:3 * D], "dma_bv")
        P.ins("scalar", lambda e: e.activation(RB(2), RS(0), AF.Copy),
              reads=[kRS(0)], writes=[kRB(2)], tag="cast_bv")
        P.ins("vector", lambda e: e.tensor_scalar(
            RS(3), RS(0), float(SEQ), None, ALU.mult),
            reads=[kRS(0)], writes=[kRS(3)], tag="sbv")
        row_dma(gnw_d[:], "dma_gnw")
        for c in range(8):
            pp = ctp_ps[0]
            kpp = k_ctp[0]
            P.ins("tensor", (lambda pp=pp, c=c: lambda e: e.transpose(
                pp[:, 0:1], RS(0)[0:1, c * 128:(c + 1) * 128],
                ident_f[0:1, 0:1]))(),
                reads=[kRS(0), R("ident_f")], writes=[kpp], tag=f"tp_gnw{c}")
            P.ins("scalar", (lambda pp=pp, c=c: lambda e: e.copy(
                CF(0 + c), pp[:, 0:1]))(), reads=[kpp], writes=[kCF(c)],
                tag=f"ev_gnw{c}")
        row_dma(gnb_d[:], "dma_gnb")
        for c in range(8):
            pp = ctp_ps[0]
            kpp = k_ctp[0]
            P.ins("tensor", (lambda pp=pp, c=c: lambda e: e.transpose(
                pp[:, 0:1], RS(0)[0:1, c * 128:(c + 1) * 128],
                ident_f[0:1, 0:1]))(),
                reads=[kRS(0), R("ident_f")], writes=[kpp], tag=f"tp_gnb{c}")
            P.ins("scalar", (lambda pp=pp, c=c: lambda e: e.copy(
                CF(8 + c), pp[:, 0:1]))(), reads=[kpp], writes=[kCF(8 + c)],
                tag=f"ev_gnb{c}")
        row_dma(outb_d[:], "dma_outb")
        P.ins("scalar", lambda e: e.copy(RS(4), RS(0)),
              reads=[kRS(0)], writes=[kRS(4)], tag="keep_outb")
        row_dma(bff_d[:], "dma_bff")
        P.ins("vector", lambda e: e.tensor_tensor(RS(4), RS(4), RS(0), ALU.add),
              reads=[kRS(4), kRS(0)], writes=[kRS(4)], tag="obf")
        P.ins("vector", lambda e: e.tensor_copy(CB(16, 8), CF(16, 8)),
              reads=[kCF(16, 8)], writes=[kCB(16, 8)], tag="bqcol_bf")

        ps0b.close()

        for b in range(NB):
            if b > 0:
                a1t = make_A1(b)
                for k in range(NT + 1):
                    a1t(k)
            psA = ExitStack()
            Xps = [psA.enter_context(
                nc.psum_tensor(f"X{b}_{i}", [128, D], F32)) for i in range(3)]
            Srow = psA.enter_context(nc.psum_tensor(f"xsr{b}", [1, D], F32))
            k_X = [psum_key(t) for t in Xps]
            k_Sr = psum_key(Srow)

            def x_chunk_mms(c, i):
                lhs = xn_tm(i)[:, c * 128:(c + 1) * 128]
                kx = k_xn(i)
                pp, kpp = Xps[c % 3], k_X[c % 3]
                st_ = (i == 0)
                sp_ = (i == NT - 1)
                for j in range(2):
                    P.ins("tensor", (lambda lhs=lhs, pp=pp, i=i, j=j, st_=st_,
                                     sp_=sp_: lambda e: e.matmul(
                        pp[:, j * 512:(j + 1) * 512], lhs,
                        xn_tm(i)[:, j * 512:(j + 1) * 512],
                        start=st_, stop=sp_))(),
                        reads=[kx], writes=[kpp], tag=f"mmX{b}_{c}_{i}_{j}")

            def xnsum_row_mms(i):
                kx = k_xn(i)
                st_ = (i == 0)
                sp_ = (i == NT - 1)
                for j in range(2):
                    P.ins("tensor", (lambda i=i, j=j, st_=st_, sp_=sp_:
                                     lambda e: e.matmul(
                        Srow[0:1, j * 512:(j + 1) * 512], ones_col[:, :],
                        xn_tm(i)[:, j * 512:(j + 1) * 512],
                        start=st_, stop=sp_))(),
                        reads=[kx, R("ones_col")], writes=[k_Sr],
                        tag=f"mmSr{b}_{i}")

            def x_evac(c):
                pp, kpp = Xps[c % 3], k_X[c % 3]
                P.ins("scalar", (lambda pp=pp, c=c: lambda e: e.copy(
                    x_sb(c), pp[:, :]))(), reads=[kpp], writes=[k_xsb(c)],
                    tag=f"evX{b}_{c}")

            for p, grp in enumerate(((0, 1, 2), (3, 4, 5), (6, 7))):
                for i in range(NT):
                    for c in grp:
                        x_chunk_mms(c, i)
                    if p == 0:
                        xnsum_row_mms(i)
                for c in grp:
                    x_evac(c)
                if p == 0:
                    P.ins("scalar", lambda e: e.copy(RS(0), Srow[0:1, :]),
                          reads=[k_Sr], writes=[kRS(0)], tag=f"evSr{b}")

            if debug:
                P.ins("gpsimd", (lambda b=b: lambda e: e.dma_start(
                    dbg_d["X"][b], arena2[:, :]))(),
                    reads=[R("arena2")], dma=True, tag=f"dbgX{b}")
                P.ins("gpsimd", (lambda b=b: lambda e: e.dma_start(
                    dbg_d["xn"][b], arena[:, :]))(),
                    reads=[R("arena", 0, 16 * D)], dma=True, tag=f"dbgxn{b}")
            psA.close()

            psB2 = ExitStack()
            Tps = [psB2.enter_context(
                nc.psum_tensor(f"T{b}_{i}", [128, 2 * D], F32)) for i in range(2)]
            k_T = [psum_key(t) for t in Tps]
            for f in range(8):
                pp, kpp = Tps[f % 2], k_T[f % 2]
                for g in range(8):
                    lhs = x_sb(g)[:, f * 128:(f + 1) * 128]
                    klhs = k_xsb(g)
                    for q in range(4):
                        off = 2 * D + (q % 2) * 512 if q < 2 else (q % 2) * 512
                        P.ins("tensor", (lambda pp=pp, lhs=lhs, g=g, q=q,
                                         off=off: lambda e: e.matmul(
                            pp[:, q * 512:(q + 1) * 512], lhs,
                            wqkvT[:, g * 3 * D + off:g * 3 * D + off + 512],
                            start=(g == 0), stop=(g == 7)))(),
                            reads=[klhs, R("wqkvT", g * 3 * D, (g + 1) * 3 * D)],
                            writes=[kpp], tag=f"mmT{b}_{f}_{g}_{q}")
                if f % 2 == 0:
                    P.ins("scalar", (lambda pp=pp, f=f: lambda e: e.copy(
                        t_cat(f), pp[:, :]))(), reads=[kpp], writes=[k_tcat(f)],
                        tag=f"evT{b}_{f}")
                else:
                    P.ins("vector", (lambda pp=pp, f=f: lambda e: e.tensor_copy(
                        t_cat(f), pp[:, :]))(), reads=[kpp], writes=[k_tcat(f)],
                        tag=f"evT{b}_{f}")
            psB2.close()
            psB1 = ExitStack()
            sums_ps = psB1.enter_context(
                nc.psum_tensor(f"sums{b}", [1, 3 * D], F32))
            k_sums = psum_key(sums_ps)
            ctpB1 = psB1.enter_context(
                nc.psum_tensor(f"ctpBB{b}", [128, 4], F32))
            k_ctpB1 = psum_key(ctpB1)
            for c in range(8):
                P.ins("tensor", (lambda c=c: lambda e: e.transpose(
                    ctpB1[:, 0:1], RS(0)[0:1, c * 128:(c + 1) * 128],
                    ident_f[0:1, 0:1]))(),
                    reads=[kRS(0), R("ident_f")], writes=[k_ctpB1],
                    tag=f"tp_xs{b}_{c}")
                P.ins("scalar", (lambda c=c: lambda e: e.copy(
                    CF(24 + c), ctpB1[:, 0:1]))(), reads=[k_ctpB1],
                    writes=[kCF(24 + c)], tag=f"ev_xs{b}_{c}")
            P.ins("vector", lambda e: e.tensor_copy(CB(0, 8), CF(24, 8)),
                  reads=[kCF(24, 8)], writes=[kCB(0, 8)], tag=f"xnsum_bf{b}")
            for c in range(8):
                for j in range(6):
                    P.ins("tensor", (lambda c=c, j=j: lambda e: e.matmul(
                        sums_ps[0:1, j * 512:(j + 1) * 512], CB(0 + c),
                        wqkvT[:, c * 3 * D + j * 512:c * 3 * D + (j + 1) * 512],
                        start=(c == 0), stop=(c == 7)))(),
                        reads=[kCB(c), R("wqkvT", c * 3 * D, (c + 1) * 3 * D)],
                        writes=[k_sums], tag=f"mm_sums{b}_{c}_{j}")
            P.ins("scalar", lambda e: e.copy(RS(1), sums_ps[0:1, 0:D]),
                  reads=[k_sums], writes=[kRS(1)], tag=f"ev_qsum0{b}")
            P.ins("vector", lambda e: e.tensor_copy(RB(3), sums_ps[0:1, D:2 * D]),
                  reads=[k_sums], writes=[kRB(3)], tag=f"ksum_bf{b}")
            P.ins("vector", lambda e: e.scalar_tensor_tensor(
                RB(4), RS(3), 1.0, sums_ps[0:1, 2 * D:3 * D], ALU.mult, ALU.add),
                reads=[kRS(3), k_sums], writes=[kRB(4)], tag=f"vsumS{b}")
            P.ins("vector", lambda e: e.tensor_copy(RB(5), sums_ps[0:1, 0:D]),
                  reads=[k_sums], writes=[kRB(5)], tag=f"qsum0_bf{b}")
            P.ins("vector", lambda e: e.scalar_tensor_tensor(
                RB(6), RS(2), float(SEQ), sums_ps[0:1, 0:D], ALU.mult, ALU.add),
                reads=[kRS(2), k_sums], writes=[kRB(6)], tag=f"qsum0S{b}")
            psB1.close()

            psC1 = ExitStack()
            Mps = [psC1.enter_context(
                nc.psum_tensor(f"M{b}_{i}", [128, 128], F32)) for i in range(2)]
            Gps = [psC1.enter_context(
                nc.psum_tensor(f"G{b}_{i}", [128, 128], F32)) for i in range(3)]
            Hps = psC1.enter_context(nc.psum_tensor(f"H{b}", [128, 128], F32))
            GHps = psC1.enter_context(nc.psum_tensor(f"GH{b}", [128, 128], F32))
            k_M = [psum_key(t) for t in Mps]
            k_G = [psum_key(t) for t in Gps]
            k_H = psum_key(Hps)
            k_GH = psum_key(GHps)
            for c in range(8):
                mp, kmp = Mps[c % 2], k_M[c % 2]
                gp, kgp = Gps[c % 2], k_G[c % 2]
                for f in range(8):
                    P.ins("tensor", (lambda mp=mp, f=f, c=c: lambda e: e.matmul(
                        mp[:, :], t_cat(f)[:, c * 128:(c + 1) * 128],
                        wqkvT[:, f * 3 * D + D + c * 128:
                              f * 3 * D + D + (c + 1) * 128],
                        start=(f == 0), stop=False))(),
                        reads=[k_tcat(f), R("wqkvT", f * 3 * D, (f + 1) * 3 * D)],
                        writes=[kmp], tag=f"mmM{b}_{c}_{f}")
                P.ins("tensor", (lambda mp=mp, c=c: lambda e: e.matmul(
                    mp[:, :], RB(4, c * 128, 128), RB(1, c * 128, 128),
                    start=False, stop=False))(),
                    reads=[kRB(4), kRB(1)], writes=[kmp], tag=f"mmMr1{b}_{c}")
                P.ins("tensor", (lambda mp=mp, c=c: lambda e: e.matmul(
                    mp[:, :], RB(2, c * 128, 128), RB(3, c * 128, 128),
                    start=False, stop=True))(),
                    reads=[kRB(2), kRB(3)], writes=[kmp], tag=f"mmMr2{b}_{c}")
                for f in range(8):
                    P.ins("tensor", (lambda gp=gp, f=f, c=c: lambda e: e.matmul(
                        gp[:, :], t_cat(f)[:, D + c * 128:D + (c + 1) * 128],
                        wqkvT[:, f * 3 * D + c * 128:f * 3 * D + (c + 1) * 128],
                        start=(f == 0), stop=False))(),
                        reads=[k_tcat(f), R("wqkvT", f * 3 * D, (f + 1) * 3 * D)],
                        writes=[kgp], tag=f"mmG{b}_{c}_{f}")
                P.ins("tensor", (lambda gp=gp, c=c: lambda e: e.matmul(
                    gp[:, :], RB(5, c * 128, 128), RB(0, c * 128, 128),
                    start=False, stop=False))(),
                    reads=[kRB(5), kRB(0)], writes=[kgp], tag=f"mmGr1{b}_{c}")
                P.ins("tensor", (lambda gp=gp, c=c: lambda e: e.matmul(
                    gp[:, :], RB(0, c * 128, 128), RB(6, c * 128, 128),
                    start=False, stop=True))(),
                    reads=[kRB(0), kRB(6)], writes=[kgp], tag=f"mmGr2{b}_{c}")
                for jj in range(4):
                    eng_ = "scalar" if jj % 2 == 0 else "vector"
                    fn_ = (lambda mp=mp, c=c, jj=jj, eng_=eng_: lambda e:
                           (e.copy if eng_ == "scalar" else e.tensor_copy)(
                        mblkT[32 * jj:32 * (jj + 1),
                              c * 128 + 32 * jj:c * 128 + 32 * (jj + 1)],
                        mp[32 * jj:32 * (jj + 1), 32 * jj:32 * (jj + 1)]))()
                    P.ins(eng_, fn_, reads=[kmp],
                          writes=[R("mblkT", c * 128, (c + 1) * 128)],
                          tag=f"exM{b}_{c}_{jj}")
                P.ins("tensor", (lambda c=c: lambda e: e.matmul(
                    Hps[:, :], mblkT[:, c * 128:(c + 1) * 128],
                    mblkT[:, c * 128:(c + 1) * 128], start=True, stop=True))(),
                    reads=[R("mblkT", c * 128, (c + 1) * 128)], writes=[k_H],
                    tag=f"mmH{b}_{c}")
                P.ins("scalar", lambda e: e.copy(h_sb, Hps[:, :]),
                      reads=[k_H], writes=[R("sqscr", 256, 384)], tag=f"evH{b}_{c}")
                P.ins("vector", (lambda gp=gp: lambda e: e.tensor_tensor(
                    GHps[:, :], gp[:, :], h_sb, ALU.mult))(),
                    reads=[kgp, R("sqscr", 256, 384)], writes=[k_GH],
                    tag=f"gh{b}_{c}")
                P.ins("vector", (lambda c=c: lambda e: e.reduce_sum(
                    CF(40 + c), GHps[:, :], axis=AX.X))(),
                    reads=[k_GH], writes=[kCF(40 + c)],
                    tag=f"ghcol{b}_{c}")

            for c in range(8):
                P.ins("vector", (lambda c=c: lambda e: e.transpose(
                    mblk[:, c * 128:(c + 1) * 128],
                    mblkT[:, c * 128:(c + 1) * 128]))(),
                    reads=[R("mblkT", c * 128, (c + 1) * 128)],
                    writes=[R("mblk", c * 128, (c + 1) * 128)], tag=f"vtM{b}_{c}")

            psQC = ExitStack()
            ctpB = [psQC.enter_context(
                nc.psum_tensor(f"ctpB{b}_{i}", [128, 4], F32)) for i in range(1)]
            k_ctpB = [psum_key(t) for t in ctpB]
            for c in range(8):
                pp, kpp = ctpB[0], k_ctpB[0]
                P.ins("tensor", (lambda pp=pp, c=c: lambda e: e.transpose(
                    pp[:, 0:1], RS(1)[0:1, c * 128:(c + 1) * 128],
                    ident_f[0:1, 0:1]))(),
                    reads=[kRS(1), R("ident_f")], writes=[kpp],
                    tag=f"tp_qs{b}_{c}")
                P.ins("scalar", (lambda pp=pp, c=c: lambda e: e.copy(
                    CF(32 + c), pp[:, 0:1]))(), reads=[kpp],
                    writes=[kCF(32 + c)], tag=f"ev_qs{b}_{c}")
            P.ins("vector", lambda e: e.scalar_tensor_tensor(
                CB(8, 8), CF(16, 8), float(SEQ), CF(32, 8), ALU.mult, ALU.add),
                reads=[kCF(16, 8), kCF(32, 8)], writes=[kCB(8, 8)],
                tag=f"qsumcol_bf{b}")
            psQC.close()
            if debug:
                P.ins("gpsimd", (lambda b=b: lambda e: e.dma_start(
                    dbg_d["MT"][b], mblkT[:, :]))(),
                    reads=[R("mblkT")], dma=True, tag=f"dbgMT{b}")
            psC1.close()

            psC1t = ExitStack()
            csum_ps = psC1t.enter_context(
                nc.psum_tensor(f"csum{b}", [1, D], F32))
            u_ps = psC1t.enter_context(nc.psum_tensor(f"u{b}", [1, D], F32))
            stat_ps = psC1t.enter_context(
                nc.psum_tensor(f"stat{b}", [128, 8], F32))
            ctpC = [psC1t.enter_context(
                nc.psum_tensor(f"ctpC{b}_{i}", [128, 4], F32)) for i in range(2)]
            k_cs = psum_key(csum_ps)
            k_u = psum_key(u_ps)
            k_st = psum_key(stat_ps)
            k_ctpC = [psum_key(t) for t in ctpC]
            for c in range(8):
                P.ins("tensor", (lambda c=c: lambda e: e.matmul(
                    csum_ps[0:1, c * 128:(c + 1) * 128], CB(8 + c),
                    mblk[:, c * 128:(c + 1) * 128],
                    start=(c % 4 == 0), stop=(c % 4 == 3)))(),
                    reads=[kCB(8 + c), R("mblk", c * 128, (c + 1) * 128)],
                    writes=[k_cs], tag=f"mmcs{b}_{c}")
            for c in range(8):
                P.ins("tensor", (lambda c=c: lambda e: e.matmul(
                    u_ps[0:1, c * 128:(c + 1) * 128], CB(16 + c),
                    mblk[:, c * 128:(c + 1) * 128],
                    start=(c % 4 == 0), stop=(c % 4 == 3)))(),
                    reads=[kCB(16 + c), R("mblk", c * 128, (c + 1) * 128)],
                    writes=[k_u], tag=f"mmu{b}_{c}")
            P.ins("scalar", lambda e: e.copy(RS(1), csum_ps[0:1, :]),
                  reads=[k_cs], writes=[kRS(1)], tag=f"ev_cs{b}")
            P.ins("scalar", lambda e: e.copy(RS(5), u_ps[0:1, :]),
                  reads=[k_u], writes=[kRS(5)], tag=f"ev_u{b}")
            for c in range(8):
                pp, kpp = ctpC[c % 2], k_ctpC[c % 2]
                P.ins("tensor", (lambda pp=pp, c=c: lambda e: e.transpose(
                    pp[:, 0:1], RS(1)[0:1, c * 128:(c + 1) * 128],
                    ident_f[0:1, 0:1]))(),
                    reads=[kRS(1), R("ident_f")], writes=[kpp],
                    tag=f"tp_cs{b}_{c}")
                P.ins("scalar", (lambda pp=pp, c=c: lambda e: e.copy(
                    CF(56 + c), pp[:, 0:1]))(), reads=[kpp],
                    writes=[kCF(56 + c)], tag=f"ev_csc{b}_{c}")
            for c in range(8):
                pp, kpp = ctpC[c % 2], k_ctpC[c % 2]
                P.ins("tensor", (lambda pp=pp, c=c: lambda e: e.transpose(
                    pp[:, 0:1], RS(5)[0:1, c * 128:(c + 1) * 128],
                    ident_f[0:1, 0:1]))(),
                    reads=[kRS(5), R("ident_f")], writes=[kpp],
                    tag=f"tp_u{b}_{c}")
                P.ins("scalar", (lambda pp=pp, c=c: lambda e: e.copy(
                    CF(48 + c), pp[:, 0:1]))(), reads=[kpp],
                    writes=[kCF(48 + c)], tag=f"ev_uc{b}_{c}")
            P.ins("tensor", lambda e: e.matmul(
                stat_ps[0:4, 0:8], sel[:, :], CF(40, 8), start=True, stop=True),
                reads=[R("sel"), kCF(40, 8)], writes=[k_st], tag=f"mmssq{b}")
            P.ins("scalar", lambda e: e.copy(st44[0:4, 0:8], stat_ps[0:4, 0:8]),
                  reads=[k_st], writes=[R("st44", 0, 8)], tag=f"ev_ssq44{b}")
            P.ins("tensor", lambda e: e.matmul(
                stat_ps[0:4, 0:8], sel[:, :], CF(56, 8), start=True, stop=True),
                reads=[R("sel"), kCF(56, 8)], writes=[k_st], tag=f"mmcs44{b}")
            P.ins("scalar", lambda e: e.copy(st44[0:4, 8:16], stat_ps[0:4, 0:8]),
                  reads=[k_st], writes=[R("st44", 8, 16)], tag=f"ev_cs44{b}")
            P.ins("vector", lambda e: e.tensor_scalar(
                st44[0:4, 16:24], st44[0:4, 8:16], 1.0 / (32 * SEQ), None,
                ALU.mult), reads=[R("st44", 8, 16)], writes=[R("st44", 16, 24)],
                tag=f"mean44{b}")
            P.ins("vector", lambda e: e.tensor_tensor(
                st44[0:4, 24:32], st44[0:4, 16:24], st44[0:4, 16:24], ALU.mult),
                reads=[R("st44", 16, 24)], writes=[R("st44", 24, 32)],
                tag=f"msq44{b}")
            P.ins("vector", lambda e: e.scalar_tensor_tensor(
                st44[0:4, 32:40], st44[0:4, 0:8], 1.0 / (32 * SEQ),
                st44[0:4, 24:32], ALU.mult, ALU.subtract),
                reads=[R("st44", 0, 8), R("st44", 24, 32)],
                writes=[R("st44", 32, 40)], tag=f"var44{b}")
            P.ins("scalar", lambda e: e.activation(
                st44[0:4, 40:48], st44[0:4, 32:40], AF.Sqrt,
                bias=epscol[0:4, :]),
                reads=[R("st44", 32, 40), R("epscol")],
                writes=[R("st44", 40, 48)], tag=f"sd44{b}")
            P.ins("vector", lambda e: e.reciprocal(
                st44[0:4, 48:56], st44[0:4, 40:48]),
                reads=[R("st44", 40, 48)], writes=[R("st44", 48, 56)],
                tag=f"rstd44{b}")
            P.ins("tensor", lambda e: e.matmul(
                stat_ps[:, 0:8], selT[:, :], st44[0:4, 48:56],
                start=True, stop=True),
                reads=[R("selT"), R("st44", 48, 56)], writes=[k_st],
                tag=f"bc_rstd{b}")
            P.ins("vector", lambda e: e.tensor_tensor(
                CF(32, 8), stat_ps[:, 0:8], CF(0, 8), ALU.mult),
                reads=[k_st, kCF(0, 8)], writes=[kCF(32, 8)], tag=f"scols{b}")
            P.ins("tensor", lambda e: e.matmul(
                stat_ps[:, 0:8], selT[:, :], st44[0:4, 16:24],
                start=True, stop=True),
                reads=[R("selT"), R("st44", 16, 24)], writes=[k_st],
                tag=f"bc_mean{b}")
            P.ins("vector", lambda e: e.tensor_tensor(
                sqscr[:, 128:136], stat_ps[:, 0:8], CF(32, 8), ALU.mult),
                reads=[k_st, kCF(32, 8)], writes=[R("sqscr", 128, 136)],
                tag=f"mrcols{b}")
            P.ins("vector", lambda e: e.tensor_tensor(
                CF(40, 8), CF(8, 8), sqscr[:, 128:136], ALU.subtract),
                reads=[kCF(8, 8), R("sqscr", 128, 136)], writes=[kCF(40, 8)],
                tag=f"ttcols{b}")
            P.ins("vector", lambda e: e.tensor_tensor(
                sqscr[:, 136:144], CF(48, 8), CF(32, 8), ALU.mult),
                reads=[kCF(48, 8), kCF(32, 8)], writes=[R("sqscr", 136, 144)],
                tag=f"su{b}")
            P.ins("vector", lambda e: e.tensor_tensor(
                CB(24, 8), sqscr[:, 136:144], CF(40, 8), ALU.add),
                reads=[R("sqscr", 136, 144), kCF(40, 8)], writes=[kCB(24, 8)],
                tag=f"wcols{b}")
            if debug:
                P.ins("gpsimd", (lambda b=b: lambda e: e.dma_start(
                    dbg_d["cols"][b], cols_f[:, :]))(),
                    reads=[R("cols_f")], dma=True, tag=f"dbgcols{b}")
            psC1t.close()

            psC2a = ExitStack()
            beta_ps = psC2a.enter_context(
                nc.psum_tensor(f"beta{b}", [1, D], F32))
            k_beta = psum_key(beta_ps)
            for c in range(8):
                for j in range(2):
                    P.ins("tensor", (lambda c=c, j=j: lambda e: e.matmul(
                        beta_ps[0:1, j * 512:(j + 1) * 512], CB(24 + c),
                        outwT[:, c * D + j * 512:c * D + (j + 1) * 512],
                        start=(c == 0), stop=(c == 7)))(),
                        reads=[kCB(24 + c), R("outwT", c * D, (c + 1) * D)],
                        writes=[k_beta], tag=f"mmbeta{b}_{c}_{j}")
            P.ins("scalar", lambda e: e.copy(RS(5), beta_ps[0:1, :]),
                  reads=[k_beta], writes=[kRS(5)], tag=f"ev_beta{b}")
            P.ins("vector", lambda e: e.tensor_tensor(
                RB(7), RS(5), RS(4), ALU.add),
                reads=[kRS(5), kRS(4)], writes=[kRB(7)], tag=f"bias_bf{b}")
            if debug:
                P.ins("gpsimd", (lambda b=b: lambda e: e.dma_start(
                    dbg_d["rows"][b], rows_f[0:1, :]))(),
                    reads=[R("rows_f")], dma=True, tag=f"dbgrows{b}")
            psC2a.close()

            if debug and b == 0:
                P.ins("gpsimd", lambda e: e.dma_start(
                    dbg_d["weffT"], weffT[:, :]),
                    reads=[R("weffT")], dma=True, tag="dbgweff")
                P.ins("gpsimd", lambda e: e.dma_start(
                    dbg_d["wqnat"], wq_nat[:, :]),
                    reads=[R("wq_nat")], dma=True, tag="dbgwqnat")
            psC2 = ExitStack()
            Pps = [psC2.enter_context(
                nc.psum_tensor(f"P{b}_{i}", [128, D], F32)) for i in range(2)]
            ATps = [psC2.enter_context(
                nc.psum_tensor(f"AT{b}_{i}", [128, D], F32)) for i in range(2)]
            k_P = [psum_key(t) for t in Pps]
            k_AT = [psum_key(t) for t in ATps]
            for c in range(8):
                ms, kms = mblkTs[c % 2], R(f"mblkTs{c % 2}")
                P.ins("scalar", (lambda ms=ms, c=c: lambda e: e.activation(
                    ms[:, :], mblkT[:, c * 128:(c + 1) * 128], AF.Identity,
                    scale=CF(32 + c)))(),
                    reads=[R("mblkT", c * 128, (c + 1) * 128), kCF(32 + c)],
                    writes=[kms], tag=f"mts{b}_{c}")
                pp, kpp = Pps[c % 2], k_P[c % 2]
                for j in range(2):
                    P.ins("tensor", (lambda pp=pp, ms=ms, c=c, j=j:
                                     lambda e: e.matmul(
                        pp[:, j * 512:(j + 1) * 512], ms[:, :],
                        outwT[:, c * D + j * 512:c * D + (j + 1) * 512],
                        start=True, stop=True))(),
                        reads=[kms, R("outwT", c * D, (c + 1) * D)],
                        writes=[kpp], tag=f"mmP{b}_{c}_{j}")
                P.ins("scalar", (lambda pp=pp, c=c: lambda e: e.copy(
                    p_sb(c), pp[:, :]))(), reads=[kpp], writes=[k_psb(c)],
                    tag=f"evP{b}_{c}")
            if debug:
                P.ins("gpsimd", (lambda b=b: lambda e: e.dma_start(
                    dbg_d["P"][b], arena2[:, :]))(),
                    reads=[R("arena2")], dma=True, tag=f"dbgP{b}")
            for f in range(8):
                pp, kpp = ATps[f % 2], k_AT[f % 2]
                for e_ in range(8):
                    for j in range(2):
                        P.ins("tensor", (lambda pp=pp, f=f, e_=e_, j=j:
                                         lambda e: e.matmul(
                            pp[:, j * 512:(j + 1) * 512],
                            wq_nat[:, e_ * D + f * 128:e_ * D + (f + 1) * 128],
                            p_sb(e_)[:, j * 512:(j + 1) * 512],
                            start=(e_ == 0), stop=(e_ == 7)))(),
                            reads=[R("wq_nat", e_ * D, (e_ + 1) * D),
                                   k_psb(e_)], writes=[kpp],
                            tag=f"mmAT{b}_{f}_{e_}_{j}")
                kct = k_ct(f)
                P.ins("vector", (lambda pp=pp, f=f: lambda e: e.tensor_tensor(
                    ct_sb(f), pp[:, :], weffT[:, f * D:(f + 1) * D], ALU.add))(),
                    reads=[kpp, R("weffT", f * D, (f + 1) * D)], writes=[kct],
                    tag=f"ct{b}_{f}")
            if debug:
                P.ins("gpsimd", (lambda b=b: lambda e: e.dma_start(
                    dbg_d["CT"][b], arena[:, 8 * D:16 * D]))(),
                    reads=[R("arena", 8 * D, 16 * D)], dma=True, tag=f"dbgCT{b}")
            psC2.close()

            psF = ExitStack()
            Ops = [psF.enter_context(
                nc.psum_tensor(f"O{b}_{i}", [128, D], F32)) for i in range(3)]
            Fps = [psF.enter_context(
                nc.psum_tensor(f"ftp{b}_{i}", [128, 128], BF16)) for i in range(2)]
            k_O = [psum_key(t) for t in Ops]
            k_F = [psum_key(t) for t in Fps]
            def fxv(i):
                p, h = i // 2, i % 2
                lo = (p % 2) * 4 * D + h * 2 * D
                return (arena2[:, lo:lo + 2 * D].bitcast(F32),
                        R("arena2", lo, lo + 2 * D))

            def f_head(i):
                if i % 2 == 0:
                    p = i // 2
                    lo = (p % 2) * 4 * D
                    dst = arena2[:, lo:lo + 4 * D].bitcast(F32)
                    kd = R("arena2", lo, lo + 4 * D)
                    P.ins("sync", (lambda dst=dst, b=b, i=i: lambda e:
                                   e.dma_start(
                        dst.rearrange("p (a f) -> p a f", a=2),
                        xs[b, i * 128:(i + 2) * 128, :].rearrange(
                            "(a p) f -> p a f", p=128)))(),
                        writes=[kd], dma=True, tag=f"dma_xF{b}_{i}")
                sl, ksl = fxv(i)
                xv, kxv = xnf[i % 2], R(f"xnf{i % 2}")
                P.ins("scalar", (lambda sl=sl, xv=xv, i=i: lambda e: e.activation(
                    xv[:, :], sl, AF.Identity,
                    bias=lncols[:, 16 + i:17 + i], scale=lncols[:, i:i + 1]))(),
                    reads=[ksl, R("lncols", i, i + 1), R("lncols", 16 + i, 17 + i)],
                    writes=[kxv], tag=f"xnF{b}_{i}")
                for c in range(8):
                    fp, kfp = Fps[c % 2], k_F[c % 2]
                    P.ins("tensor", (lambda fp=fp, xv=xv, c=c: lambda e:
                                     e.transpose(fp[:, :],
                                                 xv[:, c * 128:(c + 1) * 128],
                                                 ident_bf[:, :]))(),
                        reads=[kxv, R("ident_bf")], writes=[kfp],
                        tag=f"ftp{b}_{i}_{c}")
                    if c % 2 == 0:
                        P.ins("scalar", (lambda fp=fp, xv=xv, c=c: lambda e:
                                         e.copy(xv[:, c * 128:(c + 1) * 128],
                                                fp[:, :]))(),
                            reads=[kfp], writes=[kxv], tag=f"fev{b}_{i}_{c}")
                    else:
                        P.ins("vector", (lambda fp=fp, xv=xv, c=c: lambda e:
                                         e.tensor_copy(
                                             xv[:, c * 128:(c + 1) * 128],
                                             fp[:, :]))(),
                            reads=[kfp], writes=[kxv], tag=f"fev{b}_{i}_{c}")

            def f_tail(i):
                xt, kxt = xnf[i % 2], R(f"xnf{i % 2}")
                op, kop = Ops[i % 3], k_O[i % 3]
                for f in range(8):
                    for j in range(2):
                        P.ins("tensor", (lambda op=op, xt=xt, f=f, j=j:
                                         lambda e: e.matmul(
                            op[:, j * 512:(j + 1) * 512],
                            xt[:, f * 128:(f + 1) * 128],
                            ct_sb(f)[:, j * 512:(j + 1) * 512],
                            start=(f == 0), stop=False))(),
                            reads=[kxt, k_ct(f)], writes=[kop],
                            tag=f"mmO{b}_{i}_{f}_{j}")
                for j in range(2):
                    P.ins("tensor", (lambda op=op, j=j: lambda e: e.matmul(
                        op[:, j * 512:(j + 1) * 512], ones_row,
                        RB(7, j * 512, 512), start=False, stop=True))(),
                        reads=[k_ones_row, kRB(7)], writes=[kop],
                        tag=f"mmOb{b}_{i}_{j}")

            def f_fin(i):
                sl, ksl = fxv(i)
                op, kop = Ops[i % 3], k_O[i % 3]
                P.ins("vector", (lambda sl=sl, op=op: lambda e: e.tensor_tensor(
                    sl, sl, op[:, :], ALU.add))(),
                    reads=[ksl, kop], writes=[ksl], tag=f"add{b}_{i}")
                if i % 2 == 1:
                    p = i // 2
                    lo = (p % 2) * 4 * D
                    srcv = arena2[:, lo:lo + 4 * D].bitcast(F32)
                    ks2 = R("arena2", lo, lo + 4 * D)
                    P.ins("gpsimd", (lambda srcv=srcv, b=b, i=i: lambda e:
                                     e.dma_start(
                        ys[b, (i - 1) * 128:(i + 1) * 128, :].rearrange(
                            "(a p) f -> p a f", p=128),
                        srcv.rearrange("p (a f) -> p a f", a=2)))(),
                        reads=[ks2], dma=True, tag=f"out{b}_{i}")

            for i in range(NT):
                if i >= 2:
                    f_tail(i - 2)
                if i >= 3:
                    f_fin(i - 3)
                f_head(i)
            f_tail(NT - 2)
            f_tail(NT - 1)
            f_fin(NT - 3)
            f_fin(NT - 2)
            f_fin(NT - 1)
            psF.close()

        import os as _os
        if _os.environ.get("KERNEL_DEBUG_SBUF"):
            print("sbuf bytes remaining (inside build):", nc.sbuf_bytes_remaining)
        if _os.environ.get("KERNEL_DEBUG_DEPS"):
            want = _os.environ["KERNEL_DEBUG_DEPS"].split(",")
            for i, insd in enumerate(P.instrs):
                if insd["tag"] in want:
                    print(f"instr {i} tag={insd['tag']} eng={insd['engine']} "
                          f"dma={insd['dma']}")
                    for d in sorted(insd["deps"]):
                        dd = P.instrs[d]
                        print(f"   dep {d}: {dd['tag']} {dd['engine']} dma={dd['dma']}")
        sems = {e: E(nc.semaphore(f"s_{e}")) for e in ENGINES}
        dsems = {e: [E(nc.semaphore(f"sd_{e}_{j}"))
                     for j in range(Sched.NDMA_LANES[e])] for e in ENGINES}
        P.emit(nc, block, sems, dsems)

    return nc, dbg_d


_NC_CACHE = {}


def _get_nc(debug=False):
    key = bool(debug)
    if key not in _NC_CACHE:
        _NC_CACHE[key] = build_nc(debug)
    return _NC_CACHE[key]


def kernel(**inputs):
    x = np.ascontiguousarray(np.asarray(inputs["L0_attn_x"], np.float32))
    feed = {
        "wqkv": np.ascontiguousarray(np.asarray(inputs["L0_attn_Wqkv_w"], np.float32)),
        "bqkv": np.ascontiguousarray(np.asarray(inputs["L0_attn_Wqkv_b"], np.float32)),
        "gnw": np.ascontiguousarray(np.asarray(inputs["L0_attn_gn_w"], np.float32)),
        "gnb": np.ascontiguousarray(np.asarray(inputs["L0_attn_gn_b"], np.float32)),
        "outw": np.ascontiguousarray(np.asarray(inputs["L0_attn_out_w"], np.float32)),
        "outb": np.ascontiguousarray(np.asarray(inputs["L0_attn_out_b"], np.float32)),
        "wff": np.ascontiguousarray(np.asarray(inputs["L0_ffn_W"], np.float32)),
        "bff": np.ascontiguousarray(np.asarray(inputs["L0_ffn_b"], np.float32)),
        "ld": np.ascontiguousarray(np.asarray(inputs["L0_ffn_lora_down"], np.float32)),
        "lu": np.ascontiguousarray(np.asarray(inputs["L0_ffn_lora_up"], np.float32)),
        "da": np.ascontiguousarray(np.asarray(inputs["L0_ffn_down_aux"], np.float32)),
        "ua": np.ascontiguousarray(np.asarray(inputs["L0_ffn_up_aux"], np.float32)),
    }
    nc, _ = _get_nc(False)
    in_maps = []
    for core in range(NCORES):
        m = dict(feed)
        m["xs"] = np.ascontiguousarray(x[core * NB:(core + 1) * NB])
        in_maps.append(m)
    res = run_bass_kernel_spmd(nc, in_maps, core_ids=list(range(NCORES)))
    return np.concatenate([r["ys"] for r in res.results], axis=0)


if __name__ == "__main__":
    nc, _ = build_nc(False)
    counts = {}
    print("built ok")


# version 22
# speedup vs baseline: 1.0972x; 1.0972x over previous
import numpy as np
from contextlib import ExitStack

import concourse.bass as bass
import concourse.mybir as mybir
from concourse.bass_utils import run_bass_kernel_spmd

F32 = mybir.dt.float32
BF16 = mybir.dt.bfloat16
AF = mybir.ActivationFunctionType
ALU = mybir.AluOpType
AX = mybir.AxisListType

D = 1024
SEQ = 2048
NB = 2
NT = 16
EPS = 1e-5
NCORES = 8
ENGINES = ("sync", "gpsimd", "scalar", "vector", "tensor")


class Sched:

    def __init__(self):
        self.instrs = []
        self.acc = {}
        self._dma_count = {e: 0 for e in ENGINES}

    @staticmethod
    def r(name, lo=0, hi=1 << 60):
        return (name, lo, hi)

    def _overlap(self, key, want_writes):
        name, lo, hi = key
        out = []
        for (l, h, iid, w) in self.acc.get(name, ()):
            if w == want_writes and l < hi and lo < h:
                out.append(iid)
        return out

    def ins(self, engine, fn, reads=(), writes=(), dma=False, tag=""):
        assert engine in ENGINES
        deps = set()
        for k in reads:
            deps.update(self._overlap(k, True))
        for k in writes:
            deps.update(self._overlap(k, True))
            deps.update(self._overlap(k, False))
        iid = len(self.instrs)
        lane = self._dma_count[engine] % self.NDMA_LANES[engine] if dma else None

        def keep(d):
            dd = self.instrs[d]
            de = dd["engine"]
            if de != engine:
                return True
            if dd["dma"] and dma:
                return False
            if engine == "tensor":
                return False
            return True
        deps = {d for d in deps if keep(d)}
        best = {}
        for d in deps:
            dd = self.instrs[d]
            ch = (dd["engine"], dd["lane"]) if dd["dma"] else (dd["engine"],)
            if best.get(ch, -1) < d:
                best[ch] = d
        deps = set(best.values())
        self.instrs.append(dict(engine=engine, fn=fn, deps=deps, dma=dma,
                                lane=lane, tag=tag))
        if dma:
            self._dma_count[engine] += 1
        for k in reads:
            self.acc.setdefault(k[0], []).append((k[1], k[2], iid, False))
        for k in writes:
            self.acc.setdefault(k[0], []).append((k[1], k[2], iid, True))
        return iid

    NDMA_LANES = {"sync": 16, "gpsimd": 12, "scalar": 8, "vector": 4,
                  "tensor": 2}

    def emit(self, nc, block, sems, dsems, final_waits=("gpsimd",)):
        needed = set()
        for ins in self.instrs:
            needed.update(ins["deps"])
        cum = {}
        chan = {}
        val = {}
        for i, ins in enumerate(self.instrs):
            if ins["dma"]:
                ch = (ins["engine"], ins["lane"])
                cum[ch] = cum.get(ch, 0) + 16
                chan[i] = ch
                val[i] = cum[ch]
                needed.add(i)
            else:
                ch = (ins["engine"],)
                if i in needed:
                    cum[ch] = cum.get(ch, 0) + 1
                chan[i] = ch
                val[i] = cum.get(ch, 0)
        cum2 = {}
        for i, ins in enumerate(self.instrs):
            if not ins["dma"]:
                ch = chan[i]
                if i in needed:
                    cum2[ch] = cum2.get(ch, 0) + 1
                val[i] = cum2.get(ch, 0)
        order = {e: [i for i, ins in enumerate(self.instrs) if ins["engine"] == e]
                 for e in ENGINES}

        def hnd(ch):
            return dsems[ch[0]][ch[1]] if len(ch) == 2 else sems[ch[0]]

        totals = dict(cum)
        totals.update(cum2)

        def section(e):
            def body(eng):
                waited = {}
                for iid in order[e]:
                    ins = self.instrs[iid]
                    for d in sorted(ins["deps"], key=lambda d: val[d]):
                        dch = chan[d]
                        v = val[d]
                        assert v > 0, (d, iid)
                        if waited.get(dch, 0) < v:
                            eng.wait_ge(hnd(dch), v)
                            waited[dch] = v
                    bi = ins["fn"](eng)
                    if iid in needed:
                        ch = chan[iid]
                        bi.then_inc(hnd(ch), 16 if ins["dma"] else 1)
                if e in final_waits:
                    for ch, tot in totals.items():
                        if len(ch) == 2 and ch[0] == e and tot:
                            eng.wait_ge(hnd(ch), tot)
            return body

        block.sync(section("sync"))
        block.gpsimd(section("gpsimd"))
        block.scalar(section("scalar"))
        block.vector(section("vector"))
        block.tensor(section("tensor"))
        return totals


def build_nc(debug=False):
    nc = bass.Bass("TRN2", target_bir_lowering=False)
    P = Sched()
    R = Sched.r

    xs = nc.dram_tensor("xs", [NB, SEQ, D], F32, kind="ExternalInput").ap()
    wqkv_d = nc.dram_tensor("wqkv", [3 * D, D], F32, kind="ExternalInput").ap()
    bqkv_d = nc.dram_tensor("bqkv", [3 * D], F32, kind="ExternalInput").ap()
    gnw_d = nc.dram_tensor("gnw", [D], F32, kind="ExternalInput").ap()
    gnb_d = nc.dram_tensor("gnb", [D], F32, kind="ExternalInput").ap()
    outw_d = nc.dram_tensor("outw", [D, D], F32, kind="ExternalInput").ap()
    outb_d = nc.dram_tensor("outb", [D], F32, kind="ExternalInput").ap()
    wff_d = nc.dram_tensor("wff", [D, D], F32, kind="ExternalInput").ap()
    bff_d = nc.dram_tensor("bff", [D], F32, kind="ExternalInput").ap()
    ld_d = nc.dram_tensor("ld", [4, 100], F32, kind="ExternalInput").ap()
    lu_d = nc.dram_tensor("lu", [50, 4], F32, kind="ExternalInput").ap()
    da_d = nc.dram_tensor("da", [100, D], F32, kind="ExternalInput").ap()
    ua_d = nc.dram_tensor("ua", [D, 50], F32, kind="ExternalInput").ap()
    ys = nc.dram_tensor("ys", [NB, SEQ, D], F32, kind="ExternalOutput").ap()
    dbg_d = {}
    if debug:
        dbg_d["xn"] = nc.dram_tensor("dbg_xn", [NB, 128, 16 * D], BF16,
                                     kind="ExternalOutput").ap()
        dbg_d["X"] = nc.dram_tensor("dbg_X", [NB, 128, 8 * D], BF16,
                                    kind="ExternalOutput").ap()
        dbg_d["MT"] = nc.dram_tensor("dbg_MT", [NB, 128, D], BF16,
                                     kind="ExternalOutput").ap()
        dbg_d["CT"] = nc.dram_tensor("dbg_CT", [NB, 128, 8 * D], BF16,
                                     kind="ExternalOutput").ap()
        dbg_d["rows"] = nc.dram_tensor("dbg_rows", [NB, 1, 5 * D], F32,
                                       kind="ExternalOutput").ap()
        dbg_d["P"] = nc.dram_tensor("dbg_P", [NB, 128, 8 * D], BF16,
                                    kind="ExternalOutput").ap()
        dbg_d["weffT"] = nc.dram_tensor("dbg_weffT", [128, 8 * D], BF16,
                                        kind="ExternalOutput").ap()
        dbg_d["wqnat"] = nc.dram_tensor("dbg_wqnat", [128, 8 * D], BF16,
                                        kind="ExternalOutput").ap()
        dbg_d["lil"] = nc.dram_tensor("dbg_lil", [128, 4 * D], BF16,
                                      kind="ExternalOutput").ap()
        dbg_d["cols"] = nc.dram_tensor("dbg_cols", [NB, 128, 64], F32,
                                       kind="ExternalOutput").ap()

    with ExitStack() as st, nc.Block() as block:
        E = st.enter_context

        arena = E(nc.sbuf_tensor("arena", [128, 16 * D], BF16))
        arena2 = E(nc.sbuf_tensor("arena2", [128, 8 * D], BF16))
        wqkvT = E(nc.sbuf_tensor("wqkvT", [128, 24 * D], BF16))
        wq_nat = E(nc.sbuf_tensor("wq_nat", [128, 8 * D], BF16))
        outwT = E(nc.sbuf_tensor("outwT", [128, 8 * D], BF16))
        weffT = E(nc.sbuf_tensor("weffT", [128, 8 * D], BF16))
        stage = [E(nc.sbuf_tensor(f"stage{i}", [128, D], F32)) for i in range(3)]
        sqscr = E(nc.sbuf_tensor("sqscr", [128, D], F32))
        xnf = [E(nc.sbuf_tensor(f"xnf{i}", [128, D], BF16)) for i in range(2)]
        mblkT = E(nc.sbuf_tensor("mblkT", [128, D], BF16))
        mblk = E(nc.sbuf_tensor("mblk", [128, D], BF16))
        mblkTs = [E(nc.sbuf_tensor(f"mblkTs{i}", [128, 128], BF16)) for i in range(2)]
        lil_small = E(nc.sbuf_tensor("lil_small", [128, 128], BF16))
        ident_bf = E(nc.sbuf_tensor("ident_bf", [128, 128], BF16))
        ident_f = E(nc.sbuf_tensor("ident_f", [128, 128], F32))
        sel = E(nc.sbuf_tensor("sel", [128, 4], F32))
        selT = E(nc.sbuf_tensor("selT", [4, 128], F32))
        ones_col = E(nc.sbuf_tensor("ones_col", [128, 1], BF16))
        epscol = E(nc.sbuf_tensor("epscol", [128, 1], F32))
        rows_f = E(nc.sbuf_tensor("rows_f", [1, 5 * D], F32))
        h_sb = sqscr[:, 256:384]
        rows_b = E(nc.sbuf_tensor("rows_b", [1, 8 * D + 128], BF16))
        cols_f = E(nc.sbuf_tensor("cols_f", [128, 64], F32))
        cols_b = E(nc.sbuf_tensor("cols_b", [128, 32], BF16))
        lncols = E(nc.sbuf_tensor("lncols", [128, 32], F32))
        lnst = E(nc.sbuf_tensor("lnst", [128, 96], F32))
        st44 = E(nc.sbuf_tensor("st44", [128, 64], F32))

        RS = lambda k, o=0, n=D: (sqscr[0:1, o:o + n] if k == 0
                                  else rows_f[0:1, (k - 1) * D + o:(k - 1) * D + o + n])
        kRS = lambda k, o=0, n=D: (R("sqscr", o, o + n) if k == 0
                                   else R("rows_f", (k - 1) * D + o, (k - 1) * D + o + n))
        RB = lambda k, o=0, n=D: rows_b[0:1, k * D + o:k * D + o + n]
        kRB = lambda k, o=0, n=D: R("rows_b", k * D + o, k * D + o + n)
        ones_row = rows_b[0:1, 8 * D:8 * D + 128]
        k_ones_row = R("rows_b", 8 * D, 8 * D + 128)
        CF = lambda k, n=1: cols_f[:, k:k + n]
        kCF = lambda k, n=1: R("cols_f", k, k + n)
        CB = lambda k, n=1: cols_b[:, k:k + n]
        kCB = lambda k, n=1: R("cols_b", k, k + n)

        xn_tm = lambda i: arena[:, i * D:(i + 1) * D]
        k_xn = lambda i: R("arena", i * D, (i + 1) * D)
        t_cat = lambda f: arena[:, f * 2 * D:(f + 1) * 2 * D]
        k_tcat = lambda f: R("arena", f * 2 * D, (f + 1) * 2 * D)
        ct_sb = lambda f: arena[:, 8 * D + f * D:8 * D + (f + 1) * D]
        k_ct = lambda f: R("arena", 8 * D + f * D, 8 * D + (f + 1) * D)
        x_sb = lambda g: arena2[:, g * D:(g + 1) * D]
        k_xsb = lambda g: R("arena2", g * D, (g + 1) * D)
        p_sb = lambda e: arena2[:, e * D:(e + 1) * D]
        k_psb = k_xsb
        v_dab = xnf[0][0:100, :]
        k_dab = R("xnf0")
        v_uaT = xnf[1][0:50, :]
        k_uaT = R("xnf1")
        v_t1 = arena2[0:4, 0:D]
        k_t1 = R("arena2", 0, D)
        v_t2 = arena2[0:50, D:2 * D]
        k_t2 = R("arena2", D, 2 * D)
        v_ldT = lil_small[0:100, 0:4]
        k_ldT = R("lil_small", 0, 4)
        v_luT = lil_small[0:4, 64:114]
        k_luT = R("lil_small", 64, 114)

        def psum_key(handle):
            for alloc in nc.m.functions[0].allocations:
                mls = getattr(alloc, "memorylocations", None)
                if mls:
                    for ml in mls:
                        if ml.name == handle.name:
                            return R("PSUM", ml.addr, ml.addr + int(ml.dims[-1]))
            raise KeyError(handle.name)

        P.ins("gpsimd", lambda g: g.memset(ident_bf[:, :], 0.0),
              writes=[R("ident_bf")], tag="c0")
        P.ins("gpsimd", lambda g: g.affine_select(
            out=ident_bf[:, :], in_=ident_bf[:, :], compare_op=ALU.not_equal,
            fill=1.0, base=0, pattern=[[-1, 128]], channel_multiplier=1),
            reads=[R("ident_bf")], writes=[R("ident_bf")], tag="c1")
        P.ins("gpsimd", lambda g: g.memset(ident_f[:, :], 0.0),
              writes=[R("ident_f")])
        P.ins("gpsimd", lambda g: g.affine_select(
            out=ident_f[:, :], in_=ident_f[:, :], compare_op=ALU.not_equal,
            fill=1.0, base=0, pattern=[[-1, 128]], channel_multiplier=1),
            reads=[R("ident_f")], writes=[R("ident_f")])
        P.ins("gpsimd", lambda g: g.memset(sel[:, :], 1.0), writes=[R("sel")])
        P.ins("gpsimd", lambda g: g.affine_select(
            out=sel[:, :], in_=sel[:, :], compare_op=ALU.is_ge, fill=0.0,
            base=0, pattern=[[-32, 4]], channel_multiplier=1),
            reads=[R("sel")], writes=[R("sel")])
        P.ins("gpsimd", lambda g: g.affine_select(
            out=sel[:, :], in_=sel[:, :], compare_op=ALU.is_ge, fill=0.0,
            base=31, pattern=[[32, 4]], channel_multiplier=-1),
            reads=[R("sel")], writes=[R("sel")])
        P.ins("gpsimd", lambda g: g.memset(selT[:, :], 1.0), writes=[R("selT")])
        P.ins("gpsimd", lambda g: g.affine_select(
            out=selT[:, :], in_=selT[:, :], compare_op=ALU.is_ge, fill=0.0,
            base=0, pattern=[[1, 128]], channel_multiplier=-32),
            reads=[R("selT")], writes=[R("selT")])
        P.ins("gpsimd", lambda g: g.affine_select(
            out=selT[:, :], in_=selT[:, :], compare_op=ALU.is_ge, fill=0.0,
            base=31, pattern=[[-1, 128]], channel_multiplier=32),
            reads=[R("selT")], writes=[R("selT")])
        P.ins("gpsimd", lambda g: g.memset(ones_col[:, :], 1.0),
              writes=[R("ones_col")])
        P.ins("gpsimd", lambda g: g.memset(epscol[:, :], EPS), writes=[R("epscol")])
        P.ins("gpsimd", lambda g: g.memset(ones_row, 1.0), writes=[k_ones_row])
        P.ins("gpsimd", lambda g: g.memset(mblkT[:, :], 0.0), writes=[R("mblkT")])
        P.ins("gpsimd", lambda g: g.memset(mblk[:, :], 0.0), writes=[R("mblk")])


        def make_A1(b):
            def ln_stats(i):
                n = b * NT + i
                sl = stage[n % 3]
                ksl = R(f"stage{n % 3}")
                P.ins("gpsimd", (lambda sl=sl, b=b, i=i: lambda e: e.dma_start(
                    sl[:, :], xs[b, i * 128:(i + 1) * 128, :]))(),
                    writes=[ksl], dma=True, tag=f"dma_xA{b}_{i}")
                P.ins("tensor", (lambda: lambda e: e.ldweights(
                    ident_bf[:, :]))(),
                    reads=[ksl, R("ident_bf")], writes=[R("HEAT")],
                    tag=f"heat{b}_{i}")
                P.ins("vector", (lambda sl=sl, i=i: lambda e: e.reduce_sum(
                    lnst[:, i:i + 1], sl[:, :], axis=AX.X))(),
                    reads=[ksl], writes=[R("lnst", i, i + 1)], tag=f"ln_sum{i}")
                P.ins("scalar", (lambda sl=sl: lambda e: e.activation(
                    sqscr[:, :], sl[:, :], AF.Square))(),
                    reads=[ksl], writes=[R("sqscr")], tag=f"ln_sq{i}")
                P.ins("vector", (lambda i=i: lambda e: e.reduce_sum(
                    lnst[:, 16 + i:17 + i], sqscr[:, :], axis=AX.X))(),
                    reads=[R("sqscr")], writes=[R("lnst", 16 + i, 17 + i)],
                    tag=f"ln_ssq{i}")
                P.ins("vector", (lambda i=i: lambda e: e.tensor_scalar(
                    lnst[:, 32 + i:33 + i], lnst[:, i:i + 1], -1.0 / D, None,
                    ALU.mult))(),
                    reads=[R("lnst", i, i + 1)],
                    writes=[R("lnst", 32 + i, 33 + i)], tag=f"ln_negmean{i}")
                P.ins("vector", (lambda i=i: lambda e: e.tensor_tensor(
                    lnst[:, 48 + i:49 + i], lnst[:, 32 + i:33 + i],
                    lnst[:, 32 + i:33 + i], ALU.mult))(),
                    reads=[R("lnst", 32 + i, 33 + i)],
                    writes=[R("lnst", 48 + i, 49 + i)], tag=f"ln_msq{i}")
                P.ins("vector", (lambda i=i: lambda e: e.scalar_tensor_tensor(
                    lnst[:, 64 + i:65 + i], lnst[:, 16 + i:17 + i], 1.0 / D,
                    lnst[:, 48 + i:49 + i], ALU.mult, ALU.subtract))(),
                    reads=[R("lnst", 16 + i, 17 + i), R("lnst", 48 + i, 49 + i)],
                    writes=[R("lnst", 64 + i, 65 + i)], tag=f"ln_var{i}")
                P.ins("scalar", (lambda i=i: lambda e: e.activation(
                    lnst[:, 80 + i:81 + i], lnst[:, 64 + i:65 + i], AF.Ln,
                    bias=epscol[:, :]))(),
                    reads=[R("lnst", 64 + i, 65 + i), R("epscol")],
                    writes=[R("lnst", 80 + i, 81 + i)], tag=f"ln_lnv{i}")
                P.ins("scalar", (lambda i=i: lambda e: e.activation(
                    lncols[:, i:i + 1], lnst[:, 80 + i:81 + i], AF.Exp,
                    scale=-0.5))(),
                    reads=[R("lnst", 80 + i, 81 + i)],
                    writes=[R("lncols", i, i + 1)], tag=f"ln_rstd{i}")

            def ln_finish(i):
                n = b * NT + i
                sl = stage[n % 3]
                ksl = R(f"stage{n % 3}")
                P.ins("vector", (lambda i=i: lambda e: e.tensor_scalar(
                    lncols[:, 16 + i:17 + i], lnst[:, 32 + i:33 + i],
                    lncols[:, i:i + 1], None, ALU.mult))(),
                    reads=[R("lnst", 32 + i, 33 + i), R("lncols", i, i + 1)],
                    writes=[R("lncols", 16 + i, 17 + i)], tag=f"ln_negmr{i}")
                P.ins("vector", (lambda sl=sl, i=i, b=b: lambda e: e.tensor_scalar(
                    xn_tm(i), sl[:, :], lncols[:, i:i + 1],
                    lncols[:, 16 + i:17 + i], ALU.mult, ALU.add))(),
                    reads=[ksl, R("lncols", i, i + 1),
                           R("lncols", 16 + i, 17 + i)],
                    writes=[k_xn(i)], tag=f"xn{b}_{i}")

            def tile(k):
                if k >= 2:
                    ln_finish(k - 2)
                if k < NT:
                    ln_stats(k)
                if k == NT:
                    ln_finish(NT - 2)
                    ln_finish(NT - 1)
            return tile

        ps0b = ExitStack()
        ctp_ps = [ps0b.enter_context(nc.psum_tensor(f"ctp{i}", [128, 4], F32))
                  for i in range(2)]
        k_ctp = [psum_key(t) for t in ctp_ps]
        lil_ps = [ps0b.enter_context(nc.psum_tensor(f"lilp{i}", [128, 512], F32))
                  for i in range(2)]
        k_lilp = [psum_key(t) for t in lil_ps]
        ps0 = ExitStack()
        tp32_ps = [ps0.enter_context(nc.psum_tensor(f"tp32_{i}", [128, 512], BF16))
                   for i in range(4)]
        k_tp32 = [psum_key(t) for t in tp32_ps]

        wslot = [arena2[:, 0:8 * D].bitcast(F32),
                 arena[:, 6 * D:14 * D].bitcast(F32)]
        k_wslot = [R("arena2", 0, 8 * D), R("arena", 6 * D, 14 * D)]

        def w_grp_dram(g):
            if g < 6:
                src = wqkv_d[g * 512:(g + 1) * 512, :]
            elif g < 8:
                src = outw_d[(g - 6) * 512:(g - 5) * 512, :]
            else:
                src = wff_d[(g - 8) * 512:(g - 7) * 512, :]
            return src.rearrange("(a p) f -> p a f", p=128)

        def w_tp_dest(t, c):
            if t < 24:
                lo = c * 3 * D + t * 128
                return wqkvT[:, lo:lo + 128], R("wqkvT", lo, lo + 128)
            if t < 32:
                lo = c * D + (t - 24) * 128
                return outwT[:, lo:lo + 128], R("outwT", lo, lo + 128)
            lo = c * D + (t - 32) * 128
            return weffT[:, lo:lo + 128], R("weffT", lo, lo + 128)

        def emit_wgroup(g):
            sl = wslot[g % 2]
            ksl = k_wslot[g % 2]
            P.ins("sync" if g % 2 == 0 else "gpsimd",
                  (lambda sl=sl, g=g: lambda e: e.dma_start(
                      sl.rearrange("p (a f) -> p a f", a=4), w_grp_dram(g)))(),
                  writes=[ksl], dma=True, tag=f"dma_wg{g}")
            for q in range(4):
                t = g * 4 + q
                if t < 8:
                    cb, kcb = wq_nat[:, t * D:(t + 1) * D], R("wq_nat", t * D,
                                                              (t + 1) * D)
                else:
                    cb, kcb = xnf[t % 2][:, :], R(f"xnf{t % 2}")
                P.ins("vector", (lambda sl=sl, q=q, cb=cb: lambda e:
                                 e.tensor_copy(cb, sl[:, q * D:(q + 1) * D]))(),
                    reads=[ksl], writes=[kcb], tag=f"cast_w{t}")
                for c in range(8):
                    j = t * 8 + c
                    ps = tp32_ps[j % 4]
                    kps = k_tp32[j % 4]
                    P.ins("tensor", (lambda ps=ps, cb=cb, c=c: lambda e:
                                     e.transpose(ps[:, 0:128],
                                                 cb[:, c * 128:(c + 1) * 128],
                                                 ident_bf[:, :]))(),
                        reads=[kcb, R("ident_bf")], writes=[kps],
                        tag=f"tp_w{t}_{c}")
                    dv, kdv = w_tp_dest(t, c)
                    if c % 3 == 0:
                        P.ins("scalar", (lambda dv=dv, ps=ps: lambda e: e.copy(
                            dv, ps[:, 0:128]))(), reads=[kps], writes=[kdv],
                            tag=f"ev_w{t}_{c}")
                    else:
                        P.ins("vector", (lambda dv=dv, ps=ps: lambda e:
                                         e.tensor_copy(dv, ps[:, 0:128]))(),
                            reads=[kps], writes=[kdv], tag=f"ev_w{t}_{c}")

        a1_tile_b0 = make_A1(0)
        INTERLEAVE = True
        if INTERLEAVE:
            for k in range(NT + 5):
                if k < 10:
                    emit_wgroup(k)
                if k >= 4:
                    a1_tile_b0(k - 4)
        else:
            for k in range(10):
                emit_wgroup(k)
            for k in range(NT + 1):
                a1_tile_b0(k)

        ps0.close()
        st_da = arena2[0:100, 2 * D:4 * D].bitcast(F32)
        k_stda = R("arena2", 2 * D, 4 * D)
        st_ld = arena2[0:4, 4 * D:4 * D + 200].bitcast(F32)
        k_stld = R("arena2", 4 * D, 4 * D + 200)
        st_lu = arena2[0:50, 4 * D + 256:4 * D + 264].bitcast(F32)
        k_stlu = R("arena2", 4 * D + 256, 4 * D + 264)
        P.ins("sync", lambda e: e.dma_start(st_da, da_d[:, :]),
              writes=[k_stda], dma=True, tag="dma_da")
        P.ins("sync", lambda e: e.dma_start(st_ld, ld_d[:, :]),
              writes=[k_stld], dma=True, tag="dma_ld")
        P.ins("sync", lambda e: e.dma_start(st_lu, lu_d[:, :]),
              writes=[k_stlu], dma=True, tag="dma_lu")
        P.ins("gpsimd", lambda e: e.tensor_copy(v_dab, st_da),
              reads=[k_stda], writes=[k_dab], tag="cast_da")
        if debug:
            P.ins("gpsimd", lambda e: e.dma_start(
                dbg_d["lil"][:, 0:D], xnf[0][:, :]),
                reads=[R("xnf0")], dma=True, tag="dbglil0_early")
        P.ins("tensor", lambda e: e.transpose(
            ctp_ps[0][0:100, 0:4], st_ld[:, 0:100], ident_f[0:4, 0:4]),
            reads=[k_stld, R("ident_f")], writes=[k_ctp[0]],
            tag="tp_ldT")
        P.ins("scalar", lambda e: e.copy(v_ldT, ctp_ps[0][0:100, 0:4]),
              reads=[k_ctp[0]], writes=[k_ldT], tag="ev_ldT")
        P.ins("tensor", lambda e: e.transpose(
            lil_ps[0][0:4, 0:50], st_lu[:, 0:4], ident_f[0:50, 0:50]),
            reads=[k_stlu, R("ident_f")], writes=[k_lilp[0]],
            tag="tp_luT")
        P.ins("scalar", lambda e: e.copy(v_luT, lil_ps[0][0:4, 0:50]),
              reads=[k_lilp[0]], writes=[k_luT], tag="ev_luT")
        for t in range(8):
            lo = 5 * D + (t % 2) * 128
            sl = arena2[:, lo:lo + 100].bitcast(F32)
            ksl = R("arena2", lo, lo + 100)
            P.ins("sync", (lambda sl=sl, t=t: lambda e: e.dma_start(
                sl, ua_d[t * 128:(t + 1) * 128, :]))(), writes=[ksl], dma=True,
                tag=f"dma_ua{t}")
            pp = lil_ps[1] if t % 2 else lil_ps[0]
            kpp = k_lilp[1] if t % 2 else k_lilp[0]
            P.ins("tensor", (lambda pp=pp, sl=sl: lambda e: e.transpose(
                pp[0:50, 0:128], sl, ident_f[:, :]))(),
                reads=[ksl, R("ident_f")], writes=[kpp], tag=f"tp_ua{t}")
            dv = v_uaT[:, t * 128:(t + 1) * 128]
            P.ins("scalar", (lambda dv=dv, pp=pp: lambda e: e.copy(
                dv, pp[0:50, 0:128]))(), reads=[kpp],
                writes=[R("xnf1", t * 128, (t + 1) * 128)],
                tag=f"ev_ua{t}")
        for j in range(2):
            pp = lil_ps[j]
            P.ins("tensor", (lambda pp=pp, j=j: lambda e: e.matmul(
                pp[0:4, 0:512], v_ldT, v_dab[:, j * 512:(j + 1) * 512],
                start=True, stop=True))(),
                reads=[k_ldT, k_dab],
                writes=[k_lilp[j]], tag=f"mm_t1_{j}")
            P.ins("scalar", (lambda pp=pp, j=j: lambda e: e.copy(
                v_t1[:, j * 512:(j + 1) * 512], pp[0:4, 0:512]))(),
                reads=[k_lilp[j]],
                writes=[R("arena2", j * 512, (j + 1) * 512)],
                tag=f"ev_t1_{j}")
        for j in range(2):
            pp = lil_ps[j]
            P.ins("tensor", (lambda pp=pp, j=j: lambda e: e.matmul(
                pp[0:50, 0:512], v_luT, v_t1[:, j * 512:(j + 1) * 512],
                start=True, stop=True))(),
                reads=[k_luT, k_t1],
                writes=[k_lilp[j]], tag=f"mm_t2_{j}")
            P.ins("scalar", (lambda pp=pp, j=j: lambda e: e.copy(
                v_t2[:, j * 512:(j + 1) * 512], pp[0:50, 0:512]))(),
                reads=[k_lilp[j]],
                writes=[R("arena2", D + j * 512, D + (j + 1) * 512)],
                tag=f"ev_t2_{j}")
        for f in range(8):
            for j in range(2):
                pp = lil_ps[j]
                P.ins("tensor", (lambda pp=pp, f=f, j=j: lambda e: e.matmul(
                    pp[:, 0:512], v_t2[:, f * 128:(f + 1) * 128],
                    v_uaT[:, j * 512:(j + 1) * 512], start=True, stop=True))(),
                    reads=[k_t2, k_uaT],
                    writes=[k_lilp[j]], tag=f"mm_lt{f}_{j}")
                wv = weffT[:, f * D + j * 512:f * D + (j + 1) * 512]
                kw = R("weffT", f * D + j * 512, f * D + (j + 1) * 512)
                P.ins("vector", (lambda wv=wv, pp=pp: lambda e: e.tensor_tensor(
                    wv, pp[:, 0:512], wv, ALU.add))(),
                    reads=[k_lilp[j], kw], writes=[kw], tag=f"add_lt{f}_{j}")

        if debug:
            P.ins("gpsimd", lambda e: e.dma_start(
                dbg_d["lil"][:, D:2 * D], xnf[1][:, :]),
                reads=[R("xnf1")], dma=True, tag="dbglil1")
            P.ins("gpsimd", lambda e: e.dma_start(
                dbg_d["lil"][:, 2 * D:4 * D], arena2[:, 0:2 * D]),
                reads=[R("arena2", 0, 2 * D)], dma=True, tag="dbglil2")
        def row_dma(dram_ap, tag):
            ap2 = dram_ap.rearrange("(a d) -> a d", a=1)
            P.ins("sync", (lambda ap2=ap2: lambda e: e.dma_start(RS(0), ap2))(),
                  writes=[kRS(0)], dma=True, tag=tag)

        row_dma(bqkv_d[0:D], "dma_bq")
        P.ins("scalar", lambda e: e.activation(RB(0), RS(0), AF.Copy),
              reads=[kRS(0)], writes=[kRB(0)], tag="cast_bq")
        P.ins("scalar", lambda e: e.copy(RS(2), RS(0)),
              reads=[kRS(0)], writes=[kRS(2)], tag="keep_bq")
        for c in range(8):
            pp = ctp_ps[0]
            kpp = k_ctp[0]
            P.ins("tensor", (lambda pp=pp, c=c: lambda e: e.transpose(
                pp[:, 0:1], RS(2)[0:1, c * 128:(c + 1) * 128],
                ident_f[0:1, 0:1]))(),
                reads=[kRS(2), R("ident_f")], writes=[kpp], tag=f"tp_bqc{c}")
            P.ins("scalar", (lambda pp=pp, c=c: lambda e: e.copy(
                CF(16 + c), pp[:, 0:1]))(), reads=[kpp],
                writes=[kCF(16 + c)], tag=f"ev_bqc{c}")
        row_dma(bqkv_d[D:2 * D], "dma_bk")
        P.ins("scalar", lambda e: e.activation(RB(1), RS(0), AF.Copy),
              reads=[kRS(0)], writes=[kRB(1)], tag="cast_bk")
        row_dma(bqkv_d[2 * D:3 * D], "dma_bv")
        P.ins("scalar", lambda e: e.activation(RB(2), RS(0), AF.Copy),
              reads=[kRS(0)], writes=[kRB(2)], tag="cast_bv")
        P.ins("vector", lambda e: e.tensor_scalar(
            RS(3), RS(0), float(SEQ), None, ALU.mult),
            reads=[kRS(0)], writes=[kRS(3)], tag="sbv")
        row_dma(gnw_d[:], "dma_gnw")
        for c in range(8):
            pp = ctp_ps[0]
            kpp = k_ctp[0]
            P.ins("tensor", (lambda pp=pp, c=c: lambda e: e.transpose(
                pp[:, 0:1], RS(0)[0:1, c * 128:(c + 1) * 128],
                ident_f[0:1, 0:1]))(),
                reads=[kRS(0), R("ident_f")], writes=[kpp], tag=f"tp_gnw{c}")
            P.ins("scalar", (lambda pp=pp, c=c: lambda e: e.copy(
                CF(0 + c), pp[:, 0:1]))(), reads=[kpp], writes=[kCF(c)],
                tag=f"ev_gnw{c}")
        row_dma(gnb_d[:], "dma_gnb")
        for c in range(8):
            pp = ctp_ps[0]
            kpp = k_ctp[0]
            P.ins("tensor", (lambda pp=pp, c=c: lambda e: e.transpose(
                pp[:, 0:1], RS(0)[0:1, c * 128:(c + 1) * 128],
                ident_f[0:1, 0:1]))(),
                reads=[kRS(0), R("ident_f")], writes=[kpp], tag=f"tp_gnb{c}")
            P.ins("scalar", (lambda pp=pp, c=c: lambda e: e.copy(
                CF(8 + c), pp[:, 0:1]))(), reads=[kpp], writes=[kCF(8 + c)],
                tag=f"ev_gnb{c}")
        row_dma(outb_d[:], "dma_outb")
        P.ins("scalar", lambda e: e.copy(RS(4), RS(0)),
              reads=[kRS(0)], writes=[kRS(4)], tag="keep_outb")
        row_dma(bff_d[:], "dma_bff")
        P.ins("vector", lambda e: e.tensor_tensor(RS(4), RS(4), RS(0), ALU.add),
              reads=[kRS(4), kRS(0)], writes=[kRS(4)], tag="obf")
        P.ins("vector", lambda e: e.tensor_copy(CB(16, 8), CF(16, 8)),
              reads=[kCF(16, 8)], writes=[kCB(16, 8)], tag="bqcol_bf")

        ps0b.close()

        for b in range(NB):
            if b > 0:
                a1t = make_A1(b)
                for k in range(NT + 1):
                    a1t(k)
            psA = ExitStack()
            Xps = [psA.enter_context(
                nc.psum_tensor(f"X{b}_{i}", [128, D], F32)) for i in range(3)]
            Srow = psA.enter_context(nc.psum_tensor(f"xsr{b}", [1, D], F32))
            k_X = [psum_key(t) for t in Xps]
            k_Sr = psum_key(Srow)

            def x_chunk_mms(c, i):
                lhs = xn_tm(i)[:, c * 128:(c + 1) * 128]
                kx = k_xn(i)
                pp, kpp = Xps[c % 3], k_X[c % 3]
                st_ = (i == 0)
                sp_ = (i == NT - 1)
                for j in range(2):
                    P.ins("tensor", (lambda lhs=lhs, pp=pp, i=i, j=j, st_=st_,
                                     sp_=sp_: lambda e: e.matmul(
                        pp[:, j * 512:(j + 1) * 512], lhs,
                        xn_tm(i)[:, j * 512:(j + 1) * 512],
                        start=st_, stop=sp_))(),
                        reads=[kx], writes=[kpp], tag=f"mmX{b}_{c}_{i}_{j}")

            def xnsum_row_mms(i):
                kx = k_xn(i)
                st_ = (i == 0)
                sp_ = (i == NT - 1)
                for j in range(2):
                    P.ins("tensor", (lambda i=i, j=j, st_=st_, sp_=sp_:
                                     lambda e: e.matmul(
                        Srow[0:1, j * 512:(j + 1) * 512], ones_col[:, :],
                        xn_tm(i)[:, j * 512:(j + 1) * 512],
                        start=st_, stop=sp_))(),
                        reads=[kx, R("ones_col")], writes=[k_Sr],
                        tag=f"mmSr{b}_{i}")

            def x_evac(c):
                pp, kpp = Xps[c % 3], k_X[c % 3]
                P.ins("scalar", (lambda pp=pp, c=c: lambda e: e.copy(
                    x_sb(c), pp[:, :]))(), reads=[kpp], writes=[k_xsb(c)],
                    tag=f"evX{b}_{c}")

            for p, grp in enumerate(((0, 1, 2), (3, 4, 5), (6, 7))):
                for i in range(NT):
                    for c in grp:
                        x_chunk_mms(c, i)
                    if p == 0:
                        xnsum_row_mms(i)
                for c in grp:
                    x_evac(c)
                if p == 0:
                    P.ins("scalar", lambda e: e.copy(RS(0), Srow[0:1, :]),
                          reads=[k_Sr], writes=[kRS(0)], tag=f"evSr{b}")

            if debug:
                P.ins("gpsimd", (lambda b=b: lambda e: e.dma_start(
                    dbg_d["X"][b], arena2[:, :]))(),
                    reads=[R("arena2")], dma=True, tag=f"dbgX{b}")
                P.ins("gpsimd", (lambda b=b: lambda e: e.dma_start(
                    dbg_d["xn"][b], arena[:, :]))(),
                    reads=[R("arena", 0, 16 * D)], dma=True, tag=f"dbgxn{b}")
            psA.close()

            psB2 = ExitStack()
            Tps = [psB2.enter_context(
                nc.psum_tensor(f"T{b}_{i}", [128, 2 * D], F32)) for i in range(2)]
            k_T = [psum_key(t) for t in Tps]
            for f in range(8):
                pp, kpp = Tps[f % 2], k_T[f % 2]
                for g in range(8):
                    lhs = x_sb(g)[:, f * 128:(f + 1) * 128]
                    klhs = k_xsb(g)
                    for q in range(4):
                        off = 2 * D + (q % 2) * 512 if q < 2 else (q % 2) * 512
                        P.ins("tensor", (lambda pp=pp, lhs=lhs, g=g, q=q,
                                         off=off: lambda e: e.matmul(
                            pp[:, q * 512:(q + 1) * 512], lhs,
                            wqkvT[:, g * 3 * D + off:g * 3 * D + off + 512],
                            start=(g == 0), stop=(g == 7)))(),
                            reads=[klhs, R("wqkvT", g * 3 * D, (g + 1) * 3 * D)],
                            writes=[kpp], tag=f"mmT{b}_{f}_{g}_{q}")
                if f % 2 == 0:
                    P.ins("scalar", (lambda pp=pp, f=f: lambda e: e.copy(
                        t_cat(f), pp[:, :]))(), reads=[kpp], writes=[k_tcat(f)],
                        tag=f"evT{b}_{f}")
                else:
                    P.ins("vector", (lambda pp=pp, f=f: lambda e: e.tensor_copy(
                        t_cat(f), pp[:, :]))(), reads=[kpp], writes=[k_tcat(f)],
                        tag=f"evT{b}_{f}")
            psB2.close()
            psB1 = ExitStack()
            sums_ps = psB1.enter_context(
                nc.psum_tensor(f"sums{b}", [1, 3 * D], F32))
            k_sums = psum_key(sums_ps)
            ctpB1 = psB1.enter_context(
                nc.psum_tensor(f"ctpBB{b}", [128, 4], F32))
            k_ctpB1 = psum_key(ctpB1)
            for c in range(8):
                P.ins("tensor", (lambda c=c: lambda e: e.transpose(
                    ctpB1[:, 0:1], RS(0)[0:1, c * 128:(c + 1) * 128],
                    ident_f[0:1, 0:1]))(),
                    reads=[kRS(0), R("ident_f")], writes=[k_ctpB1],
                    tag=f"tp_xs{b}_{c}")
                P.ins("scalar", (lambda c=c: lambda e: e.copy(
                    CF(24 + c), ctpB1[:, 0:1]))(), reads=[k_ctpB1],
                    writes=[kCF(24 + c)], tag=f"ev_xs{b}_{c}")
            P.ins("vector", lambda e: e.tensor_copy(CB(0, 8), CF(24, 8)),
                  reads=[kCF(24, 8)], writes=[kCB(0, 8)], tag=f"xnsum_bf{b}")
            for c in range(8):
                for j in range(6):
                    P.ins("tensor", (lambda c=c, j=j: lambda e: e.matmul(
                        sums_ps[0:1, j * 512:(j + 1) * 512], CB(0 + c),
                        wqkvT[:, c * 3 * D + j * 512:c * 3 * D + (j + 1) * 512],
                        start=(c == 0), stop=(c == 7)))(),
                        reads=[kCB(c), R("wqkvT", c * 3 * D, (c + 1) * 3 * D)],
                        writes=[k_sums], tag=f"mm_sums{b}_{c}_{j}")
            P.ins("scalar", lambda e: e.copy(RS(1), sums_ps[0:1, 0:D]),
                  reads=[k_sums], writes=[kRS(1)], tag=f"ev_qsum0{b}")
            P.ins("vector", lambda e: e.tensor_copy(RB(3), sums_ps[0:1, D:2 * D]),
                  reads=[k_sums], writes=[kRB(3)], tag=f"ksum_bf{b}")
            P.ins("vector", lambda e: e.scalar_tensor_tensor(
                RB(4), RS(3), 1.0, sums_ps[0:1, 2 * D:3 * D], ALU.mult, ALU.add),
                reads=[kRS(3), k_sums], writes=[kRB(4)], tag=f"vsumS{b}")
            P.ins("vector", lambda e: e.tensor_copy(RB(5), sums_ps[0:1, 0:D]),
                  reads=[k_sums], writes=[kRB(5)], tag=f"qsum0_bf{b}")
            P.ins("vector", lambda e: e.scalar_tensor_tensor(
                RB(6), RS(2), float(SEQ), sums_ps[0:1, 0:D], ALU.mult, ALU.add),
                reads=[kRS(2), k_sums], writes=[kRB(6)], tag=f"qsum0S{b}")
            psB1.close()

            psC1 = ExitStack()
            Mps = [psC1.enter_context(
                nc.psum_tensor(f"M{b}_{i}", [128, 128], F32)) for i in range(2)]
            Gps = [psC1.enter_context(
                nc.psum_tensor(f"G{b}_{i}", [128, 128], F32)) for i in range(3)]
            Hps = psC1.enter_context(nc.psum_tensor(f"H{b}", [128, 128], F32))
            GHps = psC1.enter_context(nc.psum_tensor(f"GH{b}", [128, 128], F32))
            k_M = [psum_key(t) for t in Mps]
            k_G = [psum_key(t) for t in Gps]
            k_H = psum_key(Hps)
            k_GH = psum_key(GHps)
            for c in range(8):
                mp, kmp = Mps[c % 2], k_M[c % 2]
                gp, kgp = Gps[c % 2], k_G[c % 2]
                for f in range(8):
                    P.ins("tensor", (lambda mp=mp, f=f, c=c: lambda e: e.matmul(
                        mp[:, :], t_cat(f)[:, c * 128:(c + 1) * 128],
                        wqkvT[:, f * 3 * D + D + c * 128:
                              f * 3 * D + D + (c + 1) * 128],
                        start=(f == 0), stop=False))(),
                        reads=[k_tcat(f), R("wqkvT", f * 3 * D, (f + 1) * 3 * D)],
                        writes=[kmp], tag=f"mmM{b}_{c}_{f}")
                P.ins("tensor", (lambda mp=mp, c=c: lambda e: e.matmul(
                    mp[:, :], RB(4, c * 128, 128), RB(1, c * 128, 128),
                    start=False, stop=False))(),
                    reads=[kRB(4), kRB(1)], writes=[kmp], tag=f"mmMr1{b}_{c}")
                P.ins("tensor", (lambda mp=mp, c=c: lambda e: e.matmul(
                    mp[:, :], RB(2, c * 128, 128), RB(3, c * 128, 128),
                    start=False, stop=True))(),
                    reads=[kRB(2), kRB(3)], writes=[kmp], tag=f"mmMr2{b}_{c}")
                for f in range(8):
                    P.ins("tensor", (lambda gp=gp, f=f, c=c: lambda e: e.matmul(
                        gp[:, :], t_cat(f)[:, D + c * 128:D + (c + 1) * 128],
                        wqkvT[:, f * 3 * D + c * 128:f * 3 * D + (c + 1) * 128],
                        start=(f == 0), stop=False))(),
                        reads=[k_tcat(f), R("wqkvT", f * 3 * D, (f + 1) * 3 * D)],
                        writes=[kgp], tag=f"mmG{b}_{c}_{f}")
                P.ins("tensor", (lambda gp=gp, c=c: lambda e: e.matmul(
                    gp[:, :], RB(5, c * 128, 128), RB(0, c * 128, 128),
                    start=False, stop=False))(),
                    reads=[kRB(5), kRB(0)], writes=[kgp], tag=f"mmGr1{b}_{c}")
                P.ins("tensor", (lambda gp=gp, c=c: lambda e: e.matmul(
                    gp[:, :], RB(0, c * 128, 128), RB(6, c * 128, 128),
                    start=False, stop=True))(),
                    reads=[kRB(0), kRB(6)], writes=[kgp], tag=f"mmGr2{b}_{c}")
                for jj in range(4):
                    eng_ = "scalar" if jj % 2 == 0 else "vector"
                    fn_ = (lambda mp=mp, c=c, jj=jj, eng_=eng_: lambda e:
                           (e.copy if eng_ == "scalar" else e.tensor_copy)(
                        mblkT[32 * jj:32 * (jj + 1),
                              c * 128 + 32 * jj:c * 128 + 32 * (jj + 1)],
                        mp[32 * jj:32 * (jj + 1), 32 * jj:32 * (jj + 1)]))()
                    P.ins(eng_, fn_, reads=[kmp],
                          writes=[R("mblkT", c * 128, (c + 1) * 128)],
                          tag=f"exM{b}_{c}_{jj}")
                P.ins("tensor", (lambda c=c: lambda e: e.matmul(
                    Hps[:, :], mblkT[:, c * 128:(c + 1) * 128],
                    mblkT[:, c * 128:(c + 1) * 128], start=True, stop=True))(),
                    reads=[R("mblkT", c * 128, (c + 1) * 128)], writes=[k_H],
                    tag=f"mmH{b}_{c}")
                P.ins("scalar", lambda e: e.copy(h_sb, Hps[:, :]),
                      reads=[k_H], writes=[R("sqscr", 256, 384)], tag=f"evH{b}_{c}")
                P.ins("vector", (lambda gp=gp: lambda e: e.tensor_tensor(
                    GHps[:, :], gp[:, :], h_sb, ALU.mult))(),
                    reads=[kgp, R("sqscr", 256, 384)], writes=[k_GH],
                    tag=f"gh{b}_{c}")
                P.ins("vector", (lambda c=c: lambda e: e.reduce_sum(
                    CF(40 + c), GHps[:, :], axis=AX.X))(),
                    reads=[k_GH], writes=[kCF(40 + c)],
                    tag=f"ghcol{b}_{c}")

            for c in range(8):
                P.ins("vector", (lambda c=c: lambda e: e.transpose(
                    mblk[:, c * 128:(c + 1) * 128],
                    mblkT[:, c * 128:(c + 1) * 128]))(),
                    reads=[R("mblkT", c * 128, (c + 1) * 128)],
                    writes=[R("mblk", c * 128, (c + 1) * 128)], tag=f"vtM{b}_{c}")

            psQC = ExitStack()
            ctpB = [psQC.enter_context(
                nc.psum_tensor(f"ctpB{b}_{i}", [128, 4], F32)) for i in range(1)]
            k_ctpB = [psum_key(t) for t in ctpB]
            for c in range(8):
                pp, kpp = ctpB[0], k_ctpB[0]
                P.ins("tensor", (lambda pp=pp, c=c: lambda e: e.transpose(
                    pp[:, 0:1], RS(1)[0:1, c * 128:(c + 1) * 128],
                    ident_f[0:1, 0:1]))(),
                    reads=[kRS(1), R("ident_f")], writes=[kpp],
                    tag=f"tp_qs{b}_{c}")
                P.ins("scalar", (lambda pp=pp, c=c: lambda e: e.copy(
                    CF(32 + c), pp[:, 0:1]))(), reads=[kpp],
                    writes=[kCF(32 + c)], tag=f"ev_qs{b}_{c}")
            P.ins("vector", lambda e: e.scalar_tensor_tensor(
                CB(8, 8), CF(16, 8), float(SEQ), CF(32, 8), ALU.mult, ALU.add),
                reads=[kCF(16, 8), kCF(32, 8)], writes=[kCB(8, 8)],
                tag=f"qsumcol_bf{b}")
            psQC.close()
            if debug:
                P.ins("gpsimd", (lambda b=b: lambda e: e.dma_start(
                    dbg_d["MT"][b], mblkT[:, :]))(),
                    reads=[R("mblkT")], dma=True, tag=f"dbgMT{b}")
            psC1.close()

            psC1t = ExitStack()
            csum_ps = psC1t.enter_context(
                nc.psum_tensor(f"csum{b}", [1, D], F32))
            u_ps = psC1t.enter_context(nc.psum_tensor(f"u{b}", [1, D], F32))
            stat_ps = psC1t.enter_context(
                nc.psum_tensor(f"stat{b}", [128, 8], F32))
            ctpC = [psC1t.enter_context(
                nc.psum_tensor(f"ctpC{b}_{i}", [128, 4], F32)) for i in range(2)]
            k_cs = psum_key(csum_ps)
            k_u = psum_key(u_ps)
            k_st = psum_key(stat_ps)
            k_ctpC = [psum_key(t) for t in ctpC]
            for c in range(8):
                P.ins("tensor", (lambda c=c: lambda e: e.matmul(
                    csum_ps[0:1, c * 128:(c + 1) * 128], CB(8 + c),
                    mblk[:, c * 128:(c + 1) * 128],
                    start=(c % 4 == 0), stop=(c % 4 == 3)))(),
                    reads=[kCB(8 + c), R("mblk", c * 128, (c + 1) * 128)],
                    writes=[k_cs], tag=f"mmcs{b}_{c}")
            for c in range(8):
                P.ins("tensor", (lambda c=c: lambda e: e.matmul(
                    u_ps[0:1, c * 128:(c + 1) * 128], CB(16 + c),
                    mblk[:, c * 128:(c + 1) * 128],
                    start=(c % 4 == 0), stop=(c % 4 == 3)))(),
                    reads=[kCB(16 + c), R("mblk", c * 128, (c + 1) * 128)],
                    writes=[k_u], tag=f"mmu{b}_{c}")
            P.ins("scalar", lambda e: e.copy(RS(1), csum_ps[0:1, :]),
                  reads=[k_cs], writes=[kRS(1)], tag=f"ev_cs{b}")
            P.ins("scalar", lambda e: e.copy(RS(5), u_ps[0:1, :]),
                  reads=[k_u], writes=[kRS(5)], tag=f"ev_u{b}")
            for c in range(8):
                pp, kpp = ctpC[c % 2], k_ctpC[c % 2]
                P.ins("tensor", (lambda pp=pp, c=c: lambda e: e.transpose(
                    pp[:, 0:1], RS(1)[0:1, c * 128:(c + 1) * 128],
                    ident_f[0:1, 0:1]))(),
                    reads=[kRS(1), R("ident_f")], writes=[kpp],
                    tag=f"tp_cs{b}_{c}")
                P.ins("scalar", (lambda pp=pp, c=c: lambda e: e.copy(
                    CF(56 + c), pp[:, 0:1]))(), reads=[kpp],
                    writes=[kCF(56 + c)], tag=f"ev_csc{b}_{c}")
            for c in range(8):
                pp, kpp = ctpC[c % 2], k_ctpC[c % 2]
                P.ins("tensor", (lambda pp=pp, c=c: lambda e: e.transpose(
                    pp[:, 0:1], RS(5)[0:1, c * 128:(c + 1) * 128],
                    ident_f[0:1, 0:1]))(),
                    reads=[kRS(5), R("ident_f")], writes=[kpp],
                    tag=f"tp_u{b}_{c}")
                P.ins("scalar", (lambda pp=pp, c=c: lambda e: e.copy(
                    CF(48 + c), pp[:, 0:1]))(), reads=[kpp],
                    writes=[kCF(48 + c)], tag=f"ev_uc{b}_{c}")
            P.ins("tensor", lambda e: e.matmul(
                stat_ps[0:4, 0:8], sel[:, :], CF(40, 8), start=True, stop=True),
                reads=[R("sel"), kCF(40, 8)], writes=[k_st], tag=f"mmssq{b}")
            P.ins("scalar", lambda e: e.copy(st44[0:4, 0:8], stat_ps[0:4, 0:8]),
                  reads=[k_st], writes=[R("st44", 0, 8)], tag=f"ev_ssq44{b}")
            P.ins("tensor", lambda e: e.matmul(
                stat_ps[0:4, 0:8], sel[:, :], CF(56, 8), start=True, stop=True),
                reads=[R("sel"), kCF(56, 8)], writes=[k_st], tag=f"mmcs44{b}")
            P.ins("scalar", lambda e: e.copy(st44[0:4, 8:16], stat_ps[0:4, 0:8]),
                  reads=[k_st], writes=[R("st44", 8, 16)], tag=f"ev_cs44{b}")
            P.ins("vector", lambda e: e.tensor_scalar(
                st44[0:4, 16:24], st44[0:4, 8:16], 1.0 / (32 * SEQ), None,
                ALU.mult), reads=[R("st44", 8, 16)], writes=[R("st44", 16, 24)],
                tag=f"mean44{b}")
            P.ins("vector", lambda e: e.tensor_tensor(
                st44[0:4, 24:32], st44[0:4, 16:24], st44[0:4, 16:24], ALU.mult),
                reads=[R("st44", 16, 24)], writes=[R("st44", 24, 32)],
                tag=f"msq44{b}")
            P.ins("vector", lambda e: e.scalar_tensor_tensor(
                st44[0:4, 32:40], st44[0:4, 0:8], 1.0 / (32 * SEQ),
                st44[0:4, 24:32], ALU.mult, ALU.subtract),
                reads=[R("st44", 0, 8), R("st44", 24, 32)],
                writes=[R("st44", 32, 40)], tag=f"var44{b}")
            P.ins("scalar", lambda e: e.activation(
                st44[0:4, 40:48], st44[0:4, 32:40], AF.Sqrt,
                bias=epscol[0:4, :]),
                reads=[R("st44", 32, 40), R("epscol")],
                writes=[R("st44", 40, 48)], tag=f"sd44{b}")
            P.ins("vector", lambda e: e.reciprocal(
                st44[0:4, 48:56], st44[0:4, 40:48]),
                reads=[R("st44", 40, 48)], writes=[R("st44", 48, 56)],
                tag=f"rstd44{b}")
            P.ins("tensor", lambda e: e.matmul(
                stat_ps[:, 0:8], selT[:, :], st44[0:4, 48:56],
                start=True, stop=True),
                reads=[R("selT"), R("st44", 48, 56)], writes=[k_st],
                tag=f"bc_rstd{b}")
            P.ins("vector", lambda e: e.tensor_tensor(
                CF(32, 8), stat_ps[:, 0:8], CF(0, 8), ALU.mult),
                reads=[k_st, kCF(0, 8)], writes=[kCF(32, 8)], tag=f"scols{b}")
            P.ins("tensor", lambda e: e.matmul(
                stat_ps[:, 0:8], selT[:, :], st44[0:4, 16:24],
                start=True, stop=True),
                reads=[R("selT"), R("st44", 16, 24)], writes=[k_st],
                tag=f"bc_mean{b}")
            P.ins("vector", lambda e: e.tensor_tensor(
                sqscr[:, 128:136], stat_ps[:, 0:8], CF(32, 8), ALU.mult),
                reads=[k_st, kCF(32, 8)], writes=[R("sqscr", 128, 136)],
                tag=f"mrcols{b}")
            P.ins("vector", lambda e: e.tensor_tensor(
                CF(40, 8), CF(8, 8), sqscr[:, 128:136], ALU.subtract),
                reads=[kCF(8, 8), R("sqscr", 128, 136)], writes=[kCF(40, 8)],
                tag=f"ttcols{b}")
            P.ins("vector", lambda e: e.tensor_tensor(
                sqscr[:, 136:144], CF(48, 8), CF(32, 8), ALU.mult),
                reads=[kCF(48, 8), kCF(32, 8)], writes=[R("sqscr", 136, 144)],
                tag=f"su{b}")
            P.ins("vector", lambda e: e.tensor_tensor(
                CB(24, 8), sqscr[:, 136:144], CF(40, 8), ALU.add),
                reads=[R("sqscr", 136, 144), kCF(40, 8)], writes=[kCB(24, 8)],
                tag=f"wcols{b}")
            if debug:
                P.ins("gpsimd", (lambda b=b: lambda e: e.dma_start(
                    dbg_d["cols"][b], cols_f[:, :]))(),
                    reads=[R("cols_f")], dma=True, tag=f"dbgcols{b}")
            psC1t.close()

            psC2a = ExitStack()
            beta_ps = psC2a.enter_context(
                nc.psum_tensor(f"beta{b}", [1, D], F32))
            k_beta = psum_key(beta_ps)
            for c in range(8):
                for j in range(2):
                    P.ins("tensor", (lambda c=c, j=j: lambda e: e.matmul(
                        beta_ps[0:1, j * 512:(j + 1) * 512], CB(24 + c),
                        outwT[:, c * D + j * 512:c * D + (j + 1) * 512],
                        start=(c == 0), stop=(c == 7)))(),
                        reads=[kCB(24 + c), R("outwT", c * D, (c + 1) * D)],
                        writes=[k_beta], tag=f"mmbeta{b}_{c}_{j}")
            P.ins("scalar", lambda e: e.copy(RS(5), beta_ps[0:1, :]),
                  reads=[k_beta], writes=[kRS(5)], tag=f"ev_beta{b}")
            P.ins("vector", lambda e: e.tensor_tensor(
                RB(7), RS(5), RS(4), ALU.add),
                reads=[kRS(5), kRS(4)], writes=[kRB(7)], tag=f"bias_bf{b}")
            if debug:
                P.ins("gpsimd", (lambda b=b: lambda e: e.dma_start(
                    dbg_d["rows"][b], rows_f[0:1, :]))(),
                    reads=[R("rows_f")], dma=True, tag=f"dbgrows{b}")
            psC2a.close()

            if debug and b == 0:
                P.ins("gpsimd", lambda e: e.dma_start(
                    dbg_d["weffT"], weffT[:, :]),
                    reads=[R("weffT")], dma=True, tag="dbgweff")
                P.ins("gpsimd", lambda e: e.dma_start(
                    dbg_d["wqnat"], wq_nat[:, :]),
                    reads=[R("wq_nat")], dma=True, tag="dbgwqnat")
            psC2 = ExitStack()
            Pps = [psC2.enter_context(
                nc.psum_tensor(f"P{b}_{i}", [128, D], F32)) for i in range(2)]
            ATps = [psC2.enter_context(
                nc.psum_tensor(f"AT{b}_{i}", [128, D], F32)) for i in range(2)]
            k_P = [psum_key(t) for t in Pps]
            k_AT = [psum_key(t) for t in ATps]
            for c in range(8):
                ms, kms = mblkTs[c % 2], R(f"mblkTs{c % 2}")
                P.ins("scalar", (lambda ms=ms, c=c: lambda e: e.activation(
                    ms[:, :], mblkT[:, c * 128:(c + 1) * 128], AF.Identity,
                    scale=CF(32 + c)))(),
                    reads=[R("mblkT", c * 128, (c + 1) * 128), kCF(32 + c)],
                    writes=[kms], tag=f"mts{b}_{c}")
                pp, kpp = Pps[c % 2], k_P[c % 2]
                for j in range(2):
                    P.ins("tensor", (lambda pp=pp, ms=ms, c=c, j=j:
                                     lambda e: e.matmul(
                        pp[:, j * 512:(j + 1) * 512], ms[:, :],
                        outwT[:, c * D + j * 512:c * D + (j + 1) * 512],
                        start=True, stop=True))(),
                        reads=[kms, R("outwT", c * D, (c + 1) * D)],
                        writes=[kpp], tag=f"mmP{b}_{c}_{j}")
                P.ins("scalar", (lambda pp=pp, c=c: lambda e: e.copy(
                    p_sb(c), pp[:, :]))(), reads=[kpp], writes=[k_psb(c)],
                    tag=f"evP{b}_{c}")
            if debug:
                P.ins("gpsimd", (lambda b=b: lambda e: e.dma_start(
                    dbg_d["P"][b], arena2[:, :]))(),
                    reads=[R("arena2")], dma=True, tag=f"dbgP{b}")
            for f in range(8):
                pp, kpp = ATps[f % 2], k_AT[f % 2]
                for e_ in range(8):
                    for j in range(2):
                        P.ins("tensor", (lambda pp=pp, f=f, e_=e_, j=j:
                                         lambda e: e.matmul(
                            pp[:, j * 512:(j + 1) * 512],
                            wq_nat[:, e_ * D + f * 128:e_ * D + (f + 1) * 128],
                            p_sb(e_)[:, j * 512:(j + 1) * 512],
                            start=(e_ == 0), stop=(e_ == 7)))(),
                            reads=[R("wq_nat", e_ * D, (e_ + 1) * D),
                                   k_psb(e_)], writes=[kpp],
                            tag=f"mmAT{b}_{f}_{e_}_{j}")
                kct = k_ct(f)
                P.ins("vector", (lambda pp=pp, f=f: lambda e: e.tensor_tensor(
                    ct_sb(f), pp[:, :], weffT[:, f * D:(f + 1) * D], ALU.add))(),
                    reads=[kpp, R("weffT", f * D, (f + 1) * D)], writes=[kct],
                    tag=f"ct{b}_{f}")
            if debug:
                P.ins("gpsimd", (lambda b=b: lambda e: e.dma_start(
                    dbg_d["CT"][b], arena[:, 8 * D:16 * D]))(),
                    reads=[R("arena", 8 * D, 16 * D)], dma=True, tag=f"dbgCT{b}")
            psC2.close()

            psF = ExitStack()
            Ops = [psF.enter_context(
                nc.psum_tensor(f"O{b}_{i}", [128, D], F32)) for i in range(3)]
            Fps = [psF.enter_context(
                nc.psum_tensor(f"ftp{b}_{i}", [128, 128], BF16)) for i in range(2)]
            k_O = [psum_key(t) for t in Ops]
            k_F = [psum_key(t) for t in Fps]
            def fxv(i):
                p, h = i // 2, i % 2
                lo = (p % 2) * 4 * D + h * 2 * D
                return (arena2[:, lo:lo + 2 * D].bitcast(F32),
                        R("arena2", lo, lo + 2 * D))

            def f_head(i):
                if i % 2 == 0:
                    p = i // 2
                    lo = (p % 2) * 4 * D
                    dst = arena2[:, lo:lo + 4 * D].bitcast(F32)
                    kd = R("arena2", lo, lo + 4 * D)
                    P.ins("sync", (lambda dst=dst, b=b, i=i: lambda e:
                                   e.dma_start(
                        dst.rearrange("p (a f) -> p a f", a=2),
                        xs[b, i * 128:(i + 2) * 128, :].rearrange(
                            "(a p) f -> p a f", p=128)))(),
                        writes=[kd], dma=True, tag=f"dma_xF{b}_{i}")
                sl, ksl = fxv(i)
                xv, kxv = xnf[i % 2], R(f"xnf{i % 2}")
                P.ins("scalar", (lambda sl=sl, xv=xv, i=i: lambda e: e.activation(
                    xv[:, :], sl, AF.Identity,
                    bias=lncols[:, 16 + i:17 + i], scale=lncols[:, i:i + 1]))(),
                    reads=[ksl, R("lncols", i, i + 1), R("lncols", 16 + i, 17 + i)],
                    writes=[kxv], tag=f"xnF{b}_{i}")
                for c in range(8):
                    fp, kfp = Fps[c % 2], k_F[c % 2]
                    P.ins("tensor", (lambda fp=fp, xv=xv, c=c: lambda e:
                                     e.transpose(fp[:, :],
                                                 xv[:, c * 128:(c + 1) * 128],
                                                 ident_bf[:, :]))(),
                        reads=[kxv, R("ident_bf")], writes=[kfp],
                        tag=f"ftp{b}_{i}_{c}")
                    if c % 2 == 0:
                        P.ins("scalar", (lambda fp=fp, xv=xv, c=c: lambda e:
                                         e.copy(xv[:, c * 128:(c + 1) * 128],
                                                fp[:, :]))(),
                            reads=[kfp], writes=[kxv], tag=f"fev{b}_{i}_{c}")
                    else:
                        P.ins("vector", (lambda fp=fp, xv=xv, c=c: lambda e:
                                         e.tensor_copy(
                                             xv[:, c * 128:(c + 1) * 128],
                                             fp[:, :]))(),
                            reads=[kfp], writes=[kxv], tag=f"fev{b}_{i}_{c}")

            def f_tail(i):
                xt, kxt = xnf[i % 2], R(f"xnf{i % 2}")
                op, kop = Ops[i % 3], k_O[i % 3]
                for f in range(8):
                    for j in range(2):
                        P.ins("tensor", (lambda op=op, xt=xt, f=f, j=j:
                                         lambda e: e.matmul(
                            op[:, j * 512:(j + 1) * 512],
                            xt[:, f * 128:(f + 1) * 128],
                            ct_sb(f)[:, j * 512:(j + 1) * 512],
                            start=(f == 0), stop=False))(),
                            reads=[kxt, k_ct(f)], writes=[kop],
                            tag=f"mmO{b}_{i}_{f}_{j}")
                for j in range(2):
                    P.ins("tensor", (lambda op=op, j=j: lambda e: e.matmul(
                        op[:, j * 512:(j + 1) * 512], ones_row,
                        RB(7, j * 512, 512), start=False, stop=True))(),
                        reads=[k_ones_row, kRB(7)], writes=[kop],
                        tag=f"mmOb{b}_{i}_{j}")

            def f_fin(i):
                sl, ksl = fxv(i)
                op, kop = Ops[i % 3], k_O[i % 3]
                P.ins("vector", (lambda sl=sl, op=op: lambda e: e.tensor_tensor(
                    sl, sl, op[:, :], ALU.add))(),
                    reads=[ksl, kop], writes=[ksl], tag=f"add{b}_{i}")
                if i % 2 == 1:
                    p = i // 2
                    lo = (p % 2) * 4 * D
                    srcv = arena2[:, lo:lo + 4 * D].bitcast(F32)
                    ks2 = R("arena2", lo, lo + 4 * D)
                    P.ins("gpsimd", (lambda srcv=srcv, b=b, i=i: lambda e:
                                     e.dma_start(
                        ys[b, (i - 1) * 128:(i + 1) * 128, :].rearrange(
                            "(a p) f -> p a f", p=128),
                        srcv.rearrange("p (a f) -> p a f", a=2)))(),
                        reads=[ks2], dma=True, tag=f"out{b}_{i}")

            for i in range(NT):
                if i >= 2:
                    f_tail(i - 2)
                    f_fin(i - 2)
                f_head(i)
            f_tail(NT - 2)
            f_fin(NT - 2)
            f_tail(NT - 1)
            f_fin(NT - 1)
            psF.close()

        import os as _os
        if _os.environ.get("KERNEL_DEBUG_SBUF"):
            print("sbuf bytes remaining (inside build):", nc.sbuf_bytes_remaining)
        if _os.environ.get("KERNEL_DEBUG_DEPS"):
            want = _os.environ["KERNEL_DEBUG_DEPS"].split(",")
            for i, insd in enumerate(P.instrs):
                if insd["tag"] in want:
                    print(f"instr {i} tag={insd['tag']} eng={insd['engine']} "
                          f"dma={insd['dma']}")
                    for d in sorted(insd["deps"]):
                        dd = P.instrs[d]
                        print(f"   dep {d}: {dd['tag']} {dd['engine']} dma={dd['dma']}")
        sems = {e: E(nc.semaphore(f"s_{e}")) for e in ENGINES}
        dsems = {e: [E(nc.semaphore(f"sd_{e}_{j}"))
                     for j in range(Sched.NDMA_LANES[e])] for e in ENGINES}
        P.emit(nc, block, sems, dsems)

    return nc, dbg_d


_NC_CACHE = {}


def _get_nc(debug=False):
    key = bool(debug)
    if key not in _NC_CACHE:
        _NC_CACHE[key] = build_nc(debug)
    return _NC_CACHE[key]


def kernel(**inputs):
    x = np.ascontiguousarray(np.asarray(inputs["L0_attn_x"], np.float32))
    feed = {
        "wqkv": np.ascontiguousarray(np.asarray(inputs["L0_attn_Wqkv_w"], np.float32)),
        "bqkv": np.ascontiguousarray(np.asarray(inputs["L0_attn_Wqkv_b"], np.float32)),
        "gnw": np.ascontiguousarray(np.asarray(inputs["L0_attn_gn_w"], np.float32)),
        "gnb": np.ascontiguousarray(np.asarray(inputs["L0_attn_gn_b"], np.float32)),
        "outw": np.ascontiguousarray(np.asarray(inputs["L0_attn_out_w"], np.float32)),
        "outb": np.ascontiguousarray(np.asarray(inputs["L0_attn_out_b"], np.float32)),
        "wff": np.ascontiguousarray(np.asarray(inputs["L0_ffn_W"], np.float32)),
        "bff": np.ascontiguousarray(np.asarray(inputs["L0_ffn_b"], np.float32)),
        "ld": np.ascontiguousarray(np.asarray(inputs["L0_ffn_lora_down"], np.float32)),
        "lu": np.ascontiguousarray(np.asarray(inputs["L0_ffn_lora_up"], np.float32)),
        "da": np.ascontiguousarray(np.asarray(inputs["L0_ffn_down_aux"], np.float32)),
        "ua": np.ascontiguousarray(np.asarray(inputs["L0_ffn_up_aux"], np.float32)),
    }
    nc, _ = _get_nc(False)
    in_maps = []
    for core in range(NCORES):
        m = dict(feed)
        m["xs"] = np.ascontiguousarray(x[core * NB:(core + 1) * NB])
        in_maps.append(m)
    res = run_bass_kernel_spmd(nc, in_maps, core_ids=list(range(NCORES)))
    return np.concatenate([r["ys"] for r in res.results], axis=0)


if __name__ == "__main__":
    nc, _ = build_nc(False)
    counts = {}
    print("built ok")


# version 23
# speedup vs baseline: 1.1088x; 1.0106x over previous
import numpy as np
from contextlib import ExitStack

import concourse.bass as bass
import concourse.mybir as mybir
from concourse.bass_utils import run_bass_kernel_spmd

F32 = mybir.dt.float32
BF16 = mybir.dt.bfloat16
AF = mybir.ActivationFunctionType
ALU = mybir.AluOpType
AX = mybir.AxisListType

D = 1024
SEQ = 2048
NB = 2
NT = 16
EPS = 1e-5
NCORES = 8
ENGINES = ("sync", "gpsimd", "scalar", "vector", "tensor")


class Sched:

    def __init__(self):
        self.instrs = []
        self.acc = {}
        self._dma_count = {e: 0 for e in ENGINES}

    @staticmethod
    def r(name, lo=0, hi=1 << 60):
        return (name, lo, hi)

    def _overlap(self, key, want_writes):
        name, lo, hi = key
        out = []
        for (l, h, iid, w) in self.acc.get(name, ()):
            if w == want_writes and l < hi and lo < h:
                out.append(iid)
        return out

    def ins(self, engine, fn, reads=(), writes=(), dma=False, tag=""):
        assert engine in ENGINES
        deps = set()
        for k in reads:
            deps.update(self._overlap(k, True))
        for k in writes:
            deps.update(self._overlap(k, True))
            deps.update(self._overlap(k, False))
        iid = len(self.instrs)
        lane = self._dma_count[engine] % self.NDMA_LANES[engine] if dma else None

        def keep(d):
            dd = self.instrs[d]
            de = dd["engine"]
            if de != engine:
                return True
            if dd["dma"] and dma:
                return False
            if engine == "tensor":
                return False
            return True
        deps = {d for d in deps if keep(d)}
        best = {}
        for d in deps:
            dd = self.instrs[d]
            ch = (dd["engine"], dd["lane"]) if dd["dma"] else (dd["engine"],)
            if best.get(ch, -1) < d:
                best[ch] = d
        deps = set(best.values())
        self.instrs.append(dict(engine=engine, fn=fn, deps=deps, dma=dma,
                                lane=lane, tag=tag))
        if dma:
            self._dma_count[engine] += 1
        for k in reads:
            self.acc.setdefault(k[0], []).append((k[1], k[2], iid, False))
        for k in writes:
            self.acc.setdefault(k[0], []).append((k[1], k[2], iid, True))
        return iid

    NDMA_LANES = {"sync": 16, "gpsimd": 12, "scalar": 8, "vector": 4,
                  "tensor": 2}

    def emit(self, nc, block, sems, dsems, final_waits=("gpsimd",)):
        needed = set()
        for ins in self.instrs:
            needed.update(ins["deps"])
        cum = {}
        chan = {}
        val = {}
        for i, ins in enumerate(self.instrs):
            if ins["dma"]:
                ch = (ins["engine"], ins["lane"])
                cum[ch] = cum.get(ch, 0) + 16
                chan[i] = ch
                val[i] = cum[ch]
                needed.add(i)
            else:
                ch = (ins["engine"],)
                if i in needed:
                    cum[ch] = cum.get(ch, 0) + 1
                chan[i] = ch
                val[i] = cum.get(ch, 0)
        cum2 = {}
        for i, ins in enumerate(self.instrs):
            if not ins["dma"]:
                ch = chan[i]
                if i in needed:
                    cum2[ch] = cum2.get(ch, 0) + 1
                val[i] = cum2.get(ch, 0)
        order = {e: [i for i, ins in enumerate(self.instrs) if ins["engine"] == e]
                 for e in ENGINES}

        def hnd(ch):
            return dsems[ch[0]][ch[1]] if len(ch) == 2 else sems[ch[0]]

        totals = dict(cum)
        totals.update(cum2)

        def section(e):
            def body(eng):
                waited = {}
                for iid in order[e]:
                    ins = self.instrs[iid]
                    for d in sorted(ins["deps"], key=lambda d: val[d]):
                        dch = chan[d]
                        v = val[d]
                        assert v > 0, (d, iid)
                        if waited.get(dch, 0) < v:
                            eng.wait_ge(hnd(dch), v)
                            waited[dch] = v
                    bi = ins["fn"](eng)
                    if iid in needed:
                        ch = chan[iid]
                        bi.then_inc(hnd(ch), 16 if ins["dma"] else 1)
                if e in final_waits:
                    for ch, tot in totals.items():
                        if len(ch) == 2 and ch[0] == e and tot:
                            eng.wait_ge(hnd(ch), tot)
            return body

        block.sync(section("sync"))
        block.gpsimd(section("gpsimd"))
        block.scalar(section("scalar"))
        block.vector(section("vector"))
        block.tensor(section("tensor"))
        return totals


def build_nc(debug=False):
    nc = bass.Bass("TRN2", target_bir_lowering=False)
    P = Sched()
    R = Sched.r

    xs = nc.dram_tensor("xs", [NB, SEQ, D], F32, kind="ExternalInput").ap()
    wqkv_d = nc.dram_tensor("wqkv", [3 * D, D], F32, kind="ExternalInput").ap()
    bqkv_d = nc.dram_tensor("bqkv", [3 * D], F32, kind="ExternalInput").ap()
    gnw_d = nc.dram_tensor("gnw", [D], F32, kind="ExternalInput").ap()
    gnb_d = nc.dram_tensor("gnb", [D], F32, kind="ExternalInput").ap()
    outw_d = nc.dram_tensor("outw", [D, D], F32, kind="ExternalInput").ap()
    outb_d = nc.dram_tensor("outb", [D], F32, kind="ExternalInput").ap()
    wff_d = nc.dram_tensor("wff", [D, D], F32, kind="ExternalInput").ap()
    bff_d = nc.dram_tensor("bff", [D], F32, kind="ExternalInput").ap()
    ld_d = nc.dram_tensor("ld", [4, 100], F32, kind="ExternalInput").ap()
    lu_d = nc.dram_tensor("lu", [50, 4], F32, kind="ExternalInput").ap()
    da_d = nc.dram_tensor("da", [100, D], F32, kind="ExternalInput").ap()
    ua_d = nc.dram_tensor("ua", [D, 50], F32, kind="ExternalInput").ap()
    ys = nc.dram_tensor("ys", [NB, SEQ, D], F32, kind="ExternalOutput").ap()
    dbg_d = {}
    if debug:
        dbg_d["xn"] = nc.dram_tensor("dbg_xn", [NB, 128, 16 * D], BF16,
                                     kind="ExternalOutput").ap()
        dbg_d["X"] = nc.dram_tensor("dbg_X", [NB, 128, 8 * D], BF16,
                                    kind="ExternalOutput").ap()
        dbg_d["MT"] = nc.dram_tensor("dbg_MT", [NB, 128, D], BF16,
                                     kind="ExternalOutput").ap()
        dbg_d["CT"] = nc.dram_tensor("dbg_CT", [NB, 128, 8 * D], BF16,
                                     kind="ExternalOutput").ap()
        dbg_d["rows"] = nc.dram_tensor("dbg_rows", [NB, 1, 5 * D], F32,
                                       kind="ExternalOutput").ap()
        dbg_d["P"] = nc.dram_tensor("dbg_P", [NB, 128, 8 * D], BF16,
                                    kind="ExternalOutput").ap()
        dbg_d["weffT"] = nc.dram_tensor("dbg_weffT", [128, 8 * D], BF16,
                                        kind="ExternalOutput").ap()
        dbg_d["wqnat"] = nc.dram_tensor("dbg_wqnat", [128, 8 * D], BF16,
                                        kind="ExternalOutput").ap()
        dbg_d["lil"] = nc.dram_tensor("dbg_lil", [128, 4 * D], BF16,
                                      kind="ExternalOutput").ap()
        dbg_d["cols"] = nc.dram_tensor("dbg_cols", [NB, 128, 64], F32,
                                       kind="ExternalOutput").ap()

    with ExitStack() as st, nc.Block() as block:
        E = st.enter_context

        arena = E(nc.sbuf_tensor("arena", [128, 16 * D], BF16))
        arena2 = E(nc.sbuf_tensor("arena2", [128, 8 * D], BF16))
        wqkvT = E(nc.sbuf_tensor("wqkvT", [128, 24 * D], BF16))
        wq_nat = E(nc.sbuf_tensor("wq_nat", [128, 8 * D], BF16))
        outwT = E(nc.sbuf_tensor("outwT", [128, 8 * D], BF16))
        weffT = E(nc.sbuf_tensor("weffT", [128, 8 * D], BF16))
        stage = [E(nc.sbuf_tensor(f"stage{i}", [128, D], F32)) for i in range(3)]
        sqscr = E(nc.sbuf_tensor("sqscr", [128, D], F32))
        xnf = [E(nc.sbuf_tensor(f"xnf{i}", [128, D], BF16)) for i in range(2)]
        mblkT = E(nc.sbuf_tensor("mblkT", [128, D], BF16))
        mblk = E(nc.sbuf_tensor("mblk", [128, D], BF16))
        mblkTs = [E(nc.sbuf_tensor(f"mblkTs{i}", [128, 128], BF16)) for i in range(2)]
        lil_small = E(nc.sbuf_tensor("lil_small", [128, 128], BF16))
        ident_bf = E(nc.sbuf_tensor("ident_bf", [128, 128], BF16))
        ident_f = E(nc.sbuf_tensor("ident_f", [128, 128], F32))
        sel = E(nc.sbuf_tensor("sel", [128, 4], F32))
        selT = E(nc.sbuf_tensor("selT", [4, 128], F32))
        ones_col = E(nc.sbuf_tensor("ones_col", [128, 1], BF16))
        epscol = E(nc.sbuf_tensor("epscol", [128, 1], F32))
        rows_f = E(nc.sbuf_tensor("rows_f", [1, 5 * D], F32))
        h_sb = sqscr[:, 256:384]
        rows_b = E(nc.sbuf_tensor("rows_b", [1, 8 * D + 128], BF16))
        cols_f = E(nc.sbuf_tensor("cols_f", [128, 64], F32))
        cols_b = E(nc.sbuf_tensor("cols_b", [128, 32], BF16))
        lncols = E(nc.sbuf_tensor("lncols", [128, 32], F32))
        lnst = E(nc.sbuf_tensor("lnst", [128, 96], F32))
        st44 = E(nc.sbuf_tensor("st44", [128, 64], F32))

        RS = lambda k, o=0, n=D: (sqscr[0:1, o:o + n] if k == 0
                                  else rows_f[0:1, (k - 1) * D + o:(k - 1) * D + o + n])
        kRS = lambda k, o=0, n=D: (R("sqscr", o, o + n) if k == 0
                                   else R("rows_f", (k - 1) * D + o, (k - 1) * D + o + n))
        RB = lambda k, o=0, n=D: rows_b[0:1, k * D + o:k * D + o + n]
        kRB = lambda k, o=0, n=D: R("rows_b", k * D + o, k * D + o + n)
        ones_row = rows_b[0:1, 8 * D:8 * D + 128]
        k_ones_row = R("rows_b", 8 * D, 8 * D + 128)
        CF = lambda k, n=1: cols_f[:, k:k + n]
        kCF = lambda k, n=1: R("cols_f", k, k + n)
        CB = lambda k, n=1: cols_b[:, k:k + n]
        kCB = lambda k, n=1: R("cols_b", k, k + n)

        xn_tm = lambda i: arena[:, i * D:(i + 1) * D]
        k_xn = lambda i: R("arena", i * D, (i + 1) * D)
        t_cat = lambda f: arena[:, f * 2 * D:(f + 1) * 2 * D]
        k_tcat = lambda f: R("arena", f * 2 * D, (f + 1) * 2 * D)
        ct_sb = lambda f: arena[:, 8 * D + f * D:8 * D + (f + 1) * D]
        k_ct = lambda f: R("arena", 8 * D + f * D, 8 * D + (f + 1) * D)
        x_sb = lambda g: arena2[:, g * D:(g + 1) * D]
        k_xsb = lambda g: R("arena2", g * D, (g + 1) * D)
        p_sb = lambda e: arena2[:, e * D:(e + 1) * D]
        k_psb = k_xsb
        v_dab = xnf[0][0:100, :]
        k_dab = R("xnf0")
        v_uaT = xnf[1][0:50, :]
        k_uaT = R("xnf1")
        v_t1 = arena[0:4, 0:D]
        k_t1 = R("arena", 0, D)
        v_t2 = arena[0:50, D:2 * D]
        k_t2 = R("arena", D, 2 * D)
        v_ldT = lil_small[0:100, 0:4]
        k_ldT = R("lil_small", 0, 4)
        v_luT = lil_small[0:4, 64:114]
        k_luT = R("lil_small", 64, 114)

        def psum_key(handle):
            for alloc in nc.m.functions[0].allocations:
                mls = getattr(alloc, "memorylocations", None)
                if mls:
                    for ml in mls:
                        if ml.name == handle.name:
                            return R("PSUM", ml.addr, ml.addr + int(ml.dims[-1]))
            raise KeyError(handle.name)

        P.ins("gpsimd", lambda g: g.memset(ident_bf[:, :], 0.0),
              writes=[R("ident_bf")], tag="c0")
        P.ins("gpsimd", lambda g: g.affine_select(
            out=ident_bf[:, :], in_=ident_bf[:, :], compare_op=ALU.not_equal,
            fill=1.0, base=0, pattern=[[-1, 128]], channel_multiplier=1),
            reads=[R("ident_bf")], writes=[R("ident_bf")], tag="c1")
        P.ins("gpsimd", lambda g: g.memset(ident_f[:, :], 0.0),
              writes=[R("ident_f")])
        P.ins("gpsimd", lambda g: g.affine_select(
            out=ident_f[:, :], in_=ident_f[:, :], compare_op=ALU.not_equal,
            fill=1.0, base=0, pattern=[[-1, 128]], channel_multiplier=1),
            reads=[R("ident_f")], writes=[R("ident_f")])
        P.ins("gpsimd", lambda g: g.memset(sel[:, :], 1.0), writes=[R("sel")])
        P.ins("gpsimd", lambda g: g.affine_select(
            out=sel[:, :], in_=sel[:, :], compare_op=ALU.is_ge, fill=0.0,
            base=0, pattern=[[-32, 4]], channel_multiplier=1),
            reads=[R("sel")], writes=[R("sel")])
        P.ins("gpsimd", lambda g: g.affine_select(
            out=sel[:, :], in_=sel[:, :], compare_op=ALU.is_ge, fill=0.0,
            base=31, pattern=[[32, 4]], channel_multiplier=-1),
            reads=[R("sel")], writes=[R("sel")])
        P.ins("gpsimd", lambda g: g.memset(selT[:, :], 1.0), writes=[R("selT")])
        P.ins("gpsimd", lambda g: g.affine_select(
            out=selT[:, :], in_=selT[:, :], compare_op=ALU.is_ge, fill=0.0,
            base=0, pattern=[[1, 128]], channel_multiplier=-32),
            reads=[R("selT")], writes=[R("selT")])
        P.ins("gpsimd", lambda g: g.affine_select(
            out=selT[:, :], in_=selT[:, :], compare_op=ALU.is_ge, fill=0.0,
            base=31, pattern=[[-1, 128]], channel_multiplier=32),
            reads=[R("selT")], writes=[R("selT")])
        P.ins("gpsimd", lambda g: g.memset(ones_col[:, :], 1.0),
              writes=[R("ones_col")])
        P.ins("gpsimd", lambda g: g.memset(epscol[:, :], EPS), writes=[R("epscol")])
        P.ins("gpsimd", lambda g: g.memset(ones_row, 1.0), writes=[k_ones_row])
        P.ins("gpsimd", lambda g: g.memset(mblkT[:, :], 0.0), writes=[R("mblkT")])
        P.ins("gpsimd", lambda g: g.memset(mblk[:, :], 0.0), writes=[R("mblk")])


        def make_A1(b):
            def ln_stats(i):
                n = b * NT + i
                sl = stage[n % 3]
                ksl = R(f"stage{n % 3}")
                P.ins("sync", (lambda sl=sl, b=b, i=i: lambda e: e.dma_start(
                    sl[:, :], xs[b, i * 128:(i + 1) * 128, :]))(),
                    writes=[ksl], dma=True, tag=f"dma_xA{b}_{i}")
                P.ins("vector", (lambda sl=sl, i=i: lambda e: e.reduce_sum(
                    lnst[:, i:i + 1], sl[:, :], axis=AX.X))(),
                    reads=[ksl], writes=[R("lnst", i, i + 1)], tag=f"ln_sum{i}")
                P.ins("scalar", (lambda sl=sl: lambda e: e.activation(
                    sqscr[:, :], sl[:, :], AF.Square))(),
                    reads=[ksl], writes=[R("sqscr")], tag=f"ln_sq{i}")
                P.ins("vector", (lambda i=i: lambda e: e.reduce_sum(
                    lnst[:, 16 + i:17 + i], sqscr[:, :], axis=AX.X))(),
                    reads=[R("sqscr")], writes=[R("lnst", 16 + i, 17 + i)],
                    tag=f"ln_ssq{i}")
                P.ins("vector", (lambda i=i: lambda e: e.tensor_scalar(
                    lnst[:, 32 + i:33 + i], lnst[:, i:i + 1], -1.0 / D, None,
                    ALU.mult))(),
                    reads=[R("lnst", i, i + 1)],
                    writes=[R("lnst", 32 + i, 33 + i)], tag=f"ln_negmean{i}")
                P.ins("vector", (lambda i=i: lambda e: e.tensor_tensor(
                    lnst[:, 48 + i:49 + i], lnst[:, 32 + i:33 + i],
                    lnst[:, 32 + i:33 + i], ALU.mult))(),
                    reads=[R("lnst", 32 + i, 33 + i)],
                    writes=[R("lnst", 48 + i, 49 + i)], tag=f"ln_msq{i}")
                P.ins("vector", (lambda i=i: lambda e: e.scalar_tensor_tensor(
                    lnst[:, 64 + i:65 + i], lnst[:, 16 + i:17 + i], 1.0 / D,
                    lnst[:, 48 + i:49 + i], ALU.mult, ALU.subtract))(),
                    reads=[R("lnst", 16 + i, 17 + i), R("lnst", 48 + i, 49 + i)],
                    writes=[R("lnst", 64 + i, 65 + i)], tag=f"ln_var{i}")
                P.ins("scalar", (lambda i=i: lambda e: e.activation(
                    lnst[:, 80 + i:81 + i], lnst[:, 64 + i:65 + i], AF.Ln,
                    bias=epscol[:, :]))(),
                    reads=[R("lnst", 64 + i, 65 + i), R("epscol")],
                    writes=[R("lnst", 80 + i, 81 + i)], tag=f"ln_lnv{i}")
                P.ins("scalar", (lambda i=i: lambda e: e.activation(
                    lncols[:, i:i + 1], lnst[:, 80 + i:81 + i], AF.Exp,
                    scale=-0.5))(),
                    reads=[R("lnst", 80 + i, 81 + i)],
                    writes=[R("lncols", i, i + 1)], tag=f"ln_rstd{i}")

            def ln_finish(i):
                n = b * NT + i
                sl = stage[n % 3]
                ksl = R(f"stage{n % 3}")
                P.ins("vector", (lambda i=i: lambda e: e.tensor_scalar(
                    lncols[:, 16 + i:17 + i], lnst[:, 32 + i:33 + i],
                    lncols[:, i:i + 1], None, ALU.mult))(),
                    reads=[R("lnst", 32 + i, 33 + i), R("lncols", i, i + 1)],
                    writes=[R("lncols", 16 + i, 17 + i)], tag=f"ln_negmr{i}")
                P.ins("scalar", (lambda sl=sl, i=i, b=b: lambda e: e.activation(
                    xn_tm(i), sl[:, :], AF.Identity,
                    scale=lncols[:, i:i + 1],
                    bias=lncols[:, 16 + i:17 + i]))(),
                    reads=[ksl, R("lncols", i, i + 1),
                           R("lncols", 16 + i, 17 + i)],
                    writes=[k_xn(i)], tag=f"xn{b}_{i}")

            def tile(k):
                if k >= 2:
                    ln_finish(k - 2)
                if k < NT:
                    ln_stats(k)
                if k == NT:
                    ln_finish(NT - 2)
                    ln_finish(NT - 1)
            return tile

        wslot = [arena[:, 0:8 * D].bitcast(F32),
                 arena[:, 8 * D:16 * D].bitcast(F32)]
        k_wslot = [R("arena", 0, 8 * D), R("arena", 8 * D, 16 * D)]

        def w_grp_dram(g):
            if g < 6:
                src = wqkv_d[g * 512:(g + 1) * 512, :]
            elif g < 8:
                src = outw_d[(g - 6) * 512:(g - 5) * 512, :]
            else:
                src = wff_d[(g - 8) * 512:(g - 7) * 512, :]
            return src.rearrange("(a p) f -> p a f", p=128)

        def w_tp_dest(t, c):
            if t < 24:
                lo = c * 3 * D + t * 128
                return wqkvT[:, lo:lo + 128], R("wqkvT", lo, lo + 128)
            if t < 32:
                lo = c * D + (t - 24) * 128
                return outwT[:, lo:lo + 128], R("outwT", lo, lo + 128)
            lo = c * D + (t - 32) * 128
            return weffT[:, lo:lo + 128], R("weffT", lo, lo + 128)

        def emit_wgroup(g):
            sl = wslot[g % 2]
            ksl = k_wslot[g % 2]
            P.ins("sync" if g % 2 == 0 else "gpsimd",
                  (lambda sl=sl, g=g: lambda e: e.dma_start(
                      sl.rearrange("p (a f) -> p a f", a=4), w_grp_dram(g)))(),
                  writes=[ksl], dma=True, tag=f"dma_wg{g}")
            for q in range(4):
                t = g * 4 + q
                if t < 8:
                    cb, kcb = wq_nat[:, t * D:(t + 1) * D], R("wq_nat", t * D,
                                                              (t + 1) * D)
                else:
                    cb, kcb = xnf[t % 2][:, :], R(f"xnf{t % 2}")
                P.ins("vector", (lambda sl=sl, q=q, cb=cb: lambda e:
                                 e.tensor_copy(cb, sl[:, q * D:(q + 1) * D]))(),
                    reads=[ksl], writes=[kcb], tag=f"cast_w{t}")
                for c in range(8):
                    j = t * 8 + c
                    ps = tp32_ps[j % 4]
                    kps = k_tp32[j % 4]
                    P.ins("tensor", (lambda ps=ps, cb=cb, c=c: lambda e:
                                     e.transpose(ps[:, 0:128],
                                                 cb[:, c * 128:(c + 1) * 128],
                                                 ident_bf[:, :]))(),
                        reads=[kcb, R("ident_bf")], writes=[kps],
                        tag=f"tp_w{t}_{c}")
                    dv, kdv = w_tp_dest(t, c)
                    if c % 3 == 0:
                        P.ins("scalar", (lambda dv=dv, ps=ps: lambda e: e.copy(
                            dv, ps[:, 0:128]))(), reads=[kps], writes=[kdv],
                            tag=f"ev_w{t}_{c}")
                    else:
                        P.ins("vector", (lambda dv=dv, ps=ps: lambda e:
                                         e.tensor_copy(dv, ps[:, 0:128]))(),
                            reads=[kps], writes=[kdv], tag=f"ev_w{t}_{c}")


        ps_rows = ExitStack()
        ctpR = ps_rows.enter_context(nc.psum_tensor("ctpR", [128, 4], F32))
        k_ctpR = psum_key(ctpR)
        def row_dma(dram_ap, tag):
            ap2 = dram_ap.rearrange("(a d) -> a d", a=1)
            P.ins("gpsimd", (lambda ap2=ap2: lambda e: e.dma_start(RS(0), ap2))(),
                  writes=[kRS(0)], dma=True, tag=tag)

        row_dma(bqkv_d[0:D], "dma_bq")
        P.ins("scalar", lambda e: e.activation(RB(0), RS(0), AF.Copy),
              reads=[kRS(0)], writes=[kRB(0)], tag="cast_bq")
        P.ins("scalar", lambda e: e.copy(RS(2), RS(0)),
              reads=[kRS(0)], writes=[kRS(2)], tag="keep_bq")
        for c in range(8):
            pp = ctpR
            kpp = k_ctpR
            P.ins("tensor", (lambda pp=pp, c=c: lambda e: e.transpose(
                pp[:, 0:1], RS(2)[0:1, c * 128:(c + 1) * 128],
                ident_f[0:1, 0:1]))(),
                reads=[kRS(2), R("ident_f")], writes=[kpp], tag=f"tp_bqc{c}")
            P.ins("scalar", (lambda pp=pp, c=c: lambda e: e.copy(
                CF(16 + c), pp[:, 0:1]))(), reads=[kpp],
                writes=[kCF(16 + c)], tag=f"ev_bqc{c}")
        row_dma(bqkv_d[D:2 * D], "dma_bk")
        P.ins("scalar", lambda e: e.activation(RB(1), RS(0), AF.Copy),
              reads=[kRS(0)], writes=[kRB(1)], tag="cast_bk")
        row_dma(bqkv_d[2 * D:3 * D], "dma_bv")
        P.ins("scalar", lambda e: e.activation(RB(2), RS(0), AF.Copy),
              reads=[kRS(0)], writes=[kRB(2)], tag="cast_bv")
        P.ins("vector", lambda e: e.tensor_scalar(
            RS(3), RS(0), float(SEQ), None, ALU.mult),
            reads=[kRS(0)], writes=[kRS(3)], tag="sbv")
        row_dma(gnw_d[:], "dma_gnw")
        for c in range(8):
            pp = ctpR
            kpp = k_ctpR
            P.ins("tensor", (lambda pp=pp, c=c: lambda e: e.transpose(
                pp[:, 0:1], RS(0)[0:1, c * 128:(c + 1) * 128],
                ident_f[0:1, 0:1]))(),
                reads=[kRS(0), R("ident_f")], writes=[kpp], tag=f"tp_gnw{c}")
            P.ins("scalar", (lambda pp=pp, c=c: lambda e: e.copy(
                CF(0 + c), pp[:, 0:1]))(), reads=[kpp], writes=[kCF(c)],
                tag=f"ev_gnw{c}")
        row_dma(gnb_d[:], "dma_gnb")
        for c in range(8):
            pp = ctpR
            kpp = k_ctpR
            P.ins("tensor", (lambda pp=pp, c=c: lambda e: e.transpose(
                pp[:, 0:1], RS(0)[0:1, c * 128:(c + 1) * 128],
                ident_f[0:1, 0:1]))(),
                reads=[kRS(0), R("ident_f")], writes=[kpp], tag=f"tp_gnb{c}")
            P.ins("scalar", (lambda pp=pp, c=c: lambda e: e.copy(
                CF(8 + c), pp[:, 0:1]))(), reads=[kpp], writes=[kCF(8 + c)],
                tag=f"ev_gnb{c}")
        row_dma(outb_d[:], "dma_outb")
        P.ins("scalar", lambda e: e.copy(RS(4), RS(0)),
              reads=[kRS(0)], writes=[kRS(4)], tag="keep_outb")
        row_dma(bff_d[:], "dma_bff")
        P.ins("vector", lambda e: e.tensor_tensor(RS(4), RS(4), RS(0), ALU.add),
              reads=[kRS(4), kRS(0)], writes=[kRS(4)], tag="obf")
        P.ins("vector", lambda e: e.tensor_copy(CB(16, 8), CF(16, 8)),
              reads=[kCF(16, 8)], writes=[kCB(16, 8)], tag="bqcol_bf")

        ps_rows.close()

        for b in range(NB):
            psA = ExitStack()
            Xps = [psA.enter_context(
                nc.psum_tensor(f"X{b}_{i}", [128, D], F32)) for i in range(3)]
            Srow = psA.enter_context(nc.psum_tensor(f"xsr{b}", [1, D], F32))
            k_X = [psum_key(t) for t in Xps]
            k_Sr = psum_key(Srow)

            def x_chunk_mms(c, i):
                lhs = xn_tm(i)[:, c * 128:(c + 1) * 128]
                kx = k_xn(i)
                pp, kpp = Xps[c % 3], k_X[c % 3]
                st_ = (i == 0)
                sp_ = (i == NT - 1)
                for j in range(2):
                    P.ins("tensor", (lambda lhs=lhs, pp=pp, i=i, j=j, st_=st_,
                                     sp_=sp_: lambda e: e.matmul(
                        pp[:, j * 512:(j + 1) * 512], lhs,
                        xn_tm(i)[:, j * 512:(j + 1) * 512],
                        start=st_, stop=sp_))(),
                        reads=[kx], writes=[kpp], tag=f"mmX{b}_{c}_{i}_{j}")

            def xnsum_row_mms(i):
                kx = k_xn(i)
                st_ = (i == 0)
                sp_ = (i == NT - 1)
                for j in range(2):
                    P.ins("tensor", (lambda i=i, j=j, st_=st_, sp_=sp_,
                                     Srow=Srow: lambda e: e.matmul(
                        Srow[0:1, j * 512:(j + 1) * 512], ones_col[:, :],
                        xn_tm(i)[:, j * 512:(j + 1) * 512],
                        start=st_, stop=sp_))(),
                        reads=[kx, R("ones_col")], writes=[k_Sr],
                        tag=f"mmSr{b}_{i}")

            def x_evac(c):
                pp, kpp = Xps[c % 3], k_X[c % 3]
                P.ins("scalar", (lambda pp=pp, c=c: lambda e: e.copy(
                    x_sb(c), pp[:, :]))(), reads=[kpp], writes=[k_xsb(c)],
                    tag=f"evX{b}_{c}")

            a1t = make_A1(b)
            for k in range(NT + 2):
                if k <= NT:
                    a1t(k)
                i = k - 2
                if 0 <= i < NT:
                    for c in (0, 1, 2):
                        x_chunk_mms(c, i)
                    xnsum_row_mms(i)
            for c in (0, 1, 2):
                x_evac(c)
            P.ins("scalar", (lambda Srow=Srow: lambda e: e.copy(
                RS(0), Srow[0:1, :]))(),
                reads=[k_Sr], writes=[kRS(0)], tag=f"evSr{b}")
            for grp in ((3, 4, 5), (6, 7)):
                for i in range(NT):
                    for c in grp:
                        x_chunk_mms(c, i)
                for c in grp:
                    x_evac(c)

            if debug:
                P.ins("gpsimd", (lambda b=b: lambda e: e.dma_start(
                    dbg_d["X"][b], arena2[:, :]))(),
                    reads=[R("arena2")], dma=True, tag=f"dbgX{b}")
                P.ins("gpsimd", (lambda b=b: lambda e: e.dma_start(
                    dbg_d["xn"][b], arena[:, :]))(),
                    reads=[R("arena", 0, 16 * D)], dma=True, tag=f"dbgxn{b}")
            psA.close()

            if b == 0:
                ps0 = ExitStack()
                tp32_ps = [ps0.enter_context(
                    nc.psum_tensor(f"tp32_{i}", [128, 512], BF16))
                    for i in range(4)]
                k_tp32 = [psum_key(t) for t in tp32_ps]
                for g in range(10):
                    emit_wgroup(g)
                ps_lil = ExitStack()
                ctp_ps = [ps_lil.enter_context(
                    nc.psum_tensor("ctpL", [128, 4], F32))]
                k_ctp = [psum_key(ctp_ps[0])]
                lil_ps = [ps_lil.enter_context(
                    nc.psum_tensor(f"lilp{i}", [128, 512], F32))
                    for i in range(2)]
                k_lilp = [psum_key(t) for t in lil_ps]
                st_da = stage[0][0:100, :]
                k_stda = R("stage0")
                st_ld = stage[1][0:4, 0:100]
                k_stld = R("stage1", 0, 100)
                st_lu = stage[1][0:50, 128:132]
                k_stlu = R("stage1", 128, 132)
                P.ins("sync", lambda e: e.dma_start(st_da, da_d[:, :]),
                      writes=[k_stda], dma=True, tag="dma_da")
                P.ins("sync", lambda e: e.dma_start(st_ld, ld_d[:, :]),
                      writes=[k_stld], dma=True, tag="dma_ld")
                P.ins("sync", lambda e: e.dma_start(st_lu, lu_d[:, :]),
                      writes=[k_stlu], dma=True, tag="dma_lu")
                P.ins("gpsimd", lambda e: e.tensor_copy(v_dab, st_da),
                      reads=[k_stda], writes=[k_dab], tag="cast_da")
                if debug:
                    P.ins("gpsimd", lambda e: e.dma_start(
                        dbg_d["lil"][:, 0:D], xnf[0][:, :]),
                        reads=[R("xnf0")], dma=True, tag="dbglil0_early")
                P.ins("tensor", lambda e: e.transpose(
                    ctp_ps[0][0:100, 0:4], st_ld[:, 0:100], ident_f[0:4, 0:4]),
                    reads=[k_stld, R("ident_f")], writes=[k_ctp[0]],
                    tag="tp_ldT")
                P.ins("scalar", lambda e: e.copy(v_ldT, ctp_ps[0][0:100, 0:4]),
                      reads=[k_ctp[0]], writes=[k_ldT], tag="ev_ldT")
                P.ins("tensor", lambda e: e.transpose(
                    lil_ps[0][0:4, 0:50], st_lu[:, 0:4], ident_f[0:50, 0:50]),
                    reads=[k_stlu, R("ident_f")], writes=[k_lilp[0]],
                    tag="tp_luT")
                P.ins("scalar", lambda e: e.copy(v_luT, lil_ps[0][0:4, 0:50]),
                      reads=[k_lilp[0]], writes=[k_luT], tag="ev_luT")
                for t in range(8):
                    sl = stage[2][:, (t % 2) * 64:(t % 2) * 64 + 50]
                    ksl = R("stage2", (t % 2) * 64, (t % 2) * 64 + 50)
                    P.ins("sync", (lambda sl=sl, t=t: lambda e: e.dma_start(
                        sl, ua_d[t * 128:(t + 1) * 128, :]))(), writes=[ksl], dma=True,
                        tag=f"dma_ua{t}")
                    pp = lil_ps[1] if t % 2 else lil_ps[0]
                    kpp = k_lilp[1] if t % 2 else k_lilp[0]
                    P.ins("tensor", (lambda pp=pp, sl=sl: lambda e: e.transpose(
                        pp[0:50, 0:128], sl, ident_f[:, :]))(),
                        reads=[ksl, R("ident_f")], writes=[kpp], tag=f"tp_ua{t}")
                    dv = v_uaT[:, t * 128:(t + 1) * 128]
                    P.ins("scalar", (lambda dv=dv, pp=pp: lambda e: e.copy(
                        dv, pp[0:50, 0:128]))(), reads=[kpp],
                        writes=[R("xnf1", t * 128, (t + 1) * 128)],
                        tag=f"ev_ua{t}")
                for j in range(2):
                    pp = lil_ps[j]
                    P.ins("tensor", (lambda pp=pp, j=j: lambda e: e.matmul(
                        pp[0:4, 0:512], v_ldT, v_dab[:, j * 512:(j + 1) * 512],
                        start=True, stop=True))(),
                        reads=[k_ldT, k_dab],
                        writes=[k_lilp[j]], tag=f"mm_t1_{j}")
                    P.ins("scalar", (lambda pp=pp, j=j: lambda e: e.copy(
                        v_t1[:, j * 512:(j + 1) * 512], pp[0:4, 0:512]))(),
                        reads=[k_lilp[j]],
                        writes=[R("arena", j * 512, (j + 1) * 512)],
                        tag=f"ev_t1_{j}")
                for j in range(2):
                    pp = lil_ps[j]
                    P.ins("tensor", (lambda pp=pp, j=j: lambda e: e.matmul(
                        pp[0:50, 0:512], v_luT, v_t1[:, j * 512:(j + 1) * 512],
                        start=True, stop=True))(),
                        reads=[k_luT, k_t1],
                        writes=[k_lilp[j]], tag=f"mm_t2_{j}")
                    P.ins("scalar", (lambda pp=pp, j=j: lambda e: e.copy(
                        v_t2[:, j * 512:(j + 1) * 512], pp[0:50, 0:512]))(),
                        reads=[k_lilp[j]],
                        writes=[R("arena", D + j * 512, D + (j + 1) * 512)],
                        tag=f"ev_t2_{j}")
                for f in range(8):
                    for j in range(2):
                        pp = lil_ps[j]
                        P.ins("tensor", (lambda pp=pp, f=f, j=j: lambda e: e.matmul(
                            pp[:, 0:512], v_t2[:, f * 128:(f + 1) * 128],
                            v_uaT[:, j * 512:(j + 1) * 512], start=True, stop=True))(),
                            reads=[k_t2, k_uaT],
                            writes=[k_lilp[j]], tag=f"mm_lt{f}_{j}")
                        wv = weffT[:, f * D + j * 512:f * D + (j + 1) * 512]
                        kw = R("weffT", f * D + j * 512, f * D + (j + 1) * 512)
                        P.ins("vector", (lambda wv=wv, pp=pp: lambda e: e.tensor_tensor(
                            wv, pp[:, 0:512], wv, ALU.add))(),
                            reads=[k_lilp[j], kw], writes=[kw], tag=f"add_lt{f}_{j}")

                if debug:
                    P.ins("gpsimd", lambda e: e.dma_start(
                        dbg_d["lil"][:, D:2 * D], xnf[1][:, :]),
                        reads=[R("xnf1")], dma=True, tag="dbglil1")
                    P.ins("gpsimd", lambda e: e.dma_start(
                        dbg_d["lil"][:, 2 * D:4 * D], arena[:, 0:2 * D]),
                        reads=[R("arena", 0, 2 * D)], dma=True, tag="dbglil2")
                ps_lil.close()
                ps0.close()


            psB2 = ExitStack()
            Tps = [psB2.enter_context(
                nc.psum_tensor(f"T{b}_{i}", [128, 2 * D], F32)) for i in range(2)]
            k_T = [psum_key(t) for t in Tps]
            for f in range(8):
                pp, kpp = Tps[f % 2], k_T[f % 2]
                for g in range(8):
                    lhs = x_sb(g)[:, f * 128:(f + 1) * 128]
                    klhs = k_xsb(g)
                    for q in range(4):
                        off = 2 * D + (q % 2) * 512 if q < 2 else (q % 2) * 512
                        P.ins("tensor", (lambda pp=pp, lhs=lhs, g=g, q=q,
                                         off=off: lambda e: e.matmul(
                            pp[:, q * 512:(q + 1) * 512], lhs,
                            wqkvT[:, g * 3 * D + off:g * 3 * D + off + 512],
                            start=(g == 0), stop=(g == 7)))(),
                            reads=[klhs, R("wqkvT", g * 3 * D, (g + 1) * 3 * D)],
                            writes=[kpp], tag=f"mmT{b}_{f}_{g}_{q}")
                if f % 2 == 0:
                    P.ins("scalar", (lambda pp=pp, f=f: lambda e: e.copy(
                        t_cat(f), pp[:, :]))(), reads=[kpp], writes=[k_tcat(f)],
                        tag=f"evT{b}_{f}")
                else:
                    P.ins("vector", (lambda pp=pp, f=f: lambda e: e.tensor_copy(
                        t_cat(f), pp[:, :]))(), reads=[kpp], writes=[k_tcat(f)],
                        tag=f"evT{b}_{f}")
            psB2.close()
            psB1 = ExitStack()
            sums_ps = psB1.enter_context(
                nc.psum_tensor(f"sums{b}", [1, 3 * D], F32))
            k_sums = psum_key(sums_ps)
            ctpB1 = psB1.enter_context(
                nc.psum_tensor(f"ctpBB{b}", [128, 4], F32))
            k_ctpB1 = psum_key(ctpB1)
            for c in range(8):
                P.ins("tensor", (lambda c=c: lambda e: e.transpose(
                    ctpB1[:, 0:1], RS(0)[0:1, c * 128:(c + 1) * 128],
                    ident_f[0:1, 0:1]))(),
                    reads=[kRS(0), R("ident_f")], writes=[k_ctpB1],
                    tag=f"tp_xs{b}_{c}")
                P.ins("scalar", (lambda c=c: lambda e: e.copy(
                    CF(24 + c), ctpB1[:, 0:1]))(), reads=[k_ctpB1],
                    writes=[kCF(24 + c)], tag=f"ev_xs{b}_{c}")
            P.ins("vector", lambda e: e.tensor_copy(CB(0, 8), CF(24, 8)),
                  reads=[kCF(24, 8)], writes=[kCB(0, 8)], tag=f"xnsum_bf{b}")
            for c in range(8):
                for j in range(6):
                    P.ins("tensor", (lambda c=c, j=j: lambda e: e.matmul(
                        sums_ps[0:1, j * 512:(j + 1) * 512], CB(0 + c),
                        wqkvT[:, c * 3 * D + j * 512:c * 3 * D + (j + 1) * 512],
                        start=(c == 0), stop=(c == 7)))(),
                        reads=[kCB(c), R("wqkvT", c * 3 * D, (c + 1) * 3 * D)],
                        writes=[k_sums], tag=f"mm_sums{b}_{c}_{j}")
            P.ins("scalar", lambda e: e.copy(RS(1), sums_ps[0:1, 0:D]),
                  reads=[k_sums], writes=[kRS(1)], tag=f"ev_qsum0{b}")
            P.ins("vector", lambda e: e.tensor_copy(RB(3), sums_ps[0:1, D:2 * D]),
                  reads=[k_sums], writes=[kRB(3)], tag=f"ksum_bf{b}")
            P.ins("vector", lambda e: e.scalar_tensor_tensor(
                RB(4), RS(3), 1.0, sums_ps[0:1, 2 * D:3 * D], ALU.mult, ALU.add),
                reads=[kRS(3), k_sums], writes=[kRB(4)], tag=f"vsumS{b}")
            P.ins("vector", lambda e: e.tensor_copy(RB(5), sums_ps[0:1, 0:D]),
                  reads=[k_sums], writes=[kRB(5)], tag=f"qsum0_bf{b}")
            P.ins("vector", lambda e: e.scalar_tensor_tensor(
                RB(6), RS(2), float(SEQ), sums_ps[0:1, 0:D], ALU.mult, ALU.add),
                reads=[kRS(2), k_sums], writes=[kRB(6)], tag=f"qsum0S{b}")
            psB1.close()

            psC1 = ExitStack()
            Mps = [psC1.enter_context(
                nc.psum_tensor(f"M{b}_{i}", [128, 128], F32)) for i in range(2)]
            Gps = [psC1.enter_context(
                nc.psum_tensor(f"G{b}_{i}", [128, 128], F32)) for i in range(3)]
            Hps = psC1.enter_context(nc.psum_tensor(f"H{b}", [128, 128], F32))
            GHps = psC1.enter_context(nc.psum_tensor(f"GH{b}", [128, 128], F32))
            k_M = [psum_key(t) for t in Mps]
            k_G = [psum_key(t) for t in Gps]
            k_H = psum_key(Hps)
            k_GH = psum_key(GHps)
            for c in range(8):
                mp, kmp = Mps[c % 2], k_M[c % 2]
                gp, kgp = Gps[c % 2], k_G[c % 2]
                for f in range(8):
                    P.ins("tensor", (lambda mp=mp, f=f, c=c: lambda e: e.matmul(
                        mp[:, :], t_cat(f)[:, c * 128:(c + 1) * 128],
                        wqkvT[:, f * 3 * D + D + c * 128:
                              f * 3 * D + D + (c + 1) * 128],
                        start=(f == 0), stop=False))(),
                        reads=[k_tcat(f), R("wqkvT", f * 3 * D, (f + 1) * 3 * D)],
                        writes=[kmp], tag=f"mmM{b}_{c}_{f}")
                P.ins("tensor", (lambda mp=mp, c=c: lambda e: e.matmul(
                    mp[:, :], RB(4, c * 128, 128), RB(1, c * 128, 128),
                    start=False, stop=False))(),
                    reads=[kRB(4), kRB(1)], writes=[kmp], tag=f"mmMr1{b}_{c}")
                P.ins("tensor", (lambda mp=mp, c=c: lambda e: e.matmul(
                    mp[:, :], RB(2, c * 128, 128), RB(3, c * 128, 128),
                    start=False, stop=True))(),
                    reads=[kRB(2), kRB(3)], writes=[kmp], tag=f"mmMr2{b}_{c}")
                for f in range(8):
                    P.ins("tensor", (lambda gp=gp, f=f, c=c: lambda e: e.matmul(
                        gp[:, :], t_cat(f)[:, D + c * 128:D + (c + 1) * 128],
                        wqkvT[:, f * 3 * D + c * 128:f * 3 * D + (c + 1) * 128],
                        start=(f == 0), stop=False))(),
                        reads=[k_tcat(f), R("wqkvT", f * 3 * D, (f + 1) * 3 * D)],
                        writes=[kgp], tag=f"mmG{b}_{c}_{f}")
                P.ins("tensor", (lambda gp=gp, c=c: lambda e: e.matmul(
                    gp[:, :], RB(5, c * 128, 128), RB(0, c * 128, 128),
                    start=False, stop=False))(),
                    reads=[kRB(5), kRB(0)], writes=[kgp], tag=f"mmGr1{b}_{c}")
                P.ins("tensor", (lambda gp=gp, c=c: lambda e: e.matmul(
                    gp[:, :], RB(0, c * 128, 128), RB(6, c * 128, 128),
                    start=False, stop=True))(),
                    reads=[kRB(0), kRB(6)], writes=[kgp], tag=f"mmGr2{b}_{c}")
                for jj in range(4):
                    eng_ = "scalar" if jj % 2 == 0 else "vector"
                    fn_ = (lambda mp=mp, c=c, jj=jj, eng_=eng_: lambda e:
                           (e.copy if eng_ == "scalar" else e.tensor_copy)(
                        mblkT[32 * jj:32 * (jj + 1),
                              c * 128 + 32 * jj:c * 128 + 32 * (jj + 1)],
                        mp[32 * jj:32 * (jj + 1), 32 * jj:32 * (jj + 1)]))()
                    P.ins(eng_, fn_, reads=[kmp],
                          writes=[R("mblkT", c * 128, (c + 1) * 128)],
                          tag=f"exM{b}_{c}_{jj}")
                P.ins("tensor", (lambda c=c: lambda e: e.matmul(
                    Hps[:, :], mblkT[:, c * 128:(c + 1) * 128],
                    mblkT[:, c * 128:(c + 1) * 128], start=True, stop=True))(),
                    reads=[R("mblkT", c * 128, (c + 1) * 128)], writes=[k_H],
                    tag=f"mmH{b}_{c}")
                P.ins("scalar", lambda e: e.copy(h_sb, Hps[:, :]),
                      reads=[k_H], writes=[R("sqscr", 256, 384)], tag=f"evH{b}_{c}")
                P.ins("vector", (lambda gp=gp: lambda e: e.tensor_tensor(
                    GHps[:, :], gp[:, :], h_sb, ALU.mult))(),
                    reads=[kgp, R("sqscr", 256, 384)], writes=[k_GH],
                    tag=f"gh{b}_{c}")
                P.ins("vector", (lambda c=c: lambda e: e.reduce_sum(
                    CF(40 + c), GHps[:, :], axis=AX.X))(),
                    reads=[k_GH], writes=[kCF(40 + c)],
                    tag=f"ghcol{b}_{c}")

            for c in range(8):
                P.ins("vector", (lambda c=c: lambda e: e.transpose(
                    mblk[:, c * 128:(c + 1) * 128],
                    mblkT[:, c * 128:(c + 1) * 128]))(),
                    reads=[R("mblkT", c * 128, (c + 1) * 128)],
                    writes=[R("mblk", c * 128, (c + 1) * 128)], tag=f"vtM{b}_{c}")

            psQC = ExitStack()
            ctpB = [psQC.enter_context(
                nc.psum_tensor(f"ctpB{b}_{i}", [128, 4], F32)) for i in range(1)]
            k_ctpB = [psum_key(t) for t in ctpB]
            for c in range(8):
                pp, kpp = ctpB[0], k_ctpB[0]
                P.ins("tensor", (lambda pp=pp, c=c: lambda e: e.transpose(
                    pp[:, 0:1], RS(1)[0:1, c * 128:(c + 1) * 128],
                    ident_f[0:1, 0:1]))(),
                    reads=[kRS(1), R("ident_f")], writes=[kpp],
                    tag=f"tp_qs{b}_{c}")
                P.ins("scalar", (lambda pp=pp, c=c: lambda e: e.copy(
                    CF(32 + c), pp[:, 0:1]))(), reads=[kpp],
                    writes=[kCF(32 + c)], tag=f"ev_qs{b}_{c}")
            P.ins("vector", lambda e: e.scalar_tensor_tensor(
                CB(8, 8), CF(16, 8), float(SEQ), CF(32, 8), ALU.mult, ALU.add),
                reads=[kCF(16, 8), kCF(32, 8)], writes=[kCB(8, 8)],
                tag=f"qsumcol_bf{b}")
            psQC.close()
            if debug:
                P.ins("gpsimd", (lambda b=b: lambda e: e.dma_start(
                    dbg_d["MT"][b], mblkT[:, :]))(),
                    reads=[R("mblkT")], dma=True, tag=f"dbgMT{b}")
            psC1.close()

            psC1t = ExitStack()
            csum_ps = psC1t.enter_context(
                nc.psum_tensor(f"csum{b}", [1, D], F32))
            u_ps = psC1t.enter_context(nc.psum_tensor(f"u{b}", [1, D], F32))
            stat_ps = psC1t.enter_context(
                nc.psum_tensor(f"stat{b}", [128, 8], F32))
            ctpC = [psC1t.enter_context(
                nc.psum_tensor(f"ctpC{b}_{i}", [128, 4], F32)) for i in range(2)]
            k_cs = psum_key(csum_ps)
            k_u = psum_key(u_ps)
            k_st = psum_key(stat_ps)
            k_ctpC = [psum_key(t) for t in ctpC]
            for c in range(8):
                P.ins("tensor", (lambda c=c: lambda e: e.matmul(
                    csum_ps[0:1, c * 128:(c + 1) * 128], CB(8 + c),
                    mblk[:, c * 128:(c + 1) * 128],
                    start=(c % 4 == 0), stop=(c % 4 == 3)))(),
                    reads=[kCB(8 + c), R("mblk", c * 128, (c + 1) * 128)],
                    writes=[k_cs], tag=f"mmcs{b}_{c}")
            for c in range(8):
                P.ins("tensor", (lambda c=c: lambda e: e.matmul(
                    u_ps[0:1, c * 128:(c + 1) * 128], CB(16 + c),
                    mblk[:, c * 128:(c + 1) * 128],
                    start=(c % 4 == 0), stop=(c % 4 == 3)))(),
                    reads=[kCB(16 + c), R("mblk", c * 128, (c + 1) * 128)],
                    writes=[k_u], tag=f"mmu{b}_{c}")
            P.ins("scalar", lambda e: e.copy(RS(1), csum_ps[0:1, :]),
                  reads=[k_cs], writes=[kRS(1)], tag=f"ev_cs{b}")
            P.ins("scalar", lambda e: e.copy(RS(5), u_ps[0:1, :]),
                  reads=[k_u], writes=[kRS(5)], tag=f"ev_u{b}")
            for c in range(8):
                pp, kpp = ctpC[c % 2], k_ctpC[c % 2]
                P.ins("tensor", (lambda pp=pp, c=c: lambda e: e.transpose(
                    pp[:, 0:1], RS(1)[0:1, c * 128:(c + 1) * 128],
                    ident_f[0:1, 0:1]))(),
                    reads=[kRS(1), R("ident_f")], writes=[kpp],
                    tag=f"tp_cs{b}_{c}")
                P.ins("scalar", (lambda pp=pp, c=c: lambda e: e.copy(
                    CF(56 + c), pp[:, 0:1]))(), reads=[kpp],
                    writes=[kCF(56 + c)], tag=f"ev_csc{b}_{c}")
            for c in range(8):
                pp, kpp = ctpC[c % 2], k_ctpC[c % 2]
                P.ins("tensor", (lambda pp=pp, c=c: lambda e: e.transpose(
                    pp[:, 0:1], RS(5)[0:1, c * 128:(c + 1) * 128],
                    ident_f[0:1, 0:1]))(),
                    reads=[kRS(5), R("ident_f")], writes=[kpp],
                    tag=f"tp_u{b}_{c}")
                P.ins("scalar", (lambda pp=pp, c=c: lambda e: e.copy(
                    CF(48 + c), pp[:, 0:1]))(), reads=[kpp],
                    writes=[kCF(48 + c)], tag=f"ev_uc{b}_{c}")
            P.ins("tensor", lambda e: e.matmul(
                stat_ps[0:4, 0:8], sel[:, :], CF(40, 8), start=True, stop=True),
                reads=[R("sel"), kCF(40, 8)], writes=[k_st], tag=f"mmssq{b}")
            P.ins("scalar", lambda e: e.copy(st44[0:4, 0:8], stat_ps[0:4, 0:8]),
                  reads=[k_st], writes=[R("st44", 0, 8)], tag=f"ev_ssq44{b}")
            P.ins("tensor", lambda e: e.matmul(
                stat_ps[0:4, 0:8], sel[:, :], CF(56, 8), start=True, stop=True),
                reads=[R("sel"), kCF(56, 8)], writes=[k_st], tag=f"mmcs44{b}")
            P.ins("scalar", lambda e: e.copy(st44[0:4, 8:16], stat_ps[0:4, 0:8]),
                  reads=[k_st], writes=[R("st44", 8, 16)], tag=f"ev_cs44{b}")
            P.ins("vector", lambda e: e.tensor_scalar(
                st44[0:4, 16:24], st44[0:4, 8:16], 1.0 / (32 * SEQ), None,
                ALU.mult), reads=[R("st44", 8, 16)], writes=[R("st44", 16, 24)],
                tag=f"mean44{b}")
            P.ins("vector", lambda e: e.tensor_tensor(
                st44[0:4, 24:32], st44[0:4, 16:24], st44[0:4, 16:24], ALU.mult),
                reads=[R("st44", 16, 24)], writes=[R("st44", 24, 32)],
                tag=f"msq44{b}")
            P.ins("vector", lambda e: e.scalar_tensor_tensor(
                st44[0:4, 32:40], st44[0:4, 0:8], 1.0 / (32 * SEQ),
                st44[0:4, 24:32], ALU.mult, ALU.subtract),
                reads=[R("st44", 0, 8), R("st44", 24, 32)],
                writes=[R("st44", 32, 40)], tag=f"var44{b}")
            P.ins("scalar", lambda e: e.activation(
                st44[0:4, 40:48], st44[0:4, 32:40], AF.Sqrt,
                bias=epscol[0:4, :]),
                reads=[R("st44", 32, 40), R("epscol")],
                writes=[R("st44", 40, 48)], tag=f"sd44{b}")
            P.ins("vector", lambda e: e.reciprocal(
                st44[0:4, 48:56], st44[0:4, 40:48]),
                reads=[R("st44", 40, 48)], writes=[R("st44", 48, 56)],
                tag=f"rstd44{b}")
            P.ins("tensor", lambda e: e.matmul(
                stat_ps[:, 0:8], selT[:, :], st44[0:4, 48:56],
                start=True, stop=True),
                reads=[R("selT"), R("st44", 48, 56)], writes=[k_st],
                tag=f"bc_rstd{b}")
            P.ins("vector", lambda e: e.tensor_tensor(
                CF(32, 8), stat_ps[:, 0:8], CF(0, 8), ALU.mult),
                reads=[k_st, kCF(0, 8)], writes=[kCF(32, 8)], tag=f"scols{b}")
            P.ins("tensor", lambda e: e.matmul(
                stat_ps[:, 0:8], selT[:, :], st44[0:4, 16:24],
                start=True, stop=True),
                reads=[R("selT"), R("st44", 16, 24)], writes=[k_st],
                tag=f"bc_mean{b}")
            P.ins("vector", lambda e: e.tensor_tensor(
                sqscr[:, 128:136], stat_ps[:, 0:8], CF(32, 8), ALU.mult),
                reads=[k_st, kCF(32, 8)], writes=[R("sqscr", 128, 136)],
                tag=f"mrcols{b}")
            P.ins("vector", lambda e: e.tensor_tensor(
                CF(40, 8), CF(8, 8), sqscr[:, 128:136], ALU.subtract),
                reads=[kCF(8, 8), R("sqscr", 128, 136)], writes=[kCF(40, 8)],
                tag=f"ttcols{b}")
            P.ins("vector", lambda e: e.tensor_tensor(
                sqscr[:, 136:144], CF(48, 8), CF(32, 8), ALU.mult),
                reads=[kCF(48, 8), kCF(32, 8)], writes=[R("sqscr", 136, 144)],
                tag=f"su{b}")
            P.ins("vector", lambda e: e.tensor_tensor(
                CB(24, 8), sqscr[:, 136:144], CF(40, 8), ALU.add),
                reads=[R("sqscr", 136, 144), kCF(40, 8)], writes=[kCB(24, 8)],
                tag=f"wcols{b}")
            if debug:
                P.ins("gpsimd", (lambda b=b: lambda e: e.dma_start(
                    dbg_d["cols"][b], cols_f[:, :]))(),
                    reads=[R("cols_f")], dma=True, tag=f"dbgcols{b}")
            psC1t.close()

            psC2a = ExitStack()
            beta_ps = psC2a.enter_context(
                nc.psum_tensor(f"beta{b}", [1, D], F32))
            k_beta = psum_key(beta_ps)
            for c in range(8):
                for j in range(2):
                    P.ins("tensor", (lambda c=c, j=j: lambda e: e.matmul(
                        beta_ps[0:1, j * 512:(j + 1) * 512], CB(24 + c),
                        outwT[:, c * D + j * 512:c * D + (j + 1) * 512],
                        start=(c == 0), stop=(c == 7)))(),
                        reads=[kCB(24 + c), R("outwT", c * D, (c + 1) * D)],
                        writes=[k_beta], tag=f"mmbeta{b}_{c}_{j}")
            P.ins("scalar", lambda e: e.copy(RS(5), beta_ps[0:1, :]),
                  reads=[k_beta], writes=[kRS(5)], tag=f"ev_beta{b}")
            P.ins("vector", lambda e: e.tensor_tensor(
                RB(7), RS(5), RS(4), ALU.add),
                reads=[kRS(5), kRS(4)], writes=[kRB(7)], tag=f"bias_bf{b}")
            if debug:
                P.ins("gpsimd", (lambda b=b: lambda e: e.dma_start(
                    dbg_d["rows"][b], rows_f[0:1, :]))(),
                    reads=[R("rows_f")], dma=True, tag=f"dbgrows{b}")
            psC2a.close()

            if debug and b == 0:
                P.ins("gpsimd", lambda e: e.dma_start(
                    dbg_d["weffT"], weffT[:, :]),
                    reads=[R("weffT")], dma=True, tag="dbgweff")
                P.ins("gpsimd", lambda e: e.dma_start(
                    dbg_d["wqnat"], wq_nat[:, :]),
                    reads=[R("wq_nat")], dma=True, tag="dbgwqnat")
            psC2 = ExitStack()
            Pps = [psC2.enter_context(
                nc.psum_tensor(f"P{b}_{i}", [128, D], F32)) for i in range(2)]
            ATps = [psC2.enter_context(
                nc.psum_tensor(f"AT{b}_{i}", [128, D], F32)) for i in range(2)]
            k_P = [psum_key(t) for t in Pps]
            k_AT = [psum_key(t) for t in ATps]
            for c in range(8):
                ms, kms = mblkTs[c % 2], R(f"mblkTs{c % 2}")
                P.ins("scalar", (lambda ms=ms, c=c: lambda e: e.activation(
                    ms[:, :], mblkT[:, c * 128:(c + 1) * 128], AF.Identity,
                    scale=CF(32 + c)))(),
                    reads=[R("mblkT", c * 128, (c + 1) * 128), kCF(32 + c)],
                    writes=[kms], tag=f"mts{b}_{c}")
                pp, kpp = Pps[c % 2], k_P[c % 2]
                for j in range(2):
                    P.ins("tensor", (lambda pp=pp, ms=ms, c=c, j=j:
                                     lambda e: e.matmul(
                        pp[:, j * 512:(j + 1) * 512], ms[:, :],
                        outwT[:, c * D + j * 512:c * D + (j + 1) * 512],
                        start=True, stop=True))(),
                        reads=[kms, R("outwT", c * D, (c + 1) * D)],
                        writes=[kpp], tag=f"mmP{b}_{c}_{j}")
                P.ins("scalar", (lambda pp=pp, c=c: lambda e: e.copy(
                    p_sb(c), pp[:, :]))(), reads=[kpp], writes=[k_psb(c)],
                    tag=f"evP{b}_{c}")
            if debug:
                P.ins("gpsimd", (lambda b=b: lambda e: e.dma_start(
                    dbg_d["P"][b], arena2[:, :]))(),
                    reads=[R("arena2")], dma=True, tag=f"dbgP{b}")
            for f in range(8):
                pp, kpp = ATps[f % 2], k_AT[f % 2]
                for e_ in range(8):
                    for j in range(2):
                        P.ins("tensor", (lambda pp=pp, f=f, e_=e_, j=j:
                                         lambda e: e.matmul(
                            pp[:, j * 512:(j + 1) * 512],
                            wq_nat[:, e_ * D + f * 128:e_ * D + (f + 1) * 128],
                            p_sb(e_)[:, j * 512:(j + 1) * 512],
                            start=(e_ == 0), stop=(e_ == 7)))(),
                            reads=[R("wq_nat", e_ * D, (e_ + 1) * D),
                                   k_psb(e_)], writes=[kpp],
                            tag=f"mmAT{b}_{f}_{e_}_{j}")
                kct = k_ct(f)
                P.ins("vector", (lambda pp=pp, f=f: lambda e: e.tensor_tensor(
                    ct_sb(f), pp[:, :], weffT[:, f * D:(f + 1) * D], ALU.add))(),
                    reads=[kpp, R("weffT", f * D, (f + 1) * D)], writes=[kct],
                    tag=f"ct{b}_{f}")
            if debug:
                P.ins("gpsimd", (lambda b=b: lambda e: e.dma_start(
                    dbg_d["CT"][b], arena[:, 8 * D:16 * D]))(),
                    reads=[R("arena", 8 * D, 16 * D)], dma=True, tag=f"dbgCT{b}")
            psC2.close()

            psF = ExitStack()
            Ops = [psF.enter_context(
                nc.psum_tensor(f"O{b}_{i}", [128, D], F32)) for i in range(3)]
            Fps = [psF.enter_context(
                nc.psum_tensor(f"ftp{b}_{i}", [128, 128], BF16)) for i in range(2)]
            k_O = [psum_key(t) for t in Ops]
            k_F = [psum_key(t) for t in Fps]
            def fxv(i):
                p, h = i // 2, i % 2
                lo = (p % 2) * 4 * D + h * 2 * D
                return (arena2[:, lo:lo + 2 * D].bitcast(F32),
                        R("arena2", lo, lo + 2 * D))

            def f_head(i):
                if i % 2 == 0:
                    p = i // 2
                    lo = (p % 2) * 4 * D
                    dst = arena2[:, lo:lo + 4 * D].bitcast(F32)
                    kd = R("arena2", lo, lo + 4 * D)
                    P.ins("sync", (lambda dst=dst, b=b, i=i: lambda e:
                                   e.dma_start(
                        dst.rearrange("p (a f) -> p a f", a=2),
                        xs[b, i * 128:(i + 2) * 128, :].rearrange(
                            "(a p) f -> p a f", p=128)))(),
                        writes=[kd], dma=True, tag=f"dma_xF{b}_{i}")
                sl, ksl = fxv(i)
                xv, kxv = xnf[i % 2], R(f"xnf{i % 2}")
                P.ins("scalar", (lambda sl=sl, xv=xv, i=i: lambda e: e.activation(
                    xv[:, :], sl, AF.Identity,
                    bias=lncols[:, 16 + i:17 + i], scale=lncols[:, i:i + 1]))(),
                    reads=[ksl, R("lncols", i, i + 1), R("lncols", 16 + i, 17 + i)],
                    writes=[kxv], tag=f"xnF{b}_{i}")
                for c in range(8):
                    fp, kfp = Fps[c % 2], k_F[c % 2]
                    P.ins("tensor", (lambda fp=fp, xv=xv, c=c: lambda e:
                                     e.transpose(fp[:, :],
                                                 xv[:, c * 128:(c + 1) * 128],
                                                 ident_bf[:, :]))(),
                        reads=[kxv, R("ident_bf")], writes=[kfp],
                        tag=f"ftp{b}_{i}_{c}")
                    if c % 2 == 0:
                        P.ins("scalar", (lambda fp=fp, xv=xv, c=c: lambda e:
                                         e.copy(xv[:, c * 128:(c + 1) * 128],
                                                fp[:, :]))(),
                            reads=[kfp], writes=[kxv], tag=f"fev{b}_{i}_{c}")
                    else:
                        P.ins("vector", (lambda fp=fp, xv=xv, c=c: lambda e:
                                         e.tensor_copy(
                                             xv[:, c * 128:(c + 1) * 128],
                                             fp[:, :]))(),
                            reads=[kfp], writes=[kxv], tag=f"fev{b}_{i}_{c}")

            def f_tail(i):
                xt, kxt = xnf[i % 2], R(f"xnf{i % 2}")
                op, kop = Ops[i % 3], k_O[i % 3]
                for f in range(8):
                    for j in range(2):
                        P.ins("tensor", (lambda op=op, xt=xt, f=f, j=j:
                                         lambda e: e.matmul(
                            op[:, j * 512:(j + 1) * 512],
                            xt[:, f * 128:(f + 1) * 128],
                            ct_sb(f)[:, j * 512:(j + 1) * 512],
                            start=(f == 0), stop=False))(),
                            reads=[kxt, k_ct(f)], writes=[kop],
                            tag=f"mmO{b}_{i}_{f}_{j}")
                for j in range(2):
                    P.ins("tensor", (lambda op=op, j=j: lambda e: e.matmul(
                        op[:, j * 512:(j + 1) * 512], ones_row,
                        RB(7, j * 512, 512), start=False, stop=True))(),
                        reads=[k_ones_row, kRB(7)], writes=[kop],
                        tag=f"mmOb{b}_{i}_{j}")

            def f_fin(i):
                sl, ksl = fxv(i)
                op, kop = Ops[i % 3], k_O[i % 3]
                P.ins("vector", (lambda sl=sl, op=op: lambda e: e.tensor_tensor(
                    sl, sl, op[:, :], ALU.add))(),
                    reads=[ksl, kop], writes=[ksl], tag=f"add{b}_{i}")
                if i % 2 == 1:
                    p = i // 2
                    lo = (p % 2) * 4 * D
                    srcv = arena2[:, lo:lo + 4 * D].bitcast(F32)
                    ks2 = R("arena2", lo, lo + 4 * D)
                    P.ins("gpsimd", (lambda srcv=srcv, b=b, i=i: lambda e:
                                     e.dma_start(
                        ys[b, (i - 1) * 128:(i + 1) * 128, :].rearrange(
                            "(a p) f -> p a f", p=128),
                        srcv.rearrange("p (a f) -> p a f", a=2)))(),
                        reads=[ks2], dma=True, tag=f"out{b}_{i}")

            for i in range(NT):
                if i >= 2:
                    f_tail(i - 2)
                    f_fin(i - 2)
                f_head(i)
            f_tail(NT - 2)
            f_fin(NT - 2)
            f_tail(NT - 1)
            f_fin(NT - 1)
            psF.close()

        import os as _os
        if _os.environ.get("KERNEL_DEBUG_SBUF"):
            print("sbuf bytes remaining (inside build):", nc.sbuf_bytes_remaining)
        if _os.environ.get("KERNEL_DEBUG_DEPS"):
            want = _os.environ["KERNEL_DEBUG_DEPS"].split(",")
            for i, insd in enumerate(P.instrs):
                if insd["tag"] in want:
                    print(f"instr {i} tag={insd['tag']} eng={insd['engine']} "
                          f"dma={insd['dma']}")
                    for d in sorted(insd["deps"]):
                        dd = P.instrs[d]
                        print(f"   dep {d}: {dd['tag']} {dd['engine']} dma={dd['dma']}")
        sems = {e: E(nc.semaphore(f"s_{e}")) for e in ENGINES}
        dsems = {e: [E(nc.semaphore(f"sd_{e}_{j}"))
                     for j in range(Sched.NDMA_LANES[e])] for e in ENGINES}
        P.emit(nc, block, sems, dsems)

    return nc, dbg_d


_NC_CACHE = {}


def _get_nc(debug=False):
    key = bool(debug)
    if key not in _NC_CACHE:
        _NC_CACHE[key] = build_nc(debug)
    return _NC_CACHE[key]


def kernel(**inputs):
    x = np.ascontiguousarray(np.asarray(inputs["L0_attn_x"], np.float32))
    feed = {
        "wqkv": np.ascontiguousarray(np.asarray(inputs["L0_attn_Wqkv_w"], np.float32)),
        "bqkv": np.ascontiguousarray(np.asarray(inputs["L0_attn_Wqkv_b"], np.float32)),
        "gnw": np.ascontiguousarray(np.asarray(inputs["L0_attn_gn_w"], np.float32)),
        "gnb": np.ascontiguousarray(np.asarray(inputs["L0_attn_gn_b"], np.float32)),
        "outw": np.ascontiguousarray(np.asarray(inputs["L0_attn_out_w"], np.float32)),
        "outb": np.ascontiguousarray(np.asarray(inputs["L0_attn_out_b"], np.float32)),
        "wff": np.ascontiguousarray(np.asarray(inputs["L0_ffn_W"], np.float32)),
        "bff": np.ascontiguousarray(np.asarray(inputs["L0_ffn_b"], np.float32)),
        "ld": np.ascontiguousarray(np.asarray(inputs["L0_ffn_lora_down"], np.float32)),
        "lu": np.ascontiguousarray(np.asarray(inputs["L0_ffn_lora_up"], np.float32)),
        "da": np.ascontiguousarray(np.asarray(inputs["L0_ffn_down_aux"], np.float32)),
        "ua": np.ascontiguousarray(np.asarray(inputs["L0_ffn_up_aux"], np.float32)),
    }
    nc, _ = _get_nc(False)
    in_maps = []
    for core in range(NCORES):
        m = dict(feed)
        m["xs"] = np.ascontiguousarray(x[core * NB:(core + 1) * NB])
        in_maps.append(m)
    res = run_bass_kernel_spmd(nc, in_maps, core_ids=list(range(NCORES)))
    return np.concatenate([r["ys"] for r in res.results], axis=0)


if __name__ == "__main__":
    nc, _ = build_nc(False)
    counts = {}
    print("built ok")


# version 25
# speedup vs baseline: 1.1370x; 1.0254x over previous
import numpy as np
from contextlib import ExitStack

import concourse.bass as bass
import concourse.mybir as mybir
from concourse.bass_utils import run_bass_kernel_spmd

F32 = mybir.dt.float32
BF16 = mybir.dt.bfloat16
AF = mybir.ActivationFunctionType
ALU = mybir.AluOpType
AX = mybir.AxisListType

D = 1024
SEQ = 2048
NB = 2
NT = 16
EPS = 1e-5
NCORES = 8
ENGINES = ("sync", "gpsimd", "scalar", "vector", "tensor")


class Sched:

    def __init__(self):
        self.instrs = []
        self.acc = {}
        self._dma_count = {e: 0 for e in ENGINES}

    @staticmethod
    def r(name, lo=0, hi=1 << 60):
        return (name, lo, hi)

    def _overlap(self, key, want_writes):
        name, lo, hi = key
        out = []
        for (l, h, iid, w) in self.acc.get(name, ()):
            if w == want_writes and l < hi and lo < h:
                out.append(iid)
        return out

    def ins(self, engine, fn, reads=(), writes=(), dma=False, tag=""):
        assert engine in ENGINES
        deps = set()
        for k in reads:
            deps.update(self._overlap(k, True))
        for k in writes:
            deps.update(self._overlap(k, True))
            deps.update(self._overlap(k, False))
        iid = len(self.instrs)
        lane = self._dma_count[engine] % self.NDMA_LANES[engine] if dma else None

        def keep(d):
            dd = self.instrs[d]
            de = dd["engine"]
            if de != engine:
                return True
            if dd["dma"] and dma:
                return False
            if engine == "tensor":
                return False
            return True
        deps = {d for d in deps if keep(d)}
        best = {}
        for d in deps:
            dd = self.instrs[d]
            ch = (dd["engine"], dd["lane"]) if dd["dma"] else (dd["engine"],)
            if best.get(ch, -1) < d:
                best[ch] = d
        deps = set(best.values())
        self.instrs.append(dict(engine=engine, fn=fn, deps=deps, dma=dma,
                                lane=lane, tag=tag))
        if dma:
            self._dma_count[engine] += 1
        for k in reads:
            self.acc.setdefault(k[0], []).append((k[1], k[2], iid, False))
        for k in writes:
            self.acc.setdefault(k[0], []).append((k[1], k[2], iid, True))
        return iid

    NDMA_LANES = {"sync": 16, "gpsimd": 12, "scalar": 8, "vector": 4,
                  "tensor": 2}

    def emit(self, nc, block, sems, dsems, final_waits=("gpsimd",)):
        needed = set()
        for ins in self.instrs:
            needed.update(ins["deps"])
        cum = {}
        chan = {}
        val = {}
        for i, ins in enumerate(self.instrs):
            if ins["dma"]:
                ch = (ins["engine"], ins["lane"])
                cum[ch] = cum.get(ch, 0) + 16
                chan[i] = ch
                val[i] = cum[ch]
                needed.add(i)
            else:
                ch = (ins["engine"],)
                if i in needed:
                    cum[ch] = cum.get(ch, 0) + 1
                chan[i] = ch
                val[i] = cum.get(ch, 0)
        cum2 = {}
        for i, ins in enumerate(self.instrs):
            if not ins["dma"]:
                ch = chan[i]
                if i in needed:
                    cum2[ch] = cum2.get(ch, 0) + 1
                val[i] = cum2.get(ch, 0)
        order = {e: [i for i, ins in enumerate(self.instrs) if ins["engine"] == e]
                 for e in ENGINES}

        def hnd(ch):
            return dsems[ch[0]][ch[1]] if len(ch) == 2 else sems[ch[0]]

        totals = dict(cum)
        totals.update(cum2)

        def section(e):
            def body(eng):
                waited = {}
                for iid in order[e]:
                    ins = self.instrs[iid]
                    for d in sorted(ins["deps"], key=lambda d: val[d]):
                        dch = chan[d]
                        v = val[d]
                        assert v > 0, (d, iid)
                        if waited.get(dch, 0) < v:
                            eng.wait_ge(hnd(dch), v)
                            waited[dch] = v
                    bi = ins["fn"](eng)
                    if iid in needed:
                        ch = chan[iid]
                        bi.then_inc(hnd(ch), 16 if ins["dma"] else 1)
                if e in final_waits:
                    for ch, tot in totals.items():
                        if len(ch) == 2 and ch[0] == e and tot:
                            eng.wait_ge(hnd(ch), tot)
            return body

        block.sync(section("sync"))
        block.gpsimd(section("gpsimd"))
        block.scalar(section("scalar"))
        block.vector(section("vector"))
        block.tensor(section("tensor"))
        return totals


def build_nc(debug=False):
    nc = bass.Bass("TRN2", target_bir_lowering=False)
    P = Sched()
    R = Sched.r

    xs = nc.dram_tensor("xs", [NB, SEQ, D], F32, kind="ExternalInput").ap()
    wqkv_d = nc.dram_tensor("wqkv", [3 * D, D], F32, kind="ExternalInput").ap()
    bqkv_d = nc.dram_tensor("bqkv", [3 * D], F32, kind="ExternalInput").ap()
    gnw_d = nc.dram_tensor("gnw", [D], F32, kind="ExternalInput").ap()
    gnb_d = nc.dram_tensor("gnb", [D], F32, kind="ExternalInput").ap()
    outw_d = nc.dram_tensor("outw", [D, D], F32, kind="ExternalInput").ap()
    outb_d = nc.dram_tensor("outb", [D], F32, kind="ExternalInput").ap()
    wff_d = nc.dram_tensor("wff", [D, D], F32, kind="ExternalInput").ap()
    bff_d = nc.dram_tensor("bff", [D], F32, kind="ExternalInput").ap()
    ld_d = nc.dram_tensor("ld", [4, 100], F32, kind="ExternalInput").ap()
    lu_d = nc.dram_tensor("lu", [50, 4], F32, kind="ExternalInput").ap()
    da_d = nc.dram_tensor("da", [100, D], F32, kind="ExternalInput").ap()
    ua_d = nc.dram_tensor("ua", [D, 50], F32, kind="ExternalInput").ap()
    ys = nc.dram_tensor("ys", [NB, SEQ, D], F32, kind="ExternalOutput").ap()
    dbg_d = {}
    if debug:
        dbg_d["xn"] = nc.dram_tensor("dbg_xn", [NB, 128, 16 * D], BF16,
                                     kind="ExternalOutput").ap()
        dbg_d["X"] = nc.dram_tensor("dbg_X", [NB, 128, 8 * D], BF16,
                                    kind="ExternalOutput").ap()
        dbg_d["MT"] = nc.dram_tensor("dbg_MT", [NB, 128, D], BF16,
                                     kind="ExternalOutput").ap()
        dbg_d["CT"] = nc.dram_tensor("dbg_CT", [NB, 128, 8 * D], BF16,
                                     kind="ExternalOutput").ap()
        dbg_d["rows"] = nc.dram_tensor("dbg_rows", [NB, 1, 5 * D], F32,
                                       kind="ExternalOutput").ap()
        dbg_d["P"] = nc.dram_tensor("dbg_P", [NB, 128, 8 * D], BF16,
                                    kind="ExternalOutput").ap()
        dbg_d["weffT"] = nc.dram_tensor("dbg_weffT", [128, 8 * D], BF16,
                                        kind="ExternalOutput").ap()
        dbg_d["wqnat"] = nc.dram_tensor("dbg_wqnat", [128, 8 * D], BF16,
                                        kind="ExternalOutput").ap()
        dbg_d["lil"] = nc.dram_tensor("dbg_lil", [128, 4 * D], BF16,
                                      kind="ExternalOutput").ap()
        dbg_d["cols"] = nc.dram_tensor("dbg_cols", [NB, 128, 64], F32,
                                       kind="ExternalOutput").ap()

    with ExitStack() as st, nc.Block() as block:
        E = st.enter_context

        arena = E(nc.sbuf_tensor("arena", [128, 16 * D], BF16))
        arena2 = E(nc.sbuf_tensor("arena2", [128, 8 * D], BF16))
        wqkvT = E(nc.sbuf_tensor("wqkvT", [128, 24 * D], BF16))
        wq_nat = E(nc.sbuf_tensor("wq_nat", [128, 8 * D], BF16))
        outwT = E(nc.sbuf_tensor("outwT", [128, 8 * D], BF16))
        weffT = E(nc.sbuf_tensor("weffT", [128, 8 * D], BF16))
        stage = [E(nc.sbuf_tensor(f"stage{i}", [128, D], F32)) for i in range(3)]
        sqscr = E(nc.sbuf_tensor("sqscr", [128, D], F32))
        xnf = [E(nc.sbuf_tensor(f"xnf{i}", [128, D], BF16)) for i in range(2)]
        mblkT = E(nc.sbuf_tensor("mblkT", [128, D], BF16))
        mblk = E(nc.sbuf_tensor("mblk", [128, D], BF16))
        mblkTs = [E(nc.sbuf_tensor(f"mblkTs{i}", [128, 128], BF16)) for i in range(2)]
        lil_small = E(nc.sbuf_tensor("lil_small", [128, 128], BF16))
        ident_bf = E(nc.sbuf_tensor("ident_bf", [128, 128], BF16))
        ident_f = E(nc.sbuf_tensor("ident_f", [128, 128], F32))
        sel = E(nc.sbuf_tensor("sel", [128, 4], F32))
        selT = E(nc.sbuf_tensor("selT", [4, 128], F32))
        ones_col = E(nc.sbuf_tensor("ones_col", [128, 1], BF16))
        epscol = E(nc.sbuf_tensor("epscol", [128, 1], F32))
        rows_f = E(nc.sbuf_tensor("rows_f", [1, 5 * D], F32))
        h_sb = sqscr[:, 256:384]
        rows_b = E(nc.sbuf_tensor("rows_b", [1, 8 * D + 128], BF16))
        cols_f = E(nc.sbuf_tensor("cols_f", [128, 64], F32))
        cols_b = E(nc.sbuf_tensor("cols_b", [128, 32], BF16))
        lncols = E(nc.sbuf_tensor("lncols", [128, 32], F32))
        lnst = E(nc.sbuf_tensor("lnst", [128, 96], F32))
        st44 = E(nc.sbuf_tensor("st44", [128, 64], F32))

        RS = lambda k, o=0, n=D: (sqscr[0:1, o:o + n] if k == 0
                                  else rows_f[0:1, (k - 1) * D + o:(k - 1) * D + o + n])
        kRS = lambda k, o=0, n=D: (R("sqscr", o, o + n) if k == 0
                                   else R("rows_f", (k - 1) * D + o, (k - 1) * D + o + n))
        RB = lambda k, o=0, n=D: rows_b[0:1, k * D + o:k * D + o + n]
        kRB = lambda k, o=0, n=D: R("rows_b", k * D + o, k * D + o + n)
        ones_row = rows_b[0:1, 8 * D:8 * D + 128]
        k_ones_row = R("rows_b", 8 * D, 8 * D + 128)
        CF = lambda k, n=1: cols_f[:, k:k + n]
        kCF = lambda k, n=1: R("cols_f", k, k + n)
        CB = lambda k, n=1: cols_b[:, k:k + n]
        kCB = lambda k, n=1: R("cols_b", k, k + n)

        xn_tm = lambda i: arena[:, i * D:(i + 1) * D]
        k_xn = lambda i: R("arena", i * D, (i + 1) * D)
        t_cat = lambda f: arena[:, f * 2 * D:(f + 1) * 2 * D]
        k_tcat = lambda f: R("arena", f * 2 * D, (f + 1) * 2 * D)
        ct_sb = lambda f: arena[:, 8 * D + f * D:8 * D + (f + 1) * D]
        k_ct = lambda f: R("arena", 8 * D + f * D, 8 * D + (f + 1) * D)
        x_sb = lambda g: arena2[:, g * D:(g + 1) * D]
        k_xsb = lambda g: R("arena2", g * D, (g + 1) * D)
        p_sb = lambda e: arena2[:, e * D:(e + 1) * D]
        k_psb = k_xsb
        v_dab = xnf[0][0:100, :]
        k_dab = R("xnf0")
        v_uaT = xnf[1][0:50, :]
        k_uaT = R("xnf1")
        v_t1 = arena[0:4, 0:D]
        k_t1 = R("arena", 0, D)
        v_t2 = arena[0:50, D:2 * D]
        k_t2 = R("arena", D, 2 * D)
        v_ldT = lil_small[0:100, 0:4]
        k_ldT = R("lil_small", 0, 4)
        v_luT = lil_small[0:4, 64:114]
        k_luT = R("lil_small", 64, 114)

        def psum_key(handle):
            for alloc in nc.m.functions[0].allocations:
                mls = getattr(alloc, "memorylocations", None)
                if mls:
                    for ml in mls:
                        if ml.name == handle.name:
                            return R("PSUM", ml.addr, ml.addr + int(ml.dims[-1]))
            raise KeyError(handle.name)

        P.ins("gpsimd", lambda g: g.memset(ident_bf[:, :], 0.0),
              writes=[R("ident_bf")], tag="c0")
        P.ins("gpsimd", lambda g: g.affine_select(
            out=ident_bf[:, :], in_=ident_bf[:, :], compare_op=ALU.not_equal,
            fill=1.0, base=0, pattern=[[-1, 128]], channel_multiplier=1),
            reads=[R("ident_bf")], writes=[R("ident_bf")], tag="c1")
        P.ins("gpsimd", lambda g: g.memset(ident_f[:, :], 0.0),
              writes=[R("ident_f")])
        P.ins("gpsimd", lambda g: g.affine_select(
            out=ident_f[:, :], in_=ident_f[:, :], compare_op=ALU.not_equal,
            fill=1.0, base=0, pattern=[[-1, 128]], channel_multiplier=1),
            reads=[R("ident_f")], writes=[R("ident_f")])
        P.ins("gpsimd", lambda g: g.memset(sel[:, :], 1.0), writes=[R("sel")])
        P.ins("gpsimd", lambda g: g.affine_select(
            out=sel[:, :], in_=sel[:, :], compare_op=ALU.is_ge, fill=0.0,
            base=0, pattern=[[-32, 4]], channel_multiplier=1),
            reads=[R("sel")], writes=[R("sel")])
        P.ins("gpsimd", lambda g: g.affine_select(
            out=sel[:, :], in_=sel[:, :], compare_op=ALU.is_ge, fill=0.0,
            base=31, pattern=[[32, 4]], channel_multiplier=-1),
            reads=[R("sel")], writes=[R("sel")])
        P.ins("gpsimd", lambda g: g.memset(selT[:, :], 1.0), writes=[R("selT")])
        P.ins("gpsimd", lambda g: g.affine_select(
            out=selT[:, :], in_=selT[:, :], compare_op=ALU.is_ge, fill=0.0,
            base=0, pattern=[[1, 128]], channel_multiplier=-32),
            reads=[R("selT")], writes=[R("selT")])
        P.ins("gpsimd", lambda g: g.affine_select(
            out=selT[:, :], in_=selT[:, :], compare_op=ALU.is_ge, fill=0.0,
            base=31, pattern=[[-1, 128]], channel_multiplier=32),
            reads=[R("selT")], writes=[R("selT")])
        P.ins("gpsimd", lambda g: g.memset(ones_col[:, :], 1.0),
              writes=[R("ones_col")])
        P.ins("gpsimd", lambda g: g.memset(epscol[:, :], EPS), writes=[R("epscol")])
        P.ins("gpsimd", lambda g: g.memset(ones_row, 1.0), writes=[k_ones_row])
        P.ins("gpsimd", lambda g: g.memset(mblkT[:, :], 0.0), writes=[R("mblkT")])
        P.ins("gpsimd", lambda g: g.memset(mblk[:, :], 0.0), writes=[R("mblk")])


        def make_A1(b):
            def ln_stats(i):
                n = b * NT + i
                sl = stage[n % 3]
                ksl = R(f"stage{n % 3}")
                P.ins("sync", (lambda sl=sl, b=b, i=i: lambda e: e.dma_start(
                    sl[:, :], xs[b, i * 128:(i + 1) * 128, :]))(),
                    writes=[ksl], dma=True, tag=f"dma_xA{b}_{i}")
                P.ins("vector", (lambda sl=sl, i=i: lambda e: e.reduce_sum(
                    lnst[:, i:i + 1], sl[:, :], axis=AX.X))(),
                    reads=[ksl], writes=[R("lnst", i, i + 1)], tag=f"ln_sum{i}")
                P.ins("scalar", (lambda sl=sl: lambda e: e.activation(
                    sqscr[:, :], sl[:, :], AF.Square))(),
                    reads=[ksl], writes=[R("sqscr")], tag=f"ln_sq{i}")
                P.ins("vector", (lambda i=i: lambda e: e.reduce_sum(
                    lnst[:, 16 + i:17 + i], sqscr[:, :], axis=AX.X))(),
                    reads=[R("sqscr")], writes=[R("lnst", 16 + i, 17 + i)],
                    tag=f"ln_ssq{i}")
                P.ins("vector", (lambda i=i: lambda e: e.tensor_scalar(
                    lnst[:, 32 + i:33 + i], lnst[:, i:i + 1], -1.0 / D, None,
                    ALU.mult))(),
                    reads=[R("lnst", i, i + 1)],
                    writes=[R("lnst", 32 + i, 33 + i)], tag=f"ln_negmean{i}")
                P.ins("vector", (lambda i=i: lambda e: e.tensor_tensor(
                    lnst[:, 48 + i:49 + i], lnst[:, 32 + i:33 + i],
                    lnst[:, 32 + i:33 + i], ALU.mult))(),
                    reads=[R("lnst", 32 + i, 33 + i)],
                    writes=[R("lnst", 48 + i, 49 + i)], tag=f"ln_msq{i}")
                P.ins("vector", (lambda i=i: lambda e: e.scalar_tensor_tensor(
                    lnst[:, 64 + i:65 + i], lnst[:, 16 + i:17 + i], 1.0 / D,
                    lnst[:, 48 + i:49 + i], ALU.mult, ALU.subtract))(),
                    reads=[R("lnst", 16 + i, 17 + i), R("lnst", 48 + i, 49 + i)],
                    writes=[R("lnst", 64 + i, 65 + i)], tag=f"ln_var{i}")
                P.ins("scalar", (lambda i=i: lambda e: e.activation(
                    lnst[:, 80 + i:81 + i], lnst[:, 64 + i:65 + i], AF.Ln,
                    bias=epscol[:, :]))(),
                    reads=[R("lnst", 64 + i, 65 + i), R("epscol")],
                    writes=[R("lnst", 80 + i, 81 + i)], tag=f"ln_lnv{i}")
                P.ins("scalar", (lambda i=i: lambda e: e.activation(
                    lncols[:, i:i + 1], lnst[:, 80 + i:81 + i], AF.Exp,
                    scale=-0.5))(),
                    reads=[R("lnst", 80 + i, 81 + i)],
                    writes=[R("lncols", i, i + 1)], tag=f"ln_rstd{i}")

            def ln_finish(i):
                n = b * NT + i
                sl = stage[n % 3]
                ksl = R(f"stage{n % 3}")
                P.ins("vector", (lambda i=i: lambda e: e.tensor_scalar(
                    lncols[:, 16 + i:17 + i], lnst[:, 32 + i:33 + i],
                    lncols[:, i:i + 1], None, ALU.mult))(),
                    reads=[R("lnst", 32 + i, 33 + i), R("lncols", i, i + 1)],
                    writes=[R("lncols", 16 + i, 17 + i)], tag=f"ln_negmr{i}")
                P.ins("vector", (lambda sl=sl, i=i, b=b: lambda e: e.tensor_scalar(
                    xn_tm(i), sl[:, :], lncols[:, i:i + 1],
                    lncols[:, 16 + i:17 + i], ALU.mult, ALU.add))(),
                    reads=[ksl, R("lncols", i, i + 1),
                           R("lncols", 16 + i, 17 + i)],
                    writes=[k_xn(i)], tag=f"xn{b}_{i}")

            def tile(k):
                if k >= 2:
                    ln_finish(k - 2)
                if k < NT:
                    ln_stats(k)
                if k == NT:
                    ln_finish(NT - 2)
                    ln_finish(NT - 1)
            return tile

        wslot = [arena[:, 0:8 * D].bitcast(F32),
                 arena[:, 8 * D:16 * D].bitcast(F32)]
        k_wslot = [R("arena", 0, 8 * D), R("arena", 8 * D, 16 * D)]

        def w_grp_dram(g):
            if g < 6:
                src = wqkv_d[g * 512:(g + 1) * 512, :]
            elif g < 8:
                src = outw_d[(g - 6) * 512:(g - 5) * 512, :]
            else:
                src = wff_d[(g - 8) * 512:(g - 7) * 512, :]
            return src.rearrange("(a p) f -> p a f", p=128)

        def w_tp_dest(t, c):
            if t < 24:
                lo = c * 3 * D + t * 128
                return wqkvT[:, lo:lo + 128], R("wqkvT", lo, lo + 128)
            if t < 32:
                lo = c * D + (t - 24) * 128
                return outwT[:, lo:lo + 128], R("outwT", lo, lo + 128)
            lo = c * D + (t - 32) * 128
            return weffT[:, lo:lo + 128], R("weffT", lo, lo + 128)

        def emit_wgroup(g):
            sl = wslot[g % 2]
            ksl = k_wslot[g % 2]
            P.ins("sync" if g % 2 == 0 else "gpsimd",
                  (lambda sl=sl, g=g: lambda e: e.dma_start(
                      sl.rearrange("p (a f) -> p a f", a=4), w_grp_dram(g)))(),
                  writes=[ksl], dma=True, tag=f"dma_wg{g}")
            for q in range(4):
                t = g * 4 + q
                if t < 8:
                    cb, kcb = wq_nat[:, t * D:(t + 1) * D], R("wq_nat", t * D,
                                                              (t + 1) * D)
                else:
                    cb, kcb = xnf[t % 2][:, :], R(f"xnf{t % 2}")
                P.ins("vector", (lambda sl=sl, q=q, cb=cb: lambda e:
                                 e.tensor_copy(cb, sl[:, q * D:(q + 1) * D]))(),
                    reads=[ksl], writes=[kcb], tag=f"cast_w{t}")
                for c in range(8):
                    j = t * 8 + c
                    ps = tp32_ps[j % 4]
                    kps = k_tp32[j % 4]
                    P.ins("tensor", (lambda ps=ps, cb=cb, c=c: lambda e:
                                     e.transpose(ps[:, 0:128],
                                                 cb[:, c * 128:(c + 1) * 128],
                                                 ident_bf[:, :]))(),
                        reads=[kcb, R("ident_bf")], writes=[kps],
                        tag=f"tp_w{t}_{c}")
                    dv, kdv = w_tp_dest(t, c)
                    if c % 2 == 0:
                        P.ins("scalar", (lambda dv=dv, ps=ps: lambda e: e.copy(
                            dv, ps[:, 0:128]))(), reads=[kps], writes=[kdv],
                            tag=f"ev_w{t}_{c}")
                    else:
                        P.ins("vector", (lambda dv=dv, ps=ps: lambda e:
                                         e.tensor_copy(dv, ps[:, 0:128]))(),
                            reads=[kps], writes=[kdv], tag=f"ev_w{t}_{c}")


        ps_rows = ExitStack()
        ctpR = ps_rows.enter_context(nc.psum_tensor("ctpR", [128, 4], F32))
        k_ctpR = psum_key(ctpR)
        def row_dma(dram_ap, tag):
            ap2 = dram_ap.rearrange("(a d) -> a d", a=1)
            P.ins("gpsimd", (lambda ap2=ap2: lambda e: e.dma_start(RS(0), ap2))(),
                  writes=[kRS(0)], dma=True, tag=tag)

        row_dma(bqkv_d[0:D], "dma_bq")
        P.ins("scalar", lambda e: e.activation(RB(0), RS(0), AF.Copy),
              reads=[kRS(0)], writes=[kRB(0)], tag="cast_bq")
        P.ins("scalar", lambda e: e.copy(RS(2), RS(0)),
              reads=[kRS(0)], writes=[kRS(2)], tag="keep_bq")
        for c in range(8):
            pp = ctpR
            kpp = k_ctpR
            P.ins("tensor", (lambda pp=pp, c=c: lambda e: e.transpose(
                pp[:, 0:1], RS(2)[0:1, c * 128:(c + 1) * 128],
                ident_f[0:1, 0:1]))(),
                reads=[kRS(2), R("ident_f")], writes=[kpp], tag=f"tp_bqc{c}")
            P.ins("scalar", (lambda pp=pp, c=c: lambda e: e.copy(
                CF(16 + c), pp[:, 0:1]))(), reads=[kpp],
                writes=[kCF(16 + c)], tag=f"ev_bqc{c}")
        row_dma(bqkv_d[D:2 * D], "dma_bk")
        P.ins("scalar", lambda e: e.activation(RB(1), RS(0), AF.Copy),
              reads=[kRS(0)], writes=[kRB(1)], tag="cast_bk")
        row_dma(bqkv_d[2 * D:3 * D], "dma_bv")
        P.ins("scalar", lambda e: e.activation(RB(2), RS(0), AF.Copy),
              reads=[kRS(0)], writes=[kRB(2)], tag="cast_bv")
        P.ins("vector", lambda e: e.tensor_scalar(
            RS(3), RS(0), float(SEQ), None, ALU.mult),
            reads=[kRS(0)], writes=[kRS(3)], tag="sbv")
        row_dma(gnw_d[:], "dma_gnw")
        for c in range(8):
            pp = ctpR
            kpp = k_ctpR
            P.ins("tensor", (lambda pp=pp, c=c: lambda e: e.transpose(
                pp[:, 0:1], RS(0)[0:1, c * 128:(c + 1) * 128],
                ident_f[0:1, 0:1]))(),
                reads=[kRS(0), R("ident_f")], writes=[kpp], tag=f"tp_gnw{c}")
            P.ins("scalar", (lambda pp=pp, c=c: lambda e: e.copy(
                CF(0 + c), pp[:, 0:1]))(), reads=[kpp], writes=[kCF(c)],
                tag=f"ev_gnw{c}")
        row_dma(gnb_d[:], "dma_gnb")
        for c in range(8):
            pp = ctpR
            kpp = k_ctpR
            P.ins("tensor", (lambda pp=pp, c=c: lambda e: e.transpose(
                pp[:, 0:1], RS(0)[0:1, c * 128:(c + 1) * 128],
                ident_f[0:1, 0:1]))(),
                reads=[kRS(0), R("ident_f")], writes=[kpp], tag=f"tp_gnb{c}")
            P.ins("scalar", (lambda pp=pp, c=c: lambda e: e.copy(
                CF(8 + c), pp[:, 0:1]))(), reads=[kpp], writes=[kCF(8 + c)],
                tag=f"ev_gnb{c}")
        row_dma(outb_d[:], "dma_outb")
        P.ins("scalar", lambda e: e.copy(RS(4), RS(0)),
              reads=[kRS(0)], writes=[kRS(4)], tag="keep_outb")
        row_dma(bff_d[:], "dma_bff")
        P.ins("vector", lambda e: e.tensor_tensor(RS(4), RS(4), RS(0), ALU.add),
              reads=[kRS(4), kRS(0)], writes=[kRS(4)], tag="obf")
        P.ins("vector", lambda e: e.tensor_copy(CB(16, 8), CF(16, 8)),
              reads=[kCF(16, 8)], writes=[kCB(16, 8)], tag="bqcol_bf")

        ps_rows.close()

        for b in range(NB):
            psA = ExitStack()
            Xps = [psA.enter_context(
                nc.psum_tensor(f"X{b}_{i}", [128, D], F32)) for i in range(3)]
            Srow = psA.enter_context(nc.psum_tensor(f"xsr{b}", [1, D], F32))
            k_X = [psum_key(t) for t in Xps]
            k_Sr = psum_key(Srow)

            def x_chunk_mms(c, i):
                lhs = xn_tm(i)[:, c * 128:(c + 1) * 128]
                kx = k_xn(i)
                pp, kpp = Xps[c % 3], k_X[c % 3]
                st_ = (i == 0)
                sp_ = (i == NT - 1)
                for j in range(2):
                    P.ins("tensor", (lambda lhs=lhs, pp=pp, i=i, j=j, st_=st_,
                                     sp_=sp_: lambda e: e.matmul(
                        pp[:, j * 512:(j + 1) * 512], lhs,
                        xn_tm(i)[:, j * 512:(j + 1) * 512],
                        start=st_, stop=sp_))(),
                        reads=[kx], writes=[kpp], tag=f"mmX{b}_{c}_{i}_{j}")

            def xnsum_row_mms(i):
                kx = k_xn(i)
                st_ = (i == 0)
                sp_ = (i == NT - 1)
                for j in range(2):
                    P.ins("tensor", (lambda i=i, j=j, st_=st_, sp_=sp_,
                                     Srow=Srow: lambda e: e.matmul(
                        Srow[0:1, j * 512:(j + 1) * 512], ones_col[:, :],
                        xn_tm(i)[:, j * 512:(j + 1) * 512],
                        start=st_, stop=sp_))(),
                        reads=[kx, R("ones_col")], writes=[k_Sr],
                        tag=f"mmSr{b}_{i}")

            def x_evac(c):
                pp, kpp = Xps[c % 3], k_X[c % 3]
                P.ins("scalar", (lambda pp=pp, c=c: lambda e: e.copy(
                    x_sb(c), pp[:, :]))(), reads=[kpp], writes=[k_xsb(c)],
                    tag=f"evX{b}_{c}")

            a1t = make_A1(b)
            for k in range(NT + 2):
                if k <= NT:
                    a1t(k)
                i = k - 2
                if 0 <= i < NT:
                    for c in (0, 1, 2):
                        x_chunk_mms(c, i)
                    xnsum_row_mms(i)
            for c in (0, 1, 2):
                x_evac(c)
            P.ins("scalar", (lambda Srow=Srow: lambda e: e.copy(
                RS(0), Srow[0:1, :]))(),
                reads=[k_Sr], writes=[kRS(0)], tag=f"evSr{b}")
            for grp in ((3, 4, 5), (6, 7)):
                for i in range(NT):
                    for c in grp:
                        x_chunk_mms(c, i)
                for c in grp:
                    x_evac(c)

            if debug:
                P.ins("gpsimd", (lambda b=b: lambda e: e.dma_start(
                    dbg_d["X"][b], arena2[:, :]))(),
                    reads=[R("arena2")], dma=True, tag=f"dbgX{b}")
                P.ins("gpsimd", (lambda b=b: lambda e: e.dma_start(
                    dbg_d["xn"][b], arena[:, :]))(),
                    reads=[R("arena", 0, 16 * D)], dma=True, tag=f"dbgxn{b}")
            psA.close()

            if b == 0:
                ps0 = ExitStack()
                tp32_ps = [ps0.enter_context(
                    nc.psum_tensor(f"tp32_{i}", [128, 512], BF16))
                    for i in range(4)]
                k_tp32 = [psum_key(t) for t in tp32_ps]
                for g in range(10):
                    emit_wgroup(g)
                ps_lil = ExitStack()
                ctp_ps = [ps_lil.enter_context(
                    nc.psum_tensor("ctpL", [128, 4], F32))]
                k_ctp = [psum_key(ctp_ps[0])]
                lil_ps = [ps_lil.enter_context(
                    nc.psum_tensor(f"lilp{i}", [128, 512], F32))
                    for i in range(2)]
                k_lilp = [psum_key(t) for t in lil_ps]
                st_da = stage[0][0:100, :]
                k_stda = R("stage0")
                st_ld = stage[1][0:4, 0:100]
                k_stld = R("stage1", 0, 100)
                st_lu = stage[1][0:50, 128:132]
                k_stlu = R("stage1", 128, 132)
                P.ins("sync", lambda e: e.dma_start(st_da, da_d[:, :]),
                      writes=[k_stda], dma=True, tag="dma_da")
                P.ins("sync", lambda e: e.dma_start(st_ld, ld_d[:, :]),
                      writes=[k_stld], dma=True, tag="dma_ld")
                P.ins("sync", lambda e: e.dma_start(st_lu, lu_d[:, :]),
                      writes=[k_stlu], dma=True, tag="dma_lu")
                P.ins("gpsimd", lambda e: e.tensor_copy(v_dab, st_da),
                      reads=[k_stda], writes=[k_dab], tag="cast_da")
                if debug:
                    P.ins("gpsimd", lambda e: e.dma_start(
                        dbg_d["lil"][:, 0:D], xnf[0][:, :]),
                        reads=[R("xnf0")], dma=True, tag="dbglil0_early")
                P.ins("tensor", lambda e: e.transpose(
                    ctp_ps[0][0:100, 0:4], st_ld[:, 0:100], ident_f[0:4, 0:4]),
                    reads=[k_stld, R("ident_f")], writes=[k_ctp[0]],
                    tag="tp_ldT")
                P.ins("scalar", lambda e: e.copy(v_ldT, ctp_ps[0][0:100, 0:4]),
                      reads=[k_ctp[0]], writes=[k_ldT], tag="ev_ldT")
                P.ins("tensor", lambda e: e.transpose(
                    lil_ps[0][0:4, 0:50], st_lu[:, 0:4], ident_f[0:50, 0:50]),
                    reads=[k_stlu, R("ident_f")], writes=[k_lilp[0]],
                    tag="tp_luT")
                P.ins("scalar", lambda e: e.copy(v_luT, lil_ps[0][0:4, 0:50]),
                      reads=[k_lilp[0]], writes=[k_luT], tag="ev_luT")
                for t in range(8):
                    sl = stage[2][:, (t % 2) * 64:(t % 2) * 64 + 50]
                    ksl = R("stage2", (t % 2) * 64, (t % 2) * 64 + 50)
                    P.ins("sync", (lambda sl=sl, t=t: lambda e: e.dma_start(
                        sl, ua_d[t * 128:(t + 1) * 128, :]))(), writes=[ksl], dma=True,
                        tag=f"dma_ua{t}")
                    pp = lil_ps[1] if t % 2 else lil_ps[0]
                    kpp = k_lilp[1] if t % 2 else k_lilp[0]
                    P.ins("tensor", (lambda pp=pp, sl=sl: lambda e: e.transpose(
                        pp[0:50, 0:128], sl, ident_f[:, :]))(),
                        reads=[ksl, R("ident_f")], writes=[kpp], tag=f"tp_ua{t}")
                    dv = v_uaT[:, t * 128:(t + 1) * 128]
                    P.ins("scalar", (lambda dv=dv, pp=pp: lambda e: e.copy(
                        dv, pp[0:50, 0:128]))(), reads=[kpp],
                        writes=[R("xnf1", t * 128, (t + 1) * 128)],
                        tag=f"ev_ua{t}")
                for j in range(2):
                    pp = lil_ps[j]
                    P.ins("tensor", (lambda pp=pp, j=j: lambda e: e.matmul(
                        pp[0:4, 0:512], v_ldT, v_dab[:, j * 512:(j + 1) * 512],
                        start=True, stop=True))(),
                        reads=[k_ldT, k_dab],
                        writes=[k_lilp[j]], tag=f"mm_t1_{j}")
                    P.ins("scalar", (lambda pp=pp, j=j: lambda e: e.copy(
                        v_t1[:, j * 512:(j + 1) * 512], pp[0:4, 0:512]))(),
                        reads=[k_lilp[j]],
                        writes=[R("arena", j * 512, (j + 1) * 512)],
                        tag=f"ev_t1_{j}")
                for j in range(2):
                    pp = lil_ps[j]
                    P.ins("tensor", (lambda pp=pp, j=j: lambda e: e.matmul(
                        pp[0:50, 0:512], v_luT, v_t1[:, j * 512:(j + 1) * 512],
                        start=True, stop=True))(),
                        reads=[k_luT, k_t1],
                        writes=[k_lilp[j]], tag=f"mm_t2_{j}")
                    P.ins("scalar", (lambda pp=pp, j=j: lambda e: e.copy(
                        v_t2[:, j * 512:(j + 1) * 512], pp[0:50, 0:512]))(),
                        reads=[k_lilp[j]],
                        writes=[R("arena", D + j * 512, D + (j + 1) * 512)],
                        tag=f"ev_t2_{j}")
                for f in range(8):
                    for j in range(2):
                        pp = lil_ps[j]
                        P.ins("tensor", (lambda pp=pp, f=f, j=j: lambda e: e.matmul(
                            pp[:, 0:512], v_t2[:, f * 128:(f + 1) * 128],
                            v_uaT[:, j * 512:(j + 1) * 512], start=True, stop=True))(),
                            reads=[k_t2, k_uaT],
                            writes=[k_lilp[j]], tag=f"mm_lt{f}_{j}")
                        wv = weffT[:, f * D + j * 512:f * D + (j + 1) * 512]
                        kw = R("weffT", f * D + j * 512, f * D + (j + 1) * 512)
                        P.ins("vector", (lambda wv=wv, pp=pp: lambda e: e.tensor_tensor(
                            wv, pp[:, 0:512], wv, ALU.add))(),
                            reads=[k_lilp[j], kw], writes=[kw], tag=f"add_lt{f}_{j}")

                if debug:
                    P.ins("gpsimd", lambda e: e.dma_start(
                        dbg_d["lil"][:, D:2 * D], xnf[1][:, :]),
                        reads=[R("xnf1")], dma=True, tag="dbglil1")
                    P.ins("gpsimd", lambda e: e.dma_start(
                        dbg_d["lil"][:, 2 * D:4 * D], arena[:, 0:2 * D]),
                        reads=[R("arena", 0, 2 * D)], dma=True, tag="dbglil2")
                ps_lil.close()
                ps0.close()


            psB2 = ExitStack()
            Tps = [psB2.enter_context(
                nc.psum_tensor(f"T{b}_{i}", [128, 2 * D], F32)) for i in range(2)]
            k_T = [psum_key(t) for t in Tps]
            for f in range(8):
                pp, kpp = Tps[f % 2], k_T[f % 2]
                for g in range(8):
                    lhs = x_sb(g)[:, f * 128:(f + 1) * 128]
                    klhs = k_xsb(g)
                    for q in range(4):
                        off = 2 * D + (q % 2) * 512 if q < 2 else (q % 2) * 512
                        P.ins("tensor", (lambda pp=pp, lhs=lhs, g=g, q=q,
                                         off=off: lambda e: e.matmul(
                            pp[:, q * 512:(q + 1) * 512], lhs,
                            wqkvT[:, g * 3 * D + off:g * 3 * D + off + 512],
                            start=(g == 0), stop=(g == 7)))(),
                            reads=[klhs, R("wqkvT", g * 3 * D, (g + 1) * 3 * D)],
                            writes=[kpp], tag=f"mmT{b}_{f}_{g}_{q}")
                if f % 2 == 0:
                    P.ins("scalar", (lambda pp=pp, f=f: lambda e: e.copy(
                        t_cat(f), pp[:, :]))(), reads=[kpp], writes=[k_tcat(f)],
                        tag=f"evT{b}_{f}")
                else:
                    P.ins("vector", (lambda pp=pp, f=f: lambda e: e.tensor_copy(
                        t_cat(f), pp[:, :]))(), reads=[kpp], writes=[k_tcat(f)],
                        tag=f"evT{b}_{f}")
            psB2.close()
            psB1 = ExitStack()
            sums_ps = psB1.enter_context(
                nc.psum_tensor(f"sums{b}", [1, 3 * D], F32))
            k_sums = psum_key(sums_ps)
            ctpB1 = psB1.enter_context(
                nc.psum_tensor(f"ctpBB{b}", [128, 4], F32))
            k_ctpB1 = psum_key(ctpB1)
            for c in range(8):
                P.ins("tensor", (lambda c=c: lambda e: e.transpose(
                    ctpB1[:, 0:1], RS(0)[0:1, c * 128:(c + 1) * 128],
                    ident_f[0:1, 0:1]))(),
                    reads=[kRS(0), R("ident_f")], writes=[k_ctpB1],
                    tag=f"tp_xs{b}_{c}")
                P.ins("scalar", (lambda c=c: lambda e: e.copy(
                    CF(24 + c), ctpB1[:, 0:1]))(), reads=[k_ctpB1],
                    writes=[kCF(24 + c)], tag=f"ev_xs{b}_{c}")
            P.ins("vector", lambda e: e.tensor_copy(CB(0, 8), CF(24, 8)),
                  reads=[kCF(24, 8)], writes=[kCB(0, 8)], tag=f"xnsum_bf{b}")
            for c in range(8):
                for j in range(6):
                    P.ins("tensor", (lambda c=c, j=j: lambda e: e.matmul(
                        sums_ps[0:1, j * 512:(j + 1) * 512], CB(0 + c),
                        wqkvT[:, c * 3 * D + j * 512:c * 3 * D + (j + 1) * 512],
                        start=(c == 0), stop=(c == 7)))(),
                        reads=[kCB(c), R("wqkvT", c * 3 * D, (c + 1) * 3 * D)],
                        writes=[k_sums], tag=f"mm_sums{b}_{c}_{j}")
            P.ins("scalar", lambda e: e.copy(RS(1), sums_ps[0:1, 0:D]),
                  reads=[k_sums], writes=[kRS(1)], tag=f"ev_qsum0{b}")
            P.ins("vector", lambda e: e.tensor_copy(RB(3), sums_ps[0:1, D:2 * D]),
                  reads=[k_sums], writes=[kRB(3)], tag=f"ksum_bf{b}")
            P.ins("vector", lambda e: e.scalar_tensor_tensor(
                RB(4), RS(3), 1.0, sums_ps[0:1, 2 * D:3 * D], ALU.mult, ALU.add),
                reads=[kRS(3), k_sums], writes=[kRB(4)], tag=f"vsumS{b}")
            P.ins("vector", lambda e: e.tensor_copy(RB(5), sums_ps[0:1, 0:D]),
                  reads=[k_sums], writes=[kRB(5)], tag=f"qsum0_bf{b}")
            P.ins("vector", lambda e: e.scalar_tensor_tensor(
                RB(6), RS(2), float(SEQ), sums_ps[0:1, 0:D], ALU.mult, ALU.add),
                reads=[kRS(2), k_sums], writes=[kRB(6)], tag=f"qsum0S{b}")
            psB1.close()

            psC1 = ExitStack()
            Mps = [psC1.enter_context(
                nc.psum_tensor(f"M{b}_{i}", [128, 128], F32)) for i in range(2)]
            Gps = [psC1.enter_context(
                nc.psum_tensor(f"G{b}_{i}", [128, 128], F32)) for i in range(3)]
            Hps = psC1.enter_context(nc.psum_tensor(f"H{b}", [128, 128], F32))
            GHps = psC1.enter_context(nc.psum_tensor(f"GH{b}", [128, 128], F32))
            k_M = [psum_key(t) for t in Mps]
            k_G = [psum_key(t) for t in Gps]
            k_H = psum_key(Hps)
            k_GH = psum_key(GHps)
            for c in range(8):
                mp, kmp = Mps[c % 2], k_M[c % 2]
                gp, kgp = Gps[c % 2], k_G[c % 2]
                for f in range(8):
                    P.ins("tensor", (lambda mp=mp, f=f, c=c: lambda e: e.matmul(
                        mp[:, :], t_cat(f)[:, c * 128:(c + 1) * 128],
                        wqkvT[:, f * 3 * D + D + c * 128:
                              f * 3 * D + D + (c + 1) * 128],
                        start=(f == 0), stop=False))(),
                        reads=[k_tcat(f), R("wqkvT", f * 3 * D, (f + 1) * 3 * D)],
                        writes=[kmp], tag=f"mmM{b}_{c}_{f}")
                P.ins("tensor", (lambda mp=mp, c=c: lambda e: e.matmul(
                    mp[:, :], RB(4, c * 128, 128), RB(1, c * 128, 128),
                    start=False, stop=False))(),
                    reads=[kRB(4), kRB(1)], writes=[kmp], tag=f"mmMr1{b}_{c}")
                P.ins("tensor", (lambda mp=mp, c=c: lambda e: e.matmul(
                    mp[:, :], RB(2, c * 128, 128), RB(3, c * 128, 128),
                    start=False, stop=True))(),
                    reads=[kRB(2), kRB(3)], writes=[kmp], tag=f"mmMr2{b}_{c}")
                for f in range(8):
                    P.ins("tensor", (lambda gp=gp, f=f, c=c: lambda e: e.matmul(
                        gp[:, :], t_cat(f)[:, D + c * 128:D + (c + 1) * 128],
                        wqkvT[:, f * 3 * D + c * 128:f * 3 * D + (c + 1) * 128],
                        start=(f == 0), stop=False))(),
                        reads=[k_tcat(f), R("wqkvT", f * 3 * D, (f + 1) * 3 * D)],
                        writes=[kgp], tag=f"mmG{b}_{c}_{f}")
                P.ins("tensor", (lambda gp=gp, c=c: lambda e: e.matmul(
                    gp[:, :], RB(5, c * 128, 128), RB(0, c * 128, 128),
                    start=False, stop=False))(),
                    reads=[kRB(5), kRB(0)], writes=[kgp], tag=f"mmGr1{b}_{c}")
                P.ins("tensor", (lambda gp=gp, c=c: lambda e: e.matmul(
                    gp[:, :], RB(0, c * 128, 128), RB(6, c * 128, 128),
                    start=False, stop=True))(),
                    reads=[kRB(0), kRB(6)], writes=[kgp], tag=f"mmGr2{b}_{c}")
                for jj in range(4):
                    eng_ = "scalar" if jj % 2 == 0 else "vector"
                    fn_ = (lambda mp=mp, c=c, jj=jj, eng_=eng_: lambda e:
                           (e.copy if eng_ == "scalar" else e.tensor_copy)(
                        mblkT[32 * jj:32 * (jj + 1),
                              c * 128 + 32 * jj:c * 128 + 32 * (jj + 1)],
                        mp[32 * jj:32 * (jj + 1), 32 * jj:32 * (jj + 1)]))()
                    P.ins(eng_, fn_, reads=[kmp],
                          writes=[R("mblkT", c * 128, (c + 1) * 128)],
                          tag=f"exM{b}_{c}_{jj}")
                P.ins("tensor", (lambda c=c: lambda e: e.matmul(
                    Hps[:, :], mblkT[:, c * 128:(c + 1) * 128],
                    mblkT[:, c * 128:(c + 1) * 128], start=True, stop=True))(),
                    reads=[R("mblkT", c * 128, (c + 1) * 128)], writes=[k_H],
                    tag=f"mmH{b}_{c}")
                P.ins("scalar", lambda e: e.copy(h_sb, Hps[:, :]),
                      reads=[k_H], writes=[R("sqscr", 256, 384)], tag=f"evH{b}_{c}")
                P.ins("vector", (lambda gp=gp: lambda e: e.tensor_tensor(
                    GHps[:, :], gp[:, :], h_sb, ALU.mult))(),
                    reads=[kgp, R("sqscr", 256, 384)], writes=[k_GH],
                    tag=f"gh{b}_{c}")
                P.ins("vector", (lambda c=c: lambda e: e.reduce_sum(
                    CF(40 + c), GHps[:, :], axis=AX.X))(),
                    reads=[k_GH], writes=[kCF(40 + c)],
                    tag=f"ghcol{b}_{c}")

            for c in range(8):
                P.ins("vector", (lambda c=c: lambda e: e.transpose(
                    mblk[:, c * 128:(c + 1) * 128],
                    mblkT[:, c * 128:(c + 1) * 128]))(),
                    reads=[R("mblkT", c * 128, (c + 1) * 128)],
                    writes=[R("mblk", c * 128, (c + 1) * 128)], tag=f"vtM{b}_{c}")

            psQC = ExitStack()
            ctpB = [psQC.enter_context(
                nc.psum_tensor(f"ctpB{b}_{i}", [128, 4], F32)) for i in range(1)]
            k_ctpB = [psum_key(t) for t in ctpB]
            for c in range(8):
                pp, kpp = ctpB[0], k_ctpB[0]
                P.ins("tensor", (lambda pp=pp, c=c: lambda e: e.transpose(
                    pp[:, 0:1], RS(1)[0:1, c * 128:(c + 1) * 128],
                    ident_f[0:1, 0:1]))(),
                    reads=[kRS(1), R("ident_f")], writes=[kpp],
                    tag=f"tp_qs{b}_{c}")
                P.ins("scalar", (lambda pp=pp, c=c: lambda e: e.copy(
                    CF(32 + c), pp[:, 0:1]))(), reads=[kpp],
                    writes=[kCF(32 + c)], tag=f"ev_qs{b}_{c}")
            P.ins("vector", lambda e: e.scalar_tensor_tensor(
                CB(8, 8), CF(16, 8), float(SEQ), CF(32, 8), ALU.mult, ALU.add),
                reads=[kCF(16, 8), kCF(32, 8)], writes=[kCB(8, 8)],
                tag=f"qsumcol_bf{b}")
            psQC.close()
            if debug:
                P.ins("gpsimd", (lambda b=b: lambda e: e.dma_start(
                    dbg_d["MT"][b], mblkT[:, :]))(),
                    reads=[R("mblkT")], dma=True, tag=f"dbgMT{b}")
            psC1.close()

            psC1t = ExitStack()
            csum_ps = psC1t.enter_context(
                nc.psum_tensor(f"csum{b}", [1, D], F32))
            u_ps = psC1t.enter_context(nc.psum_tensor(f"u{b}", [1, D], F32))
            stat_ps = psC1t.enter_context(
                nc.psum_tensor(f"stat{b}", [128, 8], F32))
            ctpC = [psC1t.enter_context(
                nc.psum_tensor(f"ctpC{b}_{i}", [128, 4], F32)) for i in range(2)]
            k_cs = psum_key(csum_ps)
            k_u = psum_key(u_ps)
            k_st = psum_key(stat_ps)
            k_ctpC = [psum_key(t) for t in ctpC]
            for c in range(8):
                P.ins("tensor", (lambda c=c: lambda e: e.matmul(
                    csum_ps[0:1, c * 128:(c + 1) * 128], CB(8 + c),
                    mblk[:, c * 128:(c + 1) * 128],
                    start=(c % 4 == 0), stop=(c % 4 == 3)))(),
                    reads=[kCB(8 + c), R("mblk", c * 128, (c + 1) * 128)],
                    writes=[k_cs], tag=f"mmcs{b}_{c}")
            for c in range(8):
                P.ins("tensor", (lambda c=c: lambda e: e.matmul(
                    u_ps[0:1, c * 128:(c + 1) * 128], CB(16 + c),
                    mblk[:, c * 128:(c + 1) * 128],
                    start=(c % 4 == 0), stop=(c % 4 == 3)))(),
                    reads=[kCB(16 + c), R("mblk", c * 128, (c + 1) * 128)],
                    writes=[k_u], tag=f"mmu{b}_{c}")
            P.ins("scalar", lambda e: e.copy(RS(1), csum_ps[0:1, :]),
                  reads=[k_cs], writes=[kRS(1)], tag=f"ev_cs{b}")
            P.ins("scalar", lambda e: e.copy(RS(5), u_ps[0:1, :]),
                  reads=[k_u], writes=[kRS(5)], tag=f"ev_u{b}")
            for c in range(8):
                pp, kpp = ctpC[c % 2], k_ctpC[c % 2]
                P.ins("tensor", (lambda pp=pp, c=c: lambda e: e.transpose(
                    pp[:, 0:1], RS(1)[0:1, c * 128:(c + 1) * 128],
                    ident_f[0:1, 0:1]))(),
                    reads=[kRS(1), R("ident_f")], writes=[kpp],
                    tag=f"tp_cs{b}_{c}")
                P.ins("scalar", (lambda pp=pp, c=c: lambda e: e.copy(
                    CF(56 + c), pp[:, 0:1]))(), reads=[kpp],
                    writes=[kCF(56 + c)], tag=f"ev_csc{b}_{c}")
            for c in range(8):
                pp, kpp = ctpC[c % 2], k_ctpC[c % 2]
                P.ins("tensor", (lambda pp=pp, c=c: lambda e: e.transpose(
                    pp[:, 0:1], RS(5)[0:1, c * 128:(c + 1) * 128],
                    ident_f[0:1, 0:1]))(),
                    reads=[kRS(5), R("ident_f")], writes=[kpp],
                    tag=f"tp_u{b}_{c}")
                P.ins("scalar", (lambda pp=pp, c=c: lambda e: e.copy(
                    CF(48 + c), pp[:, 0:1]))(), reads=[kpp],
                    writes=[kCF(48 + c)], tag=f"ev_uc{b}_{c}")
            P.ins("tensor", lambda e: e.matmul(
                stat_ps[0:4, 0:8], sel[:, :], CF(40, 8), start=True, stop=True),
                reads=[R("sel"), kCF(40, 8)], writes=[k_st], tag=f"mmssq{b}")
            P.ins("scalar", lambda e: e.copy(st44[0:4, 0:8], stat_ps[0:4, 0:8]),
                  reads=[k_st], writes=[R("st44", 0, 8)], tag=f"ev_ssq44{b}")
            P.ins("tensor", lambda e: e.matmul(
                stat_ps[0:4, 0:8], sel[:, :], CF(56, 8), start=True, stop=True),
                reads=[R("sel"), kCF(56, 8)], writes=[k_st], tag=f"mmcs44{b}")
            P.ins("scalar", lambda e: e.copy(st44[0:4, 8:16], stat_ps[0:4, 0:8]),
                  reads=[k_st], writes=[R("st44", 8, 16)], tag=f"ev_cs44{b}")
            P.ins("vector", lambda e: e.tensor_scalar(
                st44[0:4, 16:24], st44[0:4, 8:16], 1.0 / (32 * SEQ), None,
                ALU.mult), reads=[R("st44", 8, 16)], writes=[R("st44", 16, 24)],
                tag=f"mean44{b}")
            P.ins("vector", lambda e: e.tensor_tensor(
                st44[0:4, 24:32], st44[0:4, 16:24], st44[0:4, 16:24], ALU.mult),
                reads=[R("st44", 16, 24)], writes=[R("st44", 24, 32)],
                tag=f"msq44{b}")
            P.ins("vector", lambda e: e.scalar_tensor_tensor(
                st44[0:4, 32:40], st44[0:4, 0:8], 1.0 / (32 * SEQ),
                st44[0:4, 24:32], ALU.mult, ALU.subtract),
                reads=[R("st44", 0, 8), R("st44", 24, 32)],
                writes=[R("st44", 32, 40)], tag=f"var44{b}")
            P.ins("scalar", lambda e: e.activation(
                st44[0:4, 40:48], st44[0:4, 32:40], AF.Sqrt,
                bias=epscol[0:4, :]),
                reads=[R("st44", 32, 40), R("epscol")],
                writes=[R("st44", 40, 48)], tag=f"sd44{b}")
            P.ins("vector", lambda e: e.reciprocal(
                st44[0:4, 48:56], st44[0:4, 40:48]),
                reads=[R("st44", 40, 48)], writes=[R("st44", 48, 56)],
                tag=f"rstd44{b}")
            P.ins("tensor", lambda e: e.matmul(
                stat_ps[:, 0:8], selT[:, :], st44[0:4, 48:56],
                start=True, stop=True),
                reads=[R("selT"), R("st44", 48, 56)], writes=[k_st],
                tag=f"bc_rstd{b}")
            P.ins("vector", lambda e: e.tensor_tensor(
                CF(32, 8), stat_ps[:, 0:8], CF(0, 8), ALU.mult),
                reads=[k_st, kCF(0, 8)], writes=[kCF(32, 8)], tag=f"scols{b}")
            P.ins("tensor", lambda e: e.matmul(
                stat_ps[:, 0:8], selT[:, :], st44[0:4, 16:24],
                start=True, stop=True),
                reads=[R("selT"), R("st44", 16, 24)], writes=[k_st],
                tag=f"bc_mean{b}")
            P.ins("vector", lambda e: e.tensor_tensor(
                sqscr[:, 128:136], stat_ps[:, 0:8], CF(32, 8), ALU.mult),
                reads=[k_st, kCF(32, 8)], writes=[R("sqscr", 128, 136)],
                tag=f"mrcols{b}")
            P.ins("vector", lambda e: e.tensor_tensor(
                CF(40, 8), CF(8, 8), sqscr[:, 128:136], ALU.subtract),
                reads=[kCF(8, 8), R("sqscr", 128, 136)], writes=[kCF(40, 8)],
                tag=f"ttcols{b}")
            P.ins("vector", lambda e: e.tensor_tensor(
                sqscr[:, 136:144], CF(48, 8), CF(32, 8), ALU.mult),
                reads=[kCF(48, 8), kCF(32, 8)], writes=[R("sqscr", 136, 144)],
                tag=f"su{b}")
            P.ins("vector", lambda e: e.tensor_tensor(
                CB(24, 8), sqscr[:, 136:144], CF(40, 8), ALU.add),
                reads=[R("sqscr", 136, 144), kCF(40, 8)], writes=[kCB(24, 8)],
                tag=f"wcols{b}")
            if debug:
                P.ins("gpsimd", (lambda b=b: lambda e: e.dma_start(
                    dbg_d["cols"][b], cols_f[:, :]))(),
                    reads=[R("cols_f")], dma=True, tag=f"dbgcols{b}")
            psC1t.close()

            psC2a = ExitStack()
            beta_ps = psC2a.enter_context(
                nc.psum_tensor(f"beta{b}", [1, D], F32))
            k_beta = psum_key(beta_ps)
            for c in range(8):
                for j in range(2):
                    P.ins("tensor", (lambda c=c, j=j: lambda e: e.matmul(
                        beta_ps[0:1, j * 512:(j + 1) * 512], CB(24 + c),
                        outwT[:, c * D + j * 512:c * D + (j + 1) * 512],
                        start=(c == 0), stop=(c == 7)))(),
                        reads=[kCB(24 + c), R("outwT", c * D, (c + 1) * D)],
                        writes=[k_beta], tag=f"mmbeta{b}_{c}_{j}")
            P.ins("scalar", lambda e: e.copy(RS(5), beta_ps[0:1, :]),
                  reads=[k_beta], writes=[kRS(5)], tag=f"ev_beta{b}")
            P.ins("vector", lambda e: e.tensor_tensor(
                RB(7), RS(5), RS(4), ALU.add),
                reads=[kRS(5), kRS(4)], writes=[kRB(7)], tag=f"bias_bf{b}")
            if debug:
                P.ins("gpsimd", (lambda b=b: lambda e: e.dma_start(
                    dbg_d["rows"][b], rows_f[0:1, :]))(),
                    reads=[R("rows_f")], dma=True, tag=f"dbgrows{b}")
            psC2a.close()

            if debug and b == 0:
                P.ins("gpsimd", lambda e: e.dma_start(
                    dbg_d["weffT"], weffT[:, :]),
                    reads=[R("weffT")], dma=True, tag="dbgweff")
                P.ins("gpsimd", lambda e: e.dma_start(
                    dbg_d["wqnat"], wq_nat[:, :]),
                    reads=[R("wq_nat")], dma=True, tag="dbgwqnat")
            psC2 = ExitStack()
            Pps = [psC2.enter_context(
                nc.psum_tensor(f"P{b}_{i}", [128, D], F32)) for i in range(2)]
            ATps = [psC2.enter_context(
                nc.psum_tensor(f"AT{b}_{i}", [128, D], F32)) for i in range(2)]
            k_P = [psum_key(t) for t in Pps]
            k_AT = [psum_key(t) for t in ATps]
            for c in range(8):
                ms, kms = mblkTs[c % 2], R(f"mblkTs{c % 2}")
                P.ins("scalar", (lambda ms=ms, c=c: lambda e: e.activation(
                    ms[:, :], mblkT[:, c * 128:(c + 1) * 128], AF.Identity,
                    scale=CF(32 + c)))(),
                    reads=[R("mblkT", c * 128, (c + 1) * 128), kCF(32 + c)],
                    writes=[kms], tag=f"mts{b}_{c}")
                pp, kpp = Pps[c % 2], k_P[c % 2]
                for j in range(2):
                    P.ins("tensor", (lambda pp=pp, ms=ms, c=c, j=j:
                                     lambda e: e.matmul(
                        pp[:, j * 512:(j + 1) * 512], ms[:, :],
                        outwT[:, c * D + j * 512:c * D + (j + 1) * 512],
                        start=True, stop=True))(),
                        reads=[kms, R("outwT", c * D, (c + 1) * D)],
                        writes=[kpp], tag=f"mmP{b}_{c}_{j}")
                P.ins("scalar", (lambda pp=pp, c=c: lambda e: e.copy(
                    p_sb(c), pp[:, :]))(), reads=[kpp], writes=[k_psb(c)],
                    tag=f"evP{b}_{c}")
            if debug:
                P.ins("gpsimd", (lambda b=b: lambda e: e.dma_start(
                    dbg_d["P"][b], arena2[:, :]))(),
                    reads=[R("arena2")], dma=True, tag=f"dbgP{b}")
            for f in range(8):
                pp, kpp = ATps[f % 2], k_AT[f % 2]
                for e_ in range(8):
                    for j in range(2):
                        P.ins("tensor", (lambda pp=pp, f=f, e_=e_, j=j:
                                         lambda e: e.matmul(
                            pp[:, j * 512:(j + 1) * 512],
                            wq_nat[:, e_ * D + f * 128:e_ * D + (f + 1) * 128],
                            p_sb(e_)[:, j * 512:(j + 1) * 512],
                            start=(e_ == 0), stop=(e_ == 7)))(),
                            reads=[R("wq_nat", e_ * D, (e_ + 1) * D),
                                   k_psb(e_)], writes=[kpp],
                            tag=f"mmAT{b}_{f}_{e_}_{j}")
                kct = k_ct(f)
                P.ins("vector", (lambda pp=pp, f=f: lambda e: e.tensor_tensor(
                    ct_sb(f), pp[:, :], weffT[:, f * D:(f + 1) * D], ALU.add))(),
                    reads=[kpp, R("weffT", f * D, (f + 1) * D)], writes=[kct],
                    tag=f"ct{b}_{f}")
            if debug:
                P.ins("gpsimd", (lambda b=b: lambda e: e.dma_start(
                    dbg_d["CT"][b], arena[:, 8 * D:16 * D]))(),
                    reads=[R("arena", 8 * D, 16 * D)], dma=True, tag=f"dbgCT{b}")
            psC2.close()

            psF = ExitStack()
            Ops = [psF.enter_context(
                nc.psum_tensor(f"O{b}_{i}", [128, D], F32)) for i in range(3)]
            Fps = [psF.enter_context(
                nc.psum_tensor(f"ftp{b}_{i}", [128, 128], BF16)) for i in range(2)]
            k_O = [psum_key(t) for t in Ops]
            k_F = [psum_key(t) for t in Fps]
            def fxv(i):
                p, h = i // 2, i % 2
                lo = (p % 2) * 4 * D + h * 2 * D
                return (arena2[:, lo:lo + 2 * D].bitcast(F32),
                        R("arena2", lo, lo + 2 * D))

            def f_head(i):
                if i % 2 == 0:
                    p = i // 2
                    lo = (p % 2) * 4 * D
                    dst = arena2[:, lo:lo + 4 * D].bitcast(F32)
                    kd = R("arena2", lo, lo + 4 * D)
                    P.ins("sync", (lambda dst=dst, b=b, i=i: lambda e:
                                   e.dma_start(
                        dst.rearrange("p (a f) -> p a f", a=2),
                        xs[b, i * 128:(i + 2) * 128, :].rearrange(
                            "(a p) f -> p a f", p=128)))(),
                        writes=[kd], dma=True, tag=f"dma_xF{b}_{i}")
                sl, ksl = fxv(i)
                xv, kxv = xnf[i % 2], R(f"xnf{i % 2}")
                P.ins("scalar", (lambda sl=sl, xv=xv, i=i: lambda e: e.activation(
                    xv[:, :], sl, AF.Identity,
                    bias=lncols[:, 16 + i:17 + i], scale=lncols[:, i:i + 1]))(),
                    reads=[ksl, R("lncols", i, i + 1), R("lncols", 16 + i, 17 + i)],
                    writes=[kxv], tag=f"xnF{b}_{i}")
                for c in range(8):
                    fp, kfp = Fps[c % 2], k_F[c % 2]
                    P.ins("tensor", (lambda fp=fp, xv=xv, c=c: lambda e:
                                     e.transpose(fp[:, :],
                                                 xv[:, c * 128:(c + 1) * 128],
                                                 ident_bf[:, :]))(),
                        reads=[kxv, R("ident_bf")], writes=[kfp],
                        tag=f"ftp{b}_{i}_{c}")
                    if c % 2 == 0:
                        P.ins("scalar", (lambda fp=fp, xv=xv, c=c: lambda e:
                                         e.copy(xv[:, c * 128:(c + 1) * 128],
                                                fp[:, :]))(),
                            reads=[kfp], writes=[kxv], tag=f"fev{b}_{i}_{c}")
                    else:
                        P.ins("vector", (lambda fp=fp, xv=xv, c=c: lambda e:
                                         e.tensor_copy(
                                             xv[:, c * 128:(c + 1) * 128],
                                             fp[:, :]))(),
                            reads=[kfp], writes=[kxv], tag=f"fev{b}_{i}_{c}")

            def f_tail(i):
                xt, kxt = xnf[i % 2], R(f"xnf{i % 2}")
                op, kop = Ops[i % 3], k_O[i % 3]
                for f in range(8):
                    for j in range(2):
                        P.ins("tensor", (lambda op=op, xt=xt, f=f, j=j:
                                         lambda e: e.matmul(
                            op[:, j * 512:(j + 1) * 512],
                            xt[:, f * 128:(f + 1) * 128],
                            ct_sb(f)[:, j * 512:(j + 1) * 512],
                            start=(f == 0), stop=False))(),
                            reads=[kxt, k_ct(f)], writes=[kop],
                            tag=f"mmO{b}_{i}_{f}_{j}")
                for j in range(2):
                    P.ins("tensor", (lambda op=op, j=j: lambda e: e.matmul(
                        op[:, j * 512:(j + 1) * 512], ones_row,
                        RB(7, j * 512, 512), start=False, stop=True))(),
                        reads=[k_ones_row, kRB(7)], writes=[kop],
                        tag=f"mmOb{b}_{i}_{j}")

            def f_fin(i):
                sl, ksl = fxv(i)
                op, kop = Ops[i % 3], k_O[i % 3]
                P.ins("vector", (lambda sl=sl, op=op: lambda e: e.tensor_tensor(
                    sl, sl, op[:, :], ALU.add))(),
                    reads=[ksl, kop], writes=[ksl], tag=f"add{b}_{i}")
                if i % 2 == 1:
                    p = i // 2
                    lo = (p % 2) * 4 * D
                    srcv = arena2[:, lo:lo + 4 * D].bitcast(F32)
                    ks2 = R("arena2", lo, lo + 4 * D)
                    P.ins("gpsimd", (lambda srcv=srcv, b=b, i=i: lambda e:
                                     e.dma_start(
                        ys[b, (i - 1) * 128:(i + 1) * 128, :].rearrange(
                            "(a p) f -> p a f", p=128),
                        srcv.rearrange("p (a f) -> p a f", a=2)))(),
                        reads=[ks2], dma=True, tag=f"out{b}_{i}")

            for i in range(NT):
                if i >= 2:
                    f_tail(i - 2)
                    f_fin(i - 2)
                f_head(i)
            f_tail(NT - 2)
            f_fin(NT - 2)
            f_tail(NT - 1)
            f_fin(NT - 1)
            psF.close()

        import os as _os
        if _os.environ.get("KERNEL_DEBUG_SBUF"):
            print("sbuf bytes remaining (inside build):", nc.sbuf_bytes_remaining)
        if _os.environ.get("KERNEL_DEBUG_DEPS"):
            want = _os.environ["KERNEL_DEBUG_DEPS"].split(",")
            for i, insd in enumerate(P.instrs):
                if insd["tag"] in want:
                    print(f"instr {i} tag={insd['tag']} eng={insd['engine']} "
                          f"dma={insd['dma']}")
                    for d in sorted(insd["deps"]):
                        dd = P.instrs[d]
                        print(f"   dep {d}: {dd['tag']} {dd['engine']} dma={dd['dma']}")
        sems = {e: E(nc.semaphore(f"s_{e}")) for e in ENGINES}
        dsems = {e: [E(nc.semaphore(f"sd_{e}_{j}"))
                     for j in range(Sched.NDMA_LANES[e])] for e in ENGINES}
        P.emit(nc, block, sems, dsems)

    return nc, dbg_d


_NC_CACHE = {}


def _get_nc(debug=False):
    key = bool(debug)
    if key not in _NC_CACHE:
        _NC_CACHE[key] = build_nc(debug)
    return _NC_CACHE[key]


def kernel(**inputs):
    x = np.ascontiguousarray(np.asarray(inputs["L0_attn_x"], np.float32))
    feed = {
        "wqkv": np.ascontiguousarray(np.asarray(inputs["L0_attn_Wqkv_w"], np.float32)),
        "bqkv": np.ascontiguousarray(np.asarray(inputs["L0_attn_Wqkv_b"], np.float32)),
        "gnw": np.ascontiguousarray(np.asarray(inputs["L0_attn_gn_w"], np.float32)),
        "gnb": np.ascontiguousarray(np.asarray(inputs["L0_attn_gn_b"], np.float32)),
        "outw": np.ascontiguousarray(np.asarray(inputs["L0_attn_out_w"], np.float32)),
        "outb": np.ascontiguousarray(np.asarray(inputs["L0_attn_out_b"], np.float32)),
        "wff": np.ascontiguousarray(np.asarray(inputs["L0_ffn_W"], np.float32)),
        "bff": np.ascontiguousarray(np.asarray(inputs["L0_ffn_b"], np.float32)),
        "ld": np.ascontiguousarray(np.asarray(inputs["L0_ffn_lora_down"], np.float32)),
        "lu": np.ascontiguousarray(np.asarray(inputs["L0_ffn_lora_up"], np.float32)),
        "da": np.ascontiguousarray(np.asarray(inputs["L0_ffn_down_aux"], np.float32)),
        "ua": np.ascontiguousarray(np.asarray(inputs["L0_ffn_up_aux"], np.float32)),
    }
    nc, _ = _get_nc(False)
    in_maps = []
    for core in range(NCORES):
        m = dict(feed)
        m["xs"] = np.ascontiguousarray(x[core * NB:(core + 1) * NB])
        in_maps.append(m)
    res = run_bass_kernel_spmd(nc, in_maps, core_ids=list(range(NCORES)))
    return np.concatenate([r["ys"] for r in res.results], axis=0)


if __name__ == "__main__":
    nc, _ = build_nc(False)
    counts = {}
    print("built ok")


# version 26
# speedup vs baseline: 1.1377x; 1.0006x over previous
import numpy as np
from contextlib import ExitStack

import concourse.bass as bass
import concourse.mybir as mybir
from concourse.bass_utils import run_bass_kernel_spmd

F32 = mybir.dt.float32
BF16 = mybir.dt.bfloat16
AF = mybir.ActivationFunctionType
ALU = mybir.AluOpType
AX = mybir.AxisListType

D = 1024
SEQ = 2048
NB = 2
NT = 16
EPS = 1e-5
NCORES = 8
ENGINES = ("sync", "gpsimd", "scalar", "vector", "tensor")


class Sched:

    def __init__(self):
        self.instrs = []
        self.acc = {}
        self._dma_count = {e: 0 for e in ENGINES}

    @staticmethod
    def r(name, lo=0, hi=1 << 60):
        return (name, lo, hi)

    def _overlap(self, key, want_writes):
        name, lo, hi = key
        out = []
        for (l, h, iid, w) in self.acc.get(name, ()):
            if w == want_writes and l < hi and lo < h:
                out.append(iid)
        return out

    def ins(self, engine, fn, reads=(), writes=(), dma=False, tag=""):
        assert engine in ENGINES
        deps = set()
        for k in reads:
            deps.update(self._overlap(k, True))
        for k in writes:
            deps.update(self._overlap(k, True))
            deps.update(self._overlap(k, False))
        iid = len(self.instrs)
        lane = self._dma_count[engine] % self.NDMA_LANES[engine] if dma else None

        def keep(d):
            dd = self.instrs[d]
            de = dd["engine"]
            if de != engine:
                return True
            if dd["dma"] and dma:
                return False
            if engine == "tensor":
                return False
            return True
        deps = {d for d in deps if keep(d)}
        best = {}
        for d in deps:
            dd = self.instrs[d]
            ch = (dd["engine"], dd["lane"]) if dd["dma"] else (dd["engine"],)
            if best.get(ch, -1) < d:
                best[ch] = d
        deps = set(best.values())
        self.instrs.append(dict(engine=engine, fn=fn, deps=deps, dma=dma,
                                lane=lane, tag=tag))
        if dma:
            self._dma_count[engine] += 1
        for k in reads:
            self.acc.setdefault(k[0], []).append((k[1], k[2], iid, False))
        for k in writes:
            self.acc.setdefault(k[0], []).append((k[1], k[2], iid, True))
        return iid

    NDMA_LANES = {"sync": 16, "gpsimd": 12, "scalar": 8, "vector": 4,
                  "tensor": 2}

    def emit(self, nc, block, sems, dsems, final_waits=("gpsimd",)):
        needed = set()
        for ins in self.instrs:
            needed.update(ins["deps"])
        cum = {}
        chan = {}
        val = {}
        for i, ins in enumerate(self.instrs):
            if ins["dma"]:
                ch = (ins["engine"], ins["lane"])
                cum[ch] = cum.get(ch, 0) + 16
                chan[i] = ch
                val[i] = cum[ch]
                needed.add(i)
            else:
                ch = (ins["engine"],)
                if i in needed:
                    cum[ch] = cum.get(ch, 0) + 1
                chan[i] = ch
                val[i] = cum.get(ch, 0)
        cum2 = {}
        for i, ins in enumerate(self.instrs):
            if not ins["dma"]:
                ch = chan[i]
                if i in needed:
                    cum2[ch] = cum2.get(ch, 0) + 1
                val[i] = cum2.get(ch, 0)
        order = {e: [i for i, ins in enumerate(self.instrs) if ins["engine"] == e]
                 for e in ENGINES}

        def hnd(ch):
            return dsems[ch[0]][ch[1]] if len(ch) == 2 else sems[ch[0]]

        totals = dict(cum)
        totals.update(cum2)

        def section(e):
            def body(eng):
                waited = {}
                for iid in order[e]:
                    ins = self.instrs[iid]
                    for d in sorted(ins["deps"], key=lambda d: val[d]):
                        dch = chan[d]
                        v = val[d]
                        assert v > 0, (d, iid)
                        if waited.get(dch, 0) < v:
                            eng.wait_ge(hnd(dch), v)
                            waited[dch] = v
                    bi = ins["fn"](eng)
                    if iid in needed:
                        ch = chan[iid]
                        bi.then_inc(hnd(ch), 16 if ins["dma"] else 1)
                if e in final_waits:
                    for ch, tot in totals.items():
                        if len(ch) == 2 and ch[0] == e and tot:
                            eng.wait_ge(hnd(ch), tot)
            return body

        block.sync(section("sync"))
        block.gpsimd(section("gpsimd"))
        block.scalar(section("scalar"))
        block.vector(section("vector"))
        block.tensor(section("tensor"))
        return totals


def build_nc(debug=False):
    nc = bass.Bass("TRN2", target_bir_lowering=False)
    P = Sched()
    R = Sched.r

    xs = nc.dram_tensor("xs", [NB, SEQ, D], F32, kind="ExternalInput").ap()
    wqkv_d = nc.dram_tensor("wqkv", [3 * D, D], F32, kind="ExternalInput").ap()
    bqkv_d = nc.dram_tensor("bqkv", [3 * D], F32, kind="ExternalInput").ap()
    gnw_d = nc.dram_tensor("gnw", [D], F32, kind="ExternalInput").ap()
    gnb_d = nc.dram_tensor("gnb", [D], F32, kind="ExternalInput").ap()
    outw_d = nc.dram_tensor("outw", [D, D], F32, kind="ExternalInput").ap()
    outb_d = nc.dram_tensor("outb", [D], F32, kind="ExternalInput").ap()
    wff_d = nc.dram_tensor("wff", [D, D], F32, kind="ExternalInput").ap()
    bff_d = nc.dram_tensor("bff", [D], F32, kind="ExternalInput").ap()
    ld_d = nc.dram_tensor("ld", [4, 100], F32, kind="ExternalInput").ap()
    lu_d = nc.dram_tensor("lu", [50, 4], F32, kind="ExternalInput").ap()
    da_d = nc.dram_tensor("da", [100, D], F32, kind="ExternalInput").ap()
    ua_d = nc.dram_tensor("ua", [D, 50], F32, kind="ExternalInput").ap()
    ys = nc.dram_tensor("ys", [NB, SEQ, D], F32, kind="ExternalOutput").ap()
    dbg_d = {}
    if debug:
        dbg_d["xn"] = nc.dram_tensor("dbg_xn", [NB, 128, 16 * D], BF16,
                                     kind="ExternalOutput").ap()
        dbg_d["X"] = nc.dram_tensor("dbg_X", [NB, 128, 8 * D], BF16,
                                    kind="ExternalOutput").ap()
        dbg_d["MT"] = nc.dram_tensor("dbg_MT", [NB, 128, D], BF16,
                                     kind="ExternalOutput").ap()
        dbg_d["CT"] = nc.dram_tensor("dbg_CT", [NB, 128, 8 * D], BF16,
                                     kind="ExternalOutput").ap()
        dbg_d["rows"] = nc.dram_tensor("dbg_rows", [NB, 1, 5 * D], F32,
                                       kind="ExternalOutput").ap()
        dbg_d["P"] = nc.dram_tensor("dbg_P", [NB, 128, 8 * D], BF16,
                                    kind="ExternalOutput").ap()
        dbg_d["weffT"] = nc.dram_tensor("dbg_weffT", [128, 8 * D], BF16,
                                        kind="ExternalOutput").ap()
        dbg_d["wqnat"] = nc.dram_tensor("dbg_wqnat", [128, 8 * D], BF16,
                                        kind="ExternalOutput").ap()
        dbg_d["lil"] = nc.dram_tensor("dbg_lil", [128, 4 * D], BF16,
                                      kind="ExternalOutput").ap()
        dbg_d["cols"] = nc.dram_tensor("dbg_cols", [NB, 128, 64], F32,
                                       kind="ExternalOutput").ap()

    with ExitStack() as st, nc.Block() as block:
        E = st.enter_context

        arena = E(nc.sbuf_tensor("arena", [128, 16 * D], BF16))
        arena2 = E(nc.sbuf_tensor("arena2", [128, 8 * D], BF16))
        wqkvT = E(nc.sbuf_tensor("wqkvT", [128, 24 * D], BF16))
        wq_nat = E(nc.sbuf_tensor("wq_nat", [128, 8 * D], BF16))
        outwT = E(nc.sbuf_tensor("outwT", [128, 8 * D], BF16))
        weffT = E(nc.sbuf_tensor("weffT", [128, 8 * D], BF16))
        stage = [E(nc.sbuf_tensor(f"stage{i}", [128, D], F32)) for i in range(3)]
        sqscr = E(nc.sbuf_tensor("sqscr", [128, D], F32))
        xnf = [E(nc.sbuf_tensor(f"xnf{i}", [128, D], BF16)) for i in range(2)]
        mblkT = E(nc.sbuf_tensor("mblkT", [128, D], BF16))
        mblk = E(nc.sbuf_tensor("mblk", [128, D], BF16))
        mblkTs = [E(nc.sbuf_tensor(f"mblkTs{i}", [128, 128], BF16)) for i in range(2)]
        lil_small = E(nc.sbuf_tensor("lil_small", [128, 128], BF16))
        ident_bf = E(nc.sbuf_tensor("ident_bf", [128, 128], BF16))
        ident_f = E(nc.sbuf_tensor("ident_f", [128, 128], F32))
        sel = E(nc.sbuf_tensor("sel", [128, 4], F32))
        selT = E(nc.sbuf_tensor("selT", [4, 128], F32))
        ones_col = E(nc.sbuf_tensor("ones_col", [128, 1], BF16))
        epscol = E(nc.sbuf_tensor("epscol", [128, 1], F32))
        rows_f = E(nc.sbuf_tensor("rows_f", [1, 5 * D], F32))
        h_sb = sqscr[:, 256:384]
        rows_b = E(nc.sbuf_tensor("rows_b", [1, 8 * D + 128], BF16))
        cols_f = E(nc.sbuf_tensor("cols_f", [128, 64], F32))
        cols_b = E(nc.sbuf_tensor("cols_b", [128, 32], BF16))
        lncols = E(nc.sbuf_tensor("lncols", [128, 32], F32))
        lnst = E(nc.sbuf_tensor("lnst", [128, 96], F32))
        st44 = E(nc.sbuf_tensor("st44", [128, 64], F32))

        RS = lambda k, o=0, n=D: (sqscr[0:1, o:o + n] if k == 0
                                  else rows_f[0:1, (k - 1) * D + o:(k - 1) * D + o + n])
        kRS = lambda k, o=0, n=D: (R("sqscr", o, o + n) if k == 0
                                   else R("rows_f", (k - 1) * D + o, (k - 1) * D + o + n))
        RB = lambda k, o=0, n=D: rows_b[0:1, k * D + o:k * D + o + n]
        kRB = lambda k, o=0, n=D: R("rows_b", k * D + o, k * D + o + n)
        ones_row = rows_b[0:1, 8 * D:8 * D + 128]
        k_ones_row = R("rows_b", 8 * D, 8 * D + 128)
        CF = lambda k, n=1: cols_f[:, k:k + n]
        kCF = lambda k, n=1: R("cols_f", k, k + n)
        CB = lambda k, n=1: cols_b[:, k:k + n]
        kCB = lambda k, n=1: R("cols_b", k, k + n)

        xn_tm = lambda i: arena[:, i * D:(i + 1) * D]
        k_xn = lambda i: R("arena", i * D, (i + 1) * D)
        t_cat = lambda f: arena[:, f * 2 * D:(f + 1) * 2 * D]
        k_tcat = lambda f: R("arena", f * 2 * D, (f + 1) * 2 * D)
        ct_sb = lambda f: arena[:, 8 * D + f * D:8 * D + (f + 1) * D]
        k_ct = lambda f: R("arena", 8 * D + f * D, 8 * D + (f + 1) * D)
        x_sb = lambda g: arena2[:, g * D:(g + 1) * D]
        k_xsb = lambda g: R("arena2", g * D, (g + 1) * D)
        p_sb = lambda e: arena2[:, e * D:(e + 1) * D]
        k_psb = k_xsb
        v_dab = xnf[0][0:100, :]
        k_dab = R("xnf0")
        v_uaT = xnf[1][0:50, :]
        k_uaT = R("xnf1")
        v_t1 = arena[0:4, 0:D]
        k_t1 = R("arena", 0, D)
        v_t2 = arena[0:50, D:2 * D]
        k_t2 = R("arena", D, 2 * D)
        v_ldT = lil_small[0:100, 0:4]
        k_ldT = R("lil_small", 0, 4)
        v_luT = lil_small[0:4, 64:114]
        k_luT = R("lil_small", 64, 114)

        def psum_key(handle):
            for alloc in nc.m.functions[0].allocations:
                mls = getattr(alloc, "memorylocations", None)
                if mls:
                    for ml in mls:
                        if ml.name == handle.name:
                            return R("PSUM", ml.addr, ml.addr + int(ml.dims[-1]))
            raise KeyError(handle.name)

        P.ins("gpsimd", lambda g: g.memset(ident_bf[:, :], 0.0),
              writes=[R("ident_bf")], tag="c0")
        P.ins("gpsimd", lambda g: g.affine_select(
            out=ident_bf[:, :], in_=ident_bf[:, :], compare_op=ALU.not_equal,
            fill=1.0, base=0, pattern=[[-1, 128]], channel_multiplier=1),
            reads=[R("ident_bf")], writes=[R("ident_bf")], tag="c1")
        P.ins("gpsimd", lambda g: g.memset(ident_f[:, :], 0.0),
              writes=[R("ident_f")])
        P.ins("gpsimd", lambda g: g.affine_select(
            out=ident_f[:, :], in_=ident_f[:, :], compare_op=ALU.not_equal,
            fill=1.0, base=0, pattern=[[-1, 128]], channel_multiplier=1),
            reads=[R("ident_f")], writes=[R("ident_f")])
        P.ins("gpsimd", lambda g: g.memset(sel[:, :], 1.0), writes=[R("sel")])
        P.ins("gpsimd", lambda g: g.affine_select(
            out=sel[:, :], in_=sel[:, :], compare_op=ALU.is_ge, fill=0.0,
            base=0, pattern=[[-32, 4]], channel_multiplier=1),
            reads=[R("sel")], writes=[R("sel")])
        P.ins("gpsimd", lambda g: g.affine_select(
            out=sel[:, :], in_=sel[:, :], compare_op=ALU.is_ge, fill=0.0,
            base=31, pattern=[[32, 4]], channel_multiplier=-1),
            reads=[R("sel")], writes=[R("sel")])
        P.ins("gpsimd", lambda g: g.memset(selT[:, :], 1.0), writes=[R("selT")])
        P.ins("gpsimd", lambda g: g.affine_select(
            out=selT[:, :], in_=selT[:, :], compare_op=ALU.is_ge, fill=0.0,
            base=0, pattern=[[1, 128]], channel_multiplier=-32),
            reads=[R("selT")], writes=[R("selT")])
        P.ins("gpsimd", lambda g: g.affine_select(
            out=selT[:, :], in_=selT[:, :], compare_op=ALU.is_ge, fill=0.0,
            base=31, pattern=[[-1, 128]], channel_multiplier=32),
            reads=[R("selT")], writes=[R("selT")])
        P.ins("gpsimd", lambda g: g.memset(ones_col[:, :], 1.0),
              writes=[R("ones_col")])
        P.ins("gpsimd", lambda g: g.memset(epscol[:, :], EPS), writes=[R("epscol")])
        P.ins("gpsimd", lambda g: g.memset(ones_row, 1.0), writes=[k_ones_row])
        P.ins("gpsimd", lambda g: g.memset(mblkT[:, :], 0.0), writes=[R("mblkT")])
        P.ins("gpsimd", lambda g: g.memset(mblk[:, :], 0.0), writes=[R("mblk")])


        def make_A1(b):
            def ln_stats(i):
                n = b * NT + i
                sl = stage[n % 3]
                ksl = R(f"stage{n % 3}")
                P.ins("sync", (lambda sl=sl, b=b, i=i: lambda e: e.dma_start(
                    sl[:, :], xs[b, i * 128:(i + 1) * 128, :]))(),
                    writes=[ksl], dma=True, tag=f"dma_xA{b}_{i}")
                P.ins("vector", (lambda sl=sl, i=i: lambda e: e.reduce_sum(
                    lnst[:, i:i + 1], sl[:, :], axis=AX.X))(),
                    reads=[ksl], writes=[R("lnst", i, i + 1)], tag=f"ln_sum{i}")
                P.ins("scalar", (lambda sl=sl: lambda e: e.activation(
                    sqscr[:, :], sl[:, :], AF.Square))(),
                    reads=[ksl], writes=[R("sqscr")], tag=f"ln_sq{i}")
                P.ins("vector", (lambda i=i: lambda e: e.reduce_sum(
                    lnst[:, 16 + i:17 + i], sqscr[:, :], axis=AX.X))(),
                    reads=[R("sqscr")], writes=[R("lnst", 16 + i, 17 + i)],
                    tag=f"ln_ssq{i}")
                P.ins("vector", (lambda i=i: lambda e: e.tensor_scalar(
                    lnst[:, 32 + i:33 + i], lnst[:, i:i + 1], -1.0 / D, None,
                    ALU.mult))(),
                    reads=[R("lnst", i, i + 1)],
                    writes=[R("lnst", 32 + i, 33 + i)], tag=f"ln_negmean{i}")
                P.ins("vector", (lambda i=i: lambda e: e.tensor_tensor(
                    lnst[:, 48 + i:49 + i], lnst[:, 32 + i:33 + i],
                    lnst[:, 32 + i:33 + i], ALU.mult))(),
                    reads=[R("lnst", 32 + i, 33 + i)],
                    writes=[R("lnst", 48 + i, 49 + i)], tag=f"ln_msq{i}")
                P.ins("vector", (lambda i=i: lambda e: e.scalar_tensor_tensor(
                    lnst[:, 64 + i:65 + i], lnst[:, 16 + i:17 + i], 1.0 / D,
                    lnst[:, 48 + i:49 + i], ALU.mult, ALU.subtract))(),
                    reads=[R("lnst", 16 + i, 17 + i), R("lnst", 48 + i, 49 + i)],
                    writes=[R("lnst", 64 + i, 65 + i)], tag=f"ln_var{i}")
                P.ins("scalar", (lambda i=i: lambda e: e.activation(
                    lnst[:, 80 + i:81 + i], lnst[:, 64 + i:65 + i], AF.Ln,
                    bias=epscol[:, :]))(),
                    reads=[R("lnst", 64 + i, 65 + i), R("epscol")],
                    writes=[R("lnst", 80 + i, 81 + i)], tag=f"ln_lnv{i}")
                P.ins("scalar", (lambda i=i: lambda e: e.activation(
                    lncols[:, i:i + 1], lnst[:, 80 + i:81 + i], AF.Exp,
                    scale=-0.5))(),
                    reads=[R("lnst", 80 + i, 81 + i)],
                    writes=[R("lncols", i, i + 1)], tag=f"ln_rstd{i}")

            def ln_finish(i):
                n = b * NT + i
                sl = stage[n % 3]
                ksl = R(f"stage{n % 3}")
                P.ins("vector", (lambda i=i: lambda e: e.tensor_scalar(
                    lncols[:, 16 + i:17 + i], lnst[:, 32 + i:33 + i],
                    lncols[:, i:i + 1], None, ALU.mult))(),
                    reads=[R("lnst", 32 + i, 33 + i), R("lncols", i, i + 1)],
                    writes=[R("lncols", 16 + i, 17 + i)], tag=f"ln_negmr{i}")
                P.ins("vector", (lambda sl=sl, i=i, b=b: lambda e: e.tensor_scalar(
                    xn_tm(i), sl[:, :], lncols[:, i:i + 1],
                    lncols[:, 16 + i:17 + i], ALU.mult, ALU.add))(),
                    reads=[ksl, R("lncols", i, i + 1),
                           R("lncols", 16 + i, 17 + i)],
                    writes=[k_xn(i)], tag=f"xn{b}_{i}")

            def tile(k):
                if k >= 2:
                    ln_finish(k - 2)
                if k < NT:
                    ln_stats(k)
                if k == NT:
                    ln_finish(NT - 2)
                    ln_finish(NT - 1)
            return tile

        wslot = [arena[:, 0:8 * D].bitcast(F32),
                 arena[:, 8 * D:16 * D].bitcast(F32)]
        k_wslot = [R("arena", 0, 8 * D), R("arena", 8 * D, 16 * D)]

        def w_grp_dram(g, h):
            if g < 6:
                src = wqkv_d[g * 512:(g + 1) * 512, :]
            elif g < 8:
                src = outw_d[(g - 6) * 512:(g - 5) * 512, :]
            else:
                src = wff_d[(g - 8) * 512:(g - 7) * 512, :]
            return src[h * 256:(h + 1) * 256, :].rearrange(
                "(a p) f -> p a f", p=128)

        def w_tp_dest(t, c):
            if t < 24:
                lo = c * 3 * D + t * 128
                return wqkvT[:, lo:lo + 128], R("wqkvT", lo, lo + 128)
            if t < 32:
                lo = c * D + (t - 24) * 128
                return outwT[:, lo:lo + 128], R("outwT", lo, lo + 128)
            lo = c * D + (t - 32) * 128
            return weffT[:, lo:lo + 128], R("weffT", lo, lo + 128)

        def emit_wgroup(g):
            sl = wslot[g % 2]
            base = 0 if g % 2 == 0 else 8 * D
            ksl_h = [R("arena", base, base + 4 * D),
                     R("arena", base + 4 * D, base + 8 * D)]
            for h in range(2):
                P.ins("sync" if h == 0 else "gpsimd",
                      (lambda sl=sl, g=g, h=h: lambda e: e.dma_start(
                          sl[:, h * 2 * D:(h + 1) * 2 * D].rearrange(
                              "p (a f) -> p a f", a=2), w_grp_dram(g, h)))(),
                      writes=[ksl_h[h]], dma=True, tag=f"dma_wg{g}_{h}")
            for q in range(4):
                t = g * 4 + q
                if t < 8:
                    cb, kcb = wq_nat[:, t * D:(t + 1) * D], R("wq_nat", t * D,
                                                              (t + 1) * D)
                else:
                    cb, kcb = xnf[t % 2][:, :], R(f"xnf{t % 2}")
                P.ins("vector", (lambda sl=sl, q=q, cb=cb: lambda e:
                                 e.tensor_copy(cb, sl[:, q * D:(q + 1) * D]))(),
                    reads=[ksl_h[q // 2]], writes=[kcb], tag=f"cast_w{t}")
                for c in range(8):
                    j = t * 8 + c
                    ps = tp32_ps[j % 4]
                    kps = k_tp32[j % 4]
                    P.ins("tensor", (lambda ps=ps, cb=cb, c=c: lambda e:
                                     e.transpose(ps[:, 0:128],
                                                 cb[:, c * 128:(c + 1) * 128],
                                                 ident_bf[:, :]))(),
                        reads=[kcb, R("ident_bf")], writes=[kps],
                        tag=f"tp_w{t}_{c}")
                    dv, kdv = w_tp_dest(t, c)
                    if c % 3 == 0:
                        P.ins("scalar", (lambda dv=dv, ps=ps: lambda e: e.copy(
                            dv, ps[:, 0:128]))(), reads=[kps], writes=[kdv],
                            tag=f"ev_w{t}_{c}")
                    else:
                        P.ins("vector", (lambda dv=dv, ps=ps: lambda e:
                                         e.tensor_copy(dv, ps[:, 0:128]))(),
                            reads=[kps], writes=[kdv], tag=f"ev_w{t}_{c}")


        ps_rows = ExitStack()
        ctpR = ps_rows.enter_context(nc.psum_tensor("ctpR", [128, 4], F32))
        k_ctpR = psum_key(ctpR)
        def row_dma(dram_ap, tag):
            ap2 = dram_ap.rearrange("(a d) -> a d", a=1)
            P.ins("gpsimd", (lambda ap2=ap2: lambda e: e.dma_start(RS(0), ap2))(),
                  writes=[kRS(0)], dma=True, tag=tag)

        row_dma(bqkv_d[0:D], "dma_bq")
        P.ins("scalar", lambda e: e.activation(RB(0), RS(0), AF.Copy),
              reads=[kRS(0)], writes=[kRB(0)], tag="cast_bq")
        P.ins("scalar", lambda e: e.copy(RS(2), RS(0)),
              reads=[kRS(0)], writes=[kRS(2)], tag="keep_bq")
        for c in range(8):
            pp = ctpR
            kpp = k_ctpR
            P.ins("tensor", (lambda pp=pp, c=c: lambda e: e.transpose(
                pp[:, 0:1], RS(2)[0:1, c * 128:(c + 1) * 128],
                ident_f[0:1, 0:1]))(),
                reads=[kRS(2), R("ident_f")], writes=[kpp], tag=f"tp_bqc{c}")
            P.ins("scalar", (lambda pp=pp, c=c: lambda e: e.copy(
                CF(16 + c), pp[:, 0:1]))(), reads=[kpp],
                writes=[kCF(16 + c)], tag=f"ev_bqc{c}")
        row_dma(bqkv_d[D:2 * D], "dma_bk")
        P.ins("scalar", lambda e: e.activation(RB(1), RS(0), AF.Copy),
              reads=[kRS(0)], writes=[kRB(1)], tag="cast_bk")
        row_dma(bqkv_d[2 * D:3 * D], "dma_bv")
        P.ins("scalar", lambda e: e.activation(RB(2), RS(0), AF.Copy),
              reads=[kRS(0)], writes=[kRB(2)], tag="cast_bv")
        P.ins("vector", lambda e: e.tensor_scalar(
            RS(3), RS(0), float(SEQ), None, ALU.mult),
            reads=[kRS(0)], writes=[kRS(3)], tag="sbv")
        row_dma(gnw_d[:], "dma_gnw")
        for c in range(8):
            pp = ctpR
            kpp = k_ctpR
            P.ins("tensor", (lambda pp=pp, c=c: lambda e: e.transpose(
                pp[:, 0:1], RS(0)[0:1, c * 128:(c + 1) * 128],
                ident_f[0:1, 0:1]))(),
                reads=[kRS(0), R("ident_f")], writes=[kpp], tag=f"tp_gnw{c}")
            P.ins("scalar", (lambda pp=pp, c=c: lambda e: e.copy(
                CF(0 + c), pp[:, 0:1]))(), reads=[kpp], writes=[kCF(c)],
                tag=f"ev_gnw{c}")
        row_dma(gnb_d[:], "dma_gnb")
        for c in range(8):
            pp = ctpR
            kpp = k_ctpR
            P.ins("tensor", (lambda pp=pp, c=c: lambda e: e.transpose(
                pp[:, 0:1], RS(0)[0:1, c * 128:(c + 1) * 128],
                ident_f[0:1, 0:1]))(),
                reads=[kRS(0), R("ident_f")], writes=[kpp], tag=f"tp_gnb{c}")
            P.ins("scalar", (lambda pp=pp, c=c: lambda e: e.copy(
                CF(8 + c), pp[:, 0:1]))(), reads=[kpp], writes=[kCF(8 + c)],
                tag=f"ev_gnb{c}")
        row_dma(outb_d[:], "dma_outb")
        P.ins("scalar", lambda e: e.copy(RS(4), RS(0)),
              reads=[kRS(0)], writes=[kRS(4)], tag="keep_outb")
        row_dma(bff_d[:], "dma_bff")
        P.ins("vector", lambda e: e.tensor_tensor(RS(4), RS(4), RS(0), ALU.add),
              reads=[kRS(4), kRS(0)], writes=[kRS(4)], tag="obf")
        P.ins("vector", lambda e: e.tensor_copy(CB(16, 8), CF(16, 8)),
              reads=[kCF(16, 8)], writes=[kCB(16, 8)], tag="bqcol_bf")

        ps_rows.close()

        for b in range(NB):
            psA = ExitStack()
            Xps = [psA.enter_context(
                nc.psum_tensor(f"X{b}_{i}", [128, D], F32)) for i in range(3)]
            Srow = psA.enter_context(nc.psum_tensor(f"xsr{b}", [1, D], F32))
            k_X = [psum_key(t) for t in Xps]
            k_Sr = psum_key(Srow)

            def x_chunk_mms(c, i):
                lhs = xn_tm(i)[:, c * 128:(c + 1) * 128]
                kx = k_xn(i)
                pp, kpp = Xps[c % 3], k_X[c % 3]
                st_ = (i == 0)
                sp_ = (i == NT - 1)
                for j in range(2):
                    P.ins("tensor", (lambda lhs=lhs, pp=pp, i=i, j=j, st_=st_,
                                     sp_=sp_: lambda e: e.matmul(
                        pp[:, j * 512:(j + 1) * 512], lhs,
                        xn_tm(i)[:, j * 512:(j + 1) * 512],
                        start=st_, stop=sp_))(),
                        reads=[kx], writes=[kpp], tag=f"mmX{b}_{c}_{i}_{j}")

            def xnsum_row_mms(i):
                kx = k_xn(i)
                st_ = (i == 0)
                sp_ = (i == NT - 1)
                for j in range(2):
                    P.ins("tensor", (lambda i=i, j=j, st_=st_, sp_=sp_,
                                     Srow=Srow: lambda e: e.matmul(
                        Srow[0:1, j * 512:(j + 1) * 512], ones_col[:, :],
                        xn_tm(i)[:, j * 512:(j + 1) * 512],
                        start=st_, stop=sp_))(),
                        reads=[kx, R("ones_col")], writes=[k_Sr],
                        tag=f"mmSr{b}_{i}")

            def x_evac(c):
                pp, kpp = Xps[c % 3], k_X[c % 3]
                P.ins("scalar", (lambda pp=pp, c=c: lambda e: e.copy(
                    x_sb(c), pp[:, :]))(), reads=[kpp], writes=[k_xsb(c)],
                    tag=f"evX{b}_{c}")

            a1t = make_A1(b)
            for k in range(NT + 2):
                if k <= NT:
                    a1t(k)
                i = k - 2
                if 0 <= i < NT:
                    for c in (0, 1, 2):
                        x_chunk_mms(c, i)
                    xnsum_row_mms(i)
            for c in (0, 1, 2):
                x_evac(c)
            P.ins("scalar", (lambda Srow=Srow: lambda e: e.copy(
                RS(0), Srow[0:1, :]))(),
                reads=[k_Sr], writes=[kRS(0)], tag=f"evSr{b}")
            for grp in ((3, 4, 5), (6, 7)):
                for i in range(NT):
                    for c in grp:
                        x_chunk_mms(c, i)
                for c in grp:
                    x_evac(c)

            if debug:
                P.ins("gpsimd", (lambda b=b: lambda e: e.dma_start(
                    dbg_d["X"][b], arena2[:, :]))(),
                    reads=[R("arena2")], dma=True, tag=f"dbgX{b}")
                P.ins("gpsimd", (lambda b=b: lambda e: e.dma_start(
                    dbg_d["xn"][b], arena[:, :]))(),
                    reads=[R("arena", 0, 16 * D)], dma=True, tag=f"dbgxn{b}")
            psA.close()

            if b == 0:
                ps0 = ExitStack()
                tp32_ps = [ps0.enter_context(
                    nc.psum_tensor(f"tp32_{i}", [128, 512], BF16))
                    for i in range(4)]
                k_tp32 = [psum_key(t) for t in tp32_ps]
                for g in range(10):
                    emit_wgroup(g)
                ps_lil = ExitStack()
                ctp_ps = [ps_lil.enter_context(
                    nc.psum_tensor("ctpL", [128, 4], F32))]
                k_ctp = [psum_key(ctp_ps[0])]
                lil_ps = [ps_lil.enter_context(
                    nc.psum_tensor(f"lilp{i}", [128, 512], F32))
                    for i in range(2)]
                k_lilp = [psum_key(t) for t in lil_ps]
                st_da = stage[0][0:100, :]
                k_stda = R("stage0")
                st_ld = stage[1][0:4, 0:100]
                k_stld = R("stage1", 0, 100)
                st_lu = stage[1][0:50, 128:132]
                k_stlu = R("stage1", 128, 132)
                P.ins("sync", lambda e: e.dma_start(st_da, da_d[:, :]),
                      writes=[k_stda], dma=True, tag="dma_da")
                P.ins("sync", lambda e: e.dma_start(st_ld, ld_d[:, :]),
                      writes=[k_stld], dma=True, tag="dma_ld")
                P.ins("sync", lambda e: e.dma_start(st_lu, lu_d[:, :]),
                      writes=[k_stlu], dma=True, tag="dma_lu")
                P.ins("gpsimd", lambda e: e.tensor_copy(v_dab, st_da),
                      reads=[k_stda], writes=[k_dab], tag="cast_da")
                if debug:
                    P.ins("gpsimd", lambda e: e.dma_start(
                        dbg_d["lil"][:, 0:D], xnf[0][:, :]),
                        reads=[R("xnf0")], dma=True, tag="dbglil0_early")
                P.ins("tensor", lambda e: e.transpose(
                    ctp_ps[0][0:100, 0:4], st_ld[:, 0:100], ident_f[0:4, 0:4]),
                    reads=[k_stld, R("ident_f")], writes=[k_ctp[0]],
                    tag="tp_ldT")
                P.ins("scalar", lambda e: e.copy(v_ldT, ctp_ps[0][0:100, 0:4]),
                      reads=[k_ctp[0]], writes=[k_ldT], tag="ev_ldT")
                P.ins("tensor", lambda e: e.transpose(
                    lil_ps[0][0:4, 0:50], st_lu[:, 0:4], ident_f[0:50, 0:50]),
                    reads=[k_stlu, R("ident_f")], writes=[k_lilp[0]],
                    tag="tp_luT")
                P.ins("scalar", lambda e: e.copy(v_luT, lil_ps[0][0:4, 0:50]),
                      reads=[k_lilp[0]], writes=[k_luT], tag="ev_luT")
                for t in range(8):
                    sl = stage[2][:, (t % 2) * 64:(t % 2) * 64 + 50]
                    ksl = R("stage2", (t % 2) * 64, (t % 2) * 64 + 50)
                    P.ins("sync", (lambda sl=sl, t=t: lambda e: e.dma_start(
                        sl, ua_d[t * 128:(t + 1) * 128, :]))(), writes=[ksl], dma=True,
                        tag=f"dma_ua{t}")
                    pp = lil_ps[1] if t % 2 else lil_ps[0]
                    kpp = k_lilp[1] if t % 2 else k_lilp[0]
                    P.ins("tensor", (lambda pp=pp, sl=sl: lambda e: e.transpose(
                        pp[0:50, 0:128], sl, ident_f[:, :]))(),
                        reads=[ksl, R("ident_f")], writes=[kpp], tag=f"tp_ua{t}")
                    dv = v_uaT[:, t * 128:(t + 1) * 128]
                    P.ins("scalar", (lambda dv=dv, pp=pp: lambda e: e.copy(
                        dv, pp[0:50, 0:128]))(), reads=[kpp],
                        writes=[R("xnf1", t * 128, (t + 1) * 128)],
                        tag=f"ev_ua{t}")
                for j in range(2):
                    pp = lil_ps[j]
                    P.ins("tensor", (lambda pp=pp, j=j: lambda e: e.matmul(
                        pp[0:4, 0:512], v_ldT, v_dab[:, j * 512:(j + 1) * 512],
                        start=True, stop=True))(),
                        reads=[k_ldT, k_dab],
                        writes=[k_lilp[j]], tag=f"mm_t1_{j}")
                    P.ins("scalar", (lambda pp=pp, j=j: lambda e: e.copy(
                        v_t1[:, j * 512:(j + 1) * 512], pp[0:4, 0:512]))(),
                        reads=[k_lilp[j]],
                        writes=[R("arena", j * 512, (j + 1) * 512)],
                        tag=f"ev_t1_{j}")
                for j in range(2):
                    pp = lil_ps[j]
                    P.ins("tensor", (lambda pp=pp, j=j: lambda e: e.matmul(
                        pp[0:50, 0:512], v_luT, v_t1[:, j * 512:(j + 1) * 512],
                        start=True, stop=True))(),
                        reads=[k_luT, k_t1],
                        writes=[k_lilp[j]], tag=f"mm_t2_{j}")
                    P.ins("scalar", (lambda pp=pp, j=j: lambda e: e.copy(
                        v_t2[:, j * 512:(j + 1) * 512], pp[0:50, 0:512]))(),
                        reads=[k_lilp[j]],
                        writes=[R("arena", D + j * 512, D + (j + 1) * 512)],
                        tag=f"ev_t2_{j}")
                for f in range(8):
                    for j in range(2):
                        pp = lil_ps[j]
                        P.ins("tensor", (lambda pp=pp, f=f, j=j: lambda e: e.matmul(
                            pp[:, 0:512], v_t2[:, f * 128:(f + 1) * 128],
                            v_uaT[:, j * 512:(j + 1) * 512], start=True, stop=True))(),
                            reads=[k_t2, k_uaT],
                            writes=[k_lilp[j]], tag=f"mm_lt{f}_{j}")
                        wv = weffT[:, f * D + j * 512:f * D + (j + 1) * 512]
                        kw = R("weffT", f * D + j * 512, f * D + (j + 1) * 512)
                        P.ins("vector", (lambda wv=wv, pp=pp: lambda e: e.tensor_tensor(
                            wv, pp[:, 0:512], wv, ALU.add))(),
                            reads=[k_lilp[j], kw], writes=[kw], tag=f"add_lt{f}_{j}")

                if debug:
                    P.ins("gpsimd", lambda e: e.dma_start(
                        dbg_d["lil"][:, D:2 * D], xnf[1][:, :]),
                        reads=[R("xnf1")], dma=True, tag="dbglil1")
                    P.ins("gpsimd", lambda e: e.dma_start(
                        dbg_d["lil"][:, 2 * D:4 * D], arena[:, 0:2 * D]),
                        reads=[R("arena", 0, 2 * D)], dma=True, tag="dbglil2")
                ps_lil.close()
                ps0.close()


            psB2 = ExitStack()
            Tps = [psB2.enter_context(
                nc.psum_tensor(f"T{b}_{i}", [128, 2 * D], F32)) for i in range(2)]
            k_T = [psum_key(t) for t in Tps]
            for f in range(8):
                pp, kpp = Tps[f % 2], k_T[f % 2]
                for g in range(8):
                    lhs = x_sb(g)[:, f * 128:(f + 1) * 128]
                    klhs = k_xsb(g)
                    for q in range(4):
                        off = 2 * D + (q % 2) * 512 if q < 2 else (q % 2) * 512
                        P.ins("tensor", (lambda pp=pp, lhs=lhs, g=g, q=q,
                                         off=off: lambda e: e.matmul(
                            pp[:, q * 512:(q + 1) * 512], lhs,
                            wqkvT[:, g * 3 * D + off:g * 3 * D + off + 512],
                            start=(g == 0), stop=(g == 7)))(),
                            reads=[klhs, R("wqkvT", g * 3 * D, (g + 1) * 3 * D)],
                            writes=[kpp], tag=f"mmT{b}_{f}_{g}_{q}")
                if f % 2 == 0:
                    P.ins("scalar", (lambda pp=pp, f=f: lambda e: e.copy(
                        t_cat(f), pp[:, :]))(), reads=[kpp], writes=[k_tcat(f)],
                        tag=f"evT{b}_{f}")
                else:
                    P.ins("vector", (lambda pp=pp, f=f: lambda e: e.tensor_copy(
                        t_cat(f), pp[:, :]))(), reads=[kpp], writes=[k_tcat(f)],
                        tag=f"evT{b}_{f}")
            psB2.close()
            psB1 = ExitStack()
            sums_ps = psB1.enter_context(
                nc.psum_tensor(f"sums{b}", [1, 3 * D], F32))
            k_sums = psum_key(sums_ps)
            ctpB1 = psB1.enter_context(
                nc.psum_tensor(f"ctpBB{b}", [128, 4], F32))
            k_ctpB1 = psum_key(ctpB1)
            for c in range(8):
                P.ins("tensor", (lambda c=c: lambda e: e.transpose(
                    ctpB1[:, 0:1], RS(0)[0:1, c * 128:(c + 1) * 128],
                    ident_f[0:1, 0:1]))(),
                    reads=[kRS(0), R("ident_f")], writes=[k_ctpB1],
                    tag=f"tp_xs{b}_{c}")
                P.ins("scalar", (lambda c=c: lambda e: e.copy(
                    CF(24 + c), ctpB1[:, 0:1]))(), reads=[k_ctpB1],
                    writes=[kCF(24 + c)], tag=f"ev_xs{b}_{c}")
            P.ins("vector", lambda e: e.tensor_copy(CB(0, 8), CF(24, 8)),
                  reads=[kCF(24, 8)], writes=[kCB(0, 8)], tag=f"xnsum_bf{b}")
            for c in range(8):
                for j in range(6):
                    P.ins("tensor", (lambda c=c, j=j: lambda e: e.matmul(
                        sums_ps[0:1, j * 512:(j + 1) * 512], CB(0 + c),
                        wqkvT[:, c * 3 * D + j * 512:c * 3 * D + (j + 1) * 512],
                        start=(c == 0), stop=(c == 7)))(),
                        reads=[kCB(c), R("wqkvT", c * 3 * D, (c + 1) * 3 * D)],
                        writes=[k_sums], tag=f"mm_sums{b}_{c}_{j}")
            P.ins("scalar", lambda e: e.copy(RS(1), sums_ps[0:1, 0:D]),
                  reads=[k_sums], writes=[kRS(1)], tag=f"ev_qsum0{b}")
            P.ins("vector", lambda e: e.tensor_copy(RB(3), sums_ps[0:1, D:2 * D]),
                  reads=[k_sums], writes=[kRB(3)], tag=f"ksum_bf{b}")
            P.ins("vector", lambda e: e.scalar_tensor_tensor(
                RB(4), RS(3), 1.0, sums_ps[0:1, 2 * D:3 * D], ALU.mult, ALU.add),
                reads=[kRS(3), k_sums], writes=[kRB(4)], tag=f"vsumS{b}")
            P.ins("vector", lambda e: e.tensor_copy(RB(5), sums_ps[0:1, 0:D]),
                  reads=[k_sums], writes=[kRB(5)], tag=f"qsum0_bf{b}")
            P.ins("vector", lambda e: e.scalar_tensor_tensor(
                RB(6), RS(2), float(SEQ), sums_ps[0:1, 0:D], ALU.mult, ALU.add),
                reads=[kRS(2), k_sums], writes=[kRB(6)], tag=f"qsum0S{b}")
            psB1.close()

            psC1 = ExitStack()
            Mps = [psC1.enter_context(
                nc.psum_tensor(f"M{b}_{i}", [128, 128], F32)) for i in range(2)]
            Gps = [psC1.enter_context(
                nc.psum_tensor(f"G{b}_{i}", [128, 128], F32)) for i in range(3)]
            Hps = psC1.enter_context(nc.psum_tensor(f"H{b}", [128, 128], F32))
            GHps = psC1.enter_context(nc.psum_tensor(f"GH{b}", [128, 128], F32))
            k_M = [psum_key(t) for t in Mps]
            k_G = [psum_key(t) for t in Gps]
            k_H = psum_key(Hps)
            k_GH = psum_key(GHps)
            for c in range(8):
                mp, kmp = Mps[c % 2], k_M[c % 2]
                gp, kgp = Gps[c % 2], k_G[c % 2]
                for f in range(8):
                    P.ins("tensor", (lambda mp=mp, f=f, c=c: lambda e: e.matmul(
                        mp[:, :], t_cat(f)[:, c * 128:(c + 1) * 128],
                        wqkvT[:, f * 3 * D + D + c * 128:
                              f * 3 * D + D + (c + 1) * 128],
                        start=(f == 0), stop=False))(),
                        reads=[k_tcat(f), R("wqkvT", f * 3 * D, (f + 1) * 3 * D)],
                        writes=[kmp], tag=f"mmM{b}_{c}_{f}")
                P.ins("tensor", (lambda mp=mp, c=c: lambda e: e.matmul(
                    mp[:, :], RB(4, c * 128, 128), RB(1, c * 128, 128),
                    start=False, stop=False))(),
                    reads=[kRB(4), kRB(1)], writes=[kmp], tag=f"mmMr1{b}_{c}")
                P.ins("tensor", (lambda mp=mp, c=c: lambda e: e.matmul(
                    mp[:, :], RB(2, c * 128, 128), RB(3, c * 128, 128),
                    start=False, stop=True))(),
                    reads=[kRB(2), kRB(3)], writes=[kmp], tag=f"mmMr2{b}_{c}")
                for f in range(8):
                    P.ins("tensor", (lambda gp=gp, f=f, c=c: lambda e: e.matmul(
                        gp[:, :], t_cat(f)[:, D + c * 128:D + (c + 1) * 128],
                        wqkvT[:, f * 3 * D + c * 128:f * 3 * D + (c + 1) * 128],
                        start=(f == 0), stop=False))(),
                        reads=[k_tcat(f), R("wqkvT", f * 3 * D, (f + 1) * 3 * D)],
                        writes=[kgp], tag=f"mmG{b}_{c}_{f}")
                P.ins("tensor", (lambda gp=gp, c=c: lambda e: e.matmul(
                    gp[:, :], RB(5, c * 128, 128), RB(0, c * 128, 128),
                    start=False, stop=False))(),
                    reads=[kRB(5), kRB(0)], writes=[kgp], tag=f"mmGr1{b}_{c}")
                P.ins("tensor", (lambda gp=gp, c=c: lambda e: e.matmul(
                    gp[:, :], RB(0, c * 128, 128), RB(6, c * 128, 128),
                    start=False, stop=True))(),
                    reads=[kRB(0), kRB(6)], writes=[kgp], tag=f"mmGr2{b}_{c}")
                for jj in range(4):
                    eng_ = "scalar" if jj % 2 == 0 else "vector"
                    fn_ = (lambda mp=mp, c=c, jj=jj, eng_=eng_: lambda e:
                           (e.copy if eng_ == "scalar" else e.tensor_copy)(
                        mblkT[32 * jj:32 * (jj + 1),
                              c * 128 + 32 * jj:c * 128 + 32 * (jj + 1)],
                        mp[32 * jj:32 * (jj + 1), 32 * jj:32 * (jj + 1)]))()
                    P.ins(eng_, fn_, reads=[kmp],
                          writes=[R("mblkT", c * 128, (c + 1) * 128)],
                          tag=f"exM{b}_{c}_{jj}")
                P.ins("tensor", (lambda c=c: lambda e: e.matmul(
                    Hps[:, :], mblkT[:, c * 128:(c + 1) * 128],
                    mblkT[:, c * 128:(c + 1) * 128], start=True, stop=True))(),
                    reads=[R("mblkT", c * 128, (c + 1) * 128)], writes=[k_H],
                    tag=f"mmH{b}_{c}")
                P.ins("scalar", lambda e: e.copy(h_sb, Hps[:, :]),
                      reads=[k_H], writes=[R("sqscr", 256, 384)], tag=f"evH{b}_{c}")
                P.ins("vector", (lambda gp=gp: lambda e: e.tensor_tensor(
                    GHps[:, :], gp[:, :], h_sb, ALU.mult))(),
                    reads=[kgp, R("sqscr", 256, 384)], writes=[k_GH],
                    tag=f"gh{b}_{c}")
                P.ins("vector", (lambda c=c: lambda e: e.reduce_sum(
                    CF(40 + c), GHps[:, :], axis=AX.X))(),
                    reads=[k_GH], writes=[kCF(40 + c)],
                    tag=f"ghcol{b}_{c}")

            for c in range(8):
                P.ins("vector", (lambda c=c: lambda e: e.transpose(
                    mblk[:, c * 128:(c + 1) * 128],
                    mblkT[:, c * 128:(c + 1) * 128]))(),
                    reads=[R("mblkT", c * 128, (c + 1) * 128)],
                    writes=[R("mblk", c * 128, (c + 1) * 128)], tag=f"vtM{b}_{c}")

            psQC = ExitStack()
            ctpB = [psQC.enter_context(
                nc.psum_tensor(f"ctpB{b}_{i}", [128, 4], F32)) for i in range(1)]
            k_ctpB = [psum_key(t) for t in ctpB]
            for c in range(8):
                pp, kpp = ctpB[0], k_ctpB[0]
                P.ins("tensor", (lambda pp=pp, c=c: lambda e: e.transpose(
                    pp[:, 0:1], RS(1)[0:1, c * 128:(c + 1) * 128],
                    ident_f[0:1, 0:1]))(),
                    reads=[kRS(1), R("ident_f")], writes=[kpp],
                    tag=f"tp_qs{b}_{c}")
                P.ins("scalar", (lambda pp=pp, c=c: lambda e: e.copy(
                    CF(32 + c), pp[:, 0:1]))(), reads=[kpp],
                    writes=[kCF(32 + c)], tag=f"ev_qs{b}_{c}")
            P.ins("vector", lambda e: e.scalar_tensor_tensor(
                CB(8, 8), CF(16, 8), float(SEQ), CF(32, 8), ALU.mult, ALU.add),
                reads=[kCF(16, 8), kCF(32, 8)], writes=[kCB(8, 8)],
                tag=f"qsumcol_bf{b}")
            psQC.close()
            if debug:
                P.ins("gpsimd", (lambda b=b: lambda e: e.dma_start(
                    dbg_d["MT"][b], mblkT[:, :]))(),
                    reads=[R("mblkT")], dma=True, tag=f"dbgMT{b}")
            psC1.close()

            psC1t = ExitStack()
            csum_ps = psC1t.enter_context(
                nc.psum_tensor(f"csum{b}", [1, D], F32))
            u_ps = psC1t.enter_context(nc.psum_tensor(f"u{b}", [1, D], F32))
            stat_ps = psC1t.enter_context(
                nc.psum_tensor(f"stat{b}", [128, 8], F32))
            ctpC = [psC1t.enter_context(
                nc.psum_tensor(f"ctpC{b}_{i}", [128, 4], F32)) for i in range(2)]
            k_cs = psum_key(csum_ps)
            k_u = psum_key(u_ps)
            k_st = psum_key(stat_ps)
            k_ctpC = [psum_key(t) for t in ctpC]
            for c in range(8):
                P.ins("tensor", (lambda c=c: lambda e: e.matmul(
                    csum_ps[0:1, c * 128:(c + 1) * 128], CB(8 + c),
                    mblk[:, c * 128:(c + 1) * 128],
                    start=(c % 4 == 0), stop=(c % 4 == 3)))(),
                    reads=[kCB(8 + c), R("mblk", c * 128, (c + 1) * 128)],
                    writes=[k_cs], tag=f"mmcs{b}_{c}")
            for c in range(8):
                P.ins("tensor", (lambda c=c: lambda e: e.matmul(
                    u_ps[0:1, c * 128:(c + 1) * 128], CB(16 + c),
                    mblk[:, c * 128:(c + 1) * 128],
                    start=(c % 4 == 0), stop=(c % 4 == 3)))(),
                    reads=[kCB(16 + c), R("mblk", c * 128, (c + 1) * 128)],
                    writes=[k_u], tag=f"mmu{b}_{c}")
            P.ins("scalar", lambda e: e.copy(RS(1), csum_ps[0:1, :]),
                  reads=[k_cs], writes=[kRS(1)], tag=f"ev_cs{b}")
            P.ins("scalar", lambda e: e.copy(RS(5), u_ps[0:1, :]),
                  reads=[k_u], writes=[kRS(5)], tag=f"ev_u{b}")
            for c in range(8):
                pp, kpp = ctpC[c % 2], k_ctpC[c % 2]
                P.ins("tensor", (lambda pp=pp, c=c: lambda e: e.transpose(
                    pp[:, 0:1], RS(1)[0:1, c * 128:(c + 1) * 128],
                    ident_f[0:1, 0:1]))(),
                    reads=[kRS(1), R("ident_f")], writes=[kpp],
                    tag=f"tp_cs{b}_{c}")
                P.ins("scalar", (lambda pp=pp, c=c: lambda e: e.copy(
                    CF(56 + c), pp[:, 0:1]))(), reads=[kpp],
                    writes=[kCF(56 + c)], tag=f"ev_csc{b}_{c}")
            for c in range(8):
                pp, kpp = ctpC[c % 2], k_ctpC[c % 2]
                P.ins("tensor", (lambda pp=pp, c=c: lambda e: e.transpose(
                    pp[:, 0:1], RS(5)[0:1, c * 128:(c + 1) * 128],
                    ident_f[0:1, 0:1]))(),
                    reads=[kRS(5), R("ident_f")], writes=[kpp],
                    tag=f"tp_u{b}_{c}")
                P.ins("scalar", (lambda pp=pp, c=c: lambda e: e.copy(
                    CF(48 + c), pp[:, 0:1]))(), reads=[kpp],
                    writes=[kCF(48 + c)], tag=f"ev_uc{b}_{c}")
            P.ins("tensor", lambda e: e.matmul(
                stat_ps[0:4, 0:8], sel[:, :], CF(40, 8), start=True, stop=True),
                reads=[R("sel"), kCF(40, 8)], writes=[k_st], tag=f"mmssq{b}")
            P.ins("scalar", lambda e: e.copy(st44[0:4, 0:8], stat_ps[0:4, 0:8]),
                  reads=[k_st], writes=[R("st44", 0, 8)], tag=f"ev_ssq44{b}")
            P.ins("tensor", lambda e: e.matmul(
                stat_ps[0:4, 0:8], sel[:, :], CF(56, 8), start=True, stop=True),
                reads=[R("sel"), kCF(56, 8)], writes=[k_st], tag=f"mmcs44{b}")
            P.ins("scalar", lambda e: e.copy(st44[0:4, 8:16], stat_ps[0:4, 0:8]),
                  reads=[k_st], writes=[R("st44", 8, 16)], tag=f"ev_cs44{b}")
            P.ins("vector", lambda e: e.tensor_scalar(
                st44[0:4, 16:24], st44[0:4, 8:16], 1.0 / (32 * SEQ), None,
                ALU.mult), reads=[R("st44", 8, 16)], writes=[R("st44", 16, 24)],
                tag=f"mean44{b}")
            P.ins("vector", lambda e: e.tensor_tensor(
                st44[0:4, 24:32], st44[0:4, 16:24], st44[0:4, 16:24], ALU.mult),
                reads=[R("st44", 16, 24)], writes=[R("st44", 24, 32)],
                tag=f"msq44{b}")
            P.ins("vector", lambda e: e.scalar_tensor_tensor(
                st44[0:4, 32:40], st44[0:4, 0:8], 1.0 / (32 * SEQ),
                st44[0:4, 24:32], ALU.mult, ALU.subtract),
                reads=[R("st44", 0, 8), R("st44", 24, 32)],
                writes=[R("st44", 32, 40)], tag=f"var44{b}")
            P.ins("scalar", lambda e: e.activation(
                st44[0:4, 40:48], st44[0:4, 32:40], AF.Sqrt,
                bias=epscol[0:4, :]),
                reads=[R("st44", 32, 40), R("epscol")],
                writes=[R("st44", 40, 48)], tag=f"sd44{b}")
            P.ins("vector", lambda e: e.reciprocal(
                st44[0:4, 48:56], st44[0:4, 40:48]),
                reads=[R("st44", 40, 48)], writes=[R("st44", 48, 56)],
                tag=f"rstd44{b}")
            P.ins("tensor", lambda e: e.matmul(
                stat_ps[:, 0:8], selT[:, :], st44[0:4, 48:56],
                start=True, stop=True),
                reads=[R("selT"), R("st44", 48, 56)], writes=[k_st],
                tag=f"bc_rstd{b}")
            P.ins("vector", lambda e: e.tensor_tensor(
                CF(32, 8), stat_ps[:, 0:8], CF(0, 8), ALU.mult),
                reads=[k_st, kCF(0, 8)], writes=[kCF(32, 8)], tag=f"scols{b}")
            P.ins("tensor", lambda e: e.matmul(
                stat_ps[:, 0:8], selT[:, :], st44[0:4, 16:24],
                start=True, stop=True),
                reads=[R("selT"), R("st44", 16, 24)], writes=[k_st],
                tag=f"bc_mean{b}")
            P.ins("vector", lambda e: e.tensor_tensor(
                sqscr[:, 128:136], stat_ps[:, 0:8], CF(32, 8), ALU.mult),
                reads=[k_st, kCF(32, 8)], writes=[R("sqscr", 128, 136)],
                tag=f"mrcols{b}")
            P.ins("vector", lambda e: e.tensor_tensor(
                CF(40, 8), CF(8, 8), sqscr[:, 128:136], ALU.subtract),
                reads=[kCF(8, 8), R("sqscr", 128, 136)], writes=[kCF(40, 8)],
                tag=f"ttcols{b}")
            P.ins("vector", lambda e: e.tensor_tensor(
                sqscr[:, 136:144], CF(48, 8), CF(32, 8), ALU.mult),
                reads=[kCF(48, 8), kCF(32, 8)], writes=[R("sqscr", 136, 144)],
                tag=f"su{b}")
            P.ins("vector", lambda e: e.tensor_tensor(
                CB(24, 8), sqscr[:, 136:144], CF(40, 8), ALU.add),
                reads=[R("sqscr", 136, 144), kCF(40, 8)], writes=[kCB(24, 8)],
                tag=f"wcols{b}")
            if debug:
                P.ins("gpsimd", (lambda b=b: lambda e: e.dma_start(
                    dbg_d["cols"][b], cols_f[:, :]))(),
                    reads=[R("cols_f")], dma=True, tag=f"dbgcols{b}")
            psC1t.close()

            psC2a = ExitStack()
            beta_ps = psC2a.enter_context(
                nc.psum_tensor(f"beta{b}", [1, D], F32))
            k_beta = psum_key(beta_ps)
            for c in range(8):
                for j in range(2):
                    P.ins("tensor", (lambda c=c, j=j: lambda e: e.matmul(
                        beta_ps[0:1, j * 512:(j + 1) * 512], CB(24 + c),
                        outwT[:, c * D + j * 512:c * D + (j + 1) * 512],
                        start=(c == 0), stop=(c == 7)))(),
                        reads=[kCB(24 + c), R("outwT", c * D, (c + 1) * D)],
                        writes=[k_beta], tag=f"mmbeta{b}_{c}_{j}")
            P.ins("scalar", lambda e: e.copy(RS(5), beta_ps[0:1, :]),
                  reads=[k_beta], writes=[kRS(5)], tag=f"ev_beta{b}")
            P.ins("vector", lambda e: e.tensor_tensor(
                RB(7), RS(5), RS(4), ALU.add),
                reads=[kRS(5), kRS(4)], writes=[kRB(7)], tag=f"bias_bf{b}")
            if debug:
                P.ins("gpsimd", (lambda b=b: lambda e: e.dma_start(
                    dbg_d["rows"][b], rows_f[0:1, :]))(),
                    reads=[R("rows_f")], dma=True, tag=f"dbgrows{b}")
            psC2a.close()

            if debug and b == 0:
                P.ins("gpsimd", lambda e: e.dma_start(
                    dbg_d["weffT"], weffT[:, :]),
                    reads=[R("weffT")], dma=True, tag="dbgweff")
                P.ins("gpsimd", lambda e: e.dma_start(
                    dbg_d["wqnat"], wq_nat[:, :]),
                    reads=[R("wq_nat")], dma=True, tag="dbgwqnat")
            psC2 = ExitStack()
            Pps = [psC2.enter_context(
                nc.psum_tensor(f"P{b}_{i}", [128, D], F32)) for i in range(2)]
            ATps = [psC2.enter_context(
                nc.psum_tensor(f"AT{b}_{i}", [128, D], F32)) for i in range(2)]
            k_P = [psum_key(t) for t in Pps]
            k_AT = [psum_key(t) for t in ATps]
            for c in range(8):
                ms, kms = mblkTs[c % 2], R(f"mblkTs{c % 2}")
                P.ins("scalar", (lambda ms=ms, c=c: lambda e: e.activation(
                    ms[:, :], mblkT[:, c * 128:(c + 1) * 128], AF.Identity,
                    scale=CF(32 + c)))(),
                    reads=[R("mblkT", c * 128, (c + 1) * 128), kCF(32 + c)],
                    writes=[kms], tag=f"mts{b}_{c}")
                pp, kpp = Pps[c % 2], k_P[c % 2]
                for j in range(2):
                    P.ins("tensor", (lambda pp=pp, ms=ms, c=c, j=j:
                                     lambda e: e.matmul(
                        pp[:, j * 512:(j + 1) * 512], ms[:, :],
                        outwT[:, c * D + j * 512:c * D + (j + 1) * 512],
                        start=True, stop=True))(),
                        reads=[kms, R("outwT", c * D, (c + 1) * D)],
                        writes=[kpp], tag=f"mmP{b}_{c}_{j}")
                P.ins("scalar", (lambda pp=pp, c=c: lambda e: e.copy(
                    p_sb(c), pp[:, :]))(), reads=[kpp], writes=[k_psb(c)],
                    tag=f"evP{b}_{c}")
            if debug:
                P.ins("gpsimd", (lambda b=b: lambda e: e.dma_start(
                    dbg_d["P"][b], arena2[:, :]))(),
                    reads=[R("arena2")], dma=True, tag=f"dbgP{b}")
            for f in range(8):
                pp, kpp = ATps[f % 2], k_AT[f % 2]
                for e_ in range(8):
                    for j in range(2):
                        P.ins("tensor", (lambda pp=pp, f=f, e_=e_, j=j:
                                         lambda e: e.matmul(
                            pp[:, j * 512:(j + 1) * 512],
                            wq_nat[:, e_ * D + f * 128:e_ * D + (f + 1) * 128],
                            p_sb(e_)[:, j * 512:(j + 1) * 512],
                            start=(e_ == 0), stop=(e_ == 7)))(),
                            reads=[R("wq_nat", e_ * D, (e_ + 1) * D),
                                   k_psb(e_)], writes=[kpp],
                            tag=f"mmAT{b}_{f}_{e_}_{j}")
                kct = k_ct(f)
                P.ins("vector", (lambda pp=pp, f=f: lambda e: e.tensor_tensor(
                    ct_sb(f), pp[:, :], weffT[:, f * D:(f + 1) * D], ALU.add))(),
                    reads=[kpp, R("weffT", f * D, (f + 1) * D)], writes=[kct],
                    tag=f"ct{b}_{f}")
            if debug:
                P.ins("gpsimd", (lambda b=b: lambda e: e.dma_start(
                    dbg_d["CT"][b], arena[:, 8 * D:16 * D]))(),
                    reads=[R("arena", 8 * D, 16 * D)], dma=True, tag=f"dbgCT{b}")
            psC2.close()

            psF = ExitStack()
            Ops = [psF.enter_context(
                nc.psum_tensor(f"O{b}_{i}", [128, D], F32)) for i in range(3)]
            Fps = [psF.enter_context(
                nc.psum_tensor(f"ftp{b}_{i}", [128, 128], BF16)) for i in range(2)]
            k_O = [psum_key(t) for t in Ops]
            k_F = [psum_key(t) for t in Fps]
            def fxv(i):
                p, h = i // 2, i % 2
                lo = (p % 2) * 4 * D + h * 2 * D
                return (arena2[:, lo:lo + 2 * D].bitcast(F32),
                        R("arena2", lo, lo + 2 * D))

            def f_head(i):
                if i % 2 == 0:
                    p = i // 2
                    lo = (p % 2) * 4 * D
                    dst = arena2[:, lo:lo + 4 * D].bitcast(F32)
                    kd = R("arena2", lo, lo + 4 * D)
                    P.ins("sync", (lambda dst=dst, b=b, i=i: lambda e:
                                   e.dma_start(
                        dst.rearrange("p (a f) -> p a f", a=2),
                        xs[b, i * 128:(i + 2) * 128, :].rearrange(
                            "(a p) f -> p a f", p=128)))(),
                        writes=[kd], dma=True, tag=f"dma_xF{b}_{i}")
                sl, ksl = fxv(i)
                xv, kxv = xnf[i % 2], R(f"xnf{i % 2}")
                P.ins("scalar", (lambda sl=sl, xv=xv, i=i: lambda e: e.activation(
                    xv[:, :], sl, AF.Identity,
                    bias=lncols[:, 16 + i:17 + i], scale=lncols[:, i:i + 1]))(),
                    reads=[ksl, R("lncols", i, i + 1), R("lncols", 16 + i, 17 + i)],
                    writes=[kxv], tag=f"xnF{b}_{i}")
                for c in range(8):
                    fp, kfp = Fps[c % 2], k_F[c % 2]
                    P.ins("tensor", (lambda fp=fp, xv=xv, c=c: lambda e:
                                     e.transpose(fp[:, :],
                                                 xv[:, c * 128:(c + 1) * 128],
                                                 ident_bf[:, :]))(),
                        reads=[kxv, R("ident_bf")], writes=[kfp],
                        tag=f"ftp{b}_{i}_{c}")
                    if c % 2 == 0:
                        P.ins("scalar", (lambda fp=fp, xv=xv, c=c: lambda e:
                                         e.copy(xv[:, c * 128:(c + 1) * 128],
                                                fp[:, :]))(),
                            reads=[kfp], writes=[kxv], tag=f"fev{b}_{i}_{c}")
                    else:
                        P.ins("vector", (lambda fp=fp, xv=xv, c=c: lambda e:
                                         e.tensor_copy(
                                             xv[:, c * 128:(c + 1) * 128],
                                             fp[:, :]))(),
                            reads=[kfp], writes=[kxv], tag=f"fev{b}_{i}_{c}")

            def f_tail(i):
                xt, kxt = xnf[i % 2], R(f"xnf{i % 2}")
                op, kop = Ops[i % 3], k_O[i % 3]
                for f in range(8):
                    for j in range(2):
                        P.ins("tensor", (lambda op=op, xt=xt, f=f, j=j:
                                         lambda e: e.matmul(
                            op[:, j * 512:(j + 1) * 512],
                            xt[:, f * 128:(f + 1) * 128],
                            ct_sb(f)[:, j * 512:(j + 1) * 512],
                            start=(f == 0), stop=False))(),
                            reads=[kxt, k_ct(f)], writes=[kop],
                            tag=f"mmO{b}_{i}_{f}_{j}")
                for j in range(2):
                    P.ins("tensor", (lambda op=op, j=j: lambda e: e.matmul(
                        op[:, j * 512:(j + 1) * 512], ones_row,
                        RB(7, j * 512, 512), start=False, stop=True))(),
                        reads=[k_ones_row, kRB(7)], writes=[kop],
                        tag=f"mmOb{b}_{i}_{j}")

            def f_fin(i):
                sl, ksl = fxv(i)
                op, kop = Ops[i % 3], k_O[i % 3]
                P.ins("vector", (lambda sl=sl, op=op: lambda e: e.tensor_tensor(
                    sl, sl, op[:, :], ALU.add))(),
                    reads=[ksl, kop], writes=[ksl], tag=f"add{b}_{i}")
                if i % 2 == 1:
                    p = i // 2
                    lo = (p % 2) * 4 * D
                    srcv = arena2[:, lo:lo + 4 * D].bitcast(F32)
                    ks2 = R("arena2", lo, lo + 4 * D)
                    P.ins("gpsimd", (lambda srcv=srcv, b=b, i=i: lambda e:
                                     e.dma_start(
                        ys[b, (i - 1) * 128:(i + 1) * 128, :].rearrange(
                            "(a p) f -> p a f", p=128),
                        srcv.rearrange("p (a f) -> p a f", a=2)))(),
                        reads=[ks2], dma=True, tag=f"out{b}_{i}")

            for i in range(NT):
                if i >= 2:
                    f_tail(i - 2)
                    f_fin(i - 2)
                f_head(i)
            f_tail(NT - 2)
            f_fin(NT - 2)
            f_tail(NT - 1)
            f_fin(NT - 1)
            psF.close()

        import os as _os
        if _os.environ.get("KERNEL_DEBUG_SBUF"):
            print("sbuf bytes remaining (inside build):", nc.sbuf_bytes_remaining)
        if _os.environ.get("KERNEL_DEBUG_DEPS"):
            want = _os.environ["KERNEL_DEBUG_DEPS"].split(",")
            for i, insd in enumerate(P.instrs):
                if insd["tag"] in want:
                    print(f"instr {i} tag={insd['tag']} eng={insd['engine']} "
                          f"dma={insd['dma']}")
                    for d in sorted(insd["deps"]):
                        dd = P.instrs[d]
                        print(f"   dep {d}: {dd['tag']} {dd['engine']} dma={dd['dma']}")
        sems = {e: E(nc.semaphore(f"s_{e}")) for e in ENGINES}
        dsems = {e: [E(nc.semaphore(f"sd_{e}_{j}"))
                     for j in range(Sched.NDMA_LANES[e])] for e in ENGINES}
        P.emit(nc, block, sems, dsems)

    return nc, dbg_d


_NC_CACHE = {}


def _get_nc(debug=False):
    key = bool(debug)
    if key not in _NC_CACHE:
        _NC_CACHE[key] = build_nc(debug)
    return _NC_CACHE[key]


def kernel(**inputs):
    x = np.ascontiguousarray(np.asarray(inputs["L0_attn_x"], np.float32))
    feed = {
        "wqkv": np.ascontiguousarray(np.asarray(inputs["L0_attn_Wqkv_w"], np.float32)),
        "bqkv": np.ascontiguousarray(np.asarray(inputs["L0_attn_Wqkv_b"], np.float32)),
        "gnw": np.ascontiguousarray(np.asarray(inputs["L0_attn_gn_w"], np.float32)),
        "gnb": np.ascontiguousarray(np.asarray(inputs["L0_attn_gn_b"], np.float32)),
        "outw": np.ascontiguousarray(np.asarray(inputs["L0_attn_out_w"], np.float32)),
        "outb": np.ascontiguousarray(np.asarray(inputs["L0_attn_out_b"], np.float32)),
        "wff": np.ascontiguousarray(np.asarray(inputs["L0_ffn_W"], np.float32)),
        "bff": np.ascontiguousarray(np.asarray(inputs["L0_ffn_b"], np.float32)),
        "ld": np.ascontiguousarray(np.asarray(inputs["L0_ffn_lora_down"], np.float32)),
        "lu": np.ascontiguousarray(np.asarray(inputs["L0_ffn_lora_up"], np.float32)),
        "da": np.ascontiguousarray(np.asarray(inputs["L0_ffn_down_aux"], np.float32)),
        "ua": np.ascontiguousarray(np.asarray(inputs["L0_ffn_up_aux"], np.float32)),
    }
    nc, _ = _get_nc(False)
    in_maps = []
    for core in range(NCORES):
        m = dict(feed)
        m["xs"] = np.ascontiguousarray(x[core * NB:(core + 1) * NB])
        in_maps.append(m)
    res = run_bass_kernel_spmd(nc, in_maps, core_ids=list(range(NCORES)))
    return np.concatenate([r["ys"] for r in res.results], axis=0)


if __name__ == "__main__":
    nc, _ = build_nc(False)
    counts = {}
    print("built ok")


# version 27 (final)
# speedup vs baseline: 1.1548x; 1.0150x over previous
import numpy as np
from contextlib import ExitStack

import concourse.bass as bass
import concourse.mybir as mybir
from concourse.bass_utils import run_bass_kernel_spmd

F32 = mybir.dt.float32
BF16 = mybir.dt.bfloat16
AF = mybir.ActivationFunctionType
ALU = mybir.AluOpType
AX = mybir.AxisListType

D = 1024
SEQ = 2048
NB = 2
NT = 16
EPS = 1e-5
NCORES = 8
ENGINES = ("sync", "gpsimd", "scalar", "vector", "tensor")


class Sched:

    def __init__(self):
        self.instrs = []
        self.acc = {}
        self._dma_count = {e: 0 for e in ENGINES}

    @staticmethod
    def r(name, lo=0, hi=1 << 60):
        return (name, lo, hi)

    def _overlap(self, key, want_writes):
        name, lo, hi = key
        out = []
        for (l, h, iid, w) in self.acc.get(name, ()):
            if w == want_writes and l < hi and lo < h:
                out.append(iid)
        return out

    def ins(self, engine, fn, reads=(), writes=(), dma=False, tag=""):
        assert engine in ENGINES
        deps = set()
        for k in reads:
            deps.update(self._overlap(k, True))
        for k in writes:
            deps.update(self._overlap(k, True))
            deps.update(self._overlap(k, False))
        iid = len(self.instrs)
        lane = self._dma_count[engine] % self.NDMA_LANES[engine] if dma else None

        def keep(d):
            dd = self.instrs[d]
            de = dd["engine"]
            if de != engine:
                return True
            if dd["dma"] and dma:
                return False
            if engine == "tensor":
                return False
            return True
        deps = {d for d in deps if keep(d)}
        best = {}
        for d in deps:
            dd = self.instrs[d]
            ch = (dd["engine"], dd["lane"]) if dd["dma"] else (dd["engine"],)
            if best.get(ch, -1) < d:
                best[ch] = d
        deps = set(best.values())
        self.instrs.append(dict(engine=engine, fn=fn, deps=deps, dma=dma,
                                lane=lane, tag=tag))
        if dma:
            self._dma_count[engine] += 1
        for k in reads:
            self.acc.setdefault(k[0], []).append((k[1], k[2], iid, False))
        for k in writes:
            self.acc.setdefault(k[0], []).append((k[1], k[2], iid, True))
        return iid

    NDMA_LANES = {"sync": 16, "gpsimd": 12, "scalar": 8, "vector": 4,
                  "tensor": 2}

    def emit(self, nc, block, sems, dsems, final_waits=("gpsimd",)):
        needed = set()
        for ins in self.instrs:
            needed.update(ins["deps"])
        cum = {}
        chan = {}
        val = {}
        for i, ins in enumerate(self.instrs):
            if ins["dma"]:
                ch = (ins["engine"], ins["lane"])
                cum[ch] = cum.get(ch, 0) + 16
                chan[i] = ch
                val[i] = cum[ch]
                needed.add(i)
            else:
                ch = (ins["engine"],)
                if i in needed:
                    cum[ch] = cum.get(ch, 0) + 1
                chan[i] = ch
                val[i] = cum.get(ch, 0)
        cum2 = {}
        for i, ins in enumerate(self.instrs):
            if not ins["dma"]:
                ch = chan[i]
                if i in needed:
                    cum2[ch] = cum2.get(ch, 0) + 1
                val[i] = cum2.get(ch, 0)
        order = {e: [i for i, ins in enumerate(self.instrs) if ins["engine"] == e]
                 for e in ENGINES}

        def hnd(ch):
            return dsems[ch[0]][ch[1]] if len(ch) == 2 else sems[ch[0]]

        totals = dict(cum)
        totals.update(cum2)

        def section(e):
            def body(eng):
                waited = {}
                for iid in order[e]:
                    ins = self.instrs[iid]
                    for d in sorted(ins["deps"], key=lambda d: val[d]):
                        dch = chan[d]
                        v = val[d]
                        assert v > 0, (d, iid)
                        if waited.get(dch, 0) < v:
                            eng.wait_ge(hnd(dch), v)
                            waited[dch] = v
                    bi = ins["fn"](eng)
                    if iid in needed:
                        ch = chan[iid]
                        bi.then_inc(hnd(ch), 16 if ins["dma"] else 1)
                if e in final_waits:
                    for ch, tot in totals.items():
                        if len(ch) == 2 and ch[0] == e and tot:
                            eng.wait_ge(hnd(ch), tot)
            return body

        block.sync(section("sync"))
        block.gpsimd(section("gpsimd"))
        block.scalar(section("scalar"))
        block.vector(section("vector"))
        block.tensor(section("tensor"))
        return totals


def build_nc(debug=False):
    nc = bass.Bass("TRN2", target_bir_lowering=False)
    P = Sched()
    R = Sched.r

    xs = nc.dram_tensor("xs", [NB, SEQ, D], F32, kind="ExternalInput").ap()
    wqkv_d = nc.dram_tensor("wqkv", [3 * D, D], F32, kind="ExternalInput").ap()
    bqkv_d = nc.dram_tensor("bqkv", [3 * D], F32, kind="ExternalInput").ap()
    gnw_d = nc.dram_tensor("gnw", [D], F32, kind="ExternalInput").ap()
    gnb_d = nc.dram_tensor("gnb", [D], F32, kind="ExternalInput").ap()
    outw_d = nc.dram_tensor("outw", [D, D], F32, kind="ExternalInput").ap()
    outb_d = nc.dram_tensor("outb", [D], F32, kind="ExternalInput").ap()
    wff_d = nc.dram_tensor("wff", [D, D], F32, kind="ExternalInput").ap()
    bff_d = nc.dram_tensor("bff", [D], F32, kind="ExternalInput").ap()
    ld_d = nc.dram_tensor("ld", [4, 100], F32, kind="ExternalInput").ap()
    lu_d = nc.dram_tensor("lu", [50, 4], F32, kind="ExternalInput").ap()
    da_d = nc.dram_tensor("da", [100, D], F32, kind="ExternalInput").ap()
    ua_d = nc.dram_tensor("ua", [D, 50], F32, kind="ExternalInput").ap()
    ys = nc.dram_tensor("ys", [NB, SEQ, D], F32, kind="ExternalOutput").ap()
    dbg_d = {}
    if debug:
        dbg_d["xn"] = nc.dram_tensor("dbg_xn", [NB, 128, 16 * D], BF16,
                                     kind="ExternalOutput").ap()
        dbg_d["X"] = nc.dram_tensor("dbg_X", [NB, 128, 8 * D], BF16,
                                    kind="ExternalOutput").ap()
        dbg_d["MT"] = nc.dram_tensor("dbg_MT", [NB, 128, D], BF16,
                                     kind="ExternalOutput").ap()
        dbg_d["CT"] = nc.dram_tensor("dbg_CT", [NB, 128, 8 * D], BF16,
                                     kind="ExternalOutput").ap()
        dbg_d["rows"] = nc.dram_tensor("dbg_rows", [NB, 1, 5 * D], F32,
                                       kind="ExternalOutput").ap()
        dbg_d["P"] = nc.dram_tensor("dbg_P", [NB, 128, 8 * D], BF16,
                                    kind="ExternalOutput").ap()
        dbg_d["weffT"] = nc.dram_tensor("dbg_weffT", [128, 8 * D], BF16,
                                        kind="ExternalOutput").ap()
        dbg_d["wqnat"] = nc.dram_tensor("dbg_wqnat", [128, 8 * D], BF16,
                                        kind="ExternalOutput").ap()
        dbg_d["lil"] = nc.dram_tensor("dbg_lil", [128, 4 * D], BF16,
                                      kind="ExternalOutput").ap()
        dbg_d["cols"] = nc.dram_tensor("dbg_cols", [NB, 128, 64], F32,
                                       kind="ExternalOutput").ap()

    with ExitStack() as st, nc.Block() as block:
        E = st.enter_context

        arena = E(nc.sbuf_tensor("arena", [128, 16 * D], BF16))
        arena2 = E(nc.sbuf_tensor("arena2", [128, 8 * D], BF16))
        wqkvT = E(nc.sbuf_tensor("wqkvT", [128, 24 * D], BF16))
        wq_nat = E(nc.sbuf_tensor("wq_nat", [128, 8 * D], BF16))
        outwT = E(nc.sbuf_tensor("outwT", [128, 8 * D], BF16))
        weffT = E(nc.sbuf_tensor("weffT", [128, 8 * D], BF16))
        stage = [E(nc.sbuf_tensor(f"stage{i}", [128, D], F32)) for i in range(3)]
        sqscr = E(nc.sbuf_tensor("sqscr", [128, D], F32))
        xnf = [E(nc.sbuf_tensor(f"xnf{i}", [128, D], BF16)) for i in range(2)]
        mblkT = E(nc.sbuf_tensor("mblkT", [128, D], BF16))
        mblk = E(nc.sbuf_tensor("mblk", [128, D], BF16))
        mblkTs = [E(nc.sbuf_tensor(f"mblkTs{i}", [128, 128], BF16)) for i in range(2)]
        lil_small = E(nc.sbuf_tensor("lil_small", [128, 128], BF16))
        ident_bf = E(nc.sbuf_tensor("ident_bf", [128, 128], BF16))
        ident_f = E(nc.sbuf_tensor("ident_f", [128, 128], F32))
        sel = E(nc.sbuf_tensor("sel", [128, 4], F32))
        selT = E(nc.sbuf_tensor("selT", [4, 128], F32))
        ones_col = E(nc.sbuf_tensor("ones_col", [128, 1], BF16))
        epscol = E(nc.sbuf_tensor("epscol", [128, 1], F32))
        rows_f = E(nc.sbuf_tensor("rows_f", [1, 5 * D], F32))
        h_sb = sqscr[:, 256:384]
        rows_b = E(nc.sbuf_tensor("rows_b", [1, 8 * D + 128], BF16))
        cols_f = E(nc.sbuf_tensor("cols_f", [128, 64], F32))
        cols_b = E(nc.sbuf_tensor("cols_b", [128, 32], BF16))
        lncols = E(nc.sbuf_tensor("lncols", [128, 32], F32))
        lnst = E(nc.sbuf_tensor("lnst", [128, 96], F32))
        st44 = E(nc.sbuf_tensor("st44", [128, 64], F32))

        RS = lambda k, o=0, n=D: (sqscr[0:1, o:o + n] if k == 0
                                  else rows_f[0:1, (k - 1) * D + o:(k - 1) * D + o + n])
        kRS = lambda k, o=0, n=D: (R("sqscr", o, o + n) if k == 0
                                   else R("rows_f", (k - 1) * D + o, (k - 1) * D + o + n))
        RB = lambda k, o=0, n=D: rows_b[0:1, k * D + o:k * D + o + n]
        kRB = lambda k, o=0, n=D: R("rows_b", k * D + o, k * D + o + n)
        ones_row = rows_b[0:1, 8 * D:8 * D + 128]
        k_ones_row = R("rows_b", 8 * D, 8 * D + 128)
        CF = lambda k, n=1: cols_f[:, k:k + n]
        kCF = lambda k, n=1: R("cols_f", k, k + n)
        CB = lambda k, n=1: cols_b[:, k:k + n]
        kCB = lambda k, n=1: R("cols_b", k, k + n)

        xn_tm = lambda i: arena[:, i * D:(i + 1) * D]
        k_xn = lambda i: R("arena", i * D, (i + 1) * D)
        t_cat = lambda f: arena[:, f * 2 * D:(f + 1) * 2 * D]
        k_tcat = lambda f: R("arena", f * 2 * D, (f + 1) * 2 * D)
        ct_sb = lambda f: arena[:, 8 * D + f * D:8 * D + (f + 1) * D]
        k_ct = lambda f: R("arena", 8 * D + f * D, 8 * D + (f + 1) * D)
        x_sb = lambda g: arena2[:, g * D:(g + 1) * D]
        k_xsb = lambda g: R("arena2", g * D, (g + 1) * D)
        p_sb = lambda e: arena2[:, e * D:(e + 1) * D]
        k_psb = k_xsb
        v_dab = xnf[0][0:100, :]
        k_dab = R("xnf0")
        v_uaT = xnf[1][0:50, :]
        k_uaT = R("xnf1")
        v_t1 = arena[0:4, 0:D]
        k_t1 = R("arena", 0, D)
        v_t2 = arena[0:50, D:2 * D]
        k_t2 = R("arena", D, 2 * D)
        v_ldT = lil_small[0:100, 0:4]
        k_ldT = R("lil_small", 0, 4)
        v_luT = lil_small[0:4, 64:114]
        k_luT = R("lil_small", 64, 114)

        def psum_key(handle):
            for alloc in nc.m.functions[0].allocations:
                mls = getattr(alloc, "memorylocations", None)
                if mls:
                    for ml in mls:
                        if ml.name == handle.name:
                            return R("PSUM", ml.addr, ml.addr + int(ml.dims[-1]))
            raise KeyError(handle.name)

        P.ins("gpsimd", lambda g: g.memset(ident_bf[:, :], 0.0),
              writes=[R("ident_bf")], tag="c0")
        P.ins("gpsimd", lambda g: g.affine_select(
            out=ident_bf[:, :], in_=ident_bf[:, :], compare_op=ALU.not_equal,
            fill=1.0, base=0, pattern=[[-1, 128]], channel_multiplier=1),
            reads=[R("ident_bf")], writes=[R("ident_bf")], tag="c1")
        P.ins("gpsimd", lambda g: g.memset(ident_f[:, :], 0.0),
              writes=[R("ident_f")])
        P.ins("gpsimd", lambda g: g.affine_select(
            out=ident_f[:, :], in_=ident_f[:, :], compare_op=ALU.not_equal,
            fill=1.0, base=0, pattern=[[-1, 128]], channel_multiplier=1),
            reads=[R("ident_f")], writes=[R("ident_f")])
        P.ins("gpsimd", lambda g: g.memset(sel[:, :], 1.0), writes=[R("sel")])
        P.ins("gpsimd", lambda g: g.affine_select(
            out=sel[:, :], in_=sel[:, :], compare_op=ALU.is_ge, fill=0.0,
            base=0, pattern=[[-32, 4]], channel_multiplier=1),
            reads=[R("sel")], writes=[R("sel")])
        P.ins("gpsimd", lambda g: g.affine_select(
            out=sel[:, :], in_=sel[:, :], compare_op=ALU.is_ge, fill=0.0,
            base=31, pattern=[[32, 4]], channel_multiplier=-1),
            reads=[R("sel")], writes=[R("sel")])
        P.ins("gpsimd", lambda g: g.memset(selT[:, :], 1.0), writes=[R("selT")])
        P.ins("gpsimd", lambda g: g.affine_select(
            out=selT[:, :], in_=selT[:, :], compare_op=ALU.is_ge, fill=0.0,
            base=0, pattern=[[1, 128]], channel_multiplier=-32),
            reads=[R("selT")], writes=[R("selT")])
        P.ins("gpsimd", lambda g: g.affine_select(
            out=selT[:, :], in_=selT[:, :], compare_op=ALU.is_ge, fill=0.0,
            base=31, pattern=[[-1, 128]], channel_multiplier=32),
            reads=[R("selT")], writes=[R("selT")])
        P.ins("gpsimd", lambda g: g.memset(ones_col[:, :], 1.0),
              writes=[R("ones_col")])
        P.ins("gpsimd", lambda g: g.memset(epscol[:, :], EPS), writes=[R("epscol")])
        P.ins("gpsimd", lambda g: g.memset(ones_row, 1.0), writes=[k_ones_row])
        P.ins("gpsimd", lambda g: g.memset(mblkT[:, :], 0.0), writes=[R("mblkT")])
        P.ins("gpsimd", lambda g: g.memset(mblk[:, :], 0.0), writes=[R("mblk")])


        def make_A1(b):
            def ln_stats(i):
                n = b * NT + i
                sl = stage[n % 3]
                ksl = R(f"stage{n % 3}")
                P.ins("sync", (lambda sl=sl, b=b, i=i: lambda e: e.dma_start(
                    sl[:, :], xs[b, i * 128:(i + 1) * 128, :]))(),
                    writes=[ksl], dma=True, tag=f"dma_xA{b}_{i}")
                P.ins("vector", (lambda sl=sl, i=i: lambda e: e.reduce_sum(
                    lnst[:, i:i + 1], sl[:, :], axis=AX.X))(),
                    reads=[ksl], writes=[R("lnst", i, i + 1)], tag=f"ln_sum{i}")
                P.ins("scalar", (lambda sl=sl: lambda e: e.activation(
                    sqscr[:, :], sl[:, :], AF.Square))(),
                    reads=[ksl], writes=[R("sqscr")], tag=f"ln_sq{i}")
                P.ins("vector", (lambda i=i: lambda e: e.reduce_sum(
                    lnst[:, 16 + i:17 + i], sqscr[:, :], axis=AX.X))(),
                    reads=[R("sqscr")], writes=[R("lnst", 16 + i, 17 + i)],
                    tag=f"ln_ssq{i}")
                P.ins("vector", (lambda i=i: lambda e: e.tensor_scalar(
                    lnst[:, 32 + i:33 + i], lnst[:, i:i + 1], -1.0 / D, None,
                    ALU.mult))(),
                    reads=[R("lnst", i, i + 1)],
                    writes=[R("lnst", 32 + i, 33 + i)], tag=f"ln_negmean{i}")
                P.ins("vector", (lambda i=i: lambda e: e.tensor_tensor(
                    lnst[:, 48 + i:49 + i], lnst[:, 32 + i:33 + i],
                    lnst[:, 32 + i:33 + i], ALU.mult))(),
                    reads=[R("lnst", 32 + i, 33 + i)],
                    writes=[R("lnst", 48 + i, 49 + i)], tag=f"ln_msq{i}")
                P.ins("vector", (lambda i=i: lambda e: e.scalar_tensor_tensor(
                    lnst[:, 64 + i:65 + i], lnst[:, 16 + i:17 + i], 1.0 / D,
                    lnst[:, 48 + i:49 + i], ALU.mult, ALU.subtract))(),
                    reads=[R("lnst", 16 + i, 17 + i), R("lnst", 48 + i, 49 + i)],
                    writes=[R("lnst", 64 + i, 65 + i)], tag=f"ln_var{i}")
                P.ins("scalar", (lambda i=i: lambda e: e.activation(
                    lnst[:, 80 + i:81 + i], lnst[:, 64 + i:65 + i], AF.Ln,
                    bias=epscol[:, :]))(),
                    reads=[R("lnst", 64 + i, 65 + i), R("epscol")],
                    writes=[R("lnst", 80 + i, 81 + i)], tag=f"ln_lnv{i}")
                P.ins("scalar", (lambda i=i: lambda e: e.activation(
                    lncols[:, i:i + 1], lnst[:, 80 + i:81 + i], AF.Exp,
                    scale=-0.5))(),
                    reads=[R("lnst", 80 + i, 81 + i)],
                    writes=[R("lncols", i, i + 1)], tag=f"ln_rstd{i}")

            def ln_finish(i):
                n = b * NT + i
                sl = stage[n % 3]
                ksl = R(f"stage{n % 3}")
                P.ins("vector", (lambda i=i: lambda e: e.tensor_scalar(
                    lncols[:, 16 + i:17 + i], lnst[:, 32 + i:33 + i],
                    lncols[:, i:i + 1], None, ALU.mult))(),
                    reads=[R("lnst", 32 + i, 33 + i), R("lncols", i, i + 1)],
                    writes=[R("lncols", 16 + i, 17 + i)], tag=f"ln_negmr{i}")
                P.ins("vector", (lambda sl=sl, i=i, b=b: lambda e: e.tensor_scalar(
                    xn_tm(i), sl[:, :], lncols[:, i:i + 1],
                    lncols[:, 16 + i:17 + i], ALU.mult, ALU.add))(),
                    reads=[ksl, R("lncols", i, i + 1),
                           R("lncols", 16 + i, 17 + i)],
                    writes=[k_xn(i)], tag=f"xn{b}_{i}")

            def tile(k):
                if k >= 2:
                    ln_finish(k - 2)
                if k < NT:
                    ln_stats(k)
                if k == NT:
                    ln_finish(NT - 2)
                    ln_finish(NT - 1)
            return tile

        wslot = [arena[:, 0:8 * D].bitcast(F32),
                 arena[:, 8 * D:16 * D].bitcast(F32)]
        k_wslot = [R("arena", 0, 8 * D), R("arena", 8 * D, 16 * D)]

        def w_grp_dram(g):
            if g < 6:
                src = wqkv_d[g * 512:(g + 1) * 512, :]
            elif g < 8:
                src = outw_d[(g - 6) * 512:(g - 5) * 512, :]
            else:
                src = wff_d[(g - 8) * 512:(g - 7) * 512, :]
            return src.rearrange("(a p) f -> p a f", p=128)

        def w_tp_dest(t, c):
            if t < 24:
                lo = c * 3 * D + t * 128
                return wqkvT[:, lo:lo + 128], R("wqkvT", lo, lo + 128)
            if t < 32:
                lo = c * D + (t - 24) * 128
                return outwT[:, lo:lo + 128], R("outwT", lo, lo + 128)
            lo = c * D + (t - 32) * 128
            return weffT[:, lo:lo + 128], R("weffT", lo, lo + 128)

        def emit_wgroup(g):
            sl = wslot[g % 2]
            ksl = k_wslot[g % 2]
            P.ins("sync" if g % 2 == 0 else "gpsimd",
                  (lambda sl=sl, g=g: lambda e: e.dma_start(
                      sl.rearrange("p (a f) -> p a f", a=4), w_grp_dram(g)))(),
                  writes=[ksl], dma=True, tag=f"dma_wg{g}")
            for q in range(4):
                t = g * 4 + q
                if t < 8:
                    cb = wq_nat[:, t * D:(t + 1) * D]
                    kcbh = [R("wq_nat", t * D + hh * 512,
                              t * D + (hh + 1) * 512) for hh in range(2)]
                else:
                    cb = xnf[t % 2][:, :]
                    kcbh = [R(f"xnf{t % 2}", hh * 512, (hh + 1) * 512)
                            for hh in range(2)]
                for hh in range(2):
                    P.ins("vector", (lambda sl=sl, q=q, cb=cb, hh=hh: lambda e:
                                     e.tensor_copy(
                                         cb[:, hh * 512:(hh + 1) * 512],
                                         sl[:, q * D + hh * 512:
                                            q * D + (hh + 1) * 512]))(),
                        reads=[ksl], writes=[kcbh[hh]], tag=f"cast_w{t}_{hh}")
                for c in range(8):
                    j = t * 8 + c
                    ps = tp32_ps[j % 4]
                    kps = k_tp32[j % 4]
                    P.ins("tensor", (lambda ps=ps, cb=cb, c=c: lambda e:
                                     e.transpose(ps[:, 0:128],
                                                 cb[:, c * 128:(c + 1) * 128],
                                                 ident_bf[:, :]))(),
                        reads=[kcbh[c // 4], R("ident_bf")], writes=[kps],
                        tag=f"tp_w{t}_{c}")
                    dv, kdv = w_tp_dest(t, c)
                    if c % 3 == 0:
                        P.ins("scalar", (lambda dv=dv, ps=ps: lambda e: e.copy(
                            dv, ps[:, 0:128]))(), reads=[kps], writes=[kdv],
                            tag=f"ev_w{t}_{c}")
                    else:
                        P.ins("vector", (lambda dv=dv, ps=ps: lambda e:
                                         e.tensor_copy(dv, ps[:, 0:128]))(),
                            reads=[kps], writes=[kdv], tag=f"ev_w{t}_{c}")


        ps_rows = ExitStack()
        ctpR = ps_rows.enter_context(nc.psum_tensor("ctpR", [128, 4], F32))
        k_ctpR = psum_key(ctpR)
        def row_dma(dram_ap, tag):
            ap2 = dram_ap.rearrange("(a d) -> a d", a=1)
            P.ins("gpsimd", (lambda ap2=ap2: lambda e: e.dma_start(RS(0), ap2))(),
                  writes=[kRS(0)], dma=True, tag=tag)

        row_dma(bqkv_d[0:D], "dma_bq")
        P.ins("scalar", lambda e: e.activation(RB(0), RS(0), AF.Copy),
              reads=[kRS(0)], writes=[kRB(0)], tag="cast_bq")
        P.ins("scalar", lambda e: e.copy(RS(2), RS(0)),
              reads=[kRS(0)], writes=[kRS(2)], tag="keep_bq")
        for c in range(8):
            pp = ctpR
            kpp = k_ctpR
            P.ins("tensor", (lambda pp=pp, c=c: lambda e: e.transpose(
                pp[:, 0:1], RS(2)[0:1, c * 128:(c + 1) * 128],
                ident_f[0:1, 0:1]))(),
                reads=[kRS(2), R("ident_f")], writes=[kpp], tag=f"tp_bqc{c}")
            P.ins("scalar", (lambda pp=pp, c=c: lambda e: e.copy(
                CF(16 + c), pp[:, 0:1]))(), reads=[kpp],
                writes=[kCF(16 + c)], tag=f"ev_bqc{c}")
        row_dma(bqkv_d[D:2 * D], "dma_bk")
        P.ins("scalar", lambda e: e.activation(RB(1), RS(0), AF.Copy),
              reads=[kRS(0)], writes=[kRB(1)], tag="cast_bk")
        row_dma(bqkv_d[2 * D:3 * D], "dma_bv")
        P.ins("scalar", lambda e: e.activation(RB(2), RS(0), AF.Copy),
              reads=[kRS(0)], writes=[kRB(2)], tag="cast_bv")
        P.ins("vector", lambda e: e.tensor_scalar(
            RS(3), RS(0), float(SEQ), None, ALU.mult),
            reads=[kRS(0)], writes=[kRS(3)], tag="sbv")
        row_dma(gnw_d[:], "dma_gnw")
        for c in range(8):
            pp = ctpR
            kpp = k_ctpR
            P.ins("tensor", (lambda pp=pp, c=c: lambda e: e.transpose(
                pp[:, 0:1], RS(0)[0:1, c * 128:(c + 1) * 128],
                ident_f[0:1, 0:1]))(),
                reads=[kRS(0), R("ident_f")], writes=[kpp], tag=f"tp_gnw{c}")
            P.ins("scalar", (lambda pp=pp, c=c: lambda e: e.copy(
                CF(0 + c), pp[:, 0:1]))(), reads=[kpp], writes=[kCF(c)],
                tag=f"ev_gnw{c}")
        row_dma(gnb_d[:], "dma_gnb")
        for c in range(8):
            pp = ctpR
            kpp = k_ctpR
            P.ins("tensor", (lambda pp=pp, c=c: lambda e: e.transpose(
                pp[:, 0:1], RS(0)[0:1, c * 128:(c + 1) * 128],
                ident_f[0:1, 0:1]))(),
                reads=[kRS(0), R("ident_f")], writes=[kpp], tag=f"tp_gnb{c}")
            P.ins("scalar", (lambda pp=pp, c=c: lambda e: e.copy(
                CF(8 + c), pp[:, 0:1]))(), reads=[kpp], writes=[kCF(8 + c)],
                tag=f"ev_gnb{c}")
        row_dma(outb_d[:], "dma_outb")
        P.ins("scalar", lambda e: e.copy(RS(4), RS(0)),
              reads=[kRS(0)], writes=[kRS(4)], tag="keep_outb")
        row_dma(bff_d[:], "dma_bff")
        P.ins("vector", lambda e: e.tensor_tensor(RS(4), RS(4), RS(0), ALU.add),
              reads=[kRS(4), kRS(0)], writes=[kRS(4)], tag="obf")
        P.ins("vector", lambda e: e.tensor_copy(CB(16, 8), CF(16, 8)),
              reads=[kCF(16, 8)], writes=[kCB(16, 8)], tag="bqcol_bf")

        ps_rows.close()

        for b in range(NB):
            psA = ExitStack()
            Xps = [psA.enter_context(
                nc.psum_tensor(f"X{b}_{i}", [128, D], F32)) for i in range(3)]
            Srow = psA.enter_context(nc.psum_tensor(f"xsr{b}", [1, D], F32))
            k_X = [psum_key(t) for t in Xps]
            k_Sr = psum_key(Srow)

            def x_chunk_mms(c, i):
                lhs = xn_tm(i)[:, c * 128:(c + 1) * 128]
                kx = k_xn(i)
                pp, kpp = Xps[c % 3], k_X[c % 3]
                st_ = (i == 0)
                sp_ = (i == NT - 1)
                for j in range(2):
                    P.ins("tensor", (lambda lhs=lhs, pp=pp, i=i, j=j, st_=st_,
                                     sp_=sp_: lambda e: e.matmul(
                        pp[:, j * 512:(j + 1) * 512], lhs,
                        xn_tm(i)[:, j * 512:(j + 1) * 512],
                        start=st_, stop=sp_))(),
                        reads=[kx], writes=[kpp], tag=f"mmX{b}_{c}_{i}_{j}")

            def xnsum_row_mms(i):
                kx = k_xn(i)
                st_ = (i == 0)
                sp_ = (i == NT - 1)
                for j in range(2):
                    P.ins("tensor", (lambda i=i, j=j, st_=st_, sp_=sp_,
                                     Srow=Srow: lambda e: e.matmul(
                        Srow[0:1, j * 512:(j + 1) * 512], ones_col[:, :],
                        xn_tm(i)[:, j * 512:(j + 1) * 512],
                        start=st_, stop=sp_))(),
                        reads=[kx, R("ones_col")], writes=[k_Sr],
                        tag=f"mmSr{b}_{i}")

            def x_evac(c):
                pp, kpp = Xps[c % 3], k_X[c % 3]
                P.ins("scalar", (lambda pp=pp, c=c: lambda e: e.copy(
                    x_sb(c), pp[:, :]))(), reads=[kpp], writes=[k_xsb(c)],
                    tag=f"evX{b}_{c}")

            a1t = make_A1(b)
            for k in range(NT + 2):
                if k <= NT:
                    a1t(k)
                i = k - 2
                if 0 <= i < NT:
                    for c in (0, 1, 2):
                        x_chunk_mms(c, i)
                    xnsum_row_mms(i)
            for c in (0, 1, 2):
                x_evac(c)
            P.ins("scalar", (lambda Srow=Srow: lambda e: e.copy(
                RS(0), Srow[0:1, :]))(),
                reads=[k_Sr], writes=[kRS(0)], tag=f"evSr{b}")
            for grp in ((3, 4, 5), (6, 7)):
                for i in range(NT):
                    for c in grp:
                        x_chunk_mms(c, i)
                for c in grp:
                    x_evac(c)

            if debug:
                P.ins("gpsimd", (lambda b=b: lambda e: e.dma_start(
                    dbg_d["X"][b], arena2[:, :]))(),
                    reads=[R("arena2")], dma=True, tag=f"dbgX{b}")
                P.ins("gpsimd", (lambda b=b: lambda e: e.dma_start(
                    dbg_d["xn"][b], arena[:, :]))(),
                    reads=[R("arena", 0, 16 * D)], dma=True, tag=f"dbgxn{b}")
            psA.close()

            if b == 0:
                ps0 = ExitStack()
                tp32_ps = [ps0.enter_context(
                    nc.psum_tensor(f"tp32_{i}", [128, 512], BF16))
                    for i in range(4)]
                k_tp32 = [psum_key(t) for t in tp32_ps]
                for g in range(10):
                    emit_wgroup(g)
                ps_lil = ExitStack()
                ctp_ps = [ps_lil.enter_context(
                    nc.psum_tensor("ctpL", [128, 4], F32))]
                k_ctp = [psum_key(ctp_ps[0])]
                lil_ps = [ps_lil.enter_context(
                    nc.psum_tensor(f"lilp{i}", [128, 512], F32))
                    for i in range(2)]
                k_lilp = [psum_key(t) for t in lil_ps]
                st_da = stage[0][0:100, :]
                k_stda = R("stage0")
                st_ld = stage[1][0:4, 0:100]
                k_stld = R("stage1", 0, 100)
                st_lu = stage[1][0:50, 128:132]
                k_stlu = R("stage1", 128, 132)
                P.ins("sync", lambda e: e.dma_start(st_da, da_d[:, :]),
                      writes=[k_stda], dma=True, tag="dma_da")
                P.ins("sync", lambda e: e.dma_start(st_ld, ld_d[:, :]),
                      writes=[k_stld], dma=True, tag="dma_ld")
                P.ins("sync", lambda e: e.dma_start(st_lu, lu_d[:, :]),
                      writes=[k_stlu], dma=True, tag="dma_lu")
                P.ins("gpsimd", lambda e: e.tensor_copy(v_dab, st_da),
                      reads=[k_stda], writes=[k_dab], tag="cast_da")
                if debug:
                    P.ins("gpsimd", lambda e: e.dma_start(
                        dbg_d["lil"][:, 0:D], xnf[0][:, :]),
                        reads=[R("xnf0")], dma=True, tag="dbglil0_early")
                P.ins("tensor", lambda e: e.transpose(
                    ctp_ps[0][0:100, 0:4], st_ld[:, 0:100], ident_f[0:4, 0:4]),
                    reads=[k_stld, R("ident_f")], writes=[k_ctp[0]],
                    tag="tp_ldT")
                P.ins("scalar", lambda e: e.copy(v_ldT, ctp_ps[0][0:100, 0:4]),
                      reads=[k_ctp[0]], writes=[k_ldT], tag="ev_ldT")
                P.ins("tensor", lambda e: e.transpose(
                    lil_ps[0][0:4, 0:50], st_lu[:, 0:4], ident_f[0:50, 0:50]),
                    reads=[k_stlu, R("ident_f")], writes=[k_lilp[0]],
                    tag="tp_luT")
                P.ins("scalar", lambda e: e.copy(v_luT, lil_ps[0][0:4, 0:50]),
                      reads=[k_lilp[0]], writes=[k_luT], tag="ev_luT")
                for t in range(8):
                    sl = stage[2][:, (t % 2) * 64:(t % 2) * 64 + 50]
                    ksl = R("stage2", (t % 2) * 64, (t % 2) * 64 + 50)
                    P.ins("sync", (lambda sl=sl, t=t: lambda e: e.dma_start(
                        sl, ua_d[t * 128:(t + 1) * 128, :]))(), writes=[ksl], dma=True,
                        tag=f"dma_ua{t}")
                    pp = lil_ps[1] if t % 2 else lil_ps[0]
                    kpp = k_lilp[1] if t % 2 else k_lilp[0]
                    P.ins("tensor", (lambda pp=pp, sl=sl: lambda e: e.transpose(
                        pp[0:50, 0:128], sl, ident_f[:, :]))(),
                        reads=[ksl, R("ident_f")], writes=[kpp], tag=f"tp_ua{t}")
                    dv = v_uaT[:, t * 128:(t + 1) * 128]
                    P.ins("scalar", (lambda dv=dv, pp=pp: lambda e: e.copy(
                        dv, pp[0:50, 0:128]))(), reads=[kpp],
                        writes=[R("xnf1", t * 128, (t + 1) * 128)],
                        tag=f"ev_ua{t}")
                for j in range(2):
                    pp = lil_ps[j]
                    P.ins("tensor", (lambda pp=pp, j=j: lambda e: e.matmul(
                        pp[0:4, 0:512], v_ldT, v_dab[:, j * 512:(j + 1) * 512],
                        start=True, stop=True))(),
                        reads=[k_ldT, k_dab],
                        writes=[k_lilp[j]], tag=f"mm_t1_{j}")
                    P.ins("scalar", (lambda pp=pp, j=j: lambda e: e.copy(
                        v_t1[:, j * 512:(j + 1) * 512], pp[0:4, 0:512]))(),
                        reads=[k_lilp[j]],
                        writes=[R("arena", j * 512, (j + 1) * 512)],
                        tag=f"ev_t1_{j}")
                for j in range(2):
                    pp = lil_ps[j]
                    P.ins("tensor", (lambda pp=pp, j=j: lambda e: e.matmul(
                        pp[0:50, 0:512], v_luT, v_t1[:, j * 512:(j + 1) * 512],
                        start=True, stop=True))(),
                        reads=[k_luT, k_t1],
                        writes=[k_lilp[j]], tag=f"mm_t2_{j}")
                    P.ins("scalar", (lambda pp=pp, j=j: lambda e: e.copy(
                        v_t2[:, j * 512:(j + 1) * 512], pp[0:50, 0:512]))(),
                        reads=[k_lilp[j]],
                        writes=[R("arena", D + j * 512, D + (j + 1) * 512)],
                        tag=f"ev_t2_{j}")
                for f in range(8):
                    for j in range(2):
                        pp = lil_ps[j]
                        P.ins("tensor", (lambda pp=pp, f=f, j=j: lambda e: e.matmul(
                            pp[:, 0:512], v_t2[:, f * 128:(f + 1) * 128],
                            v_uaT[:, j * 512:(j + 1) * 512], start=True, stop=True))(),
                            reads=[k_t2, k_uaT],
                            writes=[k_lilp[j]], tag=f"mm_lt{f}_{j}")
                        wv = weffT[:, f * D + j * 512:f * D + (j + 1) * 512]
                        kw = R("weffT", f * D + j * 512, f * D + (j + 1) * 512)
                        P.ins("vector", (lambda wv=wv, pp=pp: lambda e: e.tensor_tensor(
                            wv, pp[:, 0:512], wv, ALU.add))(),
                            reads=[k_lilp[j], kw], writes=[kw], tag=f"add_lt{f}_{j}")

                if debug:
                    P.ins("gpsimd", lambda e: e.dma_start(
                        dbg_d["lil"][:, D:2 * D], xnf[1][:, :]),
                        reads=[R("xnf1")], dma=True, tag="dbglil1")
                    P.ins("gpsimd", lambda e: e.dma_start(
                        dbg_d["lil"][:, 2 * D:4 * D], arena[:, 0:2 * D]),
                        reads=[R("arena", 0, 2 * D)], dma=True, tag="dbglil2")
                ps_lil.close()
                ps0.close()


            psB2 = ExitStack()
            Tps = [psB2.enter_context(
                nc.psum_tensor(f"T{b}_{i}", [128, 2 * D], F32)) for i in range(2)]
            k_T = [psum_key(t) for t in Tps]
            for f in range(8):
                pp, kpp = Tps[f % 2], k_T[f % 2]
                for g in range(8):
                    lhs = x_sb(g)[:, f * 128:(f + 1) * 128]
                    klhs = k_xsb(g)
                    for q in range(4):
                        off = 2 * D + (q % 2) * 512 if q < 2 else (q % 2) * 512
                        P.ins("tensor", (lambda pp=pp, lhs=lhs, g=g, q=q,
                                         off=off: lambda e: e.matmul(
                            pp[:, q * 512:(q + 1) * 512], lhs,
                            wqkvT[:, g * 3 * D + off:g * 3 * D + off + 512],
                            start=(g == 0), stop=(g == 7)))(),
                            reads=[klhs, R("wqkvT", g * 3 * D, (g + 1) * 3 * D)],
                            writes=[kpp], tag=f"mmT{b}_{f}_{g}_{q}")
                if f % 2 == 0:
                    P.ins("scalar", (lambda pp=pp, f=f: lambda e: e.copy(
                        t_cat(f), pp[:, :]))(), reads=[kpp], writes=[k_tcat(f)],
                        tag=f"evT{b}_{f}")
                else:
                    P.ins("vector", (lambda pp=pp, f=f: lambda e: e.tensor_copy(
                        t_cat(f), pp[:, :]))(), reads=[kpp], writes=[k_tcat(f)],
                        tag=f"evT{b}_{f}")
            psB2.close()
            psB1 = ExitStack()
            sums_ps = psB1.enter_context(
                nc.psum_tensor(f"sums{b}", [1, 3 * D], F32))
            k_sums = psum_key(sums_ps)
            ctpB1 = psB1.enter_context(
                nc.psum_tensor(f"ctpBB{b}", [128, 4], F32))
            k_ctpB1 = psum_key(ctpB1)
            for c in range(8):
                P.ins("tensor", (lambda c=c: lambda e: e.transpose(
                    ctpB1[:, 0:1], RS(0)[0:1, c * 128:(c + 1) * 128],
                    ident_f[0:1, 0:1]))(),
                    reads=[kRS(0), R("ident_f")], writes=[k_ctpB1],
                    tag=f"tp_xs{b}_{c}")
                P.ins("scalar", (lambda c=c: lambda e: e.copy(
                    CF(24 + c), ctpB1[:, 0:1]))(), reads=[k_ctpB1],
                    writes=[kCF(24 + c)], tag=f"ev_xs{b}_{c}")
            P.ins("vector", lambda e: e.tensor_copy(CB(0, 8), CF(24, 8)),
                  reads=[kCF(24, 8)], writes=[kCB(0, 8)], tag=f"xnsum_bf{b}")
            for c in range(8):
                for j in range(6):
                    P.ins("tensor", (lambda c=c, j=j: lambda e: e.matmul(
                        sums_ps[0:1, j * 512:(j + 1) * 512], CB(0 + c),
                        wqkvT[:, c * 3 * D + j * 512:c * 3 * D + (j + 1) * 512],
                        start=(c == 0), stop=(c == 7)))(),
                        reads=[kCB(c), R("wqkvT", c * 3 * D, (c + 1) * 3 * D)],
                        writes=[k_sums], tag=f"mm_sums{b}_{c}_{j}")
            P.ins("scalar", lambda e: e.copy(RS(1), sums_ps[0:1, 0:D]),
                  reads=[k_sums], writes=[kRS(1)], tag=f"ev_qsum0{b}")
            P.ins("vector", lambda e: e.tensor_copy(RB(3), sums_ps[0:1, D:2 * D]),
                  reads=[k_sums], writes=[kRB(3)], tag=f"ksum_bf{b}")
            P.ins("vector", lambda e: e.scalar_tensor_tensor(
                RB(4), RS(3), 1.0, sums_ps[0:1, 2 * D:3 * D], ALU.mult, ALU.add),
                reads=[kRS(3), k_sums], writes=[kRB(4)], tag=f"vsumS{b}")
            P.ins("vector", lambda e: e.tensor_copy(RB(5), sums_ps[0:1, 0:D]),
                  reads=[k_sums], writes=[kRB(5)], tag=f"qsum0_bf{b}")
            P.ins("vector", lambda e: e.scalar_tensor_tensor(
                RB(6), RS(2), float(SEQ), sums_ps[0:1, 0:D], ALU.mult, ALU.add),
                reads=[kRS(2), k_sums], writes=[kRB(6)], tag=f"qsum0S{b}")
            psB1.close()

            psC1 = ExitStack()
            Mps = [psC1.enter_context(
                nc.psum_tensor(f"M{b}_{i}", [128, 128], F32)) for i in range(2)]
            Gps = [psC1.enter_context(
                nc.psum_tensor(f"G{b}_{i}", [128, 128], F32)) for i in range(3)]
            Hps = psC1.enter_context(nc.psum_tensor(f"H{b}", [128, 128], F32))
            GHps = psC1.enter_context(nc.psum_tensor(f"GH{b}", [128, 128], F32))
            k_M = [psum_key(t) for t in Mps]
            k_G = [psum_key(t) for t in Gps]
            k_H = psum_key(Hps)
            k_GH = psum_key(GHps)
            for c in range(8):
                mp, kmp = Mps[c % 2], k_M[c % 2]
                gp, kgp = Gps[c % 2], k_G[c % 2]
                for f in range(8):
                    P.ins("tensor", (lambda mp=mp, f=f, c=c: lambda e: e.matmul(
                        mp[:, :], t_cat(f)[:, c * 128:(c + 1) * 128],
                        wqkvT[:, f * 3 * D + D + c * 128:
                              f * 3 * D + D + (c + 1) * 128],
                        start=(f == 0), stop=False))(),
                        reads=[k_tcat(f), R("wqkvT", f * 3 * D, (f + 1) * 3 * D)],
                        writes=[kmp], tag=f"mmM{b}_{c}_{f}")
                P.ins("tensor", (lambda mp=mp, c=c: lambda e: e.matmul(
                    mp[:, :], RB(4, c * 128, 128), RB(1, c * 128, 128),
                    start=False, stop=False))(),
                    reads=[kRB(4), kRB(1)], writes=[kmp], tag=f"mmMr1{b}_{c}")
                P.ins("tensor", (lambda mp=mp, c=c: lambda e: e.matmul(
                    mp[:, :], RB(2, c * 128, 128), RB(3, c * 128, 128),
                    start=False, stop=True))(),
                    reads=[kRB(2), kRB(3)], writes=[kmp], tag=f"mmMr2{b}_{c}")
                for f in range(8):
                    P.ins("tensor", (lambda gp=gp, f=f, c=c: lambda e: e.matmul(
                        gp[:, :], t_cat(f)[:, D + c * 128:D + (c + 1) * 128],
                        wqkvT[:, f * 3 * D + c * 128:f * 3 * D + (c + 1) * 128],
                        start=(f == 0), stop=False))(),
                        reads=[k_tcat(f), R("wqkvT", f * 3 * D, (f + 1) * 3 * D)],
                        writes=[kgp], tag=f"mmG{b}_{c}_{f}")
                P.ins("tensor", (lambda gp=gp, c=c: lambda e: e.matmul(
                    gp[:, :], RB(5, c * 128, 128), RB(0, c * 128, 128),
                    start=False, stop=False))(),
                    reads=[kRB(5), kRB(0)], writes=[kgp], tag=f"mmGr1{b}_{c}")
                P.ins("tensor", (lambda gp=gp, c=c: lambda e: e.matmul(
                    gp[:, :], RB(0, c * 128, 128), RB(6, c * 128, 128),
                    start=False, stop=True))(),
                    reads=[kRB(0), kRB(6)], writes=[kgp], tag=f"mmGr2{b}_{c}")
                for jj in range(4):
                    eng_ = "scalar" if jj % 2 == 0 else "vector"
                    fn_ = (lambda mp=mp, c=c, jj=jj, eng_=eng_: lambda e:
                           (e.copy if eng_ == "scalar" else e.tensor_copy)(
                        mblkT[32 * jj:32 * (jj + 1),
                              c * 128 + 32 * jj:c * 128 + 32 * (jj + 1)],
                        mp[32 * jj:32 * (jj + 1), 32 * jj:32 * (jj + 1)]))()
                    P.ins(eng_, fn_, reads=[kmp],
                          writes=[R("mblkT", c * 128, (c + 1) * 128)],
                          tag=f"exM{b}_{c}_{jj}")
                P.ins("tensor", (lambda c=c: lambda e: e.matmul(
                    Hps[:, :], mblkT[:, c * 128:(c + 1) * 128],
                    mblkT[:, c * 128:(c + 1) * 128], start=True, stop=True))(),
                    reads=[R("mblkT", c * 128, (c + 1) * 128)], writes=[k_H],
                    tag=f"mmH{b}_{c}")
                P.ins("scalar", lambda e: e.copy(h_sb, Hps[:, :]),
                      reads=[k_H], writes=[R("sqscr", 256, 384)], tag=f"evH{b}_{c}")
                P.ins("vector", (lambda gp=gp: lambda e: e.tensor_tensor(
                    GHps[:, :], gp[:, :], h_sb, ALU.mult))(),
                    reads=[kgp, R("sqscr", 256, 384)], writes=[k_GH],
                    tag=f"gh{b}_{c}")
                P.ins("vector", (lambda c=c: lambda e: e.reduce_sum(
                    CF(40 + c), GHps[:, :], axis=AX.X))(),
                    reads=[k_GH], writes=[kCF(40 + c)],
                    tag=f"ghcol{b}_{c}")

            for c in range(8):
                P.ins("vector", (lambda c=c: lambda e: e.transpose(
                    mblk[:, c * 128:(c + 1) * 128],
                    mblkT[:, c * 128:(c + 1) * 128]))(),
                    reads=[R("mblkT", c * 128, (c + 1) * 128)],
                    writes=[R("mblk", c * 128, (c + 1) * 128)], tag=f"vtM{b}_{c}")

            psQC = ExitStack()
            ctpB = [psQC.enter_context(
                nc.psum_tensor(f"ctpB{b}_{i}", [128, 4], F32)) for i in range(1)]
            k_ctpB = [psum_key(t) for t in ctpB]
            for c in range(8):
                pp, kpp = ctpB[0], k_ctpB[0]
                P.ins("tensor", (lambda pp=pp, c=c: lambda e: e.transpose(
                    pp[:, 0:1], RS(1)[0:1, c * 128:(c + 1) * 128],
                    ident_f[0:1, 0:1]))(),
                    reads=[kRS(1), R("ident_f")], writes=[kpp],
                    tag=f"tp_qs{b}_{c}")
                P.ins("scalar", (lambda pp=pp, c=c: lambda e: e.copy(
                    CF(32 + c), pp[:, 0:1]))(), reads=[kpp],
                    writes=[kCF(32 + c)], tag=f"ev_qs{b}_{c}")
            P.ins("vector", lambda e: e.scalar_tensor_tensor(
                CB(8, 8), CF(16, 8), float(SEQ), CF(32, 8), ALU.mult, ALU.add),
                reads=[kCF(16, 8), kCF(32, 8)], writes=[kCB(8, 8)],
                tag=f"qsumcol_bf{b}")
            psQC.close()
            if debug:
                P.ins("gpsimd", (lambda b=b: lambda e: e.dma_start(
                    dbg_d["MT"][b], mblkT[:, :]))(),
                    reads=[R("mblkT")], dma=True, tag=f"dbgMT{b}")
            psC1.close()

            psC1t = ExitStack()
            csum_ps = psC1t.enter_context(
                nc.psum_tensor(f"csum{b}", [1, D], F32))
            u_ps = psC1t.enter_context(nc.psum_tensor(f"u{b}", [1, D], F32))
            stat_ps = psC1t.enter_context(
                nc.psum_tensor(f"stat{b}", [128, 8], F32))
            ctpC = [psC1t.enter_context(
                nc.psum_tensor(f"ctpC{b}_{i}", [128, 4], F32)) for i in range(2)]
            k_cs = psum_key(csum_ps)
            k_u = psum_key(u_ps)
            k_st = psum_key(stat_ps)
            k_ctpC = [psum_key(t) for t in ctpC]
            for c in range(8):
                P.ins("tensor", (lambda c=c: lambda e: e.matmul(
                    csum_ps[0:1, c * 128:(c + 1) * 128], CB(8 + c),
                    mblk[:, c * 128:(c + 1) * 128],
                    start=(c % 4 == 0), stop=(c % 4 == 3)))(),
                    reads=[kCB(8 + c), R("mblk", c * 128, (c + 1) * 128)],
                    writes=[k_cs], tag=f"mmcs{b}_{c}")
            for c in range(8):
                P.ins("tensor", (lambda c=c: lambda e: e.matmul(
                    u_ps[0:1, c * 128:(c + 1) * 128], CB(16 + c),
                    mblk[:, c * 128:(c + 1) * 128],
                    start=(c % 4 == 0), stop=(c % 4 == 3)))(),
                    reads=[kCB(16 + c), R("mblk", c * 128, (c + 1) * 128)],
                    writes=[k_u], tag=f"mmu{b}_{c}")
            P.ins("scalar", lambda e: e.copy(RS(1), csum_ps[0:1, :]),
                  reads=[k_cs], writes=[kRS(1)], tag=f"ev_cs{b}")
            P.ins("scalar", lambda e: e.copy(RS(5), u_ps[0:1, :]),
                  reads=[k_u], writes=[kRS(5)], tag=f"ev_u{b}")
            for c in range(8):
                pp, kpp = ctpC[c % 2], k_ctpC[c % 2]
                P.ins("tensor", (lambda pp=pp, c=c: lambda e: e.transpose(
                    pp[:, 0:1], RS(1)[0:1, c * 128:(c + 1) * 128],
                    ident_f[0:1, 0:1]))(),
                    reads=[kRS(1), R("ident_f")], writes=[kpp],
                    tag=f"tp_cs{b}_{c}")
                P.ins("scalar", (lambda pp=pp, c=c: lambda e: e.copy(
                    CF(56 + c), pp[:, 0:1]))(), reads=[kpp],
                    writes=[kCF(56 + c)], tag=f"ev_csc{b}_{c}")
            for c in range(8):
                pp, kpp = ctpC[c % 2], k_ctpC[c % 2]
                P.ins("tensor", (lambda pp=pp, c=c: lambda e: e.transpose(
                    pp[:, 0:1], RS(5)[0:1, c * 128:(c + 1) * 128],
                    ident_f[0:1, 0:1]))(),
                    reads=[kRS(5), R("ident_f")], writes=[kpp],
                    tag=f"tp_u{b}_{c}")
                P.ins("scalar", (lambda pp=pp, c=c: lambda e: e.copy(
                    CF(48 + c), pp[:, 0:1]))(), reads=[kpp],
                    writes=[kCF(48 + c)], tag=f"ev_uc{b}_{c}")
            P.ins("tensor", lambda e: e.matmul(
                stat_ps[0:4, 0:8], sel[:, :], CF(40, 8), start=True, stop=True),
                reads=[R("sel"), kCF(40, 8)], writes=[k_st], tag=f"mmssq{b}")
            P.ins("scalar", lambda e: e.copy(st44[0:4, 0:8], stat_ps[0:4, 0:8]),
                  reads=[k_st], writes=[R("st44", 0, 8)], tag=f"ev_ssq44{b}")
            P.ins("tensor", lambda e: e.matmul(
                stat_ps[0:4, 0:8], sel[:, :], CF(56, 8), start=True, stop=True),
                reads=[R("sel"), kCF(56, 8)], writes=[k_st], tag=f"mmcs44{b}")
            P.ins("scalar", lambda e: e.copy(st44[0:4, 8:16], stat_ps[0:4, 0:8]),
                  reads=[k_st], writes=[R("st44", 8, 16)], tag=f"ev_cs44{b}")
            P.ins("vector", lambda e: e.tensor_scalar(
                st44[0:4, 16:24], st44[0:4, 8:16], 1.0 / (32 * SEQ), None,
                ALU.mult), reads=[R("st44", 8, 16)], writes=[R("st44", 16, 24)],
                tag=f"mean44{b}")
            P.ins("vector", lambda e: e.tensor_tensor(
                st44[0:4, 24:32], st44[0:4, 16:24], st44[0:4, 16:24], ALU.mult),
                reads=[R("st44", 16, 24)], writes=[R("st44", 24, 32)],
                tag=f"msq44{b}")
            P.ins("vector", lambda e: e.scalar_tensor_tensor(
                st44[0:4, 32:40], st44[0:4, 0:8], 1.0 / (32 * SEQ),
                st44[0:4, 24:32], ALU.mult, ALU.subtract),
                reads=[R("st44", 0, 8), R("st44", 24, 32)],
                writes=[R("st44", 32, 40)], tag=f"var44{b}")
            P.ins("scalar", lambda e: e.activation(
                st44[0:4, 40:48], st44[0:4, 32:40], AF.Sqrt,
                bias=epscol[0:4, :]),
                reads=[R("st44", 32, 40), R("epscol")],
                writes=[R("st44", 40, 48)], tag=f"sd44{b}")
            P.ins("vector", lambda e: e.reciprocal(
                st44[0:4, 48:56], st44[0:4, 40:48]),
                reads=[R("st44", 40, 48)], writes=[R("st44", 48, 56)],
                tag=f"rstd44{b}")
            P.ins("tensor", lambda e: e.matmul(
                stat_ps[:, 0:8], selT[:, :], st44[0:4, 48:56],
                start=True, stop=True),
                reads=[R("selT"), R("st44", 48, 56)], writes=[k_st],
                tag=f"bc_rstd{b}")
            P.ins("vector", lambda e: e.tensor_tensor(
                CF(32, 8), stat_ps[:, 0:8], CF(0, 8), ALU.mult),
                reads=[k_st, kCF(0, 8)], writes=[kCF(32, 8)], tag=f"scols{b}")
            P.ins("tensor", lambda e: e.matmul(
                stat_ps[:, 0:8], selT[:, :], st44[0:4, 16:24],
                start=True, stop=True),
                reads=[R("selT"), R("st44", 16, 24)], writes=[k_st],
                tag=f"bc_mean{b}")
            P.ins("vector", lambda e: e.tensor_tensor(
                sqscr[:, 128:136], stat_ps[:, 0:8], CF(32, 8), ALU.mult),
                reads=[k_st, kCF(32, 8)], writes=[R("sqscr", 128, 136)],
                tag=f"mrcols{b}")
            P.ins("vector", lambda e: e.tensor_tensor(
                CF(40, 8), CF(8, 8), sqscr[:, 128:136], ALU.subtract),
                reads=[kCF(8, 8), R("sqscr", 128, 136)], writes=[kCF(40, 8)],
                tag=f"ttcols{b}")
            P.ins("vector", lambda e: e.tensor_tensor(
                sqscr[:, 136:144], CF(48, 8), CF(32, 8), ALU.mult),
                reads=[kCF(48, 8), kCF(32, 8)], writes=[R("sqscr", 136, 144)],
                tag=f"su{b}")
            P.ins("vector", lambda e: e.tensor_tensor(
                CB(24, 8), sqscr[:, 136:144], CF(40, 8), ALU.add),
                reads=[R("sqscr", 136, 144), kCF(40, 8)], writes=[kCB(24, 8)],
                tag=f"wcols{b}")
            if debug:
                P.ins("gpsimd", (lambda b=b: lambda e: e.dma_start(
                    dbg_d["cols"][b], cols_f[:, :]))(),
                    reads=[R("cols_f")], dma=True, tag=f"dbgcols{b}")
            psC1t.close()

            psC2a = ExitStack()
            beta_ps = psC2a.enter_context(
                nc.psum_tensor(f"beta{b}", [1, D], F32))
            k_beta = psum_key(beta_ps)
            for c in range(8):
                for j in range(2):
                    P.ins("tensor", (lambda c=c, j=j: lambda e: e.matmul(
                        beta_ps[0:1, j * 512:(j + 1) * 512], CB(24 + c),
                        outwT[:, c * D + j * 512:c * D + (j + 1) * 512],
                        start=(c == 0), stop=(c == 7)))(),
                        reads=[kCB(24 + c), R("outwT", c * D, (c + 1) * D)],
                        writes=[k_beta], tag=f"mmbeta{b}_{c}_{j}")
            P.ins("scalar", lambda e: e.copy(RS(5), beta_ps[0:1, :]),
                  reads=[k_beta], writes=[kRS(5)], tag=f"ev_beta{b}")
            P.ins("vector", lambda e: e.tensor_tensor(
                RB(7), RS(5), RS(4), ALU.add),
                reads=[kRS(5), kRS(4)], writes=[kRB(7)], tag=f"bias_bf{b}")
            if debug:
                P.ins("gpsimd", (lambda b=b: lambda e: e.dma_start(
                    dbg_d["rows"][b], rows_f[0:1, :]))(),
                    reads=[R("rows_f")], dma=True, tag=f"dbgrows{b}")
            psC2a.close()

            if debug and b == 0:
                P.ins("gpsimd", lambda e: e.dma_start(
                    dbg_d["weffT"], weffT[:, :]),
                    reads=[R("weffT")], dma=True, tag="dbgweff")
                P.ins("gpsimd", lambda e: e.dma_start(
                    dbg_d["wqnat"], wq_nat[:, :]),
                    reads=[R("wq_nat")], dma=True, tag="dbgwqnat")
            psC2 = ExitStack()
            Pps = [psC2.enter_context(
                nc.psum_tensor(f"P{b}_{i}", [128, D], F32)) for i in range(2)]
            ATps = [psC2.enter_context(
                nc.psum_tensor(f"AT{b}_{i}", [128, D], F32)) for i in range(2)]
            k_P = [psum_key(t) for t in Pps]
            k_AT = [psum_key(t) for t in ATps]
            for c in range(8):
                ms, kms = mblkTs[c % 2], R(f"mblkTs{c % 2}")
                P.ins("scalar", (lambda ms=ms, c=c: lambda e: e.activation(
                    ms[:, :], mblkT[:, c * 128:(c + 1) * 128], AF.Identity,
                    scale=CF(32 + c)))(),
                    reads=[R("mblkT", c * 128, (c + 1) * 128), kCF(32 + c)],
                    writes=[kms], tag=f"mts{b}_{c}")
                pp, kpp = Pps[c % 2], k_P[c % 2]
                for j in range(2):
                    P.ins("tensor", (lambda pp=pp, ms=ms, c=c, j=j:
                                     lambda e: e.matmul(
                        pp[:, j * 512:(j + 1) * 512], ms[:, :],
                        outwT[:, c * D + j * 512:c * D + (j + 1) * 512],
                        start=True, stop=True))(),
                        reads=[kms, R("outwT", c * D, (c + 1) * D)],
                        writes=[kpp], tag=f"mmP{b}_{c}_{j}")
                P.ins("scalar", (lambda pp=pp, c=c: lambda e: e.copy(
                    p_sb(c), pp[:, :]))(), reads=[kpp], writes=[k_psb(c)],
                    tag=f"evP{b}_{c}")
            if debug:
                P.ins("gpsimd", (lambda b=b: lambda e: e.dma_start(
                    dbg_d["P"][b], arena2[:, :]))(),
                    reads=[R("arena2")], dma=True, tag=f"dbgP{b}")
            for f in range(8):
                pp, kpp = ATps[f % 2], k_AT[f % 2]
                for e_ in range(8):
                    for j in range(2):
                        P.ins("tensor", (lambda pp=pp, f=f, e_=e_, j=j:
                                         lambda e: e.matmul(
                            pp[:, j * 512:(j + 1) * 512],
                            wq_nat[:, e_ * D + f * 128:e_ * D + (f + 1) * 128],
                            p_sb(e_)[:, j * 512:(j + 1) * 512],
                            start=(e_ == 0), stop=(e_ == 7)))(),
                            reads=[R("wq_nat", e_ * D, (e_ + 1) * D),
                                   k_psb(e_)], writes=[kpp],
                            tag=f"mmAT{b}_{f}_{e_}_{j}")
                kct = k_ct(f)
                P.ins("vector", (lambda pp=pp, f=f: lambda e: e.tensor_tensor(
                    ct_sb(f), pp[:, :], weffT[:, f * D:(f + 1) * D], ALU.add))(),
                    reads=[kpp, R("weffT", f * D, (f + 1) * D)], writes=[kct],
                    tag=f"ct{b}_{f}")
            if debug:
                P.ins("gpsimd", (lambda b=b: lambda e: e.dma_start(
                    dbg_d["CT"][b], arena[:, 8 * D:16 * D]))(),
                    reads=[R("arena", 8 * D, 16 * D)], dma=True, tag=f"dbgCT{b}")
            psC2.close()

            psF = ExitStack()
            Ops = [psF.enter_context(
                nc.psum_tensor(f"O{b}_{i}", [128, D], F32)) for i in range(3)]
            Fps = [psF.enter_context(
                nc.psum_tensor(f"ftp{b}_{i}", [128, 128], BF16)) for i in range(2)]
            k_O = [psum_key(t) for t in Ops]
            k_F = [psum_key(t) for t in Fps]
            def fxv(i):
                p, h = i // 2, i % 2
                lo = (p % 2) * 4 * D + h * 2 * D
                return (arena2[:, lo:lo + 2 * D].bitcast(F32),
                        R("arena2", lo, lo + 2 * D))

            def f_head(i):
                if i % 2 == 0:
                    p = i // 2
                    lo = (p % 2) * 4 * D
                    dst = arena2[:, lo:lo + 4 * D].bitcast(F32)
                    kd = R("arena2", lo, lo + 4 * D)
                    P.ins("sync", (lambda dst=dst, b=b, i=i: lambda e:
                                   e.dma_start(
                        dst.rearrange("p (a f) -> p a f", a=2),
                        xs[b, i * 128:(i + 2) * 128, :].rearrange(
                            "(a p) f -> p a f", p=128)))(),
                        writes=[kd], dma=True, tag=f"dma_xF{b}_{i}")
                sl, ksl = fxv(i)
                xv, kxv = xnf[i % 2], R(f"xnf{i % 2}")
                P.ins("scalar", (lambda sl=sl, xv=xv, i=i: lambda e: e.activation(
                    xv[:, :], sl, AF.Identity,
                    bias=lncols[:, 16 + i:17 + i], scale=lncols[:, i:i + 1]))(),
                    reads=[ksl, R("lncols", i, i + 1), R("lncols", 16 + i, 17 + i)],
                    writes=[kxv], tag=f"xnF{b}_{i}")
                for c in range(8):
                    fp, kfp = Fps[c % 2], k_F[c % 2]
                    P.ins("tensor", (lambda fp=fp, xv=xv, c=c: lambda e:
                                     e.transpose(fp[:, :],
                                                 xv[:, c * 128:(c + 1) * 128],
                                                 ident_bf[:, :]))(),
                        reads=[kxv, R("ident_bf")], writes=[kfp],
                        tag=f"ftp{b}_{i}_{c}")
                    if c % 2 == 0:
                        P.ins("scalar", (lambda fp=fp, xv=xv, c=c: lambda e:
                                         e.copy(xv[:, c * 128:(c + 1) * 128],
                                                fp[:, :]))(),
                            reads=[kfp], writes=[kxv], tag=f"fev{b}_{i}_{c}")
                    else:
                        P.ins("vector", (lambda fp=fp, xv=xv, c=c: lambda e:
                                         e.tensor_copy(
                                             xv[:, c * 128:(c + 1) * 128],
                                             fp[:, :]))(),
                            reads=[kfp], writes=[kxv], tag=f"fev{b}_{i}_{c}")

            def f_tail(i):
                xt, kxt = xnf[i % 2], R(f"xnf{i % 2}")
                op, kop = Ops[i % 3], k_O[i % 3]
                for f in range(8):
                    for j in range(2):
                        P.ins("tensor", (lambda op=op, xt=xt, f=f, j=j:
                                         lambda e: e.matmul(
                            op[:, j * 512:(j + 1) * 512],
                            xt[:, f * 128:(f + 1) * 128],
                            ct_sb(f)[:, j * 512:(j + 1) * 512],
                            start=(f == 0), stop=False))(),
                            reads=[kxt, k_ct(f)], writes=[kop],
                            tag=f"mmO{b}_{i}_{f}_{j}")
                for j in range(2):
                    P.ins("tensor", (lambda op=op, j=j: lambda e: e.matmul(
                        op[:, j * 512:(j + 1) * 512], ones_row,
                        RB(7, j * 512, 512), start=False, stop=True))(),
                        reads=[k_ones_row, kRB(7)], writes=[kop],
                        tag=f"mmOb{b}_{i}_{j}")

            def f_fin(i):
                sl, ksl = fxv(i)
                op, kop = Ops[i % 3], k_O[i % 3]
                P.ins("vector", (lambda sl=sl, op=op: lambda e: e.tensor_tensor(
                    sl, sl, op[:, :], ALU.add))(),
                    reads=[ksl, kop], writes=[ksl], tag=f"add{b}_{i}")
                if i % 2 == 1:
                    p = i // 2
                    lo = (p % 2) * 4 * D
                    srcv = arena2[:, lo:lo + 4 * D].bitcast(F32)
                    ks2 = R("arena2", lo, lo + 4 * D)
                    P.ins("gpsimd", (lambda srcv=srcv, b=b, i=i: lambda e:
                                     e.dma_start(
                        ys[b, (i - 1) * 128:(i + 1) * 128, :].rearrange(
                            "(a p) f -> p a f", p=128),
                        srcv.rearrange("p (a f) -> p a f", a=2)))(),
                        reads=[ks2], dma=True, tag=f"out{b}_{i}")

            for i in range(NT):
                if i >= 2:
                    f_tail(i - 2)
                    f_fin(i - 2)
                f_head(i)
            f_tail(NT - 2)
            f_fin(NT - 2)
            f_tail(NT - 1)
            f_fin(NT - 1)
            psF.close()

        import os as _os
        if _os.environ.get("KERNEL_DEBUG_SBUF"):
            print("sbuf bytes remaining (inside build):", nc.sbuf_bytes_remaining)
        if _os.environ.get("KERNEL_DEBUG_DEPS"):
            want = _os.environ["KERNEL_DEBUG_DEPS"].split(",")
            for i, insd in enumerate(P.instrs):
                if insd["tag"] in want:
                    print(f"instr {i} tag={insd['tag']} eng={insd['engine']} "
                          f"dma={insd['dma']}")
                    for d in sorted(insd["deps"]):
                        dd = P.instrs[d]
                        print(f"   dep {d}: {dd['tag']} {dd['engine']} dma={dd['dma']}")
        sems = {e: E(nc.semaphore(f"s_{e}")) for e in ENGINES}
        dsems = {e: [E(nc.semaphore(f"sd_{e}_{j}"))
                     for j in range(Sched.NDMA_LANES[e])] for e in ENGINES}
        P.emit(nc, block, sems, dsems)

    return nc, dbg_d


_NC_CACHE = {}


def _get_nc(debug=False):
    key = bool(debug)
    if key not in _NC_CACHE:
        _NC_CACHE[key] = build_nc(debug)
    return _NC_CACHE[key]


def kernel(**inputs):
    x = np.ascontiguousarray(np.asarray(inputs["L0_attn_x"], np.float32))
    feed = {
        "wqkv": np.ascontiguousarray(np.asarray(inputs["L0_attn_Wqkv_w"], np.float32)),
        "bqkv": np.ascontiguousarray(np.asarray(inputs["L0_attn_Wqkv_b"], np.float32)),
        "gnw": np.ascontiguousarray(np.asarray(inputs["L0_attn_gn_w"], np.float32)),
        "gnb": np.ascontiguousarray(np.asarray(inputs["L0_attn_gn_b"], np.float32)),
        "outw": np.ascontiguousarray(np.asarray(inputs["L0_attn_out_w"], np.float32)),
        "outb": np.ascontiguousarray(np.asarray(inputs["L0_attn_out_b"], np.float32)),
        "wff": np.ascontiguousarray(np.asarray(inputs["L0_ffn_W"], np.float32)),
        "bff": np.ascontiguousarray(np.asarray(inputs["L0_ffn_b"], np.float32)),
        "ld": np.ascontiguousarray(np.asarray(inputs["L0_ffn_lora_down"], np.float32)),
        "lu": np.ascontiguousarray(np.asarray(inputs["L0_ffn_lora_up"], np.float32)),
        "da": np.ascontiguousarray(np.asarray(inputs["L0_ffn_down_aux"], np.float32)),
        "ua": np.ascontiguousarray(np.asarray(inputs["L0_ffn_up_aux"], np.float32)),
    }
    nc, _ = _get_nc(False)
    in_maps = []
    for core in range(NCORES):
        m = dict(feed)
        m["xs"] = np.ascontiguousarray(x[core * NB:(core + 1) * NB])
        in_maps.append(m)
    res = run_bass_kernel_spmd(nc, in_maps, core_ids=list(range(NCORES)))
    return np.concatenate([r["ys"] for r in res.results], axis=0)


if __name__ == "__main__":
    nc, _ = build_nc(False)
    counts = {}
    print("built ok")
